# version 03
# baseline (speedup 1.0000x reference)
import numpy as np
import ml_dtypes
import concourse.bass as bass
import concourse.mybir as mybir
from concourse.bass_utils import run_bass_kernel_spmd

F32 = mybir.dt.float32
BF16 = mybir.dt.bfloat16
ALU = mybir.AluOpType
AF = mybir.ActivationFunctionType
AX = mybir.AxisListType
BF = ml_dtypes.bfloat16

H, HD, D, NT, OT = 16, 128, 2048, 2048, 1024
NDC, NTC = 16, 16
KAPPA = float(HD) ** -0.5
MASKPRE = -2048.0
NEG_BIG = -3.0e6
SUP = 1.0e6
SEL_THR = -1.0e5
EPS = 1e-5

DEBUG_TAPS = ()


def _perm_ids(s):
    ids = []
    for j in range(8):
        ids.extend(range((2 * j + s) * 128, (2 * j + s) * 128 + 128))
    for j in range(8):
        ids.extend(range((2 * j + 1 - s) * 128, (2 * j + 1 - s) * 128 + 128))
    return np.array(ids)


def _consts_for_parity(s):
    perm = _perm_ids(s)
    own = perm[0:OT]
    c = {}
    inv = 1.0 / 10000.0 ** (np.arange(64) / 64.0)
    th = perm[:, None].astype(np.float64) * inv[None, :]
    ck, sk = np.cos(th).T, np.sin(th).T
    c["cosk"] = np.tile(ck, (2, 1)).astype(BF)
    c["sink"] = np.concatenate([-sk, sk], 0).astype(BF)
    c["ident"] = np.eye(128, dtype=BF)
    c["ones_lhs"] = np.ones((128, 128), dtype=BF)
    c["eexp"] = (np.arange(32)[:, None] == (np.arange(NT)[None, :] // 64)).astype(BF)
    am = np.zeros((128, NTC, 32), dtype=np.float32)
    for cc in range(NTC):
        for t in range(128):
            am[t, cc, 2 * cc + t // 64] = 1.0 / 64.0
    c["ameans"] = am.reshape(128, NTC * 32).astype(BF)
    tp = np.zeros((128, 95), dtype=BF)
    tp[:, 47] = 1.0
    c["toep"] = tp
    tp32 = np.zeros((32, 95), dtype=BF)
    tp32[:, 47] = 1.0
    c["toep32"] = tp32
    rs = np.zeros((48, 48, 128), dtype=BF)
    for r in range(48):
        rs[r, r, :] = 1.0
    c["rowsel16"] = rs.reshape(48, 48 * 128)
    blk_seq = perm.reshape(32, 64)[:, 0] // 64
    blk_end = (blk_seq + 1) * 64 - 1
    blk_start = blk_seq * 64
    cm = np.where(own[:, None] >= blk_end[None, :], 0.0, MASKPRE).astype(np.float32)
    c["cmp_maskadd"] = np.ascontiguousarray(
        cm.reshape(8, 128, 32).transpose(1, 0, 2).reshape(128, 256)).astype(BF)
    c["cmp_maskaddT"] = np.ascontiguousarray(cm.T).astype(BF)
    cs_ = np.zeros((OT, 32), np.float32)
    cs_[np.arange(OT), np.arange(OT) // 64] = NEG_BIG
    c["cur_sup"] = np.ascontiguousarray(
        cs_.reshape(8, 128, 32).transpose(1, 0, 2).reshape(128, 256)).astype(BF)
    c["cb_min"] = np.where(blk_start[:, None] <= own[None, :], 0.0, MASKPRE).astype(BF)
    tq = np.arange(128)[None, :]
    tk = np.arange(128)[:, None]
    B1 = ((tq - tk >= 0) & (tq - tk <= 64)).astype(np.float32)
    B2 = (tk - tq >= 64).astype(np.float32)
    Z = np.zeros((128, 128), np.float32)
    gen = [B1, Z, B2] if s == 1 else [B1, B2, Z]
    sl0 = [B1, Z, B2] if s == 1 else [B1, Z, Z]
    c["swa_arr"] = np.stack([np.concatenate(sl0, 1), np.concatenate(gen, 1)],
                            1).reshape(128, 2 * 384).astype(BF)
    c["uptri"] = (-(tk > tq).astype(np.float32)).astype(BF)
    any01 = (own >= 63).astype(np.float32)
    af = np.zeros((48, OT), np.float32)
    af[0:16] = (1.0 - any01)[None, :]
    c["anyfix"] = af.astype(BF)
    return c


def _tile2(w, bk0, bk1, order="km"):
    K_, M_ = w.shape
    t = w.reshape(K_ // bk0, bk0, M_ // bk1, bk1).transpose(0, 2, 1, 3)
    if order == "mk":
        t = t.transpose(1, 0, 2, 3)
    return np.ascontiguousarray(t)


def _stage(ins):
    x = np.asarray(ins["L0_attn_x"], dtype=np.float32)
    shared = {
        "wq": _tile2(np.asarray(ins["L0_attn_Wq"], np.float32), 128, 128, "mk"),
        "wk": _tile2(np.asarray(ins["L0_attn_Wk"], np.float32), 128, 128, "mk"),
        "wv": _tile2(np.asarray(ins["L0_attn_Wv"], np.float32), 128, 512, "mk"),
        "wo": _tile2(np.asarray(ins["L0_attn_Wo"], np.float32), 128, 128, "mk"),
        "wg": np.ascontiguousarray(np.asarray(ins["L0_attn_Wg"], np.float32).reshape(NDC, 128, 48)),
        "wu0": _tile2(np.asarray(ins["L0_ffn_w_gating"], np.float32)[0], 128, 128, "mk"),
        "wu1": _tile2(np.asarray(ins["L0_ffn_w_gating"], np.float32)[1], 128, 128, "mk"),
        "wl": _tile2(np.asarray(ins["L0_ffn_w_linear"], np.float32), 128, 128, "mk"),
        "wmod": _tile2(np.asarray(ins["L0_wmod"], np.float32), 128, 512, "mk"),
    }
    pc = [_consts_for_parity(s) for s in (0, 1)]
    maps = []
    for core in range(8):
        b, s = core // 2, core % 2
        m = dict(shared)
        m.update(pc[s])
        m["xT"] = np.ascontiguousarray(x[b].T[:, _perm_ids(s)])
        m["cT"] = np.ascontiguousarray(np.asarray(ins["L0_c"], np.float32)[b:b + 1].T)
        maps.append(m)
    return maps


class Sem:
    __slots__ = ("h", "n")

    def __init__(self, nc, name):
        self.h = nc.alloc_semaphore(name)
        self.n = 0


class DSem:

    def __init__(self, nc, name, depth=2):
        self.s = [Sem(nc, f"{name}_{k}") for k in range(depth)]
        self.depth = depth
        self.tot = {}
        self.uses = 0

    def handle(self, i):
        return self.s[i % self.depth].h

    def issued(self, i, k=16):
        sub = self.s[i % self.depth]
        sub.n += k
        self.tot[i] = sub.n
        self.uses = max(self.uses, i + 1)

    def wait(self, e, i):
        if i < 0:
            return
        assert i in self.tot, (i, sorted(self.tot)[:4])
        e.wait_ge(self.s[i % self.depth].h, self.tot[i])

    def wait_all(self, e):
        for sub in self.s:
            if sub.n:
                e.wait_ge(sub.h, sub.n)


def build_nc(taps=()):
    nc = bass.Bass("TRN2", target_bir_lowering=False)
    OPS = {e: [] for e in ("sync", "gpsimd", "tensor", "vector", "scalar")}

    def on(e):
        def dec(fn):
            OPS[e].append(fn)
            return fn
        return dec

    def din(name, shape, dt=F32):
        return nc.dram_tensor(name, list(shape), dt, kind="ExternalInput").ap()

    xT = din("xT", (D, NT))
    wq = din("wq", (NDC, NDC, 128, 128))
    wk = din("wk", (NDC, NDC, 128, 128))
    wv = din("wv", (4, NDC, 128, 512))
    wo = din("wo", (NDC, NDC, 128, 128))
    wg = din("wg", (NDC, 128, 48))
    wu0 = din("wu0", (64, NDC, 128, 128))
    wu1 = din("wu1", (64, NDC, 128, 128))
    wl = din("wl", (NDC, 64, 128, 128))
    wmod = din("wmod", (24, 16, 128, 512))
    cT = din("cT", (D, 1))
    cosk_c = din("cosk", (128, NT), BF16)
    sink_c = din("sink", (128, NT), BF16)
    ident_c = din("ident", (128, 128), BF16)
    ones_c = din("ones_lhs", (128, 128), BF16)
    eexp_c = din("eexp", (32, NT), BF16)
    ameans_c = din("ameans", (128, NTC * 32), BF16)
    toep_c = din("toep", (128, 95), BF16)
    toep32_c = din("toep32", (32, 95), BF16)
    rowsel_c = din("rowsel16", (48, 48 * 128), BF16)
    cmpma_c = din("cmp_maskadd", (128, 256), BF16)
    cmpmaT_c = din("cmp_maskaddT", (32, OT), BF16)
    cursup_c = din("cur_sup", (128, 256), BF16)
    cbmin_c = din("cb_min", (32, OT), BF16)
    swaarr_c = din("swa_arr", (128, 2 * 384), BF16)
    uptri_c = din("uptri", (128, 128), BF16)
    anyfix_c = din("anyfix", (48, OT), BF16)

    outT = nc.dram_tensor("outT", [D, OT], F32, kind="ExternalOutput").ap()
    kdram = nc.dram_tensor("kdram", [H, 128, NT], BF16,
                           kind="ExternalOutput" if "k" in taps else "Internal").ap()
    qdram = nc.dram_tensor("qdram", [H, 128, OT], BF16,
                           kind="ExternalOutput" if "q" in taps else "Internal").ap()
    vdram = nc.dram_tensor("vdram", [NTC, 4, 128, 512], BF16,
                           kind="ExternalOutput" if "v" in taps else "Internal").ap()
    tap_t = {}
    tap_shapes = {
        "mod": ([1, 6 * D], F32), "h": ([128, NDC, NT], BF16),
        "g": ([48, OT], F32), "kcmp": ([128, H, 32], BF16),
        "vcmp": ([32, H, 128], BF16), "imp": ([128, H * 256], BF16),
        "maskT": ([32, H, OT], BF16), "pcmpT": ([32, H, OT], BF16),
        "o": ([128, H, OT], BF16), "x2": ([128, NDC, OT], F32),
        "h2": ([128, NDC, OT], BF16), "act": ([128, 16, OT], BF16),
        "coef": ([16, H, 2, 3, 512], BF16),
    }
    for t in taps:
        if t in ("k", "q", "v"):
            continue
        sh, dt = tap_shapes[t]
        tap_t[t] = nc.dram_tensor("tap_" + t, sh, dt, kind="ExternalOutput").ap()

    names = [
        "c_ld", "c2_ld", "wg_ld", "silu_rdy", "mod_ps", "mod_evac",
        "modt2", "mod_rdy",
        "sq_rdy", "sq_free", "stats_ps", "ln_mid", "h_rdy",
        "qk_ps", "kraw_rdy", "rope_out",
        "g_ps", "g_rdy", "v_ps", "v_evac",
        "kc_rdy", "vc_ps", "vc_rdy",
        "scmp_ps", "cm_rdy", "esc_rdy", "imp_init", "topk_done",
        "krfree_a", "qrfree_a", "vrfree_a",
        "krfree_b", "qrfree_b", "vrfree_b",
        "tr_ps", "mt_rdy", "scT_ps", "scT_f", "pcT_rdy",
        "sc_ps", "px", "pvstep", "corr_i", "swa_ps", "swa_pv", "pswa_e", "pswa_m",
        "pv_done", "den_done", "coef_u", "coef_r", "coef_rdy",
        "rep_ps", "rep_ev", "cmb",
        "wo_ps", "x2_rdy",
        "x2b_rdy", "sq2_rdy", "sq2_free", "st2_ps", "ln2_mid", "h2_rdy", "x2_sp",
        "up_ps", "gel_rdy", "act_rdy", "dn_ps", "stg_rdy",
        "tap_st", "c3_ld",
    ]
    S = {n: Sem(nc, n) for n in names}
    DS = {n: DSem(nc, n) for n in (
        "wva", "xw", "wqk", "kacc", "kqw", "vst", "modch", "kr", "qr", "vr")}
    DSAL = {"wmod": "wva", "wvb": "wva", "wul": "wva",
            "xol": "vst",
            "wol": "wqk", "wll": "wqk", "acc": "xw",
            "kswp": "kacc", "kqst": "kqw",
            "kra": "kr", "krb": "kr", "qra": "qr", "qrb": "qr",
            "vra": "vr", "vrb": "vr"}
    for _a, _b in DSAL.items():
        DS[_a] = DS[_b]

    CHN = {"vector": [Sem(nc, "chv"), 0], "scalar": [Sem(nc, "chs"), 0]}

    def ci(instr, eng="vector"):
        CHN[eng][1] += 1
        return instr.then_inc(CHN[eng][0].h, 1)

    def cw(e, eng="vector"):
        e.wait_ge(CHN[eng][0].h, CHN[eng][1])

    sbt = nc.alloc_sbuf_tensor
    ident = sbt("ident_sb", [128, 128], BF16)
    ones = sbt("ones_sb", [128, 128], BF16)
    uptri = sbt("uptri_sb", [128, 128], BF16)
    modv = sbt("modv", [128, 96], F32)
    epsv = sbt("epsv", [128, 1], F32)
    clsc = sbt("clsc", [128, 48], F32)
    mrep = sbt("mrep", [128, 512], BF16)
    varrep = sbt("varrep", [128, 512], F32)
    rstdrep = sbt("rstdrep", [128, 512], BF16)
    sqring = sbt("sqring", [128, 2, 512], BF16)
    esc = sbt("esc", [128, 256], BF16)
    cmadd = sbt("cmadd", [128, 256], F32)
    mxb = sbt("mxb", [128, 128], BF16)
    ropeA = sbt("ropeA", [128, 512], BF16)
    scrB = sbt("scrB", [128, 512], BF16)
    cT_sb = sbt("cT_sb", [128, 16, 1], F32)
    siluT = sbt("siluT", [128, 16, 1], BF16)
    lnv = sbt("lnv", [128, 512], F32)
    onepc = sbt("onepc", [128, 32], F32)
    wg_sb = sbt("wg_sb", [128, NDC, 48], BF16)
    gT = sbt("gT", [48, OT], F32)
    kcmpall = sbt("kcmpall", [128, H, 32], BF16)
    kcf = sbt("kcf", [128, 32], F32)
    modrow_dr = nc.dram_tensor("modrow_dr", [1, 6 * D], F32, kind="Internal").ap()

    AMAX = 186 * 1024
    AR = sbt("ARENA", [128, AMAX], mybir.dt.uint8)
    spans = []

    def V_(name, off_kb, shape, dt, ph):
        off = int(off_kb * 1024)
        per = int(np.prod(shape[1:])) * mybir.dt.size(dt)
        assert off % 256 == 0 and off + per <= AMAX, (name, off, per, AMAX)
        for (n2, o2, p2, ph2) in spans:
            if not (off >= o2 + p2 or o2 >= off + per or ph[1] < ph2[0] or ph2[1] < ph[0]):
                raise AssertionError(f"overlap {name} vs {n2}")
        spans.append((name, off, per, ph))
        ap = AR[0:shape[0], off:off + per].bitcast(dt)
        if len(shape) > 2:
            pat = "p (" + " ".join(f"d{i}" for i in range(1, len(shape))) + ") -> p " + \
                  " ".join(f"d{i}" for i in range(1, len(shape)))
            kw = {f"d{i}": shape[i] for i in range(1, len(shape))}
            ap = ap.rearrange(pat, **kw)
        return ap

    xwin = V_("xwin", 0, [128, 2, NDC, 512], BF16, (0, 0))
    modrow = V_("modrow", 44, [1, 2, 512], F32, (0, 0))
    wmod_sb = V_("wmod", 32, [128, 2, 2, 512], BF16, (0, 0))
    cosk_sb = V_("cosk", 36, [128, NT], BF16, (0, 1))
    sink_sb = V_("sink", 40, [128, NT], BF16, (0, 1))

    wqk_sb = V_("wqk", 0, [128, 2, NDC, 128], BF16, (1, 1))
    wvb_sb = V_("wvb", 8, [128, 2, 512], BF16, (1, 1))
    kraw = V_("kraw", 10, [128, 2, 512], BF16, (1, 1))
    kswp = V_("kswp", 12, [128, 2, 512], BF16, (1, 1))
    kqst = V_("kqst", 14, [128, 2, 2048], BF16, (1, 1))
    vstg = V_("vstg", 22, [128, 2, 512], BF16, (1, 1))

    kr2 = V_("kr2", 0, [128, 2, NT], BF16, (2, 3))
    vr2 = V_("vr2", 8, [128, 2, NTC, 128], BF16, (2, 3))
    qr2 = V_("qr2", 16, [128, 2, OT], BF16, (2, 3))
    vcmpall = V_("vcmpall", 20, [32, H, 128], BF16, (2, 3))
    impa = V_("impa", 24, [128, H * 256], BF16, (2, 3))
    impb = V_("impb", 32, [128, H * 256], BF16, (2, 3))
    selb = V_("selb", 40, [128, H * 256], BF16, (2, 2))
    eexp = V_("eexp", 48, [32, NT], BF16, (2, 3))
    ameans = V_("ameans", 52, [128, NTC * 32], BF16, (2, 2))
    toep = V_("toep", 53, [128, 95], BF16, (2, 3))
    toep32 = V_("toep32", 53.25, [32, 95], BF16, (2, 3))
    rowsel = V_("rowsel", 130, [48, 48 * 128], BF16, (2, 3))
    cmpma = V_("cmpma", 57.5, [128, 256], BF16, (2, 2))
    cmpmaT = V_("cmpmaT", 58, [32, OT], BF16, (2, 3))
    cursup = V_("cursup", 60, [128, 256], BF16, (2, 2))
    cbmin = V_("cbmin", 60.5, [32, OT], BF16, (2, 3))
    swaarr = V_("swaarr", 62.5, [128, 2, 384], BF16, (2, 3))
    anyfix = V_("anyfix", 64, [48, OT], BF16, (2, 3))
    pring = V_("pring", 66, [128, 3, 512], BF16, (3, 3))
    pswa = V_("pswa", 69, [128, 2, 384], BF16, (3, 3))
    pswa2 = V_("pswa2", 70.5, [128, 2, 384], BF16, (3, 3))
    pcmpT = V_("pcmpT", 72, [32, 2, OT], BF16, (3, 3))
    maskT = V_("maskT", 76, [32, 2, OT], BF16, (3, 3))
    mtmp = V_("mtmp", 80, [32, OT], BF16, (3, 3))
    scmpTf = V_("scmpTf", 82, [32, 512], F32, (3, 3))
    coefb = V_("coefb", 84, [48, 2, 512], BF16, (3, 3))
    crep = V_("crep", 90, [128, 3, 512], BF16, (3, 3))
    csc = V_("csc", 93, [128, 3, 512], BF16, (3, 3))
    corrsc = V_("corrsc", 96, [128, 2, 128], BF16, (3, 3))

    oT = V_("oT", 97, [128, H, OT], BF16, (3, 4))
    hT = V_("hT", 122, [128, NDC, NT], BF16, (0, 1))
    xo = V_("xo", 0, [128, 2, OT], F32, (4, 4))
    wo_sb = V_("wo_sb", 8, [128, 2, NDC, 128], BF16, (4, 4))
    x2 = V_("x2", 16, [128, NDC, OT], F32, (4, 5))
    x2b = V_("x2b", 80, [128, NDC, OT], BF16, (5, 5))
    h2sb = V_("h2sb", 112, [128, NDC, OT], BF16, (5, 6))
    act = V_("act", 144, [128, 16, OT], BF16, (6, 6))
    wu_sb = V_("wu_sb", 80, [128, 2, 2, NDC, 128], BF16, (6, 6))
    wl_sb = V_("wl_sb", 96, [128, 2, 16, 128], BF16, (6, 6))
    stage = V_("stage", 104, [128, 2, OT], F32, (6, 6))
    gel_sb = V_("gel", 176, [128, 2, 512], BF16, (6, 6))

    pb = [nc.alloc_psum_tensor(f"pb{i}", [128, 512], F32) for i in range(8)]
    pb0bf = pb[0][0:32, :].bitcast(BF16)

    const_pairs = [
        (ident[:], ident_c), (ones[:], ones_c), (uptri[:], uptri_c),
        (cosk_sb, cosk_c), (sink_sb, sink_c),
    ]
    const_pairs2 = [
        (eexp, eexp_c), (ameans, ameans_c), (toep, toep_c), (toep32, toep32_c),
        (cmpma, cmpma_c), (cmpmaT, cmpmaT_c),
        (cursup, cursup_c), (cbmin, cbmin_c),
        (swaarr.rearrange("p a n -> p (a n)"), swaarr_c), (anyfix, anyfix_c),
    ]

    @on("sync")
    def _(e):
        for dst, src in const_pairs:
            e.dma_start(out=dst, in_=src).then_inc(S["c_ld"].h, 16)
        e.dma_start(out=cT_sb[:], in_=cT.rearrange("(c p) b -> p c b", p=128),
                    allow_slow_non_contiguous=True).then_inc(S["c_ld"].h, 16)
    S["c_ld"].n += 16 * (len(const_pairs) + 1)
    C1 = S["c_ld"].n

    @on("gpsimd")
    def _(e):
        e.dma_start(out=wg_sb[:], in_=wg.rearrange("c p n -> p c n")).then_inc(S["wg_ld"].h, 16)
        for jc in range(24):
            for part in range(8):
                i = jc * 8 + part
                if i >= 2:
                    e.wait_ge(S["mod_ps"].h, i - 1)
                e.dma_start(out=wmod_sb[:, i % 2],
                            in_=wmod[jc, part * 2:part * 2 + 2].rearrange("c p n -> p c n")
                            ).then_inc(DS["wmod"].handle(i), 16)
    S["wg_ld"].n += 16
    for _i in range(192):
        DS["wva"].issued(_i)

    @on("scalar")
    def _(e):
        e.wait_ge(S["c_ld"].h, C1)
        e.activation(siluT[:], cT_sb[:], AF.Silu).then_inc(S["silu_rdy"].h, 1)
    S["silu_rdy"].n += 1

    @on("tensor")
    def _(e):
        e.wait_ge(S["silu_rdy"].h, 1)
        for jc in range(24):
            if jc >= 1:
                e.wait_ge(S["mod_evac"].h, jc)
            for part in range(8):
                i = jc * 8 + part
                DS["wmod"].wait(e, i)
                e.matmul(pb[6][0:1, :], siluT[:, part * 2, :],
                         wmod_sb[:, i % 2, 0, :],
                         start=(part == 0), stop=False)
                e.matmul(pb[6][0:1, :], siluT[:, part * 2 + 1, :],
                         wmod_sb[:, i % 2, 1, :],
                         start=False, stop=(part == 7)).then_inc(S["mod_ps"].h, 1)
    S["mod_ps"].n += 192

    @on("vector")
    def _(e):
        for jc in range(24):
            e.wait_ge(S["mod_ps"].h, (jc + 1) * 8)
            DS["modch"].wait(e, jc - 2)
            e.tensor_copy(modrow[0:1, jc % 2, :], pb[6][0:1, :]
                          ).then_inc(S["mod_evac"].h, 1)
    S["mod_evac"].n += 24

    @on("sync")
    def _(e):
        for jc in range(24):
            e.wait_ge(S["mod_evac"].h, jc + 1)
            e.dma_start(out=modrow_dr[0:1, jc * 512:(jc + 1) * 512],
                        in_=modrow[0:1, jc % 2, :]).then_inc(DS["modch"].handle(jc), 16)
        DS["modch"].wait_all(e)
        e.dma_start(out=modv[:], in_=modrow_dr.rearrange("o (m p) -> (o p) m", p=128),
                    allow_slow_non_contiguous=True).then_inc(S["modt2"].h, 16)
    S["modt2"].n += 16
    for _i in range(24):
        DS["modch"].issued(_i)

    @on("vector")
    def _(e):
        e.wait_ge(S["modt2"].h, 16)
        for c0 in (32, 80):
            sl = modv[:, c0:c0 + 16]
            ci(e.tensor_scalar(clsc[:, 0:16], sl, 0.0, 2.0, ALU.is_ge, ALU.mult))
            cw(e)
            ci(e.tensor_scalar(clsc[:, 16:32], clsc[:, 0:16], 1.0, None, ALU.subtract))
            cw(e)
            ci(e.tensor_tensor(clsc[:, 0:16], sl, clsc[:, 16:32], ALU.mult))
            cw(e)
            ci(e.tensor_scalar(clsc[:, 32:48], clsc[:, 0:16], 0.25, None, ALU.max))
            cw(e)
            ci(e.tensor_tensor(sl, clsc[:, 16:32], clsc[:, 32:48], ALU.mult))
            cw(e)
        ci(e.tensor_scalar(onepc[:, 0:16], modv[:, 16:32], 1.0, None, ALU.add))
        ci(e.tensor_scalar(onepc[:, 16:32], modv[:, 64:80], 1.0, None, ALU.add))
        e.memset(epsv[:], EPS)
        e.engine_nop().then_inc(S["mod_rdy"].h, 1)
    S["mod_rdy"].n += 1

    if "mod" in tap_t:
        @on("sync")
        def _(e):
            e.wait_ge(S["mod_rdy"].h, 1)
            e.dma_start(out=tap_t["mod"], in_=modrow_dr).then_inc(S["tap_st"].h, 16)
        S["tap_st"].n += 16

    @on("gpsimd")
    def _(e):
        for w in range(4):
            if w >= 2:
                e.wait_ge(S["h_rdy"].h, (w - 1) * NDC)
            e.dma_start(out=xwin[:, w % 2],
                        in_=xT.rearrange("(c p) t -> p c t", p=128)[:, :, w * 512:(w + 1) * 512]
                        ).then_inc(DS["xw"].handle(w), 16)

    for _i in range(4):
        DS["xw"].issued(_i)

    @on("scalar")
    def _(e):
        for w in range(4):
            DS["xw"].wait(e, w)
            for dc in range(NDC):
                i = w * NDC + dc
                if i >= 2:
                    e.wait_ge(S["sq_free"].h, i - 1)
                e.activation(sqring[:, i % 2, :], xwin[:, w % 2, dc, :], AF.Square
                             ).then_inc(S["sq_rdy"].h, 1)
            e.wait_ge(S["ln_mid"].h, 2 * w + 1)
            ci(e.activation(lnv[:], varrep[:], AF.Ln, bias=epsv[:]), "scalar")
            cw(e, "scalar")
            e.activation(rstdrep[:], lnv[:], AF.Exp, scale=-0.5).then_inc(S["ln_mid"].h, 1)
    S["sq_rdy"].n += 64

    @on("tensor")
    def _(e):
        e.wait_ge(S["c_ld"].h, C1)
        for w in range(4):
            DS["xw"].wait(e, w)
            if w >= 1:
                e.wait_ge(S["ln_mid"].h, 2 * w - 1)
            for dc in range(NDC):
                i = w * NDC + dc
                mm0 = e.matmul(pb[0][:], ones[:], xwin[:, w % 2, dc, :],
                               start=(dc == 0), stop=(dc == NDC - 1))
                if dc == NDC - 1:
                    mm0.then_inc(S["stats_ps"].h, 1)
                e.wait_ge(S["sq_rdy"].h, i + 1)
                e.matmul(pb[1][:], ones[:], sqring[:, i % 2, :],
                         start=(dc == 0), stop=(dc == NDC - 1)).then_inc(S["sq_free"].h, 1)
    S["stats_ps"].n += 4
    S["sq_free"].n += 64

    @on("vector")
    def _(e):
        e.wait_ge(S["mod_rdy"].h, 1)
        for w in range(4):
            e.wait_ge(S["stats_ps"].h, w + 1)
            e.wait_ge(S["sq_free"].h, 16 * (w + 1))
            ci(e.tensor_scalar(mrep[:], pb[0][:], 1.0 / D, None, ALU.mult))
            cw(e)
            ci(e.tensor_tensor(lnv[:], mrep[:], mrep[:], ALU.mult))
            cw(e)
            e.scalar_tensor_tensor(varrep[:], pb[1][:], 1.0 / D, lnv[:],
                                   ALU.mult, ALU.subtract).then_inc(S["ln_mid"].h, 1)
            e.wait_ge(S["ln_mid"].h, 2 * w + 2)
            for dc in range(NDC):
                ci(e.tensor_tensor(ropeA[:], xwin[:, w % 2, dc, :], mrep[:], ALU.subtract))
                cw(e)
                ci(e.tensor_tensor(scrB[:], ropeA[:], rstdrep[:], ALU.mult))
                cw(e)
                e.tensor_scalar(hT[:, dc, w * 512:(w + 1) * 512], scrB[:],
                                onepc[:, dc:dc + 1], modv[:, dc:dc + 1],
                                ALU.mult, ALU.add).then_inc(S["h_rdy"].h, 1)
    S["ln_mid"].n = 8
    S["h_rdy"].n += 64
    H_ALL = S["h_rdy"].n

    if "h" in tap_t:
        @on("sync")
        def _(e):
            e.wait_ge(S["h_rdy"].h, H_ALL)
            e.dma_start(out=tap_t["h"], in_=hT[:]).then_inc(S["tap_st"].h, 16)
        S["tap_st"].n += 16

    @on("sync")
    def _(e):
        for dst, src in const_pairs2:
            e.dma_start(out=dst, in_=src).then_inc(S["c2_ld"].h, 16)
    S["c2_ld"].n += 16 * len(const_pairs2)
    C2 = S["c2_ld"].n

    def emit_qk(proj):
        wsrc, ddst, nwin = (wq, qdram, 2) if proj == 0 else (wk, kdram, 4)
        ps0 = S["qk_ps"].n
        raw0, out0 = S["kraw_rdy"].n, S["rope_out"].n
        wq0, sw0, st0 = DS["wqk"].uses, DS["kswp"].uses, DS["kqst"].uses
        NTOT = NDC * nwin

        @on("gpsimd")
        def _(e):
            for ec in range(NDC):
                if ec >= 2:
                    e.wait_ge(S["qk_ps"].h, ps0 + (ec - 1) * nwin)
                elif ps0 > 0:
                    e.wait_ge(S["qk_ps"].h, ps0)
                else:
                    e.wait_ge(S["h_rdy"].h, H_ALL)
                e.dma_start(out=wqk_sb[:, ec % 2],
                            in_=wsrc[ec].rearrange("c p n -> p c n")
                            ).then_inc(DS["wqk"].handle(wq0 + ec), 16)

        for _i in range(NDC):
            DS["wqk"].issued(wq0 + _i)

        @on("tensor")
        def _(e):
            if proj == 0:
                e.wait_ge(S["h_rdy"].h, H_ALL)
            for ec in range(NDC):
                DS["wqk"].wait(e, wq0 + ec)
                for w in range(nwin):
                    i = ec * nwin + w
                    if raw0 + i >= 4:
                        e.wait_ge(S["kraw_rdy"].h, raw0 + i - 3)
                    for dc in range(NDC):
                        mm = e.matmul(pb[i % 4][:], wqk_sb[:, ec % 2, dc, :],
                                      hT[:, dc, w * 512:(w + 1) * 512],
                                      start=(dc == 0), stop=(dc == NDC - 1))
                    mm.then_inc(S["qk_ps"].h, 1)
        S["qk_ps"].n += NTOT

        @on("scalar")
        def _(e):
            for i in range(NTOT):
                e.wait_ge(S["kraw_rdy"].h, raw0 + i + 1)
                if i >= 2:
                    e.wait_ge(S["rope_out"].h, out0 + i - 1)
                e.dma_start(out=kswp[0:64, i % 2, :], in_=kraw[64:128, i % 2, :]
                            ).then_inc(DS["kswp"].handle(sw0 + i), 16)
                e.dma_start(out=kswp[64:128, i % 2, :], in_=kraw[0:64, i % 2, :]
                            ).then_inc(DS["kswp"].handle(sw0 + i), 16)

        for _i in range(NTOT):
            DS["kacc"].issued(sw0 + _i, 32)

        @on("vector")
        def _(e):
            def rope(j):
                ec, w = j // nwin, j % nwin
                DS["kswp"].wait(e, sw0 + j)
                e.wait_ge(S["rope_out"].h, out0 + j)
                if st0 + ec >= 2 and w == 0:
                    DS["kqst"].wait(e, st0 + ec - 2)
                ci(e.tensor_tensor(ropeA[:], kraw[:, j % 2, :],
                                   cosk_sb[:, w * 512:(w + 1) * 512], ALU.mult))
                ci(e.tensor_tensor(vstg[:, j % 2, :], kswp[:, j % 2, :],
                                   sink_sb[:, w * 512:(w + 1) * 512], ALU.mult))
                cw(e)
                e.tensor_tensor(kqst[:, ec % 2, w * 512:(w + 1) * 512],
                                ropeA[:], vstg[:, j % 2, :], ALU.add
                                ).then_inc(S["rope_out"].h, 1)
            for i in range(NTOT):
                e.wait_ge(S["qk_ps"].h, ps0 + i + 1)
                DS["kswp"].wait(e, sw0 + i - 2)
                e.tensor_copy(kraw[:, i % 2, :], pb[i % 4][:]).then_inc(S["kraw_rdy"].h, 1)
                if i >= 1:
                    rope(i - 1)
            rope(NTOT - 1)
        S["kraw_rdy"].n += NTOT
        S["rope_out"].n += NTOT

        @on("sync")
        def _(e):
            for ec in range(NDC):
                e.wait_ge(S["rope_out"].h, out0 + (ec + 1) * nwin)
                e.dma_start(out=ddst[ec], in_=kqst[:, ec % 2, 0:nwin * 512]
                            ).then_inc(DS["kqst"].handle(st0 + ec), 16)

        for _i in range(NDC):
            DS["kqw"].issued(st0 + _i)

    emit_qk(0)
    emit_qk(1)
    S_KRAW_ALL = S["kraw_rdy"].n

    @on("tensor")
    def _(e):
        e.wait_ge(S["wg_ld"].h, 16)
        e.wait_ge(S["mod_evac"].h, 24)
        for w in range(2):
            if w >= 1:
                e.wait_ge(S["g_rdy"].h, w)
            for dc in range(NDC):
                mm = e.matmul(pb[6][0:48, :], wg_sb[:, dc, :],
                              hT[:, dc, w * 512:(w + 1) * 512],
                              start=(dc == 0), stop=(dc == NDC - 1))
            mm.then_inc(S["g_ps"].h, 1)
    S["g_ps"].n += 2

    @on("scalar")
    def _(e):
        for w in range(2):
            e.wait_ge(S["g_ps"].h, w + 1)
            e.activation(gT[:, w * 512:(w + 1) * 512], pb[6][0:48, :], AF.Sigmoid
                         ).then_inc(S["g_rdy"].h, 1)
    S["g_rdy"].n += 2

    if "g" in tap_t:
        @on("sync")
        def _(e):
            e.wait_ge(S["g_rdy"].h, 2)
            e.dma_start(out=tap_t["g"], in_=gT[:]).then_inc(S["tap_st"].h, 16)
        S["tap_st"].n += 16

    WVB0 = DS["wva"].uses
    for _i in range(128):
        DS["wva"].issued(WVB0 + _i)

    @on("gpsimd")
    def _(e):
        for es in range(4):
            for th in range(2):
                for dc in range(NDC):
                    i = (es * 2 + th) * NDC + dc
                    if i >= 2:
                        e.wait_ge(S["v_ps"].h, i - 1)
                    else:
                        e.wait_ge(S["mod_ps"].h, 192)
                    e.dma_start(out=wvb_sb[:, i % 2, :], in_=wv[es, dc]
                                ).then_inc(DS["wvb"].handle(WVB0 + i), 16)

    @on("tensor")
    def _(e):
        e.wait_ge(S["g_rdy"].h, 2)
        e.wait_ge(S["kraw_rdy"].h, S_KRAW_ALL)
        for es in range(4):
            for th in range(2):
                u = es * 2 + th
                if u >= 1:
                    e.wait_ge(S["v_evac"].h, u * 8)
                for dc in range(NDC):
                    i = u * NDC + dc
                    DS["wvb"].wait(e, WVB0 + i)
                    for bk in range(8):
                        tc = th * 8 + bk
                        mm = e.matmul(pb[bk][:], hT[:, dc, tc * 128:(tc + 1) * 128],
                                      wvb_sb[:, i % 2, :],
                                      start=(dc == 0), stop=(dc == NDC - 1))
                    mm.then_inc(S["v_ps"].h, 1)
    S["v_ps"].n += 128

    @on("vector")
    def _(e):
        e.wait_ge(S["rope_out"].h, S["rope_out"].n)
        for es in range(4):
            for th in range(2):
                u = es * 2 + th
                e.wait_ge(S["v_ps"].h, (u + 1) * NDC)
                for bk in range(8):
                    i = u * 8 + bk
                    DS["vst"].wait(e, i - 2)
                    e.tensor_copy(vstg[:, i % 2, :], pb[bk][:]).then_inc(S["v_evac"].h, 1)
    S["v_evac"].n += 64

    @on("sync")
    def _(e):
        for es in range(4):
            for th in range(2):
                for bk in range(8):
                    i = (es * 2 + th) * 8 + bk
                    tc = th * 8 + bk
                    e.wait_ge(S["v_evac"].h, i + 1)
                    e.dma_start(out=vdram[tc, es], in_=vstg[:, i % 2, :]
                                ).then_inc(DS["vst"].handle(i), 16)

    for _i in range(64):
        DS["vst"].issued(_i)

    @on("sync")
    def _(e):
        e.wait_ge(S["v_evac"].h, 64)
        e.dma_start(out=rowsel, in_=rowsel_c).then_inc(S["c3_ld"].h, 16)
    S["c3_ld"].n += 16

    KQ_T = tuple(sub.n for sub in DS["kqst"].s)
    VST_T = tuple(sub.n for sub in DS["vst"].s)

    @on("gpsimd")
    def _(e, KQ_T=KQ_T, VST_T=VST_T):
        for _k, _sub in enumerate(DS["kqst"].s):
            if KQ_T[_k]:
                e.wait_ge(_sub.h, KQ_T[_k])
        for _k, _sub in enumerate(DS["vst"].s):
            if VST_T[_k]:
                e.wait_ge(_sub.h, VST_T[_k])
        e.wait_ge(S["c2_ld"].h, C2)
        for h in range(H):
            if h >= 2:
                e.wait_ge(S["kc_rdy"].h, h - 1)
                e.wait_ge(S["esc_rdy"].h, h - 1)
                e.wait_ge(S["vc_rdy"].h, h - 1)
            e.dma_start(out=kr2[:, h % 2, :], in_=kdram[h]).then_inc(DS["kra"].handle(h), 16)
            e.dma_start(out=qr2[:, h % 2, :], in_=qdram[h]).then_inc(DS["qra"].handle(h), 16)
            e.dma_start(out=vr2[:, h % 2],
                        in_=vdram[:, h // 4, :, (h % 4) * 128:(h % 4 + 1) * 128
                                  ].rearrange("c p n -> p c n")).then_inc(DS["vra"].handle(h), 16)

    for _h in range(H):
        DS["kr"].issued(_h)
        DS["qr"].issued(_h)
        DS["vr"].issued(_h)

    @on("tensor")
    def _(e):
        e.wait_ge(S["c2_ld"].h, C2)
        for h in range(H):
            DS["qra"].wait(e, h)
            DS["vra"].wait(e, h)
            e.wait_ge(S["kc_rdy"].h, h + 1)
            if h >= 1:
                e.wait_ge(S["cm_rdy"].h, h)
            if h >= 2:
                e.wait_ge(S["vc_rdy"].h, h - 1)
            for qb in range(8):
                mm = e.matmul(pb[0][:, qb * 32:(qb + 1) * 32],
                              qr2[:, h % 2, qb * 128:(qb + 1) * 128],
                              kcmpall[:, h, :], start=True, stop=True)
            mm.then_inc(S["scmp_ps"].h, 1)
            for c in range(NTC):
                mm = e.matmul(pb[2 + h % 2][0:32, 0:128], ameans[:, c * 32:(c + 1) * 32],
                              vr2[:, h % 2, c, :], start=(c == 0), stop=(c == NTC - 1))
            mm.then_inc(S["vc_ps"].h, 1)
    S["scmp_ps"].n += H
    S["vc_ps"].n += H

    @on("vector")
    def _(e):
        e.wait_ge(S["c2_ld"].h, C2)
        for h in range(H):
            DS["kra"].wait(e, h)
            ci(e.reduce_sum(kcf[:], kr2[:, h % 2, :].rearrange("p (b t) -> p b t", b=32),
                            axis=AX.X))
            cw(e)
            e.tensor_scalar(kcmpall[:, h, :], kcf[:], 1.0 / 64.0, None, ALU.mult
                            ).then_inc(S["kc_rdy"].h, 1)
            e.wait_ge(S["vc_ps"].h, h + 1)
            e.tensor_copy(vcmpall[:, h, :], pb[2 + h % 2][0:32, 0:128]).then_inc(S["vc_rdy"].h, 1)
            e.wait_ge(S["scmp_ps"].h, h + 1)
            e.tensor_tensor(cmadd[:], pb[0][:, 0:256], cmpma[:], ALU.add).then_inc(S["cm_rdy"].h, 1)
            e.wait_ge(S["esc_rdy"].h, h + 1)
            e.tensor_tensor(impa[:, h * 256:(h + 1) * 256], esc[:], cursup[:], ALU.add
                            ).then_inc(S["imp_init"].h, 1)
    S["kc_rdy"].n += H
    S["vc_rdy"].n += H
    S["cm_rdy"].n += H
    S["imp_init"].n += H

    @on("scalar")
    def _(e):
        for h in range(H):
            e.wait_ge(S["cm_rdy"].h, h + 1)
            e.activation(esc[:], cmadd[:], AF.Exp, scale=KAPPA).then_inc(S["esc_rdy"].h, 1)
    S["esc_rdy"].n += H

    if "imp" in tap_t:
        @on("sync")
        def _(e):
            e.wait_ge(S["imp_init"].h, H)
            e.dma_start(out=tap_t["imp"], in_=impa[:]).then_inc(S["tap_st"].h, 16)
        S["tap_st"].n += 16
    if "kcmp" in tap_t:
        @on("sync")
        def _(e):
            e.wait_ge(S["kc_rdy"].h, H)
            e.dma_start(out=tap_t["kcmp"], in_=kcmpall[:]).then_inc(S["tap_st"].h, 16)
        S["tap_st"].n += 16
    if "vcmp" in tap_t:
        @on("sync")
        def _(e):
            e.wait_ge(S["vc_rdy"].h, H)
            e.dma_start(out=tap_t["vcmp"], in_=vcmpall[:]).then_inc(S["tap_st"].h, 16)
        S["tap_st"].n += 16

    @on("vector")
    def _(e):
        e.wait_ge(S["imp_init"].h, H)
        for it in range(7):
            src = impa if it % 2 == 0 else impb
            dst = impb if it % 2 == 0 else impa
            ci(e.reduce_max(mxb[:], src[:].rearrange("p (g b) -> p g b", b=32), axis=AX.X))
            cw(e)
            ci(e.tensor_tensor(selb[:].rearrange("p (g b) -> p g b", b=32),
                               src[:].rearrange("p (g b) -> p g b", b=32),
                               mxb[:].unsqueeze(2).broadcast_to([128, 128, 32]), ALU.is_ge))
            cw(e)
            ci(e.scalar_tensor_tensor(dst[:], selb[:], -SUP, src[:], ALU.mult, ALU.add))
            cw(e)
        e.engine_nop().then_inc(S["topk_done"].h, 1)
    S["topk_done"].n += 1

    gc = [0]
    pvst = [0]
    cr = [0]
    swp_ = [0]
    rep_ = [0]
    units = [0]

    @on("gpsimd")
    def _(e):
        for h in range(H):
            e.wait_ge(S["kc_rdy"].h, H)
            e.wait_ge(S["vc_rdy"].h, H)
            e.wait_ge(S["esc_rdy"].h, H)
            if h >= 2:
                e.wait_ge(S["den_done"].h, 2 * (h - 1))
            e.dma_start(out=kr2[:, h % 2, :], in_=kdram[h]).then_inc(DS["krb"].handle(16 + h), 16)
            e.dma_start(out=qr2[:, h % 2, :], in_=qdram[h]).then_inc(DS["qrb"].handle(16 + h), 16)
            e.dma_start(out=vr2[:, h % 2],
                        in_=vdram[:, h // 4, :, (h % 4) * 128:(h % 4 + 1) * 128
                                  ].rearrange("c p n -> p c n")).then_inc(DS["vrb"].handle(16 + h), 16)

    for _h in range(H):
        DS["kr"].issued(16 + _h)
        DS["qr"].issued(16 + _h)
        DS["vr"].issued(16 + _h)

    for h in range(H):
        trbase = S["tr_ps"].n
        gc_at_build = gc[0]

        @on("tensor")
        def _(e, h=h, gc0=gc_at_build):
            e.wait_ge(S["topk_done"].h, 1)
            if gc0 > 0:
                e.wait_ge(S["px"].h, gc0)
            if h > 0:
                e.wait_ge(S["mt_rdy"].h, h)
            for qt in range(8):
                mm = e.transpose(pb0bf[:, qt * 128:(qt + 1) * 128],
                                 impb[:, h * 256 + qt * 32:h * 256 + (qt + 1) * 32], ident[:])
            mm.then_inc(S["tr_ps"].h, 1)
        S["tr_ps"].n += 1

        @on("vector")
        def _(e, h=h):
            e.wait_ge(S["tr_ps"].h, h + 1)
            ci(e.tensor_scalar(mtmp[:], pb0bf[:], SEL_THR, MASKPRE, ALU.is_ge, ALU.mult))
            cw(e)
            e.tensor_tensor(maskT[:, h % 2, :], mtmp[:], cbmin[:], ALU.min
                            ).then_inc(S["mt_rdy"].h, 1)
        S["mt_rdy"].n += 1

        @on("tensor")
        def _(e, h=h):
            e.wait_ge(S["mt_rdy"].h, h + 1)
            DS["qrb"].wait(e, 16 + h)
            if h > 0:
                e.wait_ge(S["scT_f"].h, 2 * h)
            for g in range(2):
                e.matmul(pb[g][0:32, :], kcmpall[:, h, :], qr2[:, h % 2, g * 512:(g + 1) * 512],
                         start=True, stop=True).then_inc(S["scT_ps"].h, 1)
        S["scT_ps"].n += 2

        @on("vector")
        def _(e, h=h):
            for g in range(2):
                e.wait_ge(S["scT_ps"].h, 2 * h + g + 1)
                if 2 * h + g >= 1:
                    e.wait_ge(S["pcT_rdy"].h, 2 * h + g)
                e.tensor_tensor(scmpTf[:], pb[g][0:32, :], cmpmaT[:, g * 512:(g + 1) * 512],
                                ALU.add).then_inc(S["scT_f"].h, 1)
        S["scT_f"].n += 2

        @on("scalar")
        def _(e, h=h):
            for g in range(2):
                e.wait_ge(S["scT_f"].h, 2 * h + g + 1)
                e.activation(pcmpT[:, h % 2, g * 512:(g + 1) * 512], scmpTf[:],
                             AF.Exp, scale=KAPPA).then_inc(S["pcT_rdy"].h, 1)
        S["pcT_rdy"].n += 2

        if "maskT" in tap_t:
            @on("sync")
            def _(e, h=h):
                e.wait_ge(S["mt_rdy"].h, h + 1)
                e.dma_start(out=tap_t["maskT"][:, h, :], in_=maskT[:, h % 2, :]
                            ).then_inc(S["tap_st"].h, 16)
            S["tap_st"].n += 16
        if "pcmpT" in tap_t:
            @on("sync")
            def _(e, h=h):
                e.wait_ge(S["pcT_rdy"].h, 2 * h + 2)
                e.dma_start(out=tap_t["pcmpT"][:, h, :], in_=pcmpT[:, h % 2, :]
                            ).then_inc(S["tap_st"].h, 16)
            S["tap_st"].n += 16

        for g in range(2):
            ECH = 4 * g + 4
            chunks = list(range(ECH)) + list(range(8, 8 + ECH))
            dpar = (2 * h + g) % 2
            unit = 2 * h + g
            gbase = gc[0]
            pvbase = pvst[0]
            denb = pb[5 + dpar]

            @on("tensor")
            def _(e, h=h, g=g, chunks=tuple(chunks), gbase=gbase, unit=unit,
                  dpar=dpar, denb=denb, pvbase=pvbase, crbase=cr[0]):
                DS["krb"].wait(e, 16 + h)
                DS["vrb"].wait(e, 16 + h)
                e.wait_ge(S["mt_rdy"].h, h + 1)
                e.wait_ge(S["scT_f"].h, 2 * (h + 1))
                if unit >= 1:
                    e.wait_ge(S["cmb"].h, unit)
                if unit >= 2:
                    e.wait_ge(S["coef_u"].h, unit - 1)
                ci = 0
                for i, c in enumerate(chunks):
                    if gbase + i >= 2:
                        e.wait_ge(S["px"].h, gbase + i - 1)
                    e.matmul(pb[i % 2][:], kr2[:, h % 2, c * 128:(c + 1) * 128],
                             qr2[:, h % 2, g * 512:(g + 1) * 512], start=True, stop=False)
                    e.matmul(pb[i % 2][:], eexp[:, c * 128:(c + 1) * 128],
                             maskT[:, h % 2, g * 512:(g + 1) * 512], start=False, stop=True
                             ).then_inc(S["sc_ps"].h, 1)
                    e.wait_ge(S["px"].h, gbase + i + 1)
                    e.matmul(pb[2][:], vr2[:, h % 2, c, :], pring[:, (gbase + i) % 3, :],
                             start=(i == 0), stop=(i == len(chunks) - 1))
                    mm = e.matmul(denb[0:48, :], toep[:, 31 - h:79 - h],
                                  pring[:, (gbase + i) % 3, :],
                                  start=(i == 0), stop=False)
                    if 4 * g <= c < 4 * g + 4:
                        jl = c - 4 * g
                        e.wait_ge(S["corr_i"].h, crbase + ci + 1)
                        e.matmul(pb[2][:, jl * 128:(jl + 1) * 128], vr2[:, h % 2, c, :],
                                 corrsc[:, (ci) % 2, :], start=False, stop=False)
                        mm = e.matmul(denb[0:48, jl * 128:(jl + 1) * 128],
                                      toep[:, 31 - h:79 - h], corrsc[:, ci % 2, :],
                                      start=False, stop=False)
                        ci += 1
                    mm.then_inc(S["pvstep"].h, 1)
            S["sc_ps"].n += len(chunks)
            S["pvstep"].n += len(chunks)

            @on("scalar")
            def _(e, gbase=gbase, nch=len(chunks), pvbase=pvbase):
                for i in range(nch):
                    e.wait_ge(S["sc_ps"].h, gbase + i + 1)
                    if gbase + i >= 3:
                        e.wait_ge(S["pvstep"].h, gbase + i - 2)
                    e.activation(pring[:, (gbase + i) % 3, :], pb[i % 2][:],
                                 AF.Exp, scale=KAPPA).then_inc(S["px"].h, 1)
            S["px"].n += len(chunks)

            corr_chunks = [i for i, c in enumerate(chunks) if 4 * g <= c < 4 * g + 4]

            @on("vector")
            def _(e, g=g, chunks=tuple(chunks), gbase=gbase,
                  corr_chunks=tuple(corr_chunks)):
                for ci, i in enumerate(corr_chunks):
                    c = chunks[i]
                    jl = c - 4 * g
                    e.wait_ge(S["px"].h, gbase + i + 1)
                    if ci >= 2:
                        e.wait_ge(S["pvstep"].h, gbase + corr_chunks[ci - 2] + 1)
                    e.tensor_tensor(corrsc[:, ci % 2, :],
                                    pring[:, (gbase + i) % 3, jl * 128:(jl + 1) * 128],
                                    uptri[:], ALU.mult).then_inc(S["corr_i"].h, 1)
            S["corr_i"].n += 4
            gc[0] += len(chunks)
            pvst[0] += len(chunks)
            cr[0] += 4

            @on("tensor")
            def _(e, h=h, g=g, unit=unit, denb=denb, swb=swp_[0]):
                if unit >= 1:
                    e.wait_ge(S["rep_ev"].h, 3 * unit)
                for jl in range(4):
                    j = 4 * g + jl
                    E = 2 * (j + 1)
                    cands = [j, max(8 + j - 1, 8), 8 + j]
                    u = swb + jl
                    if u >= swb + 1:
                        e.wait_ge(S["pswa_e"].h, u)
                    for p, cp in enumerate(cands):
                        mm = e.matmul(pb[7][:, p * 128:(p + 1) * 128],
                                      kr2[:, h % 2, cp * 128:(cp + 1) * 128],
                                      qr2[:, h % 2, j * 128:(j + 1) * 128],
                                      start=True, stop=True)
                    mm.then_inc(S["swa_ps"].h, 1)
                    e.wait_ge(S["pswa_m"].h, u + 1)
                    for p, cp in enumerate(cands):
                        e.matmul(pb[3][:, jl * 128:(jl + 1) * 128], vr2[:, h % 2, cp, :],
                                 pswa2[:, jl % 2, p * 128:(p + 1) * 128],
                                 start=(jl == 0 and p == 0), stop=(jl == 3 and p == 2))
                        mm = e.matmul(denb[0:48, jl * 128:(jl + 1) * 128],
                                      toep[:, 15 - h:63 - h],
                                      pswa2[:, jl % 2, p * 128:(p + 1) * 128],
                                      start=False, stop=False)
                    mm.then_inc(S["swa_pv"].h, 1)
            S["swa_ps"].n += 4
            S["swa_pv"].n += 4

            @on("scalar")
            def _(e, swb=swp_[0]):
                for jl in range(4):
                    e.wait_ge(S["swa_ps"].h, swb + jl + 1)
                    e.activation(pswa[:, jl % 2, :], pb[7][:, 0:384], AF.Exp, scale=KAPPA
                                 ).then_inc(S["pswa_e"].h, 1)
            S["pswa_e"].n += 4

            @on("vector")
            def _(e, g=g, swb=swp_[0]):
                for jl in range(4):
                    e.wait_ge(S["pswa_e"].h, swb + jl + 1)
                    if jl >= 2:
                        e.wait_ge(S["swa_pv"].h, swb + jl - 1)
                    cls = 0 if (g == 0 and jl == 0) else 1
                    e.tensor_tensor(pswa2[:, jl % 2, :], pswa[:, jl % 2, :],
                                    swaarr[:, cls, :], ALU.mult).then_inc(S["pswa_m"].h, 1)
            S["pswa_m"].n += 4
            swp_[0] += 4

            @on("tensor")
            def _(e, h=h, g=g, denb=denb, unit=unit):
                e.wait_ge(S["pcT_rdy"].h, 2 * h + 2)
                e.matmul(pb[4][:], vcmpall[:, h, :], pcmpT[:, h % 2, g * 512:(g + 1) * 512],
                         start=True, stop=True).then_inc(S["pv_done"].h, 1)
                e.matmul(denb[0:48, :], toep32[:, 47 - h:95 - h],
                         pcmpT[:, h % 2, g * 512:(g + 1) * 512], start=False, stop=True
                         ).then_inc(S["den_done"].h, 1)
            S["pv_done"].n += 1
            S["den_done"].n += 1

            @on("vector")
            def _(e, g=g, denb=denb, unit=unit):
                e.wait_ge(S["den_done"].h, unit + 1)
                ci(e.tensor_tensor(varrep[0:48, :], denb[0:48, :],
                                   anyfix[:, g * 512:(g + 1) * 512], ALU.add))
                cw(e)
                e.tensor_scalar(lnv[0:48, :], varrep[0:48, :], 1e-20, None, ALU.max
                                ).then_inc(S["coef_u"].h, 1)
            S["coef_u"].n += 1

            @on("scalar")
            def _(e, unit=unit):
                e.wait_ge(S["coef_u"].h, unit + 1)
                ci(e.activation(varrep[0:48, :], lnv[0:48, :], AF.Ln), "scalar")
                cw(e, "scalar")
                e.activation(lnv[0:48, :], varrep[0:48, :], AF.Exp, scale=-1.0
                             ).then_inc(S["coef_r"].h, 1)
            S["coef_r"].n += 1

            @on("vector")
            def _(e, h=h, g=g, dpar=dpar, unit=unit):
                e.wait_ge(S["coef_r"].h, unit + 1)
                e.wait_ge(S["g_rdy"].h, 2)
                e.tensor_tensor(coefb[:, dpar, :], lnv[0:48, :],
                                gT[:, g * 512:(g + 1) * 512], ALU.mult)
                e.engine_nop().then_inc(S["coef_rdy"].h, 1)
            S["coef_rdy"].n += 1

            @on("tensor")
            def _(e, h=h, dpar=dpar, unit=unit, rb=rep_[0]):
                e.wait_ge(S["coef_rdy"].h, unit + 1)
                e.wait_ge(S["pswa_e"].h, (unit + 1) * 4)
                if unit == 0:
                    e.wait_ge(S["c3_ld"].h, 16)
                for b in range(3):
                    if rb + b >= 1:
                        e.wait_ge(S["rep_ev"].h, rb + b)
                    e.matmul(pb[7][:], rowsel[:, (16 * b + h) * 128:(16 * b + h + 1) * 128],
                             coefb[0:48, dpar, :], start=True, stop=True
                             ).then_inc(S["rep_ps"].h, 1)
            S["rep_ps"].n += 3

            @on("vector")
            def _(e, rb=rep_[0]):
                for b in range(3):
                    e.wait_ge(S["rep_ps"].h, rb + b + 1)
                    e.tensor_copy(crep[:, b, :], pb[7][:]).then_inc(S["rep_ev"].h, 1)
            S["rep_ev"].n += 3
            rep_[0] += 3

            @on("vector")
            def _(e, h=h, g=g, unit=unit):
                e.wait_ge(S["rep_ev"].h, 3 * (unit + 1))
                e.wait_ge(S["pv_done"].h, unit + 1)
                ci(e.tensor_tensor(csc[:, 0, :], pb[4][:], crep[:, 0, :], ALU.mult))
                ci(e.tensor_tensor(csc[:, 1, :], pb[2][:], crep[:, 1, :], ALU.mult))
                cw(e)
                ci(e.tensor_tensor(csc[:, 2, :], csc[:, 0, :], csc[:, 1, :], ALU.add))
                cw(e)
                ci(e.tensor_tensor(csc[:, 0, :], pb[3][:], crep[:, 2, :], ALU.mult))
                cw(e)
                e.tensor_tensor(oT[:, h, g * 512:(g + 1) * 512], csc[:, 2, :], csc[:, 0, :],
                                ALU.add).then_inc(S["cmb"].h, 1)
            S["cmb"].n += 1
            units[0] += 1

    CMB_ALL = S["cmb"].n

    if "o" in tap_t:
        @on("sync")
        def _(e):
            e.wait_ge(S["cmb"].h, CMB_ALL)
            e.dma_start(out=tap_t["o"], in_=oT[:]).then_inc(S["tap_st"].h, 16)
        S["tap_st"].n += 16

    XOL0 = DS["xol"].uses
    WOL0 = DS["wqk"].uses
    for _i in range(NDC):
        DS["xol"].issued(XOL0 + _i)
        DS["wqk"].issued(WOL0 + _i)
    @on("sync")
    def _(e):
        e.wait_ge(S["cmb"].h, CMB_ALL)
        for dc in range(NDC):
            if dc >= 2:
                e.wait_ge(S["x2_rdy"].h, 2 * (dc - 1))
            e.dma_start(out=xo[:, dc % 2, :],
                        in_=xT.rearrange("(c p) t -> p c t", p=128)[:, dc, 0:OT]
                        ).then_inc(DS["xol"].handle(XOL0 + dc), 16)

    @on("gpsimd")
    def _(e):
        e.wait_ge(S["cmb"].h, CMB_ALL)
        for dc in range(NDC):
            if dc >= 2:
                e.wait_ge(S["wo_ps"].h, 2 * (dc - 1))
            e.dma_start(out=wo_sb[:, dc % 2], in_=wo[dc].rearrange("c p n -> p c n")
                        ).then_inc(DS["wol"].handle(WOL0 + dc), 16)

    @on("tensor")
    def _(e):
        e.wait_ge(S["cmb"].h, CMB_ALL)
        for dc in range(NDC):
            DS["wol"].wait(e, WOL0 + dc)
            if dc >= 2:
                e.wait_ge(S["x2_rdy"].h, 2 * (dc - 1))
            for half in range(2):
                bank = pb[(dc % 2) * 2 + half]
                for ec in range(NDC):
                    mm = e.matmul(bank[:], wo_sb[:, dc % 2, ec, :],
                                  oT[:, ec, half * 512:(half + 1) * 512],
                                  start=(ec == 0), stop=(ec == NDC - 1))
                mm.then_inc(S["wo_ps"].h, 1)
    S["wo_ps"].n += 32

    @on("vector")
    def _(e):
        for dc in range(NDC):
            DS["xol"].wait(e, XOL0 + dc)
            for half in range(2):
                e.wait_ge(S["wo_ps"].h, 2 * dc + half + 1)
                e.scalar_tensor_tensor(x2[:, dc, half * 512:(half + 1) * 512],
                                       pb[(dc % 2) * 2 + half][:],
                                       modv[:, 32 + dc:33 + dc],
                                       xo[:, dc % 2, half * 512:(half + 1) * 512],
                                       ALU.mult, ALU.add).then_inc(S["x2_rdy"].h, 1)
    S["x2_rdy"].n += 32

    if "x2" in tap_t:
        @on("sync")
        def _(e):
            e.wait_ge(S["x2_rdy"].h, 32)
            e.dma_start(out=tap_t["x2"], in_=x2[:]).then_inc(S["tap_st"].h, 16)
        S["tap_st"].n += 16

    @on("vector")
    def _(e):
        e.wait_ge(S["x2_rdy"].h, 32)
        for dc in range(NDC):
            e.tensor_copy(x2b[:, dc, :], x2[:, dc, :])
        e.engine_nop().then_inc(S["x2b_rdy"].h, 1)
    S["x2b_rdy"].n += 1

    @on("scalar")
    def _(e):
        e.wait_ge(S["x2b_rdy"].h, 1)
        for w in range(2):
            for dc in range(NDC):
                i = w * NDC + dc
                if i >= 2:
                    e.wait_ge(S["sq2_free"].h, i - 1)
                e.activation(sqring[:, i % 2, :], x2b[:, dc, w * 512:(w + 1) * 512],
                             AF.Square).then_inc(S["sq2_rdy"].h, 1)
    S["sq2_rdy"].n += 32

    @on("tensor")
    def _(e):
        e.wait_ge(S["x2b_rdy"].h, 1)
        for w in range(2):
            if w >= 1:
                e.wait_ge(S["ln2_mid"].h, 2 * w - 1)
            for dc in range(NDC):
                i = w * NDC + dc
                mm0 = e.matmul(pb[0][:], ones[:], x2b[:, dc, w * 512:(w + 1) * 512],
                               start=(dc == 0), stop=(dc == NDC - 1))
                if dc == NDC - 1:
                    mm0.then_inc(S["st2_ps"].h, 1)
                e.wait_ge(S["sq2_rdy"].h, i + 1)
                e.matmul(pb[1][:], ones[:], sqring[:, i % 2, :],
                         start=(dc == 0), stop=(dc == NDC - 1)).then_inc(S["sq2_free"].h, 1)
    S["st2_ps"].n += 2
    S["sq2_free"].n += 32

    @on("scalar")
    def _(e):
        for w in range(2):
            e.wait_ge(S["ln2_mid"].h, 2 * w + 1)
            ci(e.activation(lnv[:], varrep[:], AF.Ln, bias=epsv[:]), "scalar")
            cw(e, "scalar")
            e.activation(rstdrep[:], lnv[:], AF.Exp, scale=-0.5).then_inc(S["ln2_mid"].h, 1)

    @on("vector")
    def _(e):
        for w in range(2):
            e.wait_ge(S["st2_ps"].h, w + 1)
            e.wait_ge(S["sq2_free"].h, 16 * (w + 1))
            ci(e.tensor_scalar(mrep[:], pb[0][:], 1.0 / D, None, ALU.mult))
            cw(e)
            ci(e.tensor_tensor(lnv[:], mrep[:], mrep[:], ALU.mult))
            cw(e)
            e.scalar_tensor_tensor(varrep[:], pb[1][:], 1.0 / D, lnv[:],
                                   ALU.mult, ALU.subtract).then_inc(S["ln2_mid"].h, 1)
            e.wait_ge(S["ln2_mid"].h, 2 * w + 2)
            for dc in range(NDC):
                ci(e.tensor_tensor(ropeA[:], x2[:, dc, w * 512:(w + 1) * 512], mrep[:],
                                   ALU.subtract))
                cw(e)
                ci(e.tensor_tensor(scrB[:], ropeA[:], rstdrep[:], ALU.mult))
                cw(e)
                e.tensor_scalar(h2sb[:, dc, w * 512:(w + 1) * 512], scrB[:],
                                onepc[:, 16 + dc:17 + dc], modv[:, 48 + dc:49 + dc],
                                ALU.mult, ALU.add).then_inc(S["h2_rdy"].h, 1)
    S["ln2_mid"].n = 4
    S["h2_rdy"].n += 32
    H2_ALL = S["h2_rdy"].n

    if "h2" in tap_t:
        @on("sync")
        def _(e):
            e.wait_ge(S["h2_rdy"].h, H2_ALL)
            e.dma_start(out=tap_t["h2"], in_=h2sb[:]).then_inc(S["tap_st"].h, 16)
        S["tap_st"].n += 16

    @on("sync")
    def _(e):
        e.wait_ge(S["h2_rdy"].h, H2_ALL)
        e.dma_start(out=outT.rearrange("(c p) t -> p c t", p=128), in_=x2[:]
                    ).then_inc(S["x2_sp"].h, 16)
    S["x2_sp"].n += 16

    WUL0 = DS["wul"].uses
    WLL0 = DS["wqk"].uses
    ACC0 = DS["acc"].uses
    assert (ACC0 % 2) == 0
    for _hc in range(64):
        DS["wul"].issued(WUL0 + _hc, 32)
    for _u in range(64):
        DS["wqk"].issued(WLL0 + _u)
        DS["acc"].issued(ACC0 + _u)
    @on("gpsimd")
    def _(e):
        e.wait_ge(S["x2_sp"].h, 16)
        for grp in range(4):
            for hl in range(16):
                hc = grp * 16 + hl
                if hc >= 2:
                    e.wait_ge(S["up_ps"].h, hc - 1)
                e.dma_start(out=wu_sb[:, hc % 2, 0], in_=wu0[hc].rearrange("c p n -> p c n")
                            ).then_inc(DS["wul"].handle(WUL0 + hc), 16)
                e.dma_start(out=wu_sb[:, hc % 2, 1], in_=wu1[hc].rearrange("c p n -> p c n")
                            ).then_inc(DS["wul"].handle(WUL0 + hc), 16)
            def acc_dma(ua):
                e.wait_ge(S["stg_rdy"].h, ua + 1)
                DS["acc"].wait(e, ACC0 + ua - 16)
                e.dma_start(out=outT.rearrange("(c p) t -> p c t", p=128)[:, ua % NDC, :],
                            in_=stage[:, ua % 2, :], accum_op=ALU.add
                            ).then_inc(DS["acc"].handle(ACC0 + ua), 16)
            for dc in range(NDC):
                u = grp * NDC + dc
                if u >= 2:
                    e.wait_ge(S["dn_ps"].h, u - 1)
                e.dma_start(out=wl_sb[:, u % 2],
                            in_=wl[dc, grp * 16:(grp + 1) * 16].rearrange("c p n -> p c n")
                            ).then_inc(DS["wll"].handle(WLL0 + u), 16)
                if u >= 2:
                    acc_dma(u - 2)
            if grp == 3:
                acc_dma(62)
                acc_dma(63)

    @on("tensor")
    def _(e):
        e.wait_ge(S["h2_rdy"].h, H2_ALL)
        for grp in range(4):
            for hl in range(16):
                hc = grp * 16 + hl
                DS["wul"].wait(e, WUL0 + hc)
                bs = 4 * (hl % 2)
                if hc >= 2:
                    e.wait_ge(S["act_rdy"].h, 2 * (hc - 1))
                if hl == 0 and grp >= 1:
                    e.wait_ge(S["stg_rdy"].h, grp * 16)
                for proj in range(2):
                    for half in range(2):
                        bank = pb[bs + 2 * proj + half]
                        for dc in range(NDC):
                            mm = e.matmul(bank[:], wu_sb[:, hc % 2, proj, dc, :],
                                          h2sb[:, dc, half * 512:(half + 1) * 512],
                                          start=(dc == 0), stop=(dc == NDC - 1))
                mm.then_inc(S["up_ps"].h, 1)
            e.wait_ge(S["act_rdy"].h, 2 * (grp * 16 + 16))
            for dc in range(NDC):
                u = grp * NDC + dc
                DS["wll"].wait(e, WLL0 + u)
                if u >= 2:
                    e.wait_ge(S["stg_rdy"].h, u - 1)
                for half in range(2):
                    bank = pb[(dc % 2) * 2 + half]
                    for hl in range(16):
                        mm = e.matmul(bank[:], wl_sb[:, u % 2, hl, :],
                                      act[:, hl, half * 512:(half + 1) * 512],
                                      start=(hl == 0), stop=(hl == 15))
                mm.then_inc(S["dn_ps"].h, 1)
    S["up_ps"].n += 64
    S["dn_ps"].n += 64

    @on("scalar")
    def _(e):
        for grp in range(4):
            for hl in range(16):
                hc = grp * 16 + hl
                bs = 4 * (hl % 2)
                for half in range(2):
                    e.wait_ge(S["up_ps"].h, hc + 1)
                    if hc >= 1 or half >= 1:
                        e.wait_ge(S["act_rdy"].h, 2 * hc + half - 1)
                    e.activation(gel_sb[:, half, :], pb[bs + half][:],
                                 AF.Gelu_apprx_tanh).then_inc(S["gel_rdy"].h, 1)
    S["gel_rdy"].n += 128

    @on("vector")
    def _(e):
        for grp in range(4):
            for hl in range(16):
                hc = grp * 16 + hl
                bs = 4 * (hl % 2)
                if hl == 0 and grp >= 1:
                    e.wait_ge(S["dn_ps"].h, 16 * grp)
                for half in range(2):
                    e.wait_ge(S["gel_rdy"].h, 2 * hc + half + 1)
                    e.tensor_tensor(act[:, hl, half * 512:(half + 1) * 512],
                                    gel_sb[:, half, :], pb[bs + 2 + half][:],
                                    ALU.mult).then_inc(S["act_rdy"].h, 1)
            for dc in range(NDC):
                u = grp * NDC + dc
                e.wait_ge(S["dn_ps"].h, u + 1)
                DS["acc"].wait(e, ACC0 + u - 2)
                for half in range(2):
                    e.tensor_scalar(stage[:, u % 2, half * 512:(half + 1) * 512],
                                    pb[(dc % 2) * 2 + half][:],
                                    modv[:, 80 + dc:81 + dc], None, ALU.mult)
                e.engine_nop().then_inc(S["stg_rdy"].h, 1)
    S["act_rdy"].n += 128
    S["stg_rdy"].n += 64

    if "act" in tap_t:
        @on("sync")
        def _(e):
            e.wait_ge(S["act_rdy"].h, 32)
            e.dma_start(out=tap_t["act"], in_=act[:]).then_inc(S["tap_st"].h, 16)
        S["tap_st"].n += 16

    @on("sync")
    def _(e):
        DS["acc"].wait_all(e)
        if S["tap_st"].n:
            e.wait_ge(S["tap_st"].h, S["tap_st"].n)

    with nc.Block() as blk:
        @blk.sync
        def _(eng):
            for fn in OPS["sync"]:
                fn(eng)

        @blk.gpsimd
        def _(eng):
            for fn in OPS["gpsimd"]:
                fn(eng)

        @blk.tensor
        def _(eng):
            for fn in OPS["tensor"]:
                fn(eng)

        @blk.vector
        def _(eng):
            for fn in OPS["vector"]:
                fn(eng)

        @blk.scalar
        def _(eng):
            for fn in OPS["scalar"]:
                fn(eng)
    return nc


def kernel(**inputs):
    maps = _stage(inputs)
    nc = build_nc(DEBUG_TAPS)
    res = run_bass_kernel_spmd(nc, maps, core_ids=list(range(8)))
    out = np.zeros((4, NT, D), np.float32)
    for core in range(8):
        b, s = core // 2, core % 2
        own = _perm_ids(s)[0:OT]
        oTv = np.asarray(res.results[core]["outT"])
        out[b, own] = oTv.T
    return out


# version 04
# speedup vs baseline: 1.0821x; 1.0821x over previous
import numpy as np
import ml_dtypes
import concourse.bass as bass
import concourse.mybir as mybir
from concourse.bass_utils import run_bass_kernel_spmd

F32 = mybir.dt.float32
BF16 = mybir.dt.bfloat16
ALU = mybir.AluOpType
AF = mybir.ActivationFunctionType
AX = mybir.AxisListType
BF = ml_dtypes.bfloat16

H, HD, D, NT, OT = 16, 128, 2048, 2048, 1024
NDC, NTC = 16, 16
KAPPA = float(HD) ** -0.5
MASKPRE = -2048.0
NEG_BIG = -3.0e6
SUP = 1.0e6
SEL_THR = -1.0e5
EPS = 1e-5

DEBUG_TAPS = ()


def _perm_ids(s):
    ids = []
    for j in range(8):
        ids.extend(range((2 * j + s) * 128, (2 * j + s) * 128 + 128))
    for j in range(8):
        ids.extend(range((2 * j + 1 - s) * 128, (2 * j + 1 - s) * 128 + 128))
    return np.array(ids)


def _consts_for_parity(s):
    perm = _perm_ids(s)
    own = perm[0:OT]
    c = {}
    inv = 1.0 / 10000.0 ** (np.arange(64) / 64.0)
    th = perm[:, None].astype(np.float64) * inv[None, :]
    ck, sk = np.cos(th).T, np.sin(th).T
    c["cosk"] = np.tile(ck, (2, 1)).astype(BF)
    c["sink"] = np.concatenate([-sk, sk], 0).astype(BF)
    c["ident"] = np.eye(128, dtype=BF)
    c["ones_lhs"] = np.ones((128, 128), dtype=BF)
    c["eexp"] = (np.arange(32)[:, None] == (np.arange(NT)[None, :] // 64)).astype(BF)
    am = np.zeros((128, NTC, 32), dtype=np.float32)
    for cc in range(NTC):
        for t in range(128):
            am[t, cc, 2 * cc + t // 64] = 1.0 / 64.0
    c["ameans"] = am.reshape(128, NTC * 32).astype(BF)
    tp = np.zeros((128, 95), dtype=BF)
    tp[:, 47] = 1.0
    c["toep"] = tp
    tp32 = np.zeros((32, 95), dtype=BF)
    tp32[:, 47] = 1.0
    c["toep32"] = tp32
    rs = np.zeros((48, 48, 128), dtype=BF)
    for r in range(48):
        rs[r, r, :] = 1.0
    c["rowsel16"] = rs.reshape(48, 48 * 128)
    blk_seq = perm.reshape(32, 64)[:, 0] // 64
    blk_end = (blk_seq + 1) * 64 - 1
    blk_start = blk_seq * 64
    cm = np.where(own[:, None] >= blk_end[None, :], 0.0, MASKPRE).astype(np.float32)
    c["cmp_maskadd"] = np.ascontiguousarray(
        cm.reshape(8, 128, 32).transpose(1, 0, 2).reshape(128, 256)).astype(BF)
    c["cmp_maskaddT"] = np.ascontiguousarray(cm.T).astype(BF)
    cs_ = np.zeros((OT, 32), np.float32)
    cs_[np.arange(OT), np.arange(OT) // 64] = NEG_BIG
    c["cur_sup"] = np.ascontiguousarray(
        cs_.reshape(8, 128, 32).transpose(1, 0, 2).reshape(128, 256)).astype(BF)
    c["cb_min"] = np.where(blk_start[:, None] <= own[None, :], 0.0, MASKPRE).astype(BF)
    tq = np.arange(128)[None, :]
    tk = np.arange(128)[:, None]
    B1 = ((tq - tk >= 0) & (tq - tk <= 64)).astype(np.float32)
    B2 = (tk - tq >= 64).astype(np.float32)
    Z = np.zeros((128, 128), np.float32)
    gen = [B1, Z, B2] if s == 1 else [B1, B2, Z]
    sl0 = [B1, Z, B2] if s == 1 else [B1, Z, Z]
    c["swa_arr"] = np.stack([np.concatenate(sl0, 1), np.concatenate(gen, 1)],
                            1).reshape(128, 2 * 384).astype(BF)
    c["uptri"] = (-(tk > tq).astype(np.float32)).astype(BF)
    any01 = (own >= 63).astype(np.float32)
    af = np.zeros((48, OT), np.float32)
    af[0:16] = (1.0 - any01)[None, :]
    c["anyfix"] = af.astype(BF)
    return c


def _tile2(w, bk0, bk1, order="km"):
    K_, M_ = w.shape
    t = w.reshape(K_ // bk0, bk0, M_ // bk1, bk1).transpose(0, 2, 1, 3)
    if order == "mk":
        t = t.transpose(1, 0, 2, 3)
    return np.ascontiguousarray(t)


def _stage(ins):
    x = np.asarray(ins["L0_attn_x"], dtype=np.float32)
    shared = {
        "wq": _tile2(np.asarray(ins["L0_attn_Wq"], np.float32), 128, 128, "mk"),
        "wk": _tile2(np.asarray(ins["L0_attn_Wk"], np.float32), 128, 128, "mk"),
        "wv": _tile2(np.asarray(ins["L0_attn_Wv"], np.float32), 128, 512, "mk"),
        "wo": _tile2(np.asarray(ins["L0_attn_Wo"], np.float32), 128, 128, "mk"),
        "wg": np.ascontiguousarray(np.asarray(ins["L0_attn_Wg"], np.float32).reshape(NDC, 128, 48)),
        "wu0": _tile2(np.asarray(ins["L0_ffn_w_gating"], np.float32)[0], 128, 128, "mk"),
        "wu1": _tile2(np.asarray(ins["L0_ffn_w_gating"], np.float32)[1], 128, 128, "mk"),
        "wl": _tile2(np.asarray(ins["L0_ffn_w_linear"], np.float32), 128, 128, "mk"),
        "wmod": _tile2(np.asarray(ins["L0_wmod"], np.float32), 128, 512, "mk"),
    }
    pc = [_consts_for_parity(s) for s in (0, 1)]
    maps = []
    for core in range(8):
        b, s = core // 2, core % 2
        m = dict(shared)
        m.update(pc[s])
        m["xT"] = np.ascontiguousarray(x[b].T[:, _perm_ids(s)])
        m["cT"] = np.ascontiguousarray(np.asarray(ins["L0_c"], np.float32)[b:b + 1].T)
        maps.append(m)
    return maps


class Sem:
    __slots__ = ("h", "n")

    def __init__(self, nc, name):
        self.h = nc.alloc_semaphore(name)
        self.n = 0


class DSem:

    def __init__(self, nc, name, depth=2):
        self.s = [Sem(nc, f"{name}_{k}") for k in range(depth)]
        self.depth = depth
        self.tot = {}
        self.uses = 0

    def handle(self, i):
        return self.s[i % self.depth].h

    def issued(self, i, k=16):
        sub = self.s[i % self.depth]
        sub.n += k
        self.tot[i] = sub.n
        self.uses = max(self.uses, i + 1)

    def wait(self, e, i):
        if i < 0:
            return
        assert i in self.tot, (i, sorted(self.tot)[:4])
        e.wait_ge(self.s[i % self.depth].h, self.tot[i])

    def wait_all(self, e):
        for sub in self.s:
            if sub.n:
                e.wait_ge(sub.h, sub.n)


def build_nc(taps=()):
    nc = bass.Bass("TRN2", target_bir_lowering=False)
    OPS = {e: [] for e in ("sync", "gpsimd", "tensor", "vector", "scalar")}

    def on(e):
        def dec(fn):
            OPS[e].append(fn)
            return fn
        return dec

    def din(name, shape, dt=F32):
        return nc.dram_tensor(name, list(shape), dt, kind="ExternalInput").ap()

    xT = din("xT", (D, NT))
    wq = din("wq", (NDC, NDC, 128, 128))
    wk = din("wk", (NDC, NDC, 128, 128))
    wv = din("wv", (4, NDC, 128, 512))
    wo = din("wo", (NDC, NDC, 128, 128))
    wg = din("wg", (NDC, 128, 48))
    wu0 = din("wu0", (64, NDC, 128, 128))
    wu1 = din("wu1", (64, NDC, 128, 128))
    wl = din("wl", (NDC, 64, 128, 128))
    wmod = din("wmod", (24, 16, 128, 512))
    cT = din("cT", (D, 1))
    cosk_c = din("cosk", (128, NT), BF16)
    sink_c = din("sink", (128, NT), BF16)
    ident_c = din("ident", (128, 128), BF16)
    ones_c = din("ones_lhs", (128, 128), BF16)
    eexp_c = din("eexp", (32, NT), BF16)
    ameans_c = din("ameans", (128, NTC * 32), BF16)
    toep_c = din("toep", (128, 95), BF16)
    toep32_c = din("toep32", (32, 95), BF16)
    rowsel_c = din("rowsel16", (48, 48 * 128), BF16)
    cmpma_c = din("cmp_maskadd", (128, 256), BF16)
    cmpmaT_c = din("cmp_maskaddT", (32, OT), BF16)
    cursup_c = din("cur_sup", (128, 256), BF16)
    cbmin_c = din("cb_min", (32, OT), BF16)
    swaarr_c = din("swa_arr", (128, 2 * 384), BF16)
    uptri_c = din("uptri", (128, 128), BF16)
    anyfix_c = din("anyfix", (48, OT), BF16)

    outT = nc.dram_tensor("outT", [D, OT], F32, kind="ExternalOutput").ap()
    kdram = nc.dram_tensor("kdram", [H, 128, NT], BF16,
                           kind="ExternalOutput" if "k" in taps else "Internal").ap()
    qdram = nc.dram_tensor("qdram", [H, 128, OT], BF16,
                           kind="ExternalOutput" if "q" in taps else "Internal").ap()
    vdram = nc.dram_tensor("vdram", [NTC, 4, 128, 512], BF16,
                           kind="ExternalOutput" if "v" in taps else "Internal").ap()
    tap_t = {}
    tap_shapes = {
        "mod": ([1, 6 * D], F32), "h": ([128, NDC, NT], BF16),
        "g": ([48, OT], F32), "kcmp": ([128, H, 32], BF16),
        "vcmp": ([32, H, 128], BF16), "imp": ([128, H * 256], BF16),
        "maskT": ([32, H, OT], BF16), "pcmpT": ([32, H, OT], BF16),
        "o": ([128, H, OT], BF16), "x2": ([128, NDC, OT], F32),
        "h2": ([128, NDC, OT], BF16), "act": ([128, 16, OT], BF16),
        "coef": ([16, H, 2, 3, 512], BF16),
    }
    for t in taps:
        if t in ("k", "q", "v"):
            continue
        sh, dt = tap_shapes[t]
        tap_t[t] = nc.dram_tensor("tap_" + t, sh, dt, kind="ExternalOutput").ap()

    names = [
        "c_ld", "c2_ld", "wg_ld", "silu_rdy", "mod_ps", "mod_evac",
        "modt2", "mod_rdy",
        "sq_rdy", "sq_free", "stats_ps", "ln_mid", "h_rdy",
        "qk_ps", "kraw_rdy", "rope_out",
        "g_ps", "g_rdy", "v_ps", "v_evac",
        "kc_rdy", "vc_ps", "vc_rdy",
        "scmp_ps", "cm_rdy", "esc_rdy", "imp_init", "topk_done",
        "krfree_a", "qrfree_a", "vrfree_a",
        "krfree_b", "qrfree_b", "vrfree_b",
        "tr_ps", "mt_rdy", "scT_ps", "scT_f", "pcT_rdy",
        "sc_ps", "px", "pvstep", "corr_i", "swa_ps", "swa_pv", "pswa_e", "pswa_m",
        "pv_done", "den_done", "coef_u", "coef_r", "coef_rdy",
        "rep_ps", "rep_ev", "cmb",
        "wo_ps", "x2_rdy",
        "x2b_rdy", "sq2_rdy", "sq2_free", "st2_ps", "ln2_mid", "h2_rdy", "x2_sp",
        "up_ps", "gel_rdy", "act_rdy", "dn_ps", "stg_rdy",
        "tap_st", "c3_ld",
    ]
    S = {n: Sem(nc, n) for n in names}
    DS = {n: DSem(nc, n) for n in (
        "wva", "xw", "wqk", "kacc", "kqw", "vst", "modch", "kr", "qr", "vr")}
    DSAL = {"wmod": "wva", "wvb": "wva", "wul": "wva",
            "xol": "vst",
            "wol": "wqk", "wll": "wqk", "acc": "xw",
            "kswp": "kacc", "kqst": "kqw",
            "kra": "kr", "krb": "kr", "qra": "qr", "qrb": "qr",
            "vra": "vr", "vrb": "vr"}
    for _a, _b in DSAL.items():
        DS[_a] = DS[_b]

    CHN = {"vector": [Sem(nc, "chv"), 0], "scalar": [Sem(nc, "chs"), 0]}

    def ci(instr, eng="vector"):
        CHN[eng][1] += 1
        return instr.then_inc(CHN[eng][0].h, 1)

    def cw(e, eng="vector"):
        e.wait_ge(CHN[eng][0].h, CHN[eng][1])

    sbt = nc.alloc_sbuf_tensor
    ident = sbt("ident_sb", [128, 128], BF16)
    ones = sbt("ones_sb", [128, 128], BF16)
    uptri = sbt("uptri_sb", [128, 128], BF16)
    modv = sbt("modv", [128, 96], F32)
    epsv = sbt("epsv", [128, 1], F32)
    clsc = sbt("clsc", [128, 48], F32)
    mrep = sbt("mrep", [128, 512], BF16)
    varrep = sbt("varrep", [128, 512], F32)
    rstdrep = sbt("rstdrep", [128, 512], BF16)
    sqring = sbt("sqring", [128, 2, 512], BF16)
    esc = sbt("esc", [128, 256], BF16)
    cmadd = sbt("cmadd", [128, 256], F32)
    mxb = sbt("mxb", [128, 128], BF16)
    ropeA = sbt("ropeA", [128, 512], BF16)
    scrB = sbt("scrB", [128, 512], BF16)
    cT_sb = sbt("cT_sb", [128, 16, 1], F32)
    siluT = sbt("siluT", [128, 16, 1], BF16)
    lnv = sbt("lnv", [128, 512], F32)
    onepc = sbt("onepc", [128, 32], F32)
    wg_sb = sbt("wg_sb", [128, NDC, 48], BF16)
    gT = sbt("gT", [48, OT], F32)
    kcmpall = sbt("kcmpall", [128, H, 32], BF16)
    kcf = sbt("kcf", [128, 32], F32)
    modrow_dr = nc.dram_tensor("modrow_dr", [1, 6 * D], F32, kind="Internal").ap()

    AMAX = 186 * 1024
    AR = sbt("ARENA", [128, AMAX], mybir.dt.uint8)
    spans = []

    def V_(name, off_kb, shape, dt, ph):
        off = int(off_kb * 1024)
        per = int(np.prod(shape[1:])) * mybir.dt.size(dt)
        assert off % 256 == 0 and off + per <= AMAX, (name, off, per, AMAX)
        for (n2, o2, p2, ph2) in spans:
            if not (off >= o2 + p2 or o2 >= off + per or ph[1] < ph2[0] or ph2[1] < ph[0]):
                raise AssertionError(f"overlap {name} vs {n2}")
        spans.append((name, off, per, ph))
        ap = AR[0:shape[0], off:off + per].bitcast(dt)
        if len(shape) > 2:
            pat = "p (" + " ".join(f"d{i}" for i in range(1, len(shape))) + ") -> p " + \
                  " ".join(f"d{i}" for i in range(1, len(shape)))
            kw = {f"d{i}": shape[i] for i in range(1, len(shape))}
            ap = ap.rearrange(pat, **kw)
        return ap

    xwin = V_("xwin", 0, [128, 2, NDC, 512], BF16, (0, 0))
    modrow = V_("modrow", 44, [1, 2, 512], F32, (0, 0))
    wmod_sb = V_("wmod", 32, [128, 2, 2, 512], BF16, (0, 0))
    cosk_sb = V_("cosk", 36, [128, NT], BF16, (0, 1))
    sink_sb = V_("sink", 40, [128, NT], BF16, (0, 1))

    wqk_sb = V_("wqk", 0, [128, 2, NDC, 128], BF16, (1, 1))
    wvb_sb = V_("wvb", 8, [128, 2, 512], BF16, (1, 1))
    kraw = V_("kraw", 10, [128, 2, 512], BF16, (1, 1))
    kswp = V_("kswp", 12, [128, 2, 512], BF16, (1, 1))
    kqst = V_("kqst", 14, [128, 2, 2048], BF16, (1, 1))
    vstg = V_("vstg", 22, [128, 2, 512], BF16, (1, 1))

    kr2 = V_("kr2", 0, [128, 2, NT], BF16, (2, 3))
    vr2 = V_("vr2", 8, [128, 2, NTC, 128], BF16, (2, 3))
    qr2 = V_("qr2", 16, [128, 2, OT], BF16, (2, 3))
    vcmpall = V_("vcmpall", 20, [32, H, 128], BF16, (2, 3))
    impa = V_("impa", 24, [128, H * 256], BF16, (2, 3))
    impb = V_("impb", 32, [128, H * 256], BF16, (2, 3))
    selb = V_("selb", 40, [128, H * 256], BF16, (2, 2))
    eexp = V_("eexp", 48, [32, NT], BF16, (2, 3))
    ameans = V_("ameans", 52, [128, NTC * 32], BF16, (2, 2))
    toep = V_("toep", 53, [128, 95], BF16, (2, 3))
    toep32 = V_("toep32", 53.25, [32, 95], BF16, (2, 3))
    rowsel = V_("rowsel", 130, [48, 48 * 128], BF16, (2, 3))
    cmpma = V_("cmpma", 57.5, [128, 256], BF16, (2, 2))
    cmpmaT = V_("cmpmaT", 58, [32, OT], BF16, (2, 3))
    cursup = V_("cursup", 60, [128, 256], BF16, (2, 2))
    cbmin = V_("cbmin", 60.5, [32, OT], BF16, (2, 3))
    swaarr = V_("swaarr", 62.5, [128, 2, 384], BF16, (2, 3))
    anyfix = V_("anyfix", 64, [48, OT], BF16, (2, 3))
    pring = V_("pring", 66, [128, 3, 512], BF16, (3, 3))
    pswa = V_("pswa", 69, [128, 2, 384], BF16, (3, 3))
    pswa2 = V_("pswa2", 70.5, [128, 2, 384], BF16, (3, 3))
    pcmpT = V_("pcmpT", 72, [32, 2, OT], BF16, (3, 3))
    maskT = V_("maskT", 76, [32, 2, OT], BF16, (3, 3))
    mtmp = V_("mtmp", 80, [32, OT], BF16, (3, 3))
    scmpTf = V_("scmpTf", 82, [32, 512], F32, (3, 3))
    coefb = V_("coefb", 84, [48, 2, 512], BF16, (3, 3))
    crep = V_("crep", 90, [128, 3, 512], BF16, (3, 3))
    csc = V_("csc", 93, [128, 3, 512], BF16, (3, 3))
    corrsc = V_("corrsc", 96, [128, 2, 128], BF16, (3, 3))

    oT = V_("oT", 97, [128, H, OT], BF16, (3, 4))
    hT = V_("hT", 122, [128, NDC, NT], BF16, (0, 1))
    xo = V_("xo", 0, [128, 2, OT], F32, (4, 4))
    wo_sb = V_("wo_sb", 8, [128, 2, NDC, 128], BF16, (4, 4))
    x2 = V_("x2", 16, [128, NDC, OT], F32, (4, 5))
    x2b = V_("x2b", 80, [128, NDC, OT], BF16, (5, 5))
    h2sb = V_("h2sb", 112, [128, NDC, OT], BF16, (5, 6))
    act = V_("act", 144, [128, 16, OT], BF16, (6, 6))
    wu_sb = V_("wu_sb", 80, [128, 2, 2, NDC, 128], BF16, (6, 6))
    wl_sb = V_("wl_sb", 96, [128, 2, 16, 128], BF16, (6, 6))
    stage = V_("stage", 104, [128, 2, OT], F32, (6, 6))
    gel_sb = V_("gel", 176, [128, 2, 512], BF16, (6, 6))

    pb = [nc.alloc_psum_tensor(f"pb{i}", [128, 512], F32) for i in range(8)]
    pb0bf = pb[0][0:32, :].bitcast(BF16)

    const_pairs = [
        (ident[:], ident_c), (ones[:], ones_c), (uptri[:], uptri_c),
        (cosk_sb, cosk_c), (sink_sb, sink_c),
    ]
    const_pairs2 = [
        (eexp, eexp_c), (ameans, ameans_c), (toep, toep_c), (toep32, toep32_c),
        (cmpma, cmpma_c), (cmpmaT, cmpmaT_c),
        (cursup, cursup_c), (cbmin, cbmin_c),
        (swaarr.rearrange("p a n -> p (a n)"), swaarr_c), (anyfix, anyfix_c),
    ]

    @on("sync")
    def _(e):
        for dst, src in const_pairs:
            e.dma_start(out=dst, in_=src).then_inc(S["c_ld"].h, 16)
        e.dma_start(out=cT_sb[:], in_=cT.rearrange("(c p) b -> p c b", p=128),
                    allow_slow_non_contiguous=True).then_inc(S["c_ld"].h, 16)
    S["c_ld"].n += 16 * (len(const_pairs) + 1)
    C1 = S["c_ld"].n

    @on("gpsimd")
    def _(e):
        e.dma_start(out=wg_sb[:], in_=wg.rearrange("c p n -> p c n")).then_inc(S["wg_ld"].h, 16)
        for jc in range(24):
            for part in range(8):
                i = jc * 8 + part
                if i >= 2:
                    e.wait_ge(S["mod_ps"].h, i - 1)
                e.dma_start(out=wmod_sb[:, i % 2],
                            in_=wmod[jc, part * 2:part * 2 + 2].rearrange("c p n -> p c n")
                            ).then_inc(DS["wmod"].handle(i), 16)
    S["wg_ld"].n += 16
    for _i in range(192):
        DS["wva"].issued(_i)

    @on("scalar")
    def _(e):
        e.wait_ge(S["c_ld"].h, C1)
        e.activation(siluT[:], cT_sb[:], AF.Silu).then_inc(S["silu_rdy"].h, 1)
    S["silu_rdy"].n += 1

    @on("tensor")
    def _(e):
        e.wait_ge(S["silu_rdy"].h, 1)
        for jc in range(24):
            if jc >= 1:
                e.wait_ge(S["mod_evac"].h, jc)
            for part in range(8):
                i = jc * 8 + part
                DS["wmod"].wait(e, i)
                e.matmul(pb[6][0:1, :], siluT[:, part * 2, :],
                         wmod_sb[:, i % 2, 0, :],
                         start=(part == 0), stop=False)
                e.matmul(pb[6][0:1, :], siluT[:, part * 2 + 1, :],
                         wmod_sb[:, i % 2, 1, :],
                         start=False, stop=(part == 7)).then_inc(S["mod_ps"].h, 1)
    S["mod_ps"].n += 192

    @on("vector")
    def _(e):
        for jc in range(24):
            e.wait_ge(S["mod_ps"].h, (jc + 1) * 8)
            DS["modch"].wait(e, jc - 2)
            e.tensor_copy(modrow[0:1, jc % 2, :], pb[6][0:1, :]
                          ).then_inc(S["mod_evac"].h, 1)
    S["mod_evac"].n += 24

    @on("sync")
    def _(e):
        for jc in range(24):
            e.wait_ge(S["mod_evac"].h, jc + 1)
            e.dma_start(out=modrow_dr[0:1, jc * 512:(jc + 1) * 512],
                        in_=modrow[0:1, jc % 2, :]).then_inc(DS["modch"].handle(jc), 16)
        DS["modch"].wait_all(e)
        e.dma_start(out=modv[:], in_=modrow_dr.rearrange("o (m p) -> (o p) m", p=128),
                    allow_slow_non_contiguous=True).then_inc(S["modt2"].h, 16)
    S["modt2"].n += 16
    for _i in range(24):
        DS["modch"].issued(_i)

    @on("vector")
    def _(e):
        e.wait_ge(S["modt2"].h, 16)
        for c0 in (32, 80):
            sl = modv[:, c0:c0 + 16]
            ci(e.tensor_scalar(clsc[:, 0:16], sl, 0.0, 2.0, ALU.is_ge, ALU.mult))
            cw(e)
            ci(e.tensor_scalar(clsc[:, 16:32], clsc[:, 0:16], 1.0, None, ALU.subtract))
            cw(e)
            ci(e.tensor_tensor(clsc[:, 0:16], sl, clsc[:, 16:32], ALU.mult))
            cw(e)
            ci(e.tensor_scalar(clsc[:, 32:48], clsc[:, 0:16], 0.25, None, ALU.max))
            cw(e)
            ci(e.tensor_tensor(sl, clsc[:, 16:32], clsc[:, 32:48], ALU.mult))
            cw(e)
        ci(e.tensor_scalar(onepc[:, 0:16], modv[:, 16:32], 1.0, None, ALU.add))
        ci(e.tensor_scalar(onepc[:, 16:32], modv[:, 64:80], 1.0, None, ALU.add))
        e.memset(epsv[:], EPS)
        e.engine_nop().then_inc(S["mod_rdy"].h, 1)
    S["mod_rdy"].n += 1

    if "mod" in tap_t:
        @on("sync")
        def _(e):
            e.wait_ge(S["mod_rdy"].h, 1)
            e.dma_start(out=tap_t["mod"], in_=modrow_dr).then_inc(S["tap_st"].h, 16)
        S["tap_st"].n += 16

    @on("gpsimd")
    def _(e):
        for w in range(4):
            if w >= 2:
                e.wait_ge(S["h_rdy"].h, (w - 1) * NDC)
            e.dma_start(out=xwin[:, w % 2],
                        in_=xT.rearrange("(c p) t -> p c t", p=128)[:, :, w * 512:(w + 1) * 512]
                        ).then_inc(DS["xw"].handle(w), 16)

    for _i in range(4):
        DS["xw"].issued(_i)

    @on("scalar")
    def _(e):
        for w in range(4):
            DS["xw"].wait(e, w)
            for dc in range(NDC):
                i = w * NDC + dc
                if i >= 2:
                    e.wait_ge(S["sq_free"].h, i - 1)
                e.activation(sqring[:, i % 2, :], xwin[:, w % 2, dc, :], AF.Square
                             ).then_inc(S["sq_rdy"].h, 1)
            e.wait_ge(S["ln_mid"].h, 2 * w + 1)
            ci(e.activation(lnv[:], varrep[:], AF.Ln, bias=epsv[:]), "scalar")
            cw(e, "scalar")
            e.activation(rstdrep[:], lnv[:], AF.Exp, scale=-0.5).then_inc(S["ln_mid"].h, 1)
    S["sq_rdy"].n += 64

    @on("tensor")
    def _(e):
        e.wait_ge(S["c_ld"].h, C1)
        for w in range(4):
            DS["xw"].wait(e, w)
            if w >= 1:
                e.wait_ge(S["ln_mid"].h, 2 * w - 1)
            for dc in range(NDC):
                i = w * NDC + dc
                mm0 = e.matmul(pb[0][:], ones[:], xwin[:, w % 2, dc, :],
                               start=(dc == 0), stop=(dc == NDC - 1))
                if dc == NDC - 1:
                    mm0.then_inc(S["stats_ps"].h, 1)
                e.wait_ge(S["sq_rdy"].h, i + 1)
                e.matmul(pb[1][:], ones[:], sqring[:, i % 2, :],
                         start=(dc == 0), stop=(dc == NDC - 1)).then_inc(S["sq_free"].h, 1)
    S["stats_ps"].n += 4
    S["sq_free"].n += 64

    @on("vector")
    def _(e):
        e.wait_ge(S["mod_rdy"].h, 1)
        for w in range(4):
            e.wait_ge(S["stats_ps"].h, w + 1)
            e.wait_ge(S["sq_free"].h, 16 * (w + 1))
            ci(e.tensor_scalar(mrep[:], pb[0][:], 1.0 / D, None, ALU.mult))
            cw(e)
            ci(e.tensor_tensor(lnv[:], mrep[:], mrep[:], ALU.mult))
            cw(e)
            e.scalar_tensor_tensor(varrep[:], pb[1][:], 1.0 / D, lnv[:],
                                   ALU.mult, ALU.subtract).then_inc(S["ln_mid"].h, 1)
            e.wait_ge(S["ln_mid"].h, 2 * w + 2)
            for dc in range(NDC):
                ci(e.tensor_tensor(ropeA[:], xwin[:, w % 2, dc, :], mrep[:], ALU.subtract))
                cw(e)
                ci(e.tensor_tensor(scrB[:], ropeA[:], rstdrep[:], ALU.mult))
                cw(e)
                e.tensor_scalar(hT[:, dc, w * 512:(w + 1) * 512], scrB[:],
                                onepc[:, dc:dc + 1], modv[:, dc:dc + 1],
                                ALU.mult, ALU.add).then_inc(S["h_rdy"].h, 1)
    S["ln_mid"].n = 8
    S["h_rdy"].n += 64
    H_ALL = S["h_rdy"].n

    if "h" in tap_t:
        @on("sync")
        def _(e):
            e.wait_ge(S["h_rdy"].h, H_ALL)
            e.dma_start(out=tap_t["h"], in_=hT[:]).then_inc(S["tap_st"].h, 16)
        S["tap_st"].n += 16

    @on("sync")
    def _(e):
        for dst, src in const_pairs2:
            e.dma_start(out=dst, in_=src).then_inc(S["c2_ld"].h, 16)
    S["c2_ld"].n += 16 * len(const_pairs2)
    C2 = S["c2_ld"].n

    def emit_qk(proj):
        wsrc, ddst, nwin = (wq, qdram, 2) if proj == 0 else (wk, kdram, 4)
        ps0 = S["qk_ps"].n
        raw0, out0 = S["kraw_rdy"].n, S["rope_out"].n
        wq0, sw0, st0 = DS["wqk"].uses, DS["kswp"].uses, DS["kqst"].uses
        NTOT = NDC * nwin

        @on("gpsimd")
        def _(e):
            for ec in range(NDC):
                if ec >= 2:
                    e.wait_ge(S["qk_ps"].h, ps0 + (ec - 1) * nwin)
                elif ps0 > 0:
                    e.wait_ge(S["qk_ps"].h, ps0)
                else:
                    e.wait_ge(S["h_rdy"].h, H_ALL)
                e.dma_start(out=wqk_sb[:, ec % 2],
                            in_=wsrc[ec].rearrange("c p n -> p c n")
                            ).then_inc(DS["wqk"].handle(wq0 + ec), 16)

        for _i in range(NDC):
            DS["wqk"].issued(wq0 + _i)

        @on("tensor")
        def _(e):
            if proj == 0:
                e.wait_ge(S["h_rdy"].h, H_ALL)
            for ec in range(NDC):
                DS["wqk"].wait(e, wq0 + ec)
                for w in range(nwin):
                    i = ec * nwin + w
                    if raw0 + i >= 4:
                        e.wait_ge(S["kraw_rdy"].h, raw0 + i - 3)
                    for dc in range(NDC):
                        mm = e.matmul(pb[i % 4][:], wqk_sb[:, ec % 2, dc, :],
                                      hT[:, dc, w * 512:(w + 1) * 512],
                                      start=(dc == 0), stop=(dc == NDC - 1))
                    mm.then_inc(S["qk_ps"].h, 1)
        S["qk_ps"].n += NTOT

        @on("scalar")
        def _(e):
            for i in range(NTOT):
                e.wait_ge(S["kraw_rdy"].h, raw0 + i + 1)
                if i >= 2:
                    e.wait_ge(S["rope_out"].h, out0 + i - 1)
                e.dma_start(out=kswp[0:64, i % 2, :], in_=kraw[64:128, i % 2, :]
                            ).then_inc(DS["kswp"].handle(sw0 + i), 16)
                e.dma_start(out=kswp[64:128, i % 2, :], in_=kraw[0:64, i % 2, :]
                            ).then_inc(DS["kswp"].handle(sw0 + i), 16)

        for _i in range(NTOT):
            DS["kacc"].issued(sw0 + _i, 32)

        @on("vector")
        def _(e):
            def rope(j):
                ec, w = j // nwin, j % nwin
                DS["kswp"].wait(e, sw0 + j)
                e.wait_ge(S["rope_out"].h, out0 + j)
                if st0 + ec >= 2 and w == 0:
                    DS["kqst"].wait(e, st0 + ec - 2)
                ci(e.tensor_tensor(ropeA[:], kraw[:, j % 2, :],
                                   cosk_sb[:, w * 512:(w + 1) * 512], ALU.mult))
                ci(e.tensor_tensor(vstg[:, j % 2, :], kswp[:, j % 2, :],
                                   sink_sb[:, w * 512:(w + 1) * 512], ALU.mult))
                cw(e)
                e.tensor_tensor(kqst[:, ec % 2, w * 512:(w + 1) * 512],
                                ropeA[:], vstg[:, j % 2, :], ALU.add
                                ).then_inc(S["rope_out"].h, 1)
            for i in range(NTOT):
                e.wait_ge(S["qk_ps"].h, ps0 + i + 1)
                DS["kswp"].wait(e, sw0 + i - 2)
                e.tensor_copy(kraw[:, i % 2, :], pb[i % 4][:]).then_inc(S["kraw_rdy"].h, 1)
                if i >= 1:
                    rope(i - 1)
            rope(NTOT - 1)
        S["kraw_rdy"].n += NTOT
        S["rope_out"].n += NTOT

        @on("sync")
        def _(e):
            for ec in range(NDC):
                e.wait_ge(S["rope_out"].h, out0 + (ec + 1) * nwin)
                e.dma_start(out=ddst[ec], in_=kqst[:, ec % 2, 0:nwin * 512]
                            ).then_inc(DS["kqst"].handle(st0 + ec), 16)

        for _i in range(NDC):
            DS["kqw"].issued(st0 + _i)

    emit_qk(0)
    emit_qk(1)
    S_KRAW_ALL = S["kraw_rdy"].n

    @on("tensor")
    def _(e):
        e.wait_ge(S["wg_ld"].h, 16)
        e.wait_ge(S["mod_evac"].h, 24)
        for w in range(2):
            if w >= 1:
                e.wait_ge(S["g_rdy"].h, w)
            for dc in range(NDC):
                mm = e.matmul(pb[6][0:48, :], wg_sb[:, dc, :],
                              hT[:, dc, w * 512:(w + 1) * 512],
                              start=(dc == 0), stop=(dc == NDC - 1))
            mm.then_inc(S["g_ps"].h, 1)
    S["g_ps"].n += 2

    @on("scalar")
    def _(e):
        for w in range(2):
            e.wait_ge(S["g_ps"].h, w + 1)
            e.activation(gT[:, w * 512:(w + 1) * 512], pb[6][0:48, :], AF.Sigmoid
                         ).then_inc(S["g_rdy"].h, 1)
    S["g_rdy"].n += 2

    if "g" in tap_t:
        @on("sync")
        def _(e):
            e.wait_ge(S["g_rdy"].h, 2)
            e.dma_start(out=tap_t["g"], in_=gT[:]).then_inc(S["tap_st"].h, 16)
        S["tap_st"].n += 16

    WVB0 = DS["wva"].uses
    for _i in range(128):
        DS["wva"].issued(WVB0 + _i)

    @on("gpsimd")
    def _(e):
        for es in range(4):
            for th in range(2):
                for dc in range(NDC):
                    i = (es * 2 + th) * NDC + dc
                    if i >= 2:
                        e.wait_ge(S["v_ps"].h, i - 1)
                    else:
                        e.wait_ge(S["mod_ps"].h, 192)
                    e.dma_start(out=wvb_sb[:, i % 2, :], in_=wv[es, dc]
                                ).then_inc(DS["wvb"].handle(WVB0 + i), 16)

    @on("tensor")
    def _(e):
        e.wait_ge(S["g_rdy"].h, 2)
        e.wait_ge(S["kraw_rdy"].h, S_KRAW_ALL)
        for es in range(4):
            for th in range(2):
                u = es * 2 + th
                if u >= 1:
                    e.wait_ge(S["v_evac"].h, u * 8)
                for dc in range(NDC):
                    i = u * NDC + dc
                    DS["wvb"].wait(e, WVB0 + i)
                    for bk in range(8):
                        tc = th * 8 + bk
                        mm = e.matmul(pb[bk][:], hT[:, dc, tc * 128:(tc + 1) * 128],
                                      wvb_sb[:, i % 2, :],
                                      start=(dc == 0), stop=(dc == NDC - 1))
                    mm.then_inc(S["v_ps"].h, 1)
    S["v_ps"].n += 128

    @on("vector")
    def _(e):
        e.wait_ge(S["rope_out"].h, S["rope_out"].n)
        for es in range(4):
            for th in range(2):
                u = es * 2 + th
                e.wait_ge(S["v_ps"].h, (u + 1) * NDC)
                for bk in range(8):
                    i = u * 8 + bk
                    DS["vst"].wait(e, i - 2)
                    e.tensor_copy(vstg[:, i % 2, :], pb[bk][:]).then_inc(S["v_evac"].h, 1)
    S["v_evac"].n += 64

    @on("sync")
    def _(e):
        for es in range(4):
            for th in range(2):
                for bk in range(8):
                    i = (es * 2 + th) * 8 + bk
                    tc = th * 8 + bk
                    e.wait_ge(S["v_evac"].h, i + 1)
                    e.dma_start(out=vdram[tc, es], in_=vstg[:, i % 2, :]
                                ).then_inc(DS["vst"].handle(i), 16)

    for _i in range(64):
        DS["vst"].issued(_i)

    @on("sync")
    def _(e):
        e.wait_ge(S["v_evac"].h, 64)
        e.dma_start(out=rowsel, in_=rowsel_c).then_inc(S["c3_ld"].h, 16)
    S["c3_ld"].n += 16

    KQ_T = tuple(sub.n for sub in DS["kqst"].s)
    VST_T = tuple(sub.n for sub in DS["vst"].s)

    @on("gpsimd")
    def _(e, KQ_T=KQ_T, VST_T=VST_T):
        for _k, _sub in enumerate(DS["kqst"].s):
            if KQ_T[_k]:
                e.wait_ge(_sub.h, KQ_T[_k])
        for _k, _sub in enumerate(DS["vst"].s):
            if VST_T[_k]:
                e.wait_ge(_sub.h, VST_T[_k])
        e.wait_ge(S["c2_ld"].h, C2)
        for h in range(H):
            if h >= 2:
                e.wait_ge(S["kc_rdy"].h, h - 1)
                e.wait_ge(S["esc_rdy"].h, h - 1)
                e.wait_ge(S["vc_rdy"].h, h - 1)
            e.dma_start(out=kr2[:, h % 2, :], in_=kdram[h]).then_inc(DS["kra"].handle(h), 16)
            e.dma_start(out=qr2[:, h % 2, :], in_=qdram[h]).then_inc(DS["qra"].handle(h), 16)
            e.dma_start(out=vr2[:, h % 2],
                        in_=vdram[:, h // 4, :, (h % 4) * 128:(h % 4 + 1) * 128
                                  ].rearrange("c p n -> p c n")).then_inc(DS["vra"].handle(h), 16)

    for _h in range(H):
        DS["kr"].issued(_h)
        DS["qr"].issued(_h)
        DS["vr"].issued(_h)

    @on("tensor")
    def _(e):
        e.wait_ge(S["c2_ld"].h, C2)
        for h in range(H):
            DS["qra"].wait(e, h)
            DS["vra"].wait(e, h)
            e.wait_ge(S["kc_rdy"].h, h + 1)
            if h >= 1:
                e.wait_ge(S["cm_rdy"].h, h)
            if h >= 2:
                e.wait_ge(S["vc_rdy"].h, h - 1)
            for qb in range(8):
                mm = e.matmul(pb[0][:, qb * 32:(qb + 1) * 32],
                              qr2[:, h % 2, qb * 128:(qb + 1) * 128],
                              kcmpall[:, h, :], start=True, stop=True)
            mm.then_inc(S["scmp_ps"].h, 1)
            for c in range(NTC):
                mm = e.matmul(pb[2 + h % 2][0:32, 0:128], ameans[:, c * 32:(c + 1) * 32],
                              vr2[:, h % 2, c, :], start=(c == 0), stop=(c == NTC - 1))
            mm.then_inc(S["vc_ps"].h, 1)
    S["scmp_ps"].n += H
    S["vc_ps"].n += H

    @on("vector")
    def _(e):
        e.wait_ge(S["c2_ld"].h, C2)
        for h in range(H):
            DS["kra"].wait(e, h)
            ci(e.reduce_sum(kcf[:], kr2[:, h % 2, :].rearrange("p (b t) -> p b t", b=32),
                            axis=AX.X))
            cw(e)
            e.tensor_scalar(kcmpall[:, h, :], kcf[:], 1.0 / 64.0, None, ALU.mult
                            ).then_inc(S["kc_rdy"].h, 1)
            e.wait_ge(S["vc_ps"].h, h + 1)
            e.tensor_copy(vcmpall[:, h, :], pb[2 + h % 2][0:32, 0:128]).then_inc(S["vc_rdy"].h, 1)
            e.wait_ge(S["scmp_ps"].h, h + 1)
            e.tensor_tensor(cmadd[:], pb[0][:, 0:256], cmpma[:], ALU.add).then_inc(S["cm_rdy"].h, 1)
            e.wait_ge(S["esc_rdy"].h, h + 1)
            e.tensor_tensor(impa[:, h * 256:(h + 1) * 256], esc[:], cursup[:], ALU.add
                            ).then_inc(S["imp_init"].h, 1)
    S["kc_rdy"].n += H
    S["vc_rdy"].n += H
    S["cm_rdy"].n += H
    S["imp_init"].n += H

    @on("scalar")
    def _(e):
        for h in range(H):
            e.wait_ge(S["cm_rdy"].h, h + 1)
            e.activation(esc[:], cmadd[:], AF.Exp, scale=KAPPA).then_inc(S["esc_rdy"].h, 1)
    S["esc_rdy"].n += H

    if "imp" in tap_t:
        @on("sync")
        def _(e):
            e.wait_ge(S["imp_init"].h, H)
            e.dma_start(out=tap_t["imp"], in_=impa[:]).then_inc(S["tap_st"].h, 16)
        S["tap_st"].n += 16
    if "kcmp" in tap_t:
        @on("sync")
        def _(e):
            e.wait_ge(S["kc_rdy"].h, H)
            e.dma_start(out=tap_t["kcmp"], in_=kcmpall[:]).then_inc(S["tap_st"].h, 16)
        S["tap_st"].n += 16
    if "vcmp" in tap_t:
        @on("sync")
        def _(e):
            e.wait_ge(S["vc_rdy"].h, H)
            e.dma_start(out=tap_t["vcmp"], in_=vcmpall[:]).then_inc(S["tap_st"].h, 16)
        S["tap_st"].n += 16

    @on("vector")
    def _(e):
        e.wait_ge(S["imp_init"].h, H)
        for it in range(7):
            src = impa if it % 2 == 0 else impb
            dst = impb if it % 2 == 0 else impa
            ci(e.reduce_max(mxb[:], src[:].rearrange("p (g b) -> p g b", b=32), axis=AX.X))
            cw(e)
            ci(e.tensor_tensor(selb[:].rearrange("p (g b) -> p g b", b=32),
                               src[:].rearrange("p (g b) -> p g b", b=32),
                               mxb[:].unsqueeze(2).broadcast_to([128, 128, 32]), ALU.is_ge))
            cw(e)
            ci(e.scalar_tensor_tensor(dst[:], selb[:], -SUP, src[:], ALU.mult, ALU.add))
            cw(e)
        e.engine_nop().then_inc(S["topk_done"].h, 1)
    S["topk_done"].n += 1

    gc = [0]
    pvst = [0]
    cr = [0]
    swp_ = [0]
    rep_ = [0]
    units = [0]

    @on("gpsimd")
    def _(e):
        for h in range(H):
            e.wait_ge(S["kc_rdy"].h, H)
            e.wait_ge(S["vc_rdy"].h, H)
            e.wait_ge(S["esc_rdy"].h, H)
            if h >= 2:
                e.wait_ge(S["den_done"].h, 2 * (h - 1))
            e.dma_start(out=kr2[:, h % 2, :], in_=kdram[h]).then_inc(DS["krb"].handle(16 + h), 16)
            e.dma_start(out=qr2[:, h % 2, :], in_=qdram[h]).then_inc(DS["qrb"].handle(16 + h), 16)
            e.dma_start(out=vr2[:, h % 2],
                        in_=vdram[:, h // 4, :, (h % 4) * 128:(h % 4 + 1) * 128
                                  ].rearrange("c p n -> p c n")).then_inc(DS["vrb"].handle(16 + h), 16)

    for _h in range(H):
        DS["kr"].issued(16 + _h)
        DS["qr"].issued(16 + _h)
        DS["vr"].issued(16 + _h)

    for h in range(H):
        trbase = S["tr_ps"].n
        gc_at_build = gc[0]

        @on("tensor")
        def _(e, h=h, gc0=gc_at_build):
            e.wait_ge(S["topk_done"].h, 1)
            if gc0 > 0:
                e.wait_ge(S["px"].h, gc0)
            if h > 0:
                e.wait_ge(S["mt_rdy"].h, h)
            for qt in range(8):
                mm = e.transpose(pb0bf[:, qt * 128:(qt + 1) * 128],
                                 impb[:, h * 256 + qt * 32:h * 256 + (qt + 1) * 32], ident[:])
            mm.then_inc(S["tr_ps"].h, 1)
        S["tr_ps"].n += 1

        @on("vector")
        def _(e, h=h):
            e.wait_ge(S["tr_ps"].h, h + 1)
            ci(e.tensor_scalar(mtmp[:], pb0bf[:], SEL_THR, MASKPRE, ALU.is_ge, ALU.mult))
            cw(e)
            e.tensor_tensor(maskT[:, h % 2, :], mtmp[:], cbmin[:], ALU.min
                            ).then_inc(S["mt_rdy"].h, 1)
        S["mt_rdy"].n += 1

        @on("tensor")
        def _(e, h=h):
            e.wait_ge(S["mt_rdy"].h, h + 1)
            DS["qrb"].wait(e, 16 + h)
            if h > 0:
                e.wait_ge(S["scT_f"].h, 2 * h)
            for g in range(2):
                e.matmul(pb[g][0:32, :], kcmpall[:, h, :], qr2[:, h % 2, g * 512:(g + 1) * 512],
                         start=True, stop=True).then_inc(S["scT_ps"].h, 1)
        S["scT_ps"].n += 2

        @on("vector")
        def _(e, h=h):
            for g in range(2):
                e.wait_ge(S["scT_ps"].h, 2 * h + g + 1)
                if 2 * h + g >= 1:
                    e.wait_ge(S["pcT_rdy"].h, 2 * h + g)
                e.tensor_tensor(scmpTf[:], pb[g][0:32, :], cmpmaT[:, g * 512:(g + 1) * 512],
                                ALU.add).then_inc(S["scT_f"].h, 1)
        S["scT_f"].n += 2

        @on("scalar")
        def _(e, h=h):
            for g in range(2):
                e.wait_ge(S["scT_f"].h, 2 * h + g + 1)
                e.activation(pcmpT[:, h % 2, g * 512:(g + 1) * 512], scmpTf[:],
                             AF.Exp, scale=KAPPA).then_inc(S["pcT_rdy"].h, 1)
        S["pcT_rdy"].n += 2

        if "maskT" in tap_t:
            @on("sync")
            def _(e, h=h):
                e.wait_ge(S["mt_rdy"].h, h + 1)
                e.dma_start(out=tap_t["maskT"][:, h, :], in_=maskT[:, h % 2, :]
                            ).then_inc(S["tap_st"].h, 16)
            S["tap_st"].n += 16
        if "pcmpT" in tap_t:
            @on("sync")
            def _(e, h=h):
                e.wait_ge(S["pcT_rdy"].h, 2 * h + 2)
                e.dma_start(out=tap_t["pcmpT"][:, h, :], in_=pcmpT[:, h % 2, :]
                            ).then_inc(S["tap_st"].h, 16)
            S["tap_st"].n += 16

        for g in range(2):
            ECH = 4 * g + 4
            chunks = list(range(ECH)) + list(range(8, 8 + ECH))
            dpar = (2 * h + g) % 2
            unit = 2 * h + g
            gbase = gc[0]
            pvbase = pvst[0]
            denb = pb[5 + dpar]

            @on("tensor")
            def _(e, h=h, g=g, chunks=tuple(chunks), gbase=gbase, unit=unit,
                  dpar=dpar, denb=denb, pvbase=pvbase, crbase=cr[0]):
                DS["krb"].wait(e, 16 + h)
                DS["vrb"].wait(e, 16 + h)
                e.wait_ge(S["mt_rdy"].h, h + 1)
                e.wait_ge(S["scT_f"].h, 2 * (h + 1))
                if unit >= 1:
                    e.wait_ge(S["cmb"].h, unit)
                if unit >= 2:
                    e.wait_ge(S["coef_u"].h, unit - 1)
                ci = [0]

                def pvden(j):
                    c2 = chunks[j]
                    e.wait_ge(S["px"].h, gbase + j + 1)
                    e.matmul(pb[2][:], vr2[:, h % 2, c2, :], pring[:, (gbase + j) % 3, :],
                             start=(j == 0), stop=(j == len(chunks) - 1))
                    mm = e.matmul(denb[0:48, :], toep[:, 31 - h:79 - h],
                                  pring[:, (gbase + j) % 3, :],
                                  start=(j == 0), stop=False)
                    if 4 * g <= c2 < 4 * g + 4:
                        jl = c2 - 4 * g
                        e.wait_ge(S["corr_i"].h, crbase + ci[0] + 1)
                        e.matmul(pb[2][:, jl * 128:(jl + 1) * 128], vr2[:, h % 2, c2, :],
                                 corrsc[:, ci[0] % 2, :], start=False, stop=False)
                        mm = e.matmul(denb[0:48, jl * 128:(jl + 1) * 128],
                                      toep[:, 31 - h:79 - h], corrsc[:, ci[0] % 2, :],
                                      start=False, stop=False)
                        ci[0] += 1
                    mm.then_inc(S["pvstep"].h, 1)

                for i, c in enumerate(chunks):
                    if gbase + i >= 2:
                        e.wait_ge(S["px"].h, gbase + i - 1)
                    e.matmul(pb[i % 2][:], kr2[:, h % 2, c * 128:(c + 1) * 128],
                             qr2[:, h % 2, g * 512:(g + 1) * 512], start=True, stop=False)
                    e.matmul(pb[i % 2][:], eexp[:, c * 128:(c + 1) * 128],
                             maskT[:, h % 2, g * 512:(g + 1) * 512], start=False, stop=True
                             ).then_inc(S["sc_ps"].h, 1)
                    if i >= 1:
                        pvden(i - 1)
                pvden(len(chunks) - 1)
            S["sc_ps"].n += len(chunks)
            S["pvstep"].n += len(chunks)

            @on("scalar")
            def _(e, gbase=gbase, nch=len(chunks), pvbase=pvbase):
                for i in range(nch):
                    e.wait_ge(S["sc_ps"].h, gbase + i + 1)
                    if gbase + i >= 3:
                        e.wait_ge(S["pvstep"].h, gbase + i - 2)
                    e.activation(pring[:, (gbase + i) % 3, :], pb[i % 2][:],
                                 AF.Exp, scale=KAPPA).then_inc(S["px"].h, 1)
            S["px"].n += len(chunks)

            corr_chunks = [i for i, c in enumerate(chunks) if 4 * g <= c < 4 * g + 4]

            @on("vector")
            def _(e, g=g, chunks=tuple(chunks), gbase=gbase,
                  corr_chunks=tuple(corr_chunks)):
                for ci, i in enumerate(corr_chunks):
                    c = chunks[i]
                    jl = c - 4 * g
                    e.wait_ge(S["px"].h, gbase + i + 1)
                    if ci >= 2:
                        e.wait_ge(S["pvstep"].h, gbase + corr_chunks[ci - 2] + 1)
                    e.tensor_tensor(corrsc[:, ci % 2, :],
                                    pring[:, (gbase + i) % 3, jl * 128:(jl + 1) * 128],
                                    uptri[:], ALU.mult).then_inc(S["corr_i"].h, 1)
            S["corr_i"].n += 4
            gc[0] += len(chunks)
            pvst[0] += len(chunks)
            cr[0] += 4

            @on("tensor")
            def _(e, h=h, g=g, unit=unit, denb=denb, swb=swp_[0]):
                if unit >= 1:
                    e.wait_ge(S["rep_ev"].h, 3 * unit)
                for jl in range(4):
                    j = 4 * g + jl
                    E = 2 * (j + 1)
                    cands = [j, max(8 + j - 1, 8), 8 + j]
                    u = swb + jl
                    if u >= swb + 1:
                        e.wait_ge(S["pswa_e"].h, u)
                    for p, cp in enumerate(cands):
                        mm = e.matmul(pb[7][:, p * 128:(p + 1) * 128],
                                      kr2[:, h % 2, cp * 128:(cp + 1) * 128],
                                      qr2[:, h % 2, j * 128:(j + 1) * 128],
                                      start=True, stop=True)
                    mm.then_inc(S["swa_ps"].h, 1)
                    e.wait_ge(S["pswa_m"].h, u + 1)
                    for p, cp in enumerate(cands):
                        e.matmul(pb[3][:, jl * 128:(jl + 1) * 128], vr2[:, h % 2, cp, :],
                                 pswa2[:, jl % 2, p * 128:(p + 1) * 128],
                                 start=(jl == 0 and p == 0), stop=(jl == 3 and p == 2))
                        mm = e.matmul(denb[0:48, jl * 128:(jl + 1) * 128],
                                      toep[:, 15 - h:63 - h],
                                      pswa2[:, jl % 2, p * 128:(p + 1) * 128],
                                      start=False, stop=False)
                    mm.then_inc(S["swa_pv"].h, 1)
            S["swa_ps"].n += 4
            S["swa_pv"].n += 4

            @on("scalar")
            def _(e, swb=swp_[0]):
                for jl in range(4):
                    e.wait_ge(S["swa_ps"].h, swb + jl + 1)
                    e.activation(pswa[:, jl % 2, :], pb[7][:, 0:384], AF.Exp, scale=KAPPA
                                 ).then_inc(S["pswa_e"].h, 1)
            S["pswa_e"].n += 4

            @on("vector")
            def _(e, g=g, swb=swp_[0]):
                for jl in range(4):
                    e.wait_ge(S["pswa_e"].h, swb + jl + 1)
                    if jl >= 2:
                        e.wait_ge(S["swa_pv"].h, swb + jl - 1)
                    cls = 0 if (g == 0 and jl == 0) else 1
                    e.tensor_tensor(pswa2[:, jl % 2, :], pswa[:, jl % 2, :],
                                    swaarr[:, cls, :], ALU.mult).then_inc(S["pswa_m"].h, 1)
            S["pswa_m"].n += 4
            swp_[0] += 4

            @on("tensor")
            def _(e, h=h, g=g, denb=denb, unit=unit):
                e.wait_ge(S["pcT_rdy"].h, 2 * h + 2)
                e.matmul(pb[4][:], vcmpall[:, h, :], pcmpT[:, h % 2, g * 512:(g + 1) * 512],
                         start=True, stop=True).then_inc(S["pv_done"].h, 1)
                e.matmul(denb[0:48, :], toep32[:, 47 - h:95 - h],
                         pcmpT[:, h % 2, g * 512:(g + 1) * 512], start=False, stop=True
                         ).then_inc(S["den_done"].h, 1)
            S["pv_done"].n += 1
            S["den_done"].n += 1

            @on("vector")
            def _(e, g=g, denb=denb, unit=unit):
                e.wait_ge(S["den_done"].h, unit + 1)
                ci(e.tensor_tensor(varrep[0:48, :], denb[0:48, :],
                                   anyfix[:, g * 512:(g + 1) * 512], ALU.add))
                cw(e)
                e.tensor_scalar(lnv[0:48, :], varrep[0:48, :], 1e-20, None, ALU.max
                                ).then_inc(S["coef_u"].h, 1)
            S["coef_u"].n += 1

            @on("scalar")
            def _(e, unit=unit):
                e.wait_ge(S["coef_u"].h, unit + 1)
                ci(e.activation(varrep[0:48, :], lnv[0:48, :], AF.Ln), "scalar")
                cw(e, "scalar")
                e.activation(lnv[0:48, :], varrep[0:48, :], AF.Exp, scale=-1.0
                             ).then_inc(S["coef_r"].h, 1)
            S["coef_r"].n += 1

            @on("vector")
            def _(e, h=h, g=g, dpar=dpar, unit=unit):
                e.wait_ge(S["coef_r"].h, unit + 1)
                e.wait_ge(S["g_rdy"].h, 2)
                e.tensor_tensor(coefb[:, dpar, :], lnv[0:48, :],
                                gT[:, g * 512:(g + 1) * 512], ALU.mult)
                e.engine_nop().then_inc(S["coef_rdy"].h, 1)
            S["coef_rdy"].n += 1

            @on("tensor")
            def _(e, h=h, dpar=dpar, unit=unit, rb=rep_[0]):
                e.wait_ge(S["coef_rdy"].h, unit + 1)
                e.wait_ge(S["pswa_e"].h, (unit + 1) * 4)
                if unit == 0:
                    e.wait_ge(S["c3_ld"].h, 16)
                for b in range(3):
                    if rb + b >= 1:
                        e.wait_ge(S["rep_ev"].h, rb + b)
                    e.matmul(pb[7][:], rowsel[:, (16 * b + h) * 128:(16 * b + h + 1) * 128],
                             coefb[0:48, dpar, :], start=True, stop=True
                             ).then_inc(S["rep_ps"].h, 1)
            S["rep_ps"].n += 3

            @on("vector")
            def _(e, rb=rep_[0]):
                for b in range(3):
                    e.wait_ge(S["rep_ps"].h, rb + b + 1)
                    e.tensor_copy(crep[:, b, :], pb[7][:]).then_inc(S["rep_ev"].h, 1)
            S["rep_ev"].n += 3
            rep_[0] += 3

            @on("vector")
            def _(e, h=h, g=g, unit=unit):
                e.wait_ge(S["rep_ev"].h, 3 * (unit + 1))
                e.wait_ge(S["pv_done"].h, unit + 1)
                ci(e.tensor_tensor(csc[:, 0, :], pb[4][:], crep[:, 0, :], ALU.mult))
                ci(e.tensor_tensor(csc[:, 1, :], pb[2][:], crep[:, 1, :], ALU.mult))
                cw(e)
                ci(e.tensor_tensor(csc[:, 2, :], csc[:, 0, :], csc[:, 1, :], ALU.add))
                cw(e)
                ci(e.tensor_tensor(csc[:, 0, :], pb[3][:], crep[:, 2, :], ALU.mult))
                cw(e)
                e.tensor_tensor(oT[:, h, g * 512:(g + 1) * 512], csc[:, 2, :], csc[:, 0, :],
                                ALU.add).then_inc(S["cmb"].h, 1)
            S["cmb"].n += 1
            units[0] += 1

    CMB_ALL = S["cmb"].n

    if "o" in tap_t:
        @on("sync")
        def _(e):
            e.wait_ge(S["cmb"].h, CMB_ALL)
            e.dma_start(out=tap_t["o"], in_=oT[:]).then_inc(S["tap_st"].h, 16)
        S["tap_st"].n += 16

    XOL0 = DS["xol"].uses
    WOL0 = DS["wqk"].uses
    for _i in range(NDC):
        DS["xol"].issued(XOL0 + _i)
        DS["wqk"].issued(WOL0 + _i)
    @on("sync")
    def _(e):
        e.wait_ge(S["cmb"].h, CMB_ALL)
        for dc in range(NDC):
            if dc >= 2:
                e.wait_ge(S["x2_rdy"].h, 2 * (dc - 1))
            e.dma_start(out=xo[:, dc % 2, :],
                        in_=xT.rearrange("(c p) t -> p c t", p=128)[:, dc, 0:OT]
                        ).then_inc(DS["xol"].handle(XOL0 + dc), 16)

    @on("gpsimd")
    def _(e):
        e.wait_ge(S["cmb"].h, CMB_ALL)
        for dc in range(NDC):
            if dc >= 2:
                e.wait_ge(S["wo_ps"].h, 2 * (dc - 1))
            e.dma_start(out=wo_sb[:, dc % 2], in_=wo[dc].rearrange("c p n -> p c n")
                        ).then_inc(DS["wol"].handle(WOL0 + dc), 16)

    @on("tensor")
    def _(e):
        e.wait_ge(S["cmb"].h, CMB_ALL)
        for dc in range(NDC):
            DS["wol"].wait(e, WOL0 + dc)
            if dc >= 2:
                e.wait_ge(S["x2_rdy"].h, 2 * (dc - 1))
            for half in range(2):
                bank = pb[(dc % 2) * 2 + half]
                for ec in range(NDC):
                    mm = e.matmul(bank[:], wo_sb[:, dc % 2, ec, :],
                                  oT[:, ec, half * 512:(half + 1) * 512],
                                  start=(ec == 0), stop=(ec == NDC - 1))
                mm.then_inc(S["wo_ps"].h, 1)
    S["wo_ps"].n += 32

    @on("vector")
    def _(e):
        for dc in range(NDC):
            DS["xol"].wait(e, XOL0 + dc)
            for half in range(2):
                e.wait_ge(S["wo_ps"].h, 2 * dc + half + 1)
                e.scalar_tensor_tensor(x2[:, dc, half * 512:(half + 1) * 512],
                                       pb[(dc % 2) * 2 + half][:],
                                       modv[:, 32 + dc:33 + dc],
                                       xo[:, dc % 2, half * 512:(half + 1) * 512],
                                       ALU.mult, ALU.add).then_inc(S["x2_rdy"].h, 1)
    S["x2_rdy"].n += 32

    if "x2" in tap_t:
        @on("sync")
        def _(e):
            e.wait_ge(S["x2_rdy"].h, 32)
            e.dma_start(out=tap_t["x2"], in_=x2[:]).then_inc(S["tap_st"].h, 16)
        S["tap_st"].n += 16

    @on("vector")
    def _(e):
        e.wait_ge(S["x2_rdy"].h, 32)
        for dc in range(NDC):
            e.tensor_copy(x2b[:, dc, :], x2[:, dc, :])
        e.engine_nop().then_inc(S["x2b_rdy"].h, 1)
    S["x2b_rdy"].n += 1

    @on("scalar")
    def _(e):
        e.wait_ge(S["x2b_rdy"].h, 1)
        for w in range(2):
            for dc in range(NDC):
                i = w * NDC + dc
                if i >= 2:
                    e.wait_ge(S["sq2_free"].h, i - 1)
                e.activation(sqring[:, i % 2, :], x2b[:, dc, w * 512:(w + 1) * 512],
                             AF.Square).then_inc(S["sq2_rdy"].h, 1)
    S["sq2_rdy"].n += 32

    @on("tensor")
    def _(e):
        e.wait_ge(S["x2b_rdy"].h, 1)
        for w in range(2):
            if w >= 1:
                e.wait_ge(S["ln2_mid"].h, 2 * w - 1)
            for dc in range(NDC):
                i = w * NDC + dc
                mm0 = e.matmul(pb[0][:], ones[:], x2b[:, dc, w * 512:(w + 1) * 512],
                               start=(dc == 0), stop=(dc == NDC - 1))
                if dc == NDC - 1:
                    mm0.then_inc(S["st2_ps"].h, 1)
                e.wait_ge(S["sq2_rdy"].h, i + 1)
                e.matmul(pb[1][:], ones[:], sqring[:, i % 2, :],
                         start=(dc == 0), stop=(dc == NDC - 1)).then_inc(S["sq2_free"].h, 1)
    S["st2_ps"].n += 2
    S["sq2_free"].n += 32

    @on("scalar")
    def _(e):
        for w in range(2):
            e.wait_ge(S["ln2_mid"].h, 2 * w + 1)
            ci(e.activation(lnv[:], varrep[:], AF.Ln, bias=epsv[:]), "scalar")
            cw(e, "scalar")
            e.activation(rstdrep[:], lnv[:], AF.Exp, scale=-0.5).then_inc(S["ln2_mid"].h, 1)

    @on("vector")
    def _(e):
        for w in range(2):
            e.wait_ge(S["st2_ps"].h, w + 1)
            e.wait_ge(S["sq2_free"].h, 16 * (w + 1))
            ci(e.tensor_scalar(mrep[:], pb[0][:], 1.0 / D, None, ALU.mult))
            cw(e)
            ci(e.tensor_tensor(lnv[:], mrep[:], mrep[:], ALU.mult))
            cw(e)
            e.scalar_tensor_tensor(varrep[:], pb[1][:], 1.0 / D, lnv[:],
                                   ALU.mult, ALU.subtract).then_inc(S["ln2_mid"].h, 1)
            e.wait_ge(S["ln2_mid"].h, 2 * w + 2)
            for dc in range(NDC):
                ci(e.tensor_tensor(ropeA[:], x2[:, dc, w * 512:(w + 1) * 512], mrep[:],
                                   ALU.subtract))
                cw(e)
                ci(e.tensor_tensor(scrB[:], ropeA[:], rstdrep[:], ALU.mult))
                cw(e)
                e.tensor_scalar(h2sb[:, dc, w * 512:(w + 1) * 512], scrB[:],
                                onepc[:, 16 + dc:17 + dc], modv[:, 48 + dc:49 + dc],
                                ALU.mult, ALU.add).then_inc(S["h2_rdy"].h, 1)
    S["ln2_mid"].n = 4
    S["h2_rdy"].n += 32
    H2_ALL = S["h2_rdy"].n

    if "h2" in tap_t:
        @on("sync")
        def _(e):
            e.wait_ge(S["h2_rdy"].h, H2_ALL)
            e.dma_start(out=tap_t["h2"], in_=h2sb[:]).then_inc(S["tap_st"].h, 16)
        S["tap_st"].n += 16

    @on("sync")
    def _(e):
        e.wait_ge(S["h2_rdy"].h, H2_ALL)
        e.dma_start(out=outT.rearrange("(c p) t -> p c t", p=128), in_=x2[:]
                    ).then_inc(S["x2_sp"].h, 16)
    S["x2_sp"].n += 16

    WUL0 = DS["wul"].uses
    WLL0 = DS["wqk"].uses
    ACC0 = DS["acc"].uses
    assert (ACC0 % 2) == 0
    for _hc in range(64):
        DS["wul"].issued(WUL0 + _hc, 32)
    for _u in range(64):
        DS["wqk"].issued(WLL0 + _u)
        DS["acc"].issued(ACC0 + _u)
    @on("gpsimd")
    def _(e):
        e.wait_ge(S["x2_sp"].h, 16)
        for grp in range(4):
            for hl in range(16):
                hc = grp * 16 + hl
                if hc >= 2:
                    e.wait_ge(S["up_ps"].h, hc - 1)
                e.dma_start(out=wu_sb[:, hc % 2, 0], in_=wu0[hc].rearrange("c p n -> p c n")
                            ).then_inc(DS["wul"].handle(WUL0 + hc), 16)
                e.dma_start(out=wu_sb[:, hc % 2, 1], in_=wu1[hc].rearrange("c p n -> p c n")
                            ).then_inc(DS["wul"].handle(WUL0 + hc), 16)
            def acc_dma(ua):
                e.wait_ge(S["stg_rdy"].h, ua + 1)
                DS["acc"].wait(e, ACC0 + ua - 16)
                e.dma_start(out=outT.rearrange("(c p) t -> p c t", p=128)[:, ua % NDC, :],
                            in_=stage[:, ua % 2, :], accum_op=ALU.add
                            ).then_inc(DS["acc"].handle(ACC0 + ua), 16)
            for dc in range(NDC):
                u = grp * NDC + dc
                if u >= 2:
                    e.wait_ge(S["dn_ps"].h, u - 1)
                e.dma_start(out=wl_sb[:, u % 2],
                            in_=wl[dc, grp * 16:(grp + 1) * 16].rearrange("c p n -> p c n")
                            ).then_inc(DS["wll"].handle(WLL0 + u), 16)
                if u >= 2:
                    acc_dma(u - 2)
            if grp == 3:
                acc_dma(62)
                acc_dma(63)

    @on("tensor")
    def _(e):
        e.wait_ge(S["h2_rdy"].h, H2_ALL)
        for grp in range(4):
            for hl in range(16):
                hc = grp * 16 + hl
                DS["wul"].wait(e, WUL0 + hc)
                bs = 4 * (hl % 2)
                if hc >= 2:
                    e.wait_ge(S["act_rdy"].h, 2 * (hc - 1))
                if hl == 0 and grp >= 1:
                    e.wait_ge(S["stg_rdy"].h, grp * 16)
                for proj in range(2):
                    for half in range(2):
                        bank = pb[bs + 2 * proj + half]
                        for dc in range(NDC):
                            mm = e.matmul(bank[:], wu_sb[:, hc % 2, proj, dc, :],
                                          h2sb[:, dc, half * 512:(half + 1) * 512],
                                          start=(dc == 0), stop=(dc == NDC - 1))
                mm.then_inc(S["up_ps"].h, 1)
            e.wait_ge(S["act_rdy"].h, 2 * (grp * 16 + 16))
            for dc in range(NDC):
                u = grp * NDC + dc
                DS["wll"].wait(e, WLL0 + u)
                if u >= 2:
                    e.wait_ge(S["stg_rdy"].h, u - 1)
                for half in range(2):
                    bank = pb[(dc % 2) * 2 + half]
                    for hl in range(16):
                        mm = e.matmul(bank[:], wl_sb[:, u % 2, hl, :],
                                      act[:, hl, half * 512:(half + 1) * 512],
                                      start=(hl == 0), stop=(hl == 15))
                mm.then_inc(S["dn_ps"].h, 1)
    S["up_ps"].n += 64
    S["dn_ps"].n += 64

    @on("scalar")
    def _(e):
        for grp in range(4):
            for hl in range(16):
                hc = grp * 16 + hl
                bs = 4 * (hl % 2)
                for half in range(2):
                    e.wait_ge(S["up_ps"].h, hc + 1)
                    if hc >= 1 or half >= 1:
                        e.wait_ge(S["act_rdy"].h, 2 * hc + half - 1)
                    e.activation(gel_sb[:, half, :], pb[bs + half][:],
                                 AF.Gelu_apprx_tanh).then_inc(S["gel_rdy"].h, 1)
    S["gel_rdy"].n += 128

    @on("vector")
    def _(e):
        for grp in range(4):
            for hl in range(16):
                hc = grp * 16 + hl
                bs = 4 * (hl % 2)
                if hl == 0 and grp >= 1:
                    e.wait_ge(S["dn_ps"].h, 16 * grp)
                for half in range(2):
                    e.wait_ge(S["gel_rdy"].h, 2 * hc + half + 1)
                    e.tensor_tensor(act[:, hl, half * 512:(half + 1) * 512],
                                    gel_sb[:, half, :], pb[bs + 2 + half][:],
                                    ALU.mult).then_inc(S["act_rdy"].h, 1)
            for dc in range(NDC):
                u = grp * NDC + dc
                e.wait_ge(S["dn_ps"].h, u + 1)
                DS["acc"].wait(e, ACC0 + u - 2)
                for half in range(2):
                    e.tensor_scalar(stage[:, u % 2, half * 512:(half + 1) * 512],
                                    pb[(dc % 2) * 2 + half][:],
                                    modv[:, 80 + dc:81 + dc], None, ALU.mult)
                e.engine_nop().then_inc(S["stg_rdy"].h, 1)
    S["act_rdy"].n += 128
    S["stg_rdy"].n += 64

    if "act" in tap_t:
        @on("sync")
        def _(e):
            e.wait_ge(S["act_rdy"].h, 32)
            e.dma_start(out=tap_t["act"], in_=act[:]).then_inc(S["tap_st"].h, 16)
        S["tap_st"].n += 16

    @on("sync")
    def _(e):
        DS["acc"].wait_all(e)
        if S["tap_st"].n:
            e.wait_ge(S["tap_st"].h, S["tap_st"].n)

    with nc.Block() as blk:
        @blk.sync
        def _(eng):
            for fn in OPS["sync"]:
                fn(eng)

        @blk.gpsimd
        def _(eng):
            for fn in OPS["gpsimd"]:
                fn(eng)

        @blk.tensor
        def _(eng):
            for fn in OPS["tensor"]:
                fn(eng)

        @blk.vector
        def _(eng):
            for fn in OPS["vector"]:
                fn(eng)

        @blk.scalar
        def _(eng):
            for fn in OPS["scalar"]:
                fn(eng)
    return nc


def kernel(**inputs):
    maps = _stage(inputs)
    nc = build_nc(DEBUG_TAPS)
    res = run_bass_kernel_spmd(nc, maps, core_ids=list(range(8)))
    out = np.zeros((4, NT, D), np.float32)
    for core in range(8):
        b, s = core // 2, core % 2
        own = _perm_ids(s)[0:OT]
        oTv = np.asarray(res.results[core]["outT"])
        out[b, own] = oTv.T
    return out


# version 05
# speedup vs baseline: 1.1058x; 1.0219x over previous
import numpy as np
import ml_dtypes
import concourse.bass as bass
import concourse.mybir as mybir
from concourse.bass_utils import run_bass_kernel_spmd

F32 = mybir.dt.float32
BF16 = mybir.dt.bfloat16
ALU = mybir.AluOpType
AF = mybir.ActivationFunctionType
AX = mybir.AxisListType
BF = ml_dtypes.bfloat16

H, HD, D, NT, OT = 16, 128, 2048, 2048, 1024
NDC, NTC = 16, 16
KAPPA = float(HD) ** -0.5
MASKPRE = -2048.0
NEG_BIG = -3.0e6
SUP = 1.0e6
SEL_THR = -1.0e5
EPS = 1e-5

DEBUG_TAPS = ()


def _perm_ids(s):
    ids = []
    for j in range(8):
        ids.extend(range((2 * j + s) * 128, (2 * j + s) * 128 + 128))
    for j in range(8):
        ids.extend(range((2 * j + 1 - s) * 128, (2 * j + 1 - s) * 128 + 128))
    return np.array(ids)


def _consts_for_parity(s):
    perm = _perm_ids(s)
    own = perm[0:OT]
    c = {}
    inv = 1.0 / 10000.0 ** (np.arange(64) / 64.0)
    th = perm[:, None].astype(np.float64) * inv[None, :]
    ck, sk = np.cos(th).T, np.sin(th).T
    c["cosk"] = np.tile(ck, (2, 1)).astype(BF)
    c["sink"] = np.concatenate([-sk, sk], 0).astype(BF)
    c["ident"] = np.eye(128, dtype=BF)
    c["ones_lhs"] = np.ones((128, 128), dtype=BF)
    c["eexp"] = (np.arange(32)[:, None] == (np.arange(NT)[None, :] // 64)).astype(BF)
    am = np.zeros((128, NTC, 32), dtype=np.float32)
    for cc in range(NTC):
        for t in range(128):
            am[t, cc, 2 * cc + t // 64] = 1.0 / 64.0
    c["ameans"] = am.reshape(128, NTC * 32).astype(BF)
    tp = np.zeros((128, 95), dtype=BF)
    tp[:, 47] = 1.0
    c["toep"] = tp
    tp32 = np.zeros((32, 95), dtype=BF)
    tp32[:, 47] = 1.0
    c["toep32"] = tp32
    rs = np.zeros((48, 48, 128), dtype=BF)
    for r in range(48):
        rs[r, r, :] = 1.0
    c["rowsel16"] = rs.reshape(48, 48 * 128)
    blk_seq = perm.reshape(32, 64)[:, 0] // 64
    blk_end = (blk_seq + 1) * 64 - 1
    blk_start = blk_seq * 64
    cm = np.where(own[:, None] >= blk_end[None, :], 0.0, MASKPRE).astype(np.float32)
    c["cmp_maskadd"] = np.ascontiguousarray(
        cm.reshape(8, 128, 32).transpose(1, 0, 2).reshape(128, 256)).astype(BF)
    c["cmp_maskaddT"] = np.ascontiguousarray(cm.T).astype(BF)
    cs_ = np.zeros((OT, 32), np.float32)
    cs_[np.arange(OT), np.arange(OT) // 64] = NEG_BIG
    c["cur_sup"] = np.ascontiguousarray(
        cs_.reshape(8, 128, 32).transpose(1, 0, 2).reshape(128, 256)).astype(BF)
    c["cb_min"] = np.where(blk_start[:, None] <= own[None, :], 0.0, MASKPRE).astype(BF)
    tq = np.arange(128)[None, :]
    tk = np.arange(128)[:, None]
    B1 = ((tq - tk >= 0) & (tq - tk <= 64)).astype(np.float32)
    B2 = (tk - tq >= 64).astype(np.float32)
    Z = np.zeros((128, 128), np.float32)
    gen = [B1, Z, B2] if s == 1 else [B1, B2, Z]
    sl0 = [B1, Z, B2] if s == 1 else [B1, Z, Z]
    c["swa_arr"] = np.stack([np.concatenate(sl0, 1), np.concatenate(gen, 1)],
                            1).reshape(128, 2 * 384).astype(BF)
    c["uptri"] = (-(tk > tq).astype(np.float32)).astype(BF)
    any01 = (own >= 63).astype(np.float32)
    af = np.zeros((48, OT), np.float32)
    af[0:16] = (1.0 - any01)[None, :]
    c["anyfix"] = af.astype(BF)
    return c


def _tile2(w, bk0, bk1, order="km"):
    K_, M_ = w.shape
    t = w.reshape(K_ // bk0, bk0, M_ // bk1, bk1).transpose(0, 2, 1, 3)
    if order == "mk":
        t = t.transpose(1, 0, 2, 3)
    return np.ascontiguousarray(t)


def _stage(ins):
    x = np.asarray(ins["L0_attn_x"], dtype=np.float32)
    shared = {
        "wq": _tile2(np.asarray(ins["L0_attn_Wq"], np.float32), 128, 128, "mk"),
        "wk": _tile2(np.asarray(ins["L0_attn_Wk"], np.float32), 128, 128, "mk"),
        "wv": _tile2(np.asarray(ins["L0_attn_Wv"], np.float32), 128, 512, "mk"),
        "wo": _tile2(np.asarray(ins["L0_attn_Wo"], np.float32), 128, 128, "mk"),
        "wg": np.ascontiguousarray(np.asarray(ins["L0_attn_Wg"], np.float32).reshape(NDC, 128, 48)),
        "wu0": _tile2(np.asarray(ins["L0_ffn_w_gating"], np.float32)[0], 128, 128, "mk"),
        "wu1": _tile2(np.asarray(ins["L0_ffn_w_gating"], np.float32)[1], 128, 128, "mk"),
        "wl": _tile2(np.asarray(ins["L0_ffn_w_linear"], np.float32), 128, 128, "mk"),
        "wmod": _tile2(np.asarray(ins["L0_wmod"], np.float32), 128, 512, "mk"),
    }
    pc = [_consts_for_parity(s) for s in (0, 1)]
    maps = []
    for core in range(8):
        b, s = core // 2, core % 2
        m = dict(shared)
        m.update(pc[s])
        m["xT"] = np.ascontiguousarray(x[b].T[:, _perm_ids(s)])
        m["cT"] = np.ascontiguousarray(np.asarray(ins["L0_c"], np.float32)[b:b + 1].T)
        maps.append(m)
    return maps


class Sem:
    __slots__ = ("h", "n")

    def __init__(self, nc, name):
        self.h = nc.alloc_semaphore(name)
        self.n = 0


class DSem:

    def __init__(self, nc, name, depth=2):
        self.s = [Sem(nc, f"{name}_{k}") for k in range(depth)]
        self.depth = depth
        self.tot = {}
        self.uses = 0

    def handle(self, i):
        return self.s[i % self.depth].h

    def issued(self, i, k=16):
        sub = self.s[i % self.depth]
        sub.n += k
        self.tot[i] = sub.n
        self.uses = max(self.uses, i + 1)

    def wait(self, e, i):
        if i < 0:
            return
        assert i in self.tot, (i, sorted(self.tot)[:4])
        e.wait_ge(self.s[i % self.depth].h, self.tot[i])

    def wait_all(self, e):
        for sub in self.s:
            if sub.n:
                e.wait_ge(sub.h, sub.n)


def build_nc(taps=()):
    nc = bass.Bass("TRN2", target_bir_lowering=False)
    OPS = {e: [] for e in ("sync", "gpsimd", "tensor", "vector", "scalar")}

    def on(e):
        def dec(fn):
            OPS[e].append(fn)
            return fn
        return dec

    def din(name, shape, dt=F32):
        return nc.dram_tensor(name, list(shape), dt, kind="ExternalInput").ap()

    xT = din("xT", (D, NT))
    wq = din("wq", (NDC, NDC, 128, 128))
    wk = din("wk", (NDC, NDC, 128, 128))
    wv = din("wv", (4, NDC, 128, 512))
    wo = din("wo", (NDC, NDC, 128, 128))
    wg = din("wg", (NDC, 128, 48))
    wu0 = din("wu0", (64, NDC, 128, 128))
    wu1 = din("wu1", (64, NDC, 128, 128))
    wl = din("wl", (NDC, 64, 128, 128))
    wmod = din("wmod", (24, 16, 128, 512))
    cT = din("cT", (D, 1))
    cosk_c = din("cosk", (128, NT), BF16)
    sink_c = din("sink", (128, NT), BF16)
    ident_c = din("ident", (128, 128), BF16)
    ones_c = din("ones_lhs", (128, 128), BF16)
    eexp_c = din("eexp", (32, NT), BF16)
    ameans_c = din("ameans", (128, NTC * 32), BF16)
    toep_c = din("toep", (128, 95), BF16)
    toep32_c = din("toep32", (32, 95), BF16)
    rowsel_c = din("rowsel16", (48, 48 * 128), BF16)
    cmpma_c = din("cmp_maskadd", (128, 256), BF16)
    cmpmaT_c = din("cmp_maskaddT", (32, OT), BF16)
    cursup_c = din("cur_sup", (128, 256), BF16)
    cbmin_c = din("cb_min", (32, OT), BF16)
    swaarr_c = din("swa_arr", (128, 2 * 384), BF16)
    uptri_c = din("uptri", (128, 128), BF16)
    anyfix_c = din("anyfix", (48, OT), BF16)

    outT = nc.dram_tensor("outT", [D, OT], F32, kind="ExternalOutput").ap()
    kdram = nc.dram_tensor("kdram", [H, 128, NT], BF16,
                           kind="ExternalOutput" if "k" in taps else "Internal").ap()
    qdram = nc.dram_tensor("qdram", [H, 128, OT], BF16,
                           kind="ExternalOutput" if "q" in taps else "Internal").ap()
    vdram = nc.dram_tensor("vdram", [NTC, 4, 128, 512], BF16,
                           kind="ExternalOutput" if "v" in taps else "Internal").ap()
    tap_t = {}
    tap_shapes = {
        "mod": ([1, 6 * D], F32), "h": ([128, NDC, NT], BF16),
        "g": ([48, OT], F32), "kcmp": ([128, H, 32], BF16),
        "vcmp": ([32, H, 128], BF16), "imp": ([128, H * 256], BF16),
        "maskT": ([32, H, OT], BF16), "pcmpT": ([32, H, OT], BF16),
        "o": ([128, H, OT], BF16), "x2": ([128, NDC, OT], F32),
        "h2": ([128, NDC, OT], BF16), "act": ([128, 16, OT], BF16),
        "coef": ([16, H, 2, 3, 512], BF16),
    }
    for t in taps:
        if t in ("k", "q", "v"):
            continue
        sh, dt = tap_shapes[t]
        tap_t[t] = nc.dram_tensor("tap_" + t, sh, dt, kind="ExternalOutput").ap()

    names = [
        "c_ld", "c2_ld", "wg_ld", "silu_rdy", "mod_ps", "mod_evac",
        "modt2", "mod_rdy",
        "sq_rdy", "sq_free", "stats_ps", "ln_mid", "h_rdy",
        "qk_ps", "kraw_rdy", "rope_out",
        "g_ps", "g_rdy", "v_ps", "v_evac",
        "kc_rdy", "vc_ps", "vc_rdy",
        "scmp_ps", "cm_rdy", "esc_rdy", "imp_init", "topk_done",
        "krfree_a", "qrfree_a", "vrfree_a",
        "krfree_b", "qrfree_b", "vrfree_b",
        "tr_ps", "mt_rdy", "scT_ps", "scT_f", "pcT_rdy",
        "sc_ps", "px", "pvstep", "corr_i", "swa_ps", "swa_pv", "pswa_e", "pswa_m",
        "pv_done", "den_done", "coef_u", "coef_r", "coef_rdy",
        "rep_ps", "rep_ev", "cmb",
        "wo_ps", "x2_rdy",
        "x2b_rdy", "sq2_rdy", "sq2_free", "st2_ps", "ln2_mid", "h2_rdy", "x2_sp",
        "up_ps", "gel_rdy", "act_rdy", "dn_ps", "stg_rdy",
        "tap_st", "c3_ld",
    ]
    S = {n: Sem(nc, n) for n in names}
    DS = {n: DSem(nc, n) for n in (
        "wva", "xw", "wqk", "kacc", "kqw", "vst", "modch", "kr", "qr", "vr")}
    DSAL = {"wmod": "wva", "wvb": "wva", "wul": "wva",
            "xol": "vst",
            "wol": "wqk", "wll": "wqk", "acc": "xw",
            "kswp": "kacc", "kqst": "kqw",
            "kra": "kr", "krb": "kr", "qra": "qr", "qrb": "qr",
            "vra": "vr", "vrb": "vr"}
    for _a, _b in DSAL.items():
        DS[_a] = DS[_b]

    CHN = {"vector": [Sem(nc, "chv"), 0], "scalar": [Sem(nc, "chs"), 0]}

    def ci(instr, eng="vector"):
        CHN[eng][1] += 1
        return instr.then_inc(CHN[eng][0].h, 1)

    def cw(e, eng="vector"):
        e.wait_ge(CHN[eng][0].h, CHN[eng][1])

    sbt = nc.alloc_sbuf_tensor
    ident = sbt("ident_sb", [128, 128], BF16)
    ones = sbt("ones_sb", [128, 128], BF16)
    uptri = sbt("uptri_sb", [128, 128], BF16)
    modv = sbt("modv", [128, 96], F32)
    epsv = sbt("epsv", [128, 1], F32)
    clsc = sbt("clsc", [128, 48], F32)
    mrep = sbt("mrep", [128, 512], BF16)
    varrep = sbt("varrep", [128, 512], F32)
    rstdrep = sbt("rstdrep", [128, 512], BF16)
    sqring = sbt("sqring", [128, 2, 512], BF16)
    esc = sbt("esc", [128, 256], BF16)
    cmadd = sbt("cmadd", [128, 256], F32)
    mxb = sbt("mxb", [128, 128], BF16)
    ropeA = sbt("ropeA", [128, 512], BF16)
    scrB = sbt("scrB", [128, 512], BF16)
    cT_sb = sbt("cT_sb", [128, 16, 1], F32)
    siluT = sbt("siluT", [128, 16, 1], BF16)
    lnv = sbt("lnv", [128, 512], F32)
    onepc = sbt("onepc", [128, 32], F32)
    wg_sb = sbt("wg_sb", [128, NDC, 48], BF16)
    gT = sbt("gT", [48, OT], F32)
    kcmpall = sbt("kcmpall", [128, H, 32], BF16)
    kcf = sbt("kcf", [128, 32], F32)
    modrow_dr = nc.dram_tensor("modrow_dr", [1, 6 * D], F32, kind="Internal").ap()

    AMAX = 186 * 1024
    AR = sbt("ARENA", [128, AMAX], mybir.dt.uint8)
    spans = []

    def V_(name, off_kb, shape, dt, ph):
        off = int(off_kb * 1024)
        per = int(np.prod(shape[1:])) * mybir.dt.size(dt)
        assert off % 256 == 0 and off + per <= AMAX, (name, off, per, AMAX)
        for (n2, o2, p2, ph2) in spans:
            if not (off >= o2 + p2 or o2 >= off + per or ph[1] < ph2[0] or ph2[1] < ph[0]):
                raise AssertionError(f"overlap {name} vs {n2}")
        spans.append((name, off, per, ph))
        ap = AR[0:shape[0], off:off + per].bitcast(dt)
        if len(shape) > 2:
            pat = "p (" + " ".join(f"d{i}" for i in range(1, len(shape))) + ") -> p " + \
                  " ".join(f"d{i}" for i in range(1, len(shape)))
            kw = {f"d{i}": shape[i] for i in range(1, len(shape))}
            ap = ap.rearrange(pat, **kw)
        return ap

    xwin = V_("xwin", 0, [128, 2, NDC, 512], BF16, (0, 0))
    modrow = V_("modrow", 44, [1, 2, 512], F32, (0, 0))
    wmod_sb = V_("wmod", 32, [128, 2, 2, 512], BF16, (0, 0))
    cosk_sb = V_("cosk", 36, [128, NT], BF16, (0, 1))
    sink_sb = V_("sink", 40, [128, NT], BF16, (0, 1))

    wqk_sb = V_("wqk", 0, [128, 2, NDC, 128], BF16, (1, 1))
    wvb_sb = V_("wvb", 8, [128, 2, 512], BF16, (1, 1))
    kraw = V_("kraw", 10, [128, 2, 512], BF16, (1, 1))
    kswp = V_("kswp", 12, [128, 2, 512], BF16, (1, 1))
    kqst = V_("kqst", 14, [128, 2, 2048], BF16, (1, 1))
    vstg = V_("vstg", 22, [128, 2, 512], BF16, (1, 1))

    kr2 = V_("kr2", 0, [128, 2, NT], BF16, (2, 3))
    vr2 = V_("vr2", 8, [128, 2, NTC, 128], BF16, (2, 3))
    qr2 = V_("qr2", 16, [128, 2, OT], BF16, (2, 3))
    vcmpall = V_("vcmpall", 20, [32, H, 128], BF16, (2, 3))
    impa = V_("impa", 24, [128, H * 256], BF16, (2, 3))
    impb = V_("impb", 32, [128, H * 256], BF16, (2, 3))
    selb = V_("selb", 40, [128, H * 256], BF16, (2, 2))
    eexp = V_("eexp", 48, [32, NT], BF16, (2, 3))
    ameans = V_("ameans", 52, [128, NTC * 32], BF16, (2, 2))
    toep = V_("toep", 53, [128, 95], BF16, (2, 3))
    toep32 = V_("toep32", 53.25, [32, 95], BF16, (2, 3))
    rowsel = V_("rowsel", 130, [48, 48 * 128], BF16, (2, 3))
    cmpma = V_("cmpma", 57.5, [128, 256], BF16, (2, 2))
    cmpmaT = V_("cmpmaT", 58, [32, OT], BF16, (2, 3))
    cursup = V_("cursup", 60, [128, 256], BF16, (2, 2))
    cbmin = V_("cbmin", 60.5, [32, OT], BF16, (2, 3))
    swaarr = V_("swaarr", 62.5, [128, 2, 384], BF16, (2, 3))
    anyfix = V_("anyfix", 64, [48, OT], BF16, (2, 3))
    pring = V_("pring", 66, [128, 3, 512], BF16, (3, 3))
    pswa = V_("pswa", 69, [128, 2, 384], BF16, (3, 3))
    pswa2 = V_("pswa2", 70.5, [128, 2, 384], BF16, (3, 3))
    pcmpT = V_("pcmpT", 72, [32, 2, OT], BF16, (3, 3))
    maskT = V_("maskT", 76, [32, 2, OT], BF16, (3, 3))
    mtmp = V_("mtmp", 80, [32, OT], BF16, (3, 3))
    scmpTf = V_("scmpTf", 82, [32, 512], F32, (3, 3))
    coefb = V_("coefb", 84, [48, 2, 512], BF16, (3, 3))
    crep = V_("crep", 90, [128, 3, 512], BF16, (3, 3))
    csc = V_("csc", 93, [128, 3, 512], BF16, (3, 3))
    corrsc = V_("corrsc", 96, [128, 2, 128], BF16, (3, 3))

    oT = V_("oT", 97, [128, H, OT], BF16, (3, 4))
    hT = V_("hT", 122, [128, NDC, NT], BF16, (0, 1))
    xo = V_("xo", 0, [128, 2, OT], F32, (4, 4))
    wo_sb = V_("wo_sb", 8, [128, 2, NDC, 128], BF16, (4, 4))
    x2 = V_("x2", 16, [128, NDC, OT], F32, (4, 5))
    x2b = V_("x2b", 80, [128, NDC, OT], BF16, (5, 5))
    h2sb = V_("h2sb", 112, [128, NDC, OT], BF16, (5, 6))
    act = V_("act", 144, [128, 16, OT], BF16, (6, 6))
    actB = V_("actB", 80, [128, 16, OT], BF16, (6, 6))
    wu_sb = V_("wu_sb", 16, [128, 2, 2, NDC, 128], BF16, (6, 6))
    wl_sb = V_("wl_sb", 32, [128, 2, 16, 128], BF16, (6, 6))
    stage = V_("stage", 40, [128, 2, OT], F32, (6, 6))
    gel_sb = V_("gel", 176, [128, 2, 512], BF16, (6, 6))

    pb = [nc.alloc_psum_tensor(f"pb{i}", [128, 512], F32) for i in range(8)]
    pb0bf = pb[0][0:32, :].bitcast(BF16)

    const_pairs = [
        (ident[:], ident_c), (ones[:], ones_c), (uptri[:], uptri_c),
        (cosk_sb, cosk_c), (sink_sb, sink_c),
    ]
    const_pairs2 = [
        (eexp, eexp_c), (ameans, ameans_c), (toep, toep_c), (toep32, toep32_c),
        (cmpma, cmpma_c), (cmpmaT, cmpmaT_c),
        (cursup, cursup_c), (cbmin, cbmin_c),
        (swaarr.rearrange("p a n -> p (a n)"), swaarr_c), (anyfix, anyfix_c),
    ]

    @on("sync")
    def _(e):
        for dst, src in const_pairs:
            e.dma_start(out=dst, in_=src).then_inc(S["c_ld"].h, 16)
        e.dma_start(out=cT_sb[:], in_=cT.rearrange("(c p) b -> p c b", p=128),
                    allow_slow_non_contiguous=True).then_inc(S["c_ld"].h, 16)
    S["c_ld"].n += 16 * (len(const_pairs) + 1)
    C1 = S["c_ld"].n

    @on("gpsimd")
    def _(e):
        e.dma_start(out=wg_sb[:], in_=wg.rearrange("c p n -> p c n")).then_inc(S["wg_ld"].h, 16)
        for jc in range(24):
            for part in range(8):
                i = jc * 8 + part
                if i >= 2:
                    e.wait_ge(S["mod_ps"].h, i - 1)
                e.dma_start(out=wmod_sb[:, i % 2],
                            in_=wmod[jc, part * 2:part * 2 + 2].rearrange("c p n -> p c n")
                            ).then_inc(DS["wmod"].handle(i), 16)
    S["wg_ld"].n += 16
    for _i in range(192):
        DS["wva"].issued(_i)

    @on("scalar")
    def _(e):
        e.wait_ge(S["c_ld"].h, C1)
        e.activation(siluT[:], cT_sb[:], AF.Silu).then_inc(S["silu_rdy"].h, 1)
    S["silu_rdy"].n += 1

    @on("tensor")
    def _(e):
        e.wait_ge(S["silu_rdy"].h, 1)
        for jc in range(24):
            if jc >= 1:
                e.wait_ge(S["mod_evac"].h, jc)
            for part in range(8):
                i = jc * 8 + part
                DS["wmod"].wait(e, i)
                e.matmul(pb[6][0:1, :], siluT[:, part * 2, :],
                         wmod_sb[:, i % 2, 0, :],
                         start=(part == 0), stop=False)
                e.matmul(pb[6][0:1, :], siluT[:, part * 2 + 1, :],
                         wmod_sb[:, i % 2, 1, :],
                         start=False, stop=(part == 7)).then_inc(S["mod_ps"].h, 1)
    S["mod_ps"].n += 192

    @on("vector")
    def _(e):
        for jc in range(24):
            e.wait_ge(S["mod_ps"].h, (jc + 1) * 8)
            DS["modch"].wait(e, jc - 2)
            e.tensor_copy(modrow[0:1, jc % 2, :], pb[6][0:1, :]
                          ).then_inc(S["mod_evac"].h, 1)
    S["mod_evac"].n += 24

    @on("sync")
    def _(e):
        for jc in range(24):
            e.wait_ge(S["mod_evac"].h, jc + 1)
            e.dma_start(out=modrow_dr[0:1, jc * 512:(jc + 1) * 512],
                        in_=modrow[0:1, jc % 2, :]).then_inc(DS["modch"].handle(jc), 16)
        DS["modch"].wait_all(e)
        e.dma_start(out=modv[:], in_=modrow_dr.rearrange("o (m p) -> (o p) m", p=128),
                    allow_slow_non_contiguous=True).then_inc(S["modt2"].h, 16)
    S["modt2"].n += 16
    for _i in range(24):
        DS["modch"].issued(_i)

    @on("vector")
    def _(e):
        e.wait_ge(S["modt2"].h, 16)
        for c0 in (32, 80):
            sl = modv[:, c0:c0 + 16]
            ci(e.tensor_scalar(clsc[:, 0:16], sl, 0.0, 2.0, ALU.is_ge, ALU.mult))
            cw(e)
            ci(e.tensor_scalar(clsc[:, 16:32], clsc[:, 0:16], 1.0, None, ALU.subtract))
            cw(e)
            ci(e.tensor_tensor(clsc[:, 0:16], sl, clsc[:, 16:32], ALU.mult))
            cw(e)
            ci(e.tensor_scalar(clsc[:, 32:48], clsc[:, 0:16], 0.25, None, ALU.max))
            cw(e)
            ci(e.tensor_tensor(sl, clsc[:, 16:32], clsc[:, 32:48], ALU.mult))
            cw(e)
        ci(e.tensor_scalar(onepc[:, 0:16], modv[:, 16:32], 1.0, None, ALU.add))
        ci(e.tensor_scalar(onepc[:, 16:32], modv[:, 64:80], 1.0, None, ALU.add))
        e.memset(epsv[:], EPS)
        e.engine_nop().then_inc(S["mod_rdy"].h, 1)
    S["mod_rdy"].n += 1

    if "mod" in tap_t:
        @on("sync")
        def _(e):
            e.wait_ge(S["mod_rdy"].h, 1)
            e.dma_start(out=tap_t["mod"], in_=modrow_dr).then_inc(S["tap_st"].h, 16)
        S["tap_st"].n += 16

    @on("gpsimd")
    def _(e):
        for w in range(4):
            if w >= 2:
                e.wait_ge(S["h_rdy"].h, (w - 1) * NDC)
            e.dma_start(out=xwin[:, w % 2],
                        in_=xT.rearrange("(c p) t -> p c t", p=128)[:, :, w * 512:(w + 1) * 512]
                        ).then_inc(DS["xw"].handle(w), 16)

    for _i in range(4):
        DS["xw"].issued(_i)

    @on("scalar")
    def _(e):
        for w in range(4):
            DS["xw"].wait(e, w)
            for dc in range(NDC):
                i = w * NDC + dc
                if i >= 2:
                    e.wait_ge(S["sq_free"].h, i - 1)
                e.activation(sqring[:, i % 2, :], xwin[:, w % 2, dc, :], AF.Square
                             ).then_inc(S["sq_rdy"].h, 1)
            e.wait_ge(S["ln_mid"].h, 2 * w + 1)
            ci(e.activation(lnv[:], varrep[:], AF.Ln, bias=epsv[:]), "scalar")
            cw(e, "scalar")
            e.activation(rstdrep[:], lnv[:], AF.Exp, scale=-0.5).then_inc(S["ln_mid"].h, 1)
    S["sq_rdy"].n += 64

    @on("tensor")
    def _(e):
        e.wait_ge(S["c_ld"].h, C1)
        for w in range(4):
            DS["xw"].wait(e, w)
            if w >= 1:
                e.wait_ge(S["ln_mid"].h, 2 * w - 1)
            for dc in range(NDC):
                i = w * NDC + dc
                mm0 = e.matmul(pb[0][:], ones[:], xwin[:, w % 2, dc, :],
                               start=(dc == 0), stop=(dc == NDC - 1))
                if dc == NDC - 1:
                    mm0.then_inc(S["stats_ps"].h, 1)
                e.wait_ge(S["sq_rdy"].h, i + 1)
                e.matmul(pb[1][:], ones[:], sqring[:, i % 2, :],
                         start=(dc == 0), stop=(dc == NDC - 1)).then_inc(S["sq_free"].h, 1)
    S["stats_ps"].n += 4
    S["sq_free"].n += 64

    @on("vector")
    def _(e):
        e.wait_ge(S["mod_rdy"].h, 1)
        for w in range(4):
            e.wait_ge(S["stats_ps"].h, w + 1)
            e.wait_ge(S["sq_free"].h, 16 * (w + 1))
            ci(e.tensor_scalar(mrep[:], pb[0][:], 1.0 / D, None, ALU.mult))
            cw(e)
            ci(e.tensor_tensor(lnv[:], mrep[:], mrep[:], ALU.mult))
            cw(e)
            e.scalar_tensor_tensor(varrep[:], pb[1][:], 1.0 / D, lnv[:],
                                   ALU.mult, ALU.subtract).then_inc(S["ln_mid"].h, 1)
            e.wait_ge(S["ln_mid"].h, 2 * w + 2)
            for dc in range(NDC):
                ci(e.tensor_tensor(ropeA[:], xwin[:, w % 2, dc, :], mrep[:], ALU.subtract))
                cw(e)
                ci(e.tensor_tensor(scrB[:], ropeA[:], rstdrep[:], ALU.mult))
                cw(e)
                e.tensor_scalar(hT[:, dc, w * 512:(w + 1) * 512], scrB[:],
                                onepc[:, dc:dc + 1], modv[:, dc:dc + 1],
                                ALU.mult, ALU.add).then_inc(S["h_rdy"].h, 1)
    S["ln_mid"].n = 8
    S["h_rdy"].n += 64
    H_ALL = S["h_rdy"].n

    if "h" in tap_t:
        @on("sync")
        def _(e):
            e.wait_ge(S["h_rdy"].h, H_ALL)
            e.dma_start(out=tap_t["h"], in_=hT[:]).then_inc(S["tap_st"].h, 16)
        S["tap_st"].n += 16

    @on("sync")
    def _(e):
        for dst, src in const_pairs2:
            e.dma_start(out=dst, in_=src).then_inc(S["c2_ld"].h, 16)
    S["c2_ld"].n += 16 * len(const_pairs2)
    C2 = S["c2_ld"].n

    def emit_qk(proj):
        wsrc, ddst, nwin = (wq, qdram, 2) if proj == 0 else (wk, kdram, 4)
        ps0 = S["qk_ps"].n
        raw0, out0 = S["kraw_rdy"].n, S["rope_out"].n
        wq0, sw0, st0 = DS["wqk"].uses, DS["kswp"].uses, DS["kqst"].uses
        NTOT = NDC * nwin

        @on("gpsimd")
        def _(e):
            for ec in range(NDC):
                if ec >= 2:
                    e.wait_ge(S["qk_ps"].h, ps0 + (ec - 1) * nwin)
                elif ps0 > 0:
                    e.wait_ge(S["qk_ps"].h, ps0)
                else:
                    e.wait_ge(S["h_rdy"].h, H_ALL)
                e.dma_start(out=wqk_sb[:, ec % 2],
                            in_=wsrc[ec].rearrange("c p n -> p c n")
                            ).then_inc(DS["wqk"].handle(wq0 + ec), 16)

        for _i in range(NDC):
            DS["wqk"].issued(wq0 + _i)

        @on("tensor")
        def _(e):
            if proj == 0:
                e.wait_ge(S["h_rdy"].h, H_ALL)
            for ec in range(NDC):
                DS["wqk"].wait(e, wq0 + ec)
                for w in range(nwin):
                    i = ec * nwin + w
                    if raw0 + i >= 4:
                        e.wait_ge(S["kraw_rdy"].h, raw0 + i - 3)
                    for dc in range(NDC):
                        mm = e.matmul(pb[i % 4][:], wqk_sb[:, ec % 2, dc, :],
                                      hT[:, dc, w * 512:(w + 1) * 512],
                                      start=(dc == 0), stop=(dc == NDC - 1))
                    mm.then_inc(S["qk_ps"].h, 1)
        S["qk_ps"].n += NTOT

        @on("scalar")
        def _(e):
            for i in range(NTOT):
                e.wait_ge(S["kraw_rdy"].h, raw0 + i + 1)
                if i >= 2:
                    e.wait_ge(S["rope_out"].h, out0 + i - 1)
                e.dma_start(out=kswp[0:64, i % 2, :], in_=kraw[64:128, i % 2, :]
                            ).then_inc(DS["kswp"].handle(sw0 + i), 16)
                e.dma_start(out=kswp[64:128, i % 2, :], in_=kraw[0:64, i % 2, :]
                            ).then_inc(DS["kswp"].handle(sw0 + i), 16)

        for _i in range(NTOT):
            DS["kacc"].issued(sw0 + _i, 32)

        @on("vector")
        def _(e):
            def rope(j):
                ec, w = j // nwin, j % nwin
                DS["kswp"].wait(e, sw0 + j)
                e.wait_ge(S["rope_out"].h, out0 + j)
                if st0 + ec >= 2 and w == 0:
                    DS["kqst"].wait(e, st0 + ec - 2)
                ci(e.tensor_tensor(ropeA[:], kraw[:, j % 2, :],
                                   cosk_sb[:, w * 512:(w + 1) * 512], ALU.mult))
                ci(e.tensor_tensor(vstg[:, j % 2, :], kswp[:, j % 2, :],
                                   sink_sb[:, w * 512:(w + 1) * 512], ALU.mult))
                cw(e)
                e.tensor_tensor(kqst[:, ec % 2, w * 512:(w + 1) * 512],
                                ropeA[:], vstg[:, j % 2, :], ALU.add
                                ).then_inc(S["rope_out"].h, 1)
            for i in range(NTOT):
                e.wait_ge(S["qk_ps"].h, ps0 + i + 1)
                DS["kswp"].wait(e, sw0 + i - 2)
                e.tensor_copy(kraw[:, i % 2, :], pb[i % 4][:]).then_inc(S["kraw_rdy"].h, 1)
                if i >= 1:
                    rope(i - 1)
            rope(NTOT - 1)
        S["kraw_rdy"].n += NTOT
        S["rope_out"].n += NTOT

        @on("sync")
        def _(e):
            for ec in range(NDC):
                e.wait_ge(S["rope_out"].h, out0 + (ec + 1) * nwin)
                e.dma_start(out=ddst[ec], in_=kqst[:, ec % 2, 0:nwin * 512]
                            ).then_inc(DS["kqst"].handle(st0 + ec), 16)

        for _i in range(NDC):
            DS["kqw"].issued(st0 + _i)

    emit_qk(0)
    emit_qk(1)
    S_KRAW_ALL = S["kraw_rdy"].n

    @on("tensor")
    def _(e):
        e.wait_ge(S["wg_ld"].h, 16)
        e.wait_ge(S["mod_evac"].h, 24)
        for w in range(2):
            if w >= 1:
                e.wait_ge(S["g_rdy"].h, w)
            for dc in range(NDC):
                mm = e.matmul(pb[6][0:48, :], wg_sb[:, dc, :],
                              hT[:, dc, w * 512:(w + 1) * 512],
                              start=(dc == 0), stop=(dc == NDC - 1))
            mm.then_inc(S["g_ps"].h, 1)
    S["g_ps"].n += 2

    @on("scalar")
    def _(e):
        for w in range(2):
            e.wait_ge(S["g_ps"].h, w + 1)
            e.activation(gT[:, w * 512:(w + 1) * 512], pb[6][0:48, :], AF.Sigmoid
                         ).then_inc(S["g_rdy"].h, 1)
    S["g_rdy"].n += 2

    if "g" in tap_t:
        @on("sync")
        def _(e):
            e.wait_ge(S["g_rdy"].h, 2)
            e.dma_start(out=tap_t["g"], in_=gT[:]).then_inc(S["tap_st"].h, 16)
        S["tap_st"].n += 16

    WVB0 = DS["wva"].uses
    for _i in range(128):
        DS["wva"].issued(WVB0 + _i)

    @on("gpsimd")
    def _(e):
        for es in range(4):
            for th in range(2):
                for dc in range(NDC):
                    i = (es * 2 + th) * NDC + dc
                    if i >= 2:
                        e.wait_ge(S["v_ps"].h, i - 1)
                    else:
                        e.wait_ge(S["mod_ps"].h, 192)
                    e.dma_start(out=wvb_sb[:, i % 2, :], in_=wv[es, dc]
                                ).then_inc(DS["wvb"].handle(WVB0 + i), 16)

    @on("tensor")
    def _(e):
        e.wait_ge(S["g_rdy"].h, 2)
        e.wait_ge(S["kraw_rdy"].h, S_KRAW_ALL)
        for es in range(4):
            for th in range(2):
                u = es * 2 + th
                if u >= 1:
                    e.wait_ge(S["v_evac"].h, u * 8)
                for dc in range(NDC):
                    i = u * NDC + dc
                    DS["wvb"].wait(e, WVB0 + i)
                    for bk in range(8):
                        tc = th * 8 + bk
                        mm = e.matmul(pb[bk][:], hT[:, dc, tc * 128:(tc + 1) * 128],
                                      wvb_sb[:, i % 2, :],
                                      start=(dc == 0), stop=(dc == NDC - 1))
                    mm.then_inc(S["v_ps"].h, 1)
    S["v_ps"].n += 128

    @on("vector")
    def _(e):
        e.wait_ge(S["rope_out"].h, S["rope_out"].n)
        for es in range(4):
            for th in range(2):
                u = es * 2 + th
                e.wait_ge(S["v_ps"].h, (u + 1) * NDC)
                for bk in range(8):
                    i = u * 8 + bk
                    DS["vst"].wait(e, i - 2)
                    e.tensor_copy(vstg[:, i % 2, :], pb[bk][:]).then_inc(S["v_evac"].h, 1)
    S["v_evac"].n += 64

    @on("sync")
    def _(e):
        for es in range(4):
            for th in range(2):
                for bk in range(8):
                    i = (es * 2 + th) * 8 + bk
                    tc = th * 8 + bk
                    e.wait_ge(S["v_evac"].h, i + 1)
                    e.dma_start(out=vdram[tc, es], in_=vstg[:, i % 2, :]
                                ).then_inc(DS["vst"].handle(i), 16)

    for _i in range(64):
        DS["vst"].issued(_i)

    @on("sync")
    def _(e):
        e.wait_ge(S["v_evac"].h, 64)
        e.dma_start(out=rowsel, in_=rowsel_c).then_inc(S["c3_ld"].h, 16)
    S["c3_ld"].n += 16

    KQ_T = tuple(sub.n for sub in DS["kqst"].s)
    VST_T = tuple(sub.n for sub in DS["vst"].s)

    @on("gpsimd")
    def _(e, KQ_T=KQ_T, VST_T=VST_T):
        for _k, _sub in enumerate(DS["kqst"].s):
            if KQ_T[_k]:
                e.wait_ge(_sub.h, KQ_T[_k])
        for _k, _sub in enumerate(DS["vst"].s):
            if VST_T[_k]:
                e.wait_ge(_sub.h, VST_T[_k])
        e.wait_ge(S["c2_ld"].h, C2)
        for h in range(H):
            if h >= 2:
                e.wait_ge(S["kc_rdy"].h, h - 1)
                e.wait_ge(S["esc_rdy"].h, h - 1)
                e.wait_ge(S["vc_rdy"].h, h - 1)
            e.dma_start(out=kr2[:, h % 2, :], in_=kdram[h]).then_inc(DS["kra"].handle(h), 16)
            e.dma_start(out=qr2[:, h % 2, :], in_=qdram[h]).then_inc(DS["qra"].handle(h), 16)
            e.dma_start(out=vr2[:, h % 2],
                        in_=vdram[:, h // 4, :, (h % 4) * 128:(h % 4 + 1) * 128
                                  ].rearrange("c p n -> p c n")).then_inc(DS["vra"].handle(h), 16)

    for _h in range(H):
        DS["kr"].issued(_h)
        DS["qr"].issued(_h)
        DS["vr"].issued(_h)

    @on("tensor")
    def _(e):
        e.wait_ge(S["c2_ld"].h, C2)
        for h in range(H):
            DS["qra"].wait(e, h)
            DS["vra"].wait(e, h)
            e.wait_ge(S["kc_rdy"].h, h + 1)
            if h >= 1:
                e.wait_ge(S["cm_rdy"].h, h)
            if h >= 2:
                e.wait_ge(S["vc_rdy"].h, h - 1)
            for qb in range(8):
                mm = e.matmul(pb[0][:, qb * 32:(qb + 1) * 32],
                              qr2[:, h % 2, qb * 128:(qb + 1) * 128],
                              kcmpall[:, h, :], start=True, stop=True)
            mm.then_inc(S["scmp_ps"].h, 1)
            for c in range(NTC):
                mm = e.matmul(pb[2 + h % 2][0:32, 0:128], ameans[:, c * 32:(c + 1) * 32],
                              vr2[:, h % 2, c, :], start=(c == 0), stop=(c == NTC - 1))
            mm.then_inc(S["vc_ps"].h, 1)
    S["scmp_ps"].n += H
    S["vc_ps"].n += H

    @on("vector")
    def _(e):
        e.wait_ge(S["c2_ld"].h, C2)
        for h in range(H):
            DS["kra"].wait(e, h)
            ci(e.reduce_sum(kcf[:], kr2[:, h % 2, :].rearrange("p (b t) -> p b t", b=32),
                            axis=AX.X))
            cw(e)
            e.tensor_scalar(kcmpall[:, h, :], kcf[:], 1.0 / 64.0, None, ALU.mult
                            ).then_inc(S["kc_rdy"].h, 1)
            e.wait_ge(S["vc_ps"].h, h + 1)
            e.tensor_copy(vcmpall[:, h, :], pb[2 + h % 2][0:32, 0:128]).then_inc(S["vc_rdy"].h, 1)
            e.wait_ge(S["scmp_ps"].h, h + 1)
            e.tensor_tensor(cmadd[:], pb[0][:, 0:256], cmpma[:], ALU.add).then_inc(S["cm_rdy"].h, 1)
            e.wait_ge(S["esc_rdy"].h, h + 1)
            e.tensor_tensor(impa[:, h * 256:(h + 1) * 256], esc[:], cursup[:], ALU.add
                            ).then_inc(S["imp_init"].h, 1)
    S["kc_rdy"].n += H
    S["vc_rdy"].n += H
    S["cm_rdy"].n += H
    S["imp_init"].n += H

    @on("scalar")
    def _(e):
        for h in range(H):
            e.wait_ge(S["cm_rdy"].h, h + 1)
            e.activation(esc[:], cmadd[:], AF.Exp, scale=KAPPA).then_inc(S["esc_rdy"].h, 1)
    S["esc_rdy"].n += H

    if "imp" in tap_t:
        @on("sync")
        def _(e):
            e.wait_ge(S["imp_init"].h, H)
            e.dma_start(out=tap_t["imp"], in_=impa[:]).then_inc(S["tap_st"].h, 16)
        S["tap_st"].n += 16
    if "kcmp" in tap_t:
        @on("sync")
        def _(e):
            e.wait_ge(S["kc_rdy"].h, H)
            e.dma_start(out=tap_t["kcmp"], in_=kcmpall[:]).then_inc(S["tap_st"].h, 16)
        S["tap_st"].n += 16
    if "vcmp" in tap_t:
        @on("sync")
        def _(e):
            e.wait_ge(S["vc_rdy"].h, H)
            e.dma_start(out=tap_t["vcmp"], in_=vcmpall[:]).then_inc(S["tap_st"].h, 16)
        S["tap_st"].n += 16

    @on("vector")
    def _(e):
        e.wait_ge(S["imp_init"].h, H)
        for it in range(7):
            src = impa if it % 2 == 0 else impb
            dst = impb if it % 2 == 0 else impa
            ci(e.reduce_max(mxb[:], src[:].rearrange("p (g b) -> p g b", b=32), axis=AX.X))
            cw(e)
            ci(e.tensor_tensor(selb[:].rearrange("p (g b) -> p g b", b=32),
                               src[:].rearrange("p (g b) -> p g b", b=32),
                               mxb[:].unsqueeze(2).broadcast_to([128, 128, 32]), ALU.is_ge))
            cw(e)
            ci(e.scalar_tensor_tensor(dst[:], selb[:], -SUP, src[:], ALU.mult, ALU.add))
            cw(e)
        e.engine_nop().then_inc(S["topk_done"].h, 1)
    S["topk_done"].n += 1

    gc = [0]
    pvst = [0]
    cr = [0]
    swp_ = [0]
    rep_ = [0]
    units = [0]

    @on("gpsimd")
    def _(e):
        for h in range(H):
            e.wait_ge(S["kc_rdy"].h, H)
            e.wait_ge(S["vc_rdy"].h, H)
            e.wait_ge(S["esc_rdy"].h, H)
            if h >= 2:
                e.wait_ge(S["den_done"].h, 2 * (h - 1))
            e.dma_start(out=kr2[:, h % 2, :], in_=kdram[h]).then_inc(DS["krb"].handle(16 + h), 16)
            e.dma_start(out=qr2[:, h % 2, :], in_=qdram[h]).then_inc(DS["qrb"].handle(16 + h), 16)
            e.dma_start(out=vr2[:, h % 2],
                        in_=vdram[:, h // 4, :, (h % 4) * 128:(h % 4 + 1) * 128
                                  ].rearrange("c p n -> p c n")).then_inc(DS["vrb"].handle(16 + h), 16)

    for _h in range(H):
        DS["kr"].issued(16 + _h)
        DS["qr"].issued(16 + _h)
        DS["vr"].issued(16 + _h)

    for h in range(H):
        trbase = S["tr_ps"].n
        gc_at_build = gc[0]

        @on("tensor")
        def _(e, h=h, gc0=gc_at_build):
            e.wait_ge(S["topk_done"].h, 1)
            if gc0 > 0:
                e.wait_ge(S["px"].h, gc0)
            if h > 0:
                e.wait_ge(S["mt_rdy"].h, h)
            for qt in range(8):
                mm = e.transpose(pb0bf[:, qt * 128:(qt + 1) * 128],
                                 impb[:, h * 256 + qt * 32:h * 256 + (qt + 1) * 32], ident[:])
            mm.then_inc(S["tr_ps"].h, 1)
        S["tr_ps"].n += 1

        @on("vector")
        def _(e, h=h):
            e.wait_ge(S["tr_ps"].h, h + 1)
            ci(e.tensor_scalar(mtmp[:], pb0bf[:], SEL_THR, MASKPRE, ALU.is_ge, ALU.mult))
            cw(e)
            e.tensor_tensor(maskT[:, h % 2, :], mtmp[:], cbmin[:], ALU.min
                            ).then_inc(S["mt_rdy"].h, 1)
        S["mt_rdy"].n += 1

        @on("tensor")
        def _(e, h=h):
            e.wait_ge(S["mt_rdy"].h, h + 1)
            DS["qrb"].wait(e, 16 + h)
            if h > 0:
                e.wait_ge(S["scT_f"].h, 2 * h)
            for g in range(2):
                e.matmul(pb[g][0:32, :], kcmpall[:, h, :], qr2[:, h % 2, g * 512:(g + 1) * 512],
                         start=True, stop=True).then_inc(S["scT_ps"].h, 1)
        S["scT_ps"].n += 2

        @on("vector")
        def _(e, h=h):
            for g in range(2):
                e.wait_ge(S["scT_ps"].h, 2 * h + g + 1)
                if 2 * h + g >= 1:
                    e.wait_ge(S["pcT_rdy"].h, 2 * h + g)
                e.tensor_tensor(scmpTf[:], pb[g][0:32, :], cmpmaT[:, g * 512:(g + 1) * 512],
                                ALU.add).then_inc(S["scT_f"].h, 1)
        S["scT_f"].n += 2

        @on("scalar")
        def _(e, h=h):
            for g in range(2):
                e.wait_ge(S["scT_f"].h, 2 * h + g + 1)
                e.activation(pcmpT[:, h % 2, g * 512:(g + 1) * 512], scmpTf[:],
                             AF.Exp, scale=KAPPA).then_inc(S["pcT_rdy"].h, 1)
        S["pcT_rdy"].n += 2

        if "maskT" in tap_t:
            @on("sync")
            def _(e, h=h):
                e.wait_ge(S["mt_rdy"].h, h + 1)
                e.dma_start(out=tap_t["maskT"][:, h, :], in_=maskT[:, h % 2, :]
                            ).then_inc(S["tap_st"].h, 16)
            S["tap_st"].n += 16
        if "pcmpT" in tap_t:
            @on("sync")
            def _(e, h=h):
                e.wait_ge(S["pcT_rdy"].h, 2 * h + 2)
                e.dma_start(out=tap_t["pcmpT"][:, h, :], in_=pcmpT[:, h % 2, :]
                            ).then_inc(S["tap_st"].h, 16)
            S["tap_st"].n += 16

        for g in range(2):
            ECH = 4 * g + 4
            chunks = list(range(ECH)) + list(range(8, 8 + ECH))
            dpar = (2 * h + g) % 2
            unit = 2 * h + g
            gbase = gc[0]
            pvbase = pvst[0]
            denb = pb[5 + dpar]

            @on("tensor")
            def _(e, h=h, g=g, chunks=tuple(chunks), gbase=gbase, unit=unit,
                  dpar=dpar, denb=denb, pvbase=pvbase, crbase=cr[0]):
                DS["krb"].wait(e, 16 + h)
                DS["vrb"].wait(e, 16 + h)
                e.wait_ge(S["mt_rdy"].h, h + 1)
                e.wait_ge(S["scT_f"].h, 2 * (h + 1))
                if unit >= 1:
                    e.wait_ge(S["cmb"].h, unit)
                if unit >= 2:
                    e.wait_ge(S["coef_u"].h, unit - 1)
                ci = [0]

                def pvden(j):
                    c2 = chunks[j]
                    e.wait_ge(S["px"].h, gbase + j + 1)
                    e.matmul(pb[2][:], vr2[:, h % 2, c2, :], pring[:, (gbase + j) % 3, :],
                             start=(j == 0), stop=(j == len(chunks) - 1))
                    mm = e.matmul(denb[0:48, :], toep[:, 31 - h:79 - h],
                                  pring[:, (gbase + j) % 3, :],
                                  start=(j == 0), stop=False)
                    if 4 * g <= c2 < 4 * g + 4:
                        jl = c2 - 4 * g
                        e.wait_ge(S["corr_i"].h, crbase + ci[0] + 1)
                        e.matmul(pb[2][:, jl * 128:(jl + 1) * 128], vr2[:, h % 2, c2, :],
                                 corrsc[:, ci[0] % 2, :], start=False, stop=False)
                        mm = e.matmul(denb[0:48, jl * 128:(jl + 1) * 128],
                                      toep[:, 31 - h:79 - h], corrsc[:, ci[0] % 2, :],
                                      start=False, stop=False)
                        ci[0] += 1
                    mm.then_inc(S["pvstep"].h, 1)

                for i, c in enumerate(chunks):
                    if gbase + i >= 2:
                        e.wait_ge(S["px"].h, gbase + i - 1)
                    e.matmul(pb[i % 2][:], kr2[:, h % 2, c * 128:(c + 1) * 128],
                             qr2[:, h % 2, g * 512:(g + 1) * 512], start=True, stop=False)
                    e.matmul(pb[i % 2][:], eexp[:, c * 128:(c + 1) * 128],
                             maskT[:, h % 2, g * 512:(g + 1) * 512], start=False, stop=True
                             ).then_inc(S["sc_ps"].h, 1)
                    if i >= 1:
                        pvden(i - 1)
                pvden(len(chunks) - 1)
            S["sc_ps"].n += len(chunks)
            S["pvstep"].n += len(chunks)

            @on("scalar")
            def _(e, gbase=gbase, nch=len(chunks), pvbase=pvbase):
                for i in range(nch):
                    e.wait_ge(S["sc_ps"].h, gbase + i + 1)
                    if gbase + i >= 3:
                        e.wait_ge(S["pvstep"].h, gbase + i - 2)
                    e.activation(pring[:, (gbase + i) % 3, :], pb[i % 2][:],
                                 AF.Exp, scale=KAPPA).then_inc(S["px"].h, 1)
            S["px"].n += len(chunks)

            corr_chunks = [i for i, c in enumerate(chunks) if 4 * g <= c < 4 * g + 4]

            @on("vector")
            def _(e, g=g, chunks=tuple(chunks), gbase=gbase,
                  corr_chunks=tuple(corr_chunks)):
                for ci, i in enumerate(corr_chunks):
                    c = chunks[i]
                    jl = c - 4 * g
                    e.wait_ge(S["px"].h, gbase + i + 1)
                    if ci >= 2:
                        e.wait_ge(S["pvstep"].h, gbase + corr_chunks[ci - 2] + 1)
                    e.tensor_tensor(corrsc[:, ci % 2, :],
                                    pring[:, (gbase + i) % 3, jl * 128:(jl + 1) * 128],
                                    uptri[:], ALU.mult).then_inc(S["corr_i"].h, 1)
            S["corr_i"].n += 4
            gc[0] += len(chunks)
            pvst[0] += len(chunks)
            cr[0] += 4

            @on("tensor")
            def _(e, h=h, g=g, unit=unit, denb=denb, swb=swp_[0]):
                if unit >= 1:
                    e.wait_ge(S["rep_ev"].h, 3 * unit)
                for jl in range(4):
                    j = 4 * g + jl
                    E = 2 * (j + 1)
                    cands = [j, max(8 + j - 1, 8), 8 + j]
                    u = swb + jl
                    if u >= swb + 1:
                        e.wait_ge(S["pswa_e"].h, u)
                    for p, cp in enumerate(cands):
                        mm = e.matmul(pb[7][:, p * 128:(p + 1) * 128],
                                      kr2[:, h % 2, cp * 128:(cp + 1) * 128],
                                      qr2[:, h % 2, j * 128:(j + 1) * 128],
                                      start=True, stop=True)
                    mm.then_inc(S["swa_ps"].h, 1)
                    e.wait_ge(S["pswa_m"].h, u + 1)
                    for p, cp in enumerate(cands):
                        e.matmul(pb[3][:, jl * 128:(jl + 1) * 128], vr2[:, h % 2, cp, :],
                                 pswa2[:, jl % 2, p * 128:(p + 1) * 128],
                                 start=(jl == 0 and p == 0), stop=(jl == 3 and p == 2))
                        mm = e.matmul(denb[0:48, jl * 128:(jl + 1) * 128],
                                      toep[:, 15 - h:63 - h],
                                      pswa2[:, jl % 2, p * 128:(p + 1) * 128],
                                      start=False, stop=False)
                    mm.then_inc(S["swa_pv"].h, 1)
            S["swa_ps"].n += 4
            S["swa_pv"].n += 4

            @on("scalar")
            def _(e, swb=swp_[0]):
                for jl in range(4):
                    e.wait_ge(S["swa_ps"].h, swb + jl + 1)
                    e.activation(pswa[:, jl % 2, :], pb[7][:, 0:384], AF.Exp, scale=KAPPA
                                 ).then_inc(S["pswa_e"].h, 1)
            S["pswa_e"].n += 4

            @on("vector")
            def _(e, g=g, swb=swp_[0]):
                for jl in range(4):
                    e.wait_ge(S["pswa_e"].h, swb + jl + 1)
                    if jl >= 2:
                        e.wait_ge(S["swa_pv"].h, swb + jl - 1)
                    cls = 0 if (g == 0 and jl == 0) else 1
                    e.tensor_tensor(pswa2[:, jl % 2, :], pswa[:, jl % 2, :],
                                    swaarr[:, cls, :], ALU.mult).then_inc(S["pswa_m"].h, 1)
            S["pswa_m"].n += 4
            swp_[0] += 4

            @on("tensor")
            def _(e, h=h, g=g, denb=denb, unit=unit):
                e.wait_ge(S["pcT_rdy"].h, 2 * h + 2)
                e.matmul(pb[4][:], vcmpall[:, h, :], pcmpT[:, h % 2, g * 512:(g + 1) * 512],
                         start=True, stop=True).then_inc(S["pv_done"].h, 1)
                e.matmul(denb[0:48, :], toep32[:, 47 - h:95 - h],
                         pcmpT[:, h % 2, g * 512:(g + 1) * 512], start=False, stop=True
                         ).then_inc(S["den_done"].h, 1)
            S["pv_done"].n += 1
            S["den_done"].n += 1

            @on("vector")
            def _(e, g=g, denb=denb, unit=unit):
                e.wait_ge(S["den_done"].h, unit + 1)
                ci(e.tensor_tensor(varrep[0:48, :], denb[0:48, :],
                                   anyfix[:, g * 512:(g + 1) * 512], ALU.add))
                cw(e)
                e.tensor_scalar(lnv[0:48, :], varrep[0:48, :], 1e-20, None, ALU.max
                                ).then_inc(S["coef_u"].h, 1)
            S["coef_u"].n += 1

            @on("scalar")
            def _(e, unit=unit):
                e.wait_ge(S["coef_u"].h, unit + 1)
                ci(e.activation(varrep[0:48, :], lnv[0:48, :], AF.Ln), "scalar")
                cw(e, "scalar")
                e.activation(lnv[0:48, :], varrep[0:48, :], AF.Exp, scale=-1.0
                             ).then_inc(S["coef_r"].h, 1)
            S["coef_r"].n += 1

            @on("vector")
            def _(e, h=h, g=g, dpar=dpar, unit=unit):
                e.wait_ge(S["coef_r"].h, unit + 1)
                e.wait_ge(S["g_rdy"].h, 2)
                e.tensor_tensor(coefb[:, dpar, :], lnv[0:48, :],
                                gT[:, g * 512:(g + 1) * 512], ALU.mult)
                e.engine_nop().then_inc(S["coef_rdy"].h, 1)
            S["coef_rdy"].n += 1

            @on("tensor")
            def _(e, h=h, dpar=dpar, unit=unit, rb=rep_[0]):
                e.wait_ge(S["coef_rdy"].h, unit + 1)
                e.wait_ge(S["pswa_e"].h, (unit + 1) * 4)
                if unit == 0:
                    e.wait_ge(S["c3_ld"].h, 16)
                for b in range(3):
                    if rb + b >= 1:
                        e.wait_ge(S["rep_ev"].h, rb + b)
                    e.matmul(pb[7][:], rowsel[:, (16 * b + h) * 128:(16 * b + h + 1) * 128],
                             coefb[0:48, dpar, :], start=True, stop=True
                             ).then_inc(S["rep_ps"].h, 1)
            S["rep_ps"].n += 3

            @on("vector")
            def _(e, rb=rep_[0]):
                for b in range(3):
                    e.wait_ge(S["rep_ps"].h, rb + b + 1)
                    e.tensor_copy(crep[:, b, :], pb[7][:]).then_inc(S["rep_ev"].h, 1)
            S["rep_ev"].n += 3
            rep_[0] += 3

            @on("vector")
            def _(e, h=h, g=g, unit=unit):
                e.wait_ge(S["rep_ev"].h, 3 * (unit + 1))
                e.wait_ge(S["pv_done"].h, unit + 1)
                ci(e.tensor_tensor(csc[:, 0, :], pb[4][:], crep[:, 0, :], ALU.mult))
                ci(e.tensor_tensor(csc[:, 1, :], pb[2][:], crep[:, 1, :], ALU.mult))
                cw(e)
                ci(e.tensor_tensor(csc[:, 2, :], csc[:, 0, :], csc[:, 1, :], ALU.add))
                cw(e)
                ci(e.tensor_tensor(csc[:, 0, :], pb[3][:], crep[:, 2, :], ALU.mult))
                cw(e)
                e.tensor_tensor(oT[:, h, g * 512:(g + 1) * 512], csc[:, 2, :], csc[:, 0, :],
                                ALU.add).then_inc(S["cmb"].h, 1)
            S["cmb"].n += 1
            units[0] += 1

    CMB_ALL = S["cmb"].n

    if "o" in tap_t:
        @on("sync")
        def _(e):
            e.wait_ge(S["cmb"].h, CMB_ALL)
            e.dma_start(out=tap_t["o"], in_=oT[:]).then_inc(S["tap_st"].h, 16)
        S["tap_st"].n += 16

    XOL0 = DS["xol"].uses
    WOL0 = DS["wqk"].uses
    for _i in range(NDC):
        DS["xol"].issued(XOL0 + _i)
        DS["wqk"].issued(WOL0 + _i)
    @on("sync")
    def _(e):
        e.wait_ge(S["cmb"].h, CMB_ALL)
        for dc in range(NDC):
            if dc >= 2:
                e.wait_ge(S["x2_rdy"].h, 2 * (dc - 1))
            e.dma_start(out=xo[:, dc % 2, :],
                        in_=xT.rearrange("(c p) t -> p c t", p=128)[:, dc, 0:OT]
                        ).then_inc(DS["xol"].handle(XOL0 + dc), 16)

    @on("gpsimd")
    def _(e):
        e.wait_ge(S["cmb"].h, CMB_ALL)
        for dc in range(NDC):
            if dc >= 2:
                e.wait_ge(S["wo_ps"].h, 2 * (dc - 1))
            e.dma_start(out=wo_sb[:, dc % 2], in_=wo[dc].rearrange("c p n -> p c n")
                        ).then_inc(DS["wol"].handle(WOL0 + dc), 16)

    @on("tensor")
    def _(e):
        e.wait_ge(S["cmb"].h, CMB_ALL)
        for dc in range(NDC):
            DS["wol"].wait(e, WOL0 + dc)
            if dc >= 2:
                e.wait_ge(S["x2_rdy"].h, 2 * (dc - 1))
            for half in range(2):
                bank = pb[(dc % 2) * 2 + half]
                for ec in range(NDC):
                    mm = e.matmul(bank[:], wo_sb[:, dc % 2, ec, :],
                                  oT[:, ec, half * 512:(half + 1) * 512],
                                  start=(ec == 0), stop=(ec == NDC - 1))
                mm.then_inc(S["wo_ps"].h, 1)
    S["wo_ps"].n += 32

    @on("vector")
    def _(e):
        for dc in range(NDC):
            DS["xol"].wait(e, XOL0 + dc)
            for half in range(2):
                e.wait_ge(S["wo_ps"].h, 2 * dc + half + 1)
                e.scalar_tensor_tensor(x2[:, dc, half * 512:(half + 1) * 512],
                                       pb[(dc % 2) * 2 + half][:],
                                       modv[:, 32 + dc:33 + dc],
                                       xo[:, dc % 2, half * 512:(half + 1) * 512],
                                       ALU.mult, ALU.add).then_inc(S["x2_rdy"].h, 1)
    S["x2_rdy"].n += 32

    if "x2" in tap_t:
        @on("sync")
        def _(e):
            e.wait_ge(S["x2_rdy"].h, 32)
            e.dma_start(out=tap_t["x2"], in_=x2[:]).then_inc(S["tap_st"].h, 16)
        S["tap_st"].n += 16

    @on("vector")
    def _(e):
        e.wait_ge(S["x2_rdy"].h, 32)
        for dc in range(NDC):
            e.tensor_copy(x2b[:, dc, :], x2[:, dc, :])
        e.engine_nop().then_inc(S["x2b_rdy"].h, 1)
    S["x2b_rdy"].n += 1

    @on("scalar")
    def _(e):
        e.wait_ge(S["x2b_rdy"].h, 1)
        for w in range(2):
            for dc in range(NDC):
                i = w * NDC + dc
                if i >= 2:
                    e.wait_ge(S["sq2_free"].h, i - 1)
                e.activation(sqring[:, i % 2, :], x2b[:, dc, w * 512:(w + 1) * 512],
                             AF.Square).then_inc(S["sq2_rdy"].h, 1)
    S["sq2_rdy"].n += 32

    @on("tensor")
    def _(e):
        e.wait_ge(S["x2b_rdy"].h, 1)
        for w in range(2):
            if w >= 1:
                e.wait_ge(S["ln2_mid"].h, 2 * w - 1)
            for dc in range(NDC):
                i = w * NDC + dc
                mm0 = e.matmul(pb[0][:], ones[:], x2b[:, dc, w * 512:(w + 1) * 512],
                               start=(dc == 0), stop=(dc == NDC - 1))
                if dc == NDC - 1:
                    mm0.then_inc(S["st2_ps"].h, 1)
                e.wait_ge(S["sq2_rdy"].h, i + 1)
                e.matmul(pb[1][:], ones[:], sqring[:, i % 2, :],
                         start=(dc == 0), stop=(dc == NDC - 1)).then_inc(S["sq2_free"].h, 1)
    S["st2_ps"].n += 2
    S["sq2_free"].n += 32

    @on("scalar")
    def _(e):
        for w in range(2):
            e.wait_ge(S["ln2_mid"].h, 2 * w + 1)
            ci(e.activation(lnv[:], varrep[:], AF.Ln, bias=epsv[:]), "scalar")
            cw(e, "scalar")
            e.activation(rstdrep[:], lnv[:], AF.Exp, scale=-0.5).then_inc(S["ln2_mid"].h, 1)

    @on("vector")
    def _(e):
        for w in range(2):
            e.wait_ge(S["st2_ps"].h, w + 1)
            e.wait_ge(S["sq2_free"].h, 16 * (w + 1))
            ci(e.tensor_scalar(mrep[:], pb[0][:], 1.0 / D, None, ALU.mult))
            cw(e)
            ci(e.tensor_tensor(lnv[:], mrep[:], mrep[:], ALU.mult))
            cw(e)
            e.scalar_tensor_tensor(varrep[:], pb[1][:], 1.0 / D, lnv[:],
                                   ALU.mult, ALU.subtract).then_inc(S["ln2_mid"].h, 1)
            e.wait_ge(S["ln2_mid"].h, 2 * w + 2)
            for dc in range(NDC):
                ci(e.tensor_tensor(ropeA[:], x2[:, dc, w * 512:(w + 1) * 512], mrep[:],
                                   ALU.subtract))
                cw(e)
                ci(e.tensor_tensor(scrB[:], ropeA[:], rstdrep[:], ALU.mult))
                cw(e)
                e.tensor_scalar(h2sb[:, dc, w * 512:(w + 1) * 512], scrB[:],
                                onepc[:, 16 + dc:17 + dc], modv[:, 48 + dc:49 + dc],
                                ALU.mult, ALU.add).then_inc(S["h2_rdy"].h, 1)
    S["ln2_mid"].n = 4
    S["h2_rdy"].n += 32
    H2_ALL = S["h2_rdy"].n

    if "h2" in tap_t:
        @on("sync")
        def _(e):
            e.wait_ge(S["h2_rdy"].h, H2_ALL)
            e.dma_start(out=tap_t["h2"], in_=h2sb[:]).then_inc(S["tap_st"].h, 16)
        S["tap_st"].n += 16

    @on("sync")
    def _(e):
        e.wait_ge(S["h2_rdy"].h, H2_ALL)
        e.dma_start(out=outT.rearrange("(c p) t -> p c t", p=128), in_=x2[:]
                    ).then_inc(S["x2_sp"].h, 16)
    S["x2_sp"].n += 16

    WUL0 = DS["wul"].uses
    WLL0 = DS["wqk"].uses
    ACC0 = DS["acc"].uses
    assert (ACC0 % 2) == 0
    for _hc in range(64):
        DS["wul"].issued(WUL0 + _hc, 32)
    for _u in range(64):
        DS["wqk"].issued(WLL0 + _u)
        DS["acc"].issued(ACC0 + _u)
    @on("gpsimd")
    def _(e):
        e.wait_ge(S["x2_sp"].h, 16)

        def acc_dma(ua):
            e.wait_ge(S["stg_rdy"].h, ua + 1)
            DS["acc"].wait(e, ACC0 + ua - 16)
            e.dma_start(out=outT.rearrange("(c p) t -> p c t", p=128)[:, ua % NDC, :],
                        in_=stage[:, ua % 2, :], accum_op=ALU.add
                        ).then_inc(DS["acc"].handle(ACC0 + ua), 16)

        def wll_load(u2):
            if u2 >= 2:
                e.wait_ge(S["dn_ps"].h, u2 - 1)
            e.dma_start(out=wl_sb[:, u2 % 2],
                        in_=wl[u2 % NDC, (u2 // NDC) * 16:(u2 // NDC) * 16 + 16
                               ].rearrange("c p n -> p c n")
                        ).then_inc(DS["wll"].handle(WLL0 + u2), 16)

        for grp in range(4):
            for hl in range(16):
                hc = grp * 16 + hl
                if hc >= 2:
                    e.wait_ge(S["up_ps"].h, 2 * (hc - 1))
                e.dma_start(out=wu_sb[:, hc % 2, 0], in_=wu0[hc].rearrange("c p n -> p c n")
                            ).then_inc(DS["wul"].handle(WUL0 + hc), 16)
                e.dma_start(out=wu_sb[:, hc % 2, 1], in_=wu1[hc].rearrange("c p n -> p c n")
                            ).then_inc(DS["wul"].handle(WUL0 + hc), 16)
                if grp >= 1:
                    u2 = (grp - 1) * 16 + hl
                    wll_load(u2)
                    if u2 >= 2:
                        acc_dma(u2 - 2)

        for hl in range(16):
            wll_load(48 + hl)
            acc_dma(46 + hl)
        acc_dma(62)
        acc_dma(63)

    @on("tensor")
    def _(e):
        e.wait_ge(S["h2_rdy"].h, H2_ALL)

        def dn_dc(gp, dc):
            u = gp * NDC + dc
            abuf = act if gp % 2 == 0 else actB
            DS["wll"].wait(e, WLL0 + u)
            if dc == 0:
                e.wait_ge(S["act_rdy"].h, 32 * (gp + 1))
            if u >= 2:
                e.wait_ge(S["stg_rdy"].h, u - 1)
            for half in range(2):
                bank = pb[4 + (dc % 2) * 2 + half]
                for hl2 in range(16):
                    mm = e.matmul(bank[:], wl_sb[:, u % 2, hl2, :],
                                  abuf[:, hl2, half * 512:(half + 1) * 512],
                                  start=(hl2 == 0), stop=(hl2 == 15))
            mm.then_inc(S["dn_ps"].h, 1)

        for grp in range(4):
            for hl in range(16):
                hc = grp * 16 + hl
                DS["wul"].wait(e, WUL0 + hc)
                for half in range(2):
                    hh = 2 * hc + half
                    if hh >= 2:
                        e.wait_ge(S["act_rdy"].h, hh - 1)
                    for proj in range(2):
                        bank = pb[2 * (hh % 2) + proj]
                        for dc in range(NDC):
                            mm = e.matmul(bank[:], wu_sb[:, hc % 2, proj, dc, :],
                                          h2sb[:, dc, half * 512:(half + 1) * 512],
                                          start=(dc == 0), stop=(dc == NDC - 1))
                    mm.then_inc(S["up_ps"].h, 1)
                if grp >= 1:
                    dn_dc(grp - 1, hl)
        for dc in range(NDC):
            dn_dc(3, dc)
    S["up_ps"].n += 128
    S["dn_ps"].n += 64

    @on("scalar")
    def _(e):
        for hh in range(128):
            e.wait_ge(S["up_ps"].h, hh + 1)
            if hh >= 2:
                e.wait_ge(S["act_rdy"].h, hh - 1)
            e.activation(gel_sb[:, hh % 2, :], pb[2 * (hh % 2)][:],
                         AF.Gelu_apprx_tanh).then_inc(S["gel_rdy"].h, 1)
    S["gel_rdy"].n += 128

    @on("vector")
    def _(e):
        def stage_u(u):
            e.wait_ge(S["dn_ps"].h, u + 1)
            DS["acc"].wait(e, ACC0 + u - 2)
            for half in range(2):
                e.tensor_scalar(stage[:, u % 2, half * 512:(half + 1) * 512],
                                pb[4 + (u % 2) * 2 + half][:],
                                modv[:, 80 + u % NDC:81 + u % NDC], None, ALU.mult)
            e.engine_nop().then_inc(S["stg_rdy"].h, 1)

        for grp in range(4):
            abuf = act if grp % 2 == 0 else actB
            for hl in range(16):
                hc = grp * 16 + hl
                for half in range(2):
                    hh = 2 * hc + half
                    if hl == 0 and half == 0 and grp >= 2:
                        e.wait_ge(S["dn_ps"].h, 16 * (grp - 1))
                    e.wait_ge(S["gel_rdy"].h, hh + 1)
                    e.tensor_tensor(abuf[:, hl, half * 512:(half + 1) * 512],
                                    gel_sb[:, hh % 2, :], pb[2 * (hh % 2) + 1][:],
                                    ALU.mult).then_inc(S["act_rdy"].h, 1)
                if grp >= 1:
                    stage_u((grp - 1) * NDC + hl)
        for u in range(48, 64):
            stage_u(u)
    S["act_rdy"].n += 128
    S["stg_rdy"].n += 64

    if "act" in tap_t:
        @on("sync")
        def _(e):
            e.wait_ge(S["act_rdy"].h, 32)
            e.dma_start(out=tap_t["act"], in_=act[:]).then_inc(S["tap_st"].h, 16)
        S["tap_st"].n += 16

    @on("sync")
    def _(e):
        DS["acc"].wait_all(e)
        if S["tap_st"].n:
            e.wait_ge(S["tap_st"].h, S["tap_st"].n)

    with nc.Block() as blk:
        @blk.sync
        def _(eng):
            for fn in OPS["sync"]:
                fn(eng)

        @blk.gpsimd
        def _(eng):
            for fn in OPS["gpsimd"]:
                fn(eng)

        @blk.tensor
        def _(eng):
            for fn in OPS["tensor"]:
                fn(eng)

        @blk.vector
        def _(eng):
            for fn in OPS["vector"]:
                fn(eng)

        @blk.scalar
        def _(eng):
            for fn in OPS["scalar"]:
                fn(eng)
    return nc


def kernel(**inputs):
    maps = _stage(inputs)
    nc = build_nc(DEBUG_TAPS)
    res = run_bass_kernel_spmd(nc, maps, core_ids=list(range(8)))
    out = np.zeros((4, NT, D), np.float32)
    for core in range(8):
        b, s = core // 2, core % 2
        own = _perm_ids(s)[0:OT]
        oTv = np.asarray(res.results[core]["outT"])
        out[b, own] = oTv.T
    return out


# version 06
# speedup vs baseline: 1.1098x; 1.0036x over previous
import numpy as np
import ml_dtypes
import concourse.bass as bass
import concourse.mybir as mybir
from concourse.bass_utils import run_bass_kernel_spmd

F32 = mybir.dt.float32
BF16 = mybir.dt.bfloat16
ALU = mybir.AluOpType
AF = mybir.ActivationFunctionType
AX = mybir.AxisListType
BF = ml_dtypes.bfloat16

H, HD, D, NT, OT = 16, 128, 2048, 2048, 1024
NDC, NTC = 16, 16
KAPPA = float(HD) ** -0.5
MASKPRE = -2048.0
NEG_BIG = -3.0e6
SUP = 1.0e6
SEL_THR = -1.0e5
EPS = 1e-5

DEBUG_TAPS = ()


def _perm_ids(s):
    ids = []
    for j in range(8):
        ids.extend(range((2 * j + s) * 128, (2 * j + s) * 128 + 128))
    for j in range(8):
        ids.extend(range((2 * j + 1 - s) * 128, (2 * j + 1 - s) * 128 + 128))
    return np.array(ids)


def _consts_for_parity(s):
    perm = _perm_ids(s)
    own = perm[0:OT]
    c = {}
    inv = 1.0 / 10000.0 ** (np.arange(64) / 64.0)
    th = perm[:, None].astype(np.float64) * inv[None, :]
    ck, sk = np.cos(th).T, np.sin(th).T
    c["cosk"] = np.tile(ck, (2, 1)).astype(BF)
    c["sink"] = np.concatenate([-sk, sk], 0).astype(BF)
    c["ident"] = np.eye(128, dtype=BF)
    c["ones_lhs"] = np.ones((128, 128), dtype=BF)
    c["eexp"] = (np.arange(32)[:, None] == (np.arange(NT)[None, :] // 64)).astype(BF)
    am = np.zeros((128, NTC, 32), dtype=np.float32)
    for cc in range(NTC):
        for t in range(128):
            am[t, cc, 2 * cc + t // 64] = 1.0 / 64.0
    c["ameans"] = am.reshape(128, NTC * 32).astype(BF)
    tp = np.zeros((128, 95), dtype=BF)
    tp[:, 47] = 1.0
    c["toep"] = tp
    tp32 = np.zeros((32, 95), dtype=BF)
    tp32[:, 47] = 1.0
    c["toep32"] = tp32
    rs = np.zeros((48, 48, 128), dtype=BF)
    for r in range(48):
        rs[r, r, :] = 1.0
    c["rowsel16"] = rs.reshape(48, 48 * 128)
    blk_seq = perm.reshape(32, 64)[:, 0] // 64
    blk_end = (blk_seq + 1) * 64 - 1
    blk_start = blk_seq * 64
    cm = np.where(own[:, None] >= blk_end[None, :], 0.0, MASKPRE).astype(np.float32)
    c["cmp_maskadd"] = np.ascontiguousarray(
        cm.reshape(8, 128, 32).transpose(1, 0, 2).reshape(128, 256)).astype(BF)
    c["cmp_maskaddT"] = np.ascontiguousarray(cm.T).astype(BF)
    cs_ = np.zeros((OT, 32), np.float32)
    cs_[np.arange(OT), np.arange(OT) // 64] = NEG_BIG
    c["cur_sup"] = np.ascontiguousarray(
        cs_.reshape(8, 128, 32).transpose(1, 0, 2).reshape(128, 256)).astype(BF)
    c["cb_min"] = np.where(blk_start[:, None] <= own[None, :], 0.0, MASKPRE).astype(BF)
    tq = np.arange(128)[None, :]
    tk = np.arange(128)[:, None]
    B1 = ((tq - tk >= 0) & (tq - tk <= 64)).astype(np.float32)
    B2 = (tk - tq >= 64).astype(np.float32)
    Z = np.zeros((128, 128), np.float32)
    gen = [B1, Z, B2] if s == 1 else [B1, B2, Z]
    sl0 = [B1, Z, B2] if s == 1 else [B1, Z, Z]
    c["swa_arr"] = np.stack([np.concatenate(sl0, 1), np.concatenate(gen, 1)],
                            1).reshape(128, 2 * 384).astype(BF)
    c["uptri"] = (-(tk > tq).astype(np.float32)).astype(BF)
    any01 = (own >= 63).astype(np.float32)
    af = np.zeros((48, OT), np.float32)
    af[0:16] = (1.0 - any01)[None, :]
    c["anyfix"] = af.astype(BF)
    return c


def _tile2(w, bk0, bk1, order="km"):
    K_, M_ = w.shape
    t = w.reshape(K_ // bk0, bk0, M_ // bk1, bk1).transpose(0, 2, 1, 3)
    if order == "mk":
        t = t.transpose(1, 0, 2, 3)
    return np.ascontiguousarray(t)


def _stage(ins):
    x = np.asarray(ins["L0_attn_x"], dtype=np.float32)
    shared = {
        "wq": _tile2(np.asarray(ins["L0_attn_Wq"], np.float32), 128, 128, "mk"),
        "wk": _tile2(np.asarray(ins["L0_attn_Wk"], np.float32), 128, 128, "mk"),
        "wv": _tile2(np.asarray(ins["L0_attn_Wv"], np.float32), 128, 512, "mk"),
        "wo": _tile2(np.asarray(ins["L0_attn_Wo"], np.float32), 128, 128, "mk"),
        "wg": np.ascontiguousarray(np.asarray(ins["L0_attn_Wg"], np.float32).reshape(NDC, 128, 48)),
        "wu0": _tile2(np.asarray(ins["L0_ffn_w_gating"], np.float32)[0], 128, 128, "mk"),
        "wu1": _tile2(np.asarray(ins["L0_ffn_w_gating"], np.float32)[1], 128, 128, "mk"),
        "wl": _tile2(np.asarray(ins["L0_ffn_w_linear"], np.float32), 128, 128, "mk"),
        "wmod": _tile2(np.asarray(ins["L0_wmod"], np.float32), 128, 512, "mk"),
    }
    pc = [_consts_for_parity(s) for s in (0, 1)]
    maps = []
    for core in range(8):
        b, s = core // 2, core % 2
        m = dict(shared)
        m.update(pc[s])
        m["xT"] = np.ascontiguousarray(x[b].T[:, _perm_ids(s)])
        m["cT"] = np.ascontiguousarray(np.asarray(ins["L0_c"], np.float32)[b:b + 1].T)
        maps.append(m)
    return maps


class Sem:
    __slots__ = ("h", "n")

    def __init__(self, nc, name):
        self.h = nc.alloc_semaphore(name)
        self.n = 0


class DSem:

    def __init__(self, nc, name, depth=2):
        self.s = [Sem(nc, f"{name}_{k}") for k in range(depth)]
        self.depth = depth
        self.tot = {}
        self.uses = 0

    def handle(self, i):
        return self.s[i % self.depth].h

    def issued(self, i, k=16):
        sub = self.s[i % self.depth]
        sub.n += k
        self.tot[i] = sub.n
        self.uses = max(self.uses, i + 1)

    def wait(self, e, i):
        if i < 0:
            return
        assert i in self.tot, (i, sorted(self.tot)[:4])
        e.wait_ge(self.s[i % self.depth].h, self.tot[i])

    def wait_all(self, e):
        for sub in self.s:
            if sub.n:
                e.wait_ge(sub.h, sub.n)


def build_nc(taps=()):
    nc = bass.Bass("TRN2", target_bir_lowering=False)
    OPS = {e: [] for e in ("sync", "gpsimd", "tensor", "vector", "scalar")}

    def on(e):
        def dec(fn):
            OPS[e].append(fn)
            return fn
        return dec

    def din(name, shape, dt=F32):
        return nc.dram_tensor(name, list(shape), dt, kind="ExternalInput").ap()

    xT = din("xT", (D, NT))
    wq = din("wq", (NDC, NDC, 128, 128))
    wk = din("wk", (NDC, NDC, 128, 128))
    wv = din("wv", (4, NDC, 128, 512))
    wo = din("wo", (NDC, NDC, 128, 128))
    wg = din("wg", (NDC, 128, 48))
    wu0 = din("wu0", (64, NDC, 128, 128))
    wu1 = din("wu1", (64, NDC, 128, 128))
    wl = din("wl", (NDC, 64, 128, 128))
    wmod = din("wmod", (24, 16, 128, 512))
    cT = din("cT", (D, 1))
    cosk_c = din("cosk", (128, NT), BF16)
    sink_c = din("sink", (128, NT), BF16)
    ident_c = din("ident", (128, 128), BF16)
    ones_c = din("ones_lhs", (128, 128), BF16)
    eexp_c = din("eexp", (32, NT), BF16)
    ameans_c = din("ameans", (128, NTC * 32), BF16)
    toep_c = din("toep", (128, 95), BF16)
    toep32_c = din("toep32", (32, 95), BF16)
    rowsel_c = din("rowsel16", (48, 48 * 128), BF16)
    cmpma_c = din("cmp_maskadd", (128, 256), BF16)
    cmpmaT_c = din("cmp_maskaddT", (32, OT), BF16)
    cursup_c = din("cur_sup", (128, 256), BF16)
    cbmin_c = din("cb_min", (32, OT), BF16)
    swaarr_c = din("swa_arr", (128, 2 * 384), BF16)
    uptri_c = din("uptri", (128, 128), BF16)
    anyfix_c = din("anyfix", (48, OT), BF16)

    outT = nc.dram_tensor("outT", [D, OT], F32, kind="ExternalOutput").ap()
    kdram = nc.dram_tensor("kdram", [H, 128, NT], BF16,
                           kind="ExternalOutput" if "k" in taps else "Internal").ap()
    qdram = nc.dram_tensor("qdram", [H, 128, OT], BF16,
                           kind="ExternalOutput" if "q" in taps else "Internal").ap()
    vdram = nc.dram_tensor("vdram", [NTC, 4, 128, 512], BF16,
                           kind="ExternalOutput" if "v" in taps else "Internal").ap()
    tap_t = {}
    tap_shapes = {
        "mod": ([1, 6 * D], F32), "h": ([128, NDC, NT], BF16),
        "g": ([48, OT], F32), "kcmp": ([128, H, 32], BF16),
        "vcmp": ([32, H, 128], BF16), "imp": ([128, H * 256], BF16),
        "maskT": ([32, H, OT], BF16), "pcmpT": ([32, H, OT], BF16),
        "o": ([128, H, OT], BF16), "x2": ([128, NDC, OT], F32),
        "h2": ([128, NDC, OT], BF16), "act": ([128, 16, OT], BF16),
        "coef": ([16, H, 2, 3, 512], BF16),
    }
    for t in taps:
        if t in ("k", "q", "v"):
            continue
        sh, dt = tap_shapes[t]
        tap_t[t] = nc.dram_tensor("tap_" + t, sh, dt, kind="ExternalOutput").ap()

    names = [
        "c_ld", "c2_ld", "wg_ld", "silu_rdy", "mod_ps", "mod_evac",
        "modt2", "mod_rdy",
        "sq_rdy", "sq_free", "stats_ps", "ln_mid", "h_rdy",
        "qk_ps", "kraw_rdy", "rope_out",
        "g_ps", "g_rdy", "v_ps", "v_evac",
        "kc_rdy", "vc_ps", "vc_rdy",
        "scmp_ps", "cm_rdy", "esc_rdy", "imp_init", "topk_done",
        "krfree_a", "qrfree_a", "vrfree_a",
        "krfree_b", "qrfree_b", "vrfree_b",
        "tr_ps", "mt_rdy", "scT_ps", "scT_f", "pcT_rdy",
        "sc_ps", "px", "pvstep", "corr_i", "swa_ps", "swa_pv", "pswa_e", "pswa_m",
        "pv_done", "den_done", "coef_u", "coef_r", "coef_rdy",
        "rep_ps", "rep_ev", "cmb",
        "wo_ps", "x2_rdy",
        "x2b_rdy", "sq2_rdy", "sq2_free", "st2_ps", "ln2_mid", "h2_rdy", "x2_sp",
        "up_ps", "gel_rdy", "act_rdy", "dn_ps", "stg_rdy",
        "tap_st", "c3_ld",
    ]
    S = {n: Sem(nc, n) for n in names}
    DS = {n: DSem(nc, n) for n in (
        "wva", "xw", "wqk", "kacc", "kqw", "vst", "modch", "kr", "qr", "vr")}
    DSAL = {"wmod": "wva", "wvb": "wva", "wul": "wva",
            "xol": "vst",
            "wol": "wqk", "wll": "wqk", "acc": "xw",
            "kswp": "kacc", "kqst": "kqw",
            "kra": "kr", "krb": "kr", "qra": "qr", "qrb": "qr",
            "vra": "vr", "vrb": "vr"}
    for _a, _b in DSAL.items():
        DS[_a] = DS[_b]

    CHN = {"vector": [Sem(nc, "chv"), 0], "scalar": [Sem(nc, "chs"), 0]}

    def ci(instr, eng="vector"):
        CHN[eng][1] += 1
        return instr.then_inc(CHN[eng][0].h, 1)

    def cw(e, eng="vector"):
        e.wait_ge(CHN[eng][0].h, CHN[eng][1])

    sbt = nc.alloc_sbuf_tensor
    ident = sbt("ident_sb", [128, 128], BF16)
    ones = sbt("ones_sb", [128, 128], BF16)
    uptri = sbt("uptri_sb", [128, 128], BF16)
    modv = sbt("modv", [128, 96], F32)
    epsv = sbt("epsv", [128, 1], F32)
    clsc = sbt("clsc", [128, 48], F32)
    mrep = sbt("mrep", [128, 512], BF16)
    varrep = sbt("varrep", [128, 512], F32)
    rstdrep = sbt("rstdrep", [128, 512], BF16)
    sqring = sbt("sqring", [128, 2, 512], BF16)
    esc = sbt("esc", [128, 256], BF16)
    cmadd = sbt("cmadd", [128, 256], F32)
    mxb = sbt("mxb", [128, 128], BF16)
    ropeA = sbt("ropeA", [128, 512], BF16)
    scrB = sbt("scrB", [128, 512], BF16)
    cT_sb = sbt("cT_sb", [128, 16, 1], F32)
    siluT = sbt("siluT", [128, 16, 1], BF16)
    lnv = sbt("lnv", [128, 512], F32)
    onepc = sbt("onepc", [128, 32], F32)
    wg_sb = sbt("wg_sb", [128, NDC, 48], BF16)
    gT = sbt("gT", [48, OT], F32)
    kcmpall = sbt("kcmpall", [128, H, 32], BF16)
    kcf = sbt("kcf", [128, 32], F32)
    modrow_dr = nc.dram_tensor("modrow_dr", [1, 6 * D], F32, kind="Internal").ap()

    AMAX = 186 * 1024
    AR = sbt("ARENA", [128, AMAX], mybir.dt.uint8)
    spans = []

    def V_(name, off_kb, shape, dt, ph):
        off = int(off_kb * 1024)
        per = int(np.prod(shape[1:])) * mybir.dt.size(dt)
        assert off % 256 == 0 and off + per <= AMAX, (name, off, per, AMAX)
        for (n2, o2, p2, ph2) in spans:
            if not (off >= o2 + p2 or o2 >= off + per or ph[1] < ph2[0] or ph2[1] < ph[0]):
                raise AssertionError(f"overlap {name} vs {n2}")
        spans.append((name, off, per, ph))
        ap = AR[0:shape[0], off:off + per].bitcast(dt)
        if len(shape) > 2:
            pat = "p (" + " ".join(f"d{i}" for i in range(1, len(shape))) + ") -> p " + \
                  " ".join(f"d{i}" for i in range(1, len(shape)))
            kw = {f"d{i}": shape[i] for i in range(1, len(shape))}
            ap = ap.rearrange(pat, **kw)
        return ap

    xwin = V_("xwin", 0, [128, 2, NDC, 512], BF16, (0, 0))
    modrow = V_("modrow", 44, [1, 2, 512], F32, (0, 0))
    wmod_sb = V_("wmod", 32, [128, 2, 2, 512], BF16, (0, 0))
    cosk_sb = V_("cosk", 36, [128, NT], BF16, (0, 1))
    sink_sb = V_("sink", 40, [128, NT], BF16, (0, 1))

    wqk_sb = V_("wqk", 0, [128, 2, NDC, 128], BF16, (1, 1))
    wvb_sb = V_("wvb", 8, [128, 2, 512], BF16, (1, 1))
    kraw = V_("kraw", 10, [128, 2, 512], BF16, (1, 1))
    kswp = V_("kswp", 12, [128, 2, 512], BF16, (1, 1))
    kqst = V_("kqst", 14, [128, 2, 2048], BF16, (1, 1))
    vstg = V_("vstg", 22, [128, 2, 512], BF16, (1, 1))

    kr2 = V_("kr2", 0, [128, 2, NT], BF16, (2, 3))
    vr2 = V_("vr2", 8, [128, 2, NTC, 128], BF16, (2, 3))
    qr2 = V_("qr2", 16, [128, 2, OT], BF16, (2, 3))
    vcmpall = V_("vcmpall", 20, [32, H, 128], BF16, (2, 3))
    impa = V_("impa", 24, [128, H * 256], BF16, (2, 3))
    impb = V_("impb", 32, [128, H * 256], BF16, (2, 3))
    selb = V_("selb", 40, [128, H * 256], BF16, (2, 2))
    eexp = V_("eexp", 48, [32, NT], BF16, (2, 3))
    ameans = V_("ameans", 52, [128, NTC * 32], BF16, (2, 2))
    toep = V_("toep", 53, [128, 95], BF16, (2, 3))
    toep32 = V_("toep32", 53.25, [32, 95], BF16, (2, 3))
    rowsel = V_("rowsel", 130, [48, 48 * 128], BF16, (2, 3))
    cmpma = V_("cmpma", 57.5, [128, 256], BF16, (2, 2))
    cmpmaT = V_("cmpmaT", 58, [32, OT], BF16, (2, 3))
    cursup = V_("cursup", 60, [128, 256], BF16, (2, 2))
    cbmin = V_("cbmin", 60.5, [32, OT], BF16, (2, 3))
    swaarr = V_("swaarr", 62.5, [128, 2, 384], BF16, (2, 3))
    anyfix = V_("anyfix", 64, [48, OT], BF16, (2, 3))
    pring = V_("pring", 66, [128, 3, 512], BF16, (3, 3))
    pswa = V_("pswa", 69, [128, 2, 384], BF16, (3, 3))
    pswa2 = V_("pswa2", 70.5, [128, 2, 384], BF16, (3, 3))
    pcmpT = V_("pcmpT", 72, [32, 2, OT], BF16, (3, 3))
    maskT = V_("maskT", 76, [32, 2, OT], BF16, (3, 3))
    mtmp = V_("mtmp", 80, [32, OT], BF16, (3, 3))
    scmpTf = V_("scmpTf", 82, [32, 512], F32, (3, 3))
    coefb = V_("coefb", 84, [48, 2, 512], BF16, (3, 3))
    crep = V_("crep", 90, [128, 3, 512], BF16, (3, 3))
    csc = V_("csc", 93, [128, 3, 512], BF16, (3, 3))
    corrsc = V_("corrsc", 96, [128, 2, 128], BF16, (3, 3))

    oT = V_("oT", 97, [128, H, OT], BF16, (3, 4))
    hT = V_("hT", 122, [128, NDC, NT], BF16, (0, 1))
    xo = V_("xo", 0, [128, 2, OT], F32, (4, 4))
    wo_sb = V_("wo_sb", 8, [128, 2, NDC, 128], BF16, (4, 4))
    x2 = V_("x2", 16, [128, NDC, OT], F32, (4, 5))
    x2b_lo = V_("x2b_lo", 80, [128, 8, OT], BF16, (4, 5))
    x2b_hi = V_("x2b_hi", 144, [128, 8, OT], BF16, (4, 5))

    def x2b(dc):
        return (x2b_lo if dc < 8 else x2b_hi)[:, dc % 8, :]
    h2sb = V_("h2sb", 112, [128, NDC, OT], BF16, (5, 6))
    act = V_("act", 144, [128, 16, OT], BF16, (6, 6))
    actB = V_("actB", 80, [128, 16, OT], BF16, (6, 6))
    wu_sb = V_("wu_sb", 16, [128, 2, 2, NDC, 128], BF16, (6, 6))
    wl_sb = V_("wl_sb", 32, [128, 2, 16, 128], BF16, (6, 6))
    stage = V_("stage", 40, [128, 2, OT], F32, (6, 6))
    gel_sb = V_("gel", 176, [128, 2, 512], BF16, (6, 6))

    pb = [nc.alloc_psum_tensor(f"pb{i}", [128, 512], F32) for i in range(8)]
    pb0bf = pb[0][0:32, :].bitcast(BF16)

    const_pairs = [
        (ident[:], ident_c), (ones[:], ones_c), (uptri[:], uptri_c),
        (cosk_sb, cosk_c), (sink_sb, sink_c),
    ]
    const_pairs2 = [
        (eexp, eexp_c), (ameans, ameans_c), (toep, toep_c), (toep32, toep32_c),
        (cmpma, cmpma_c), (cmpmaT, cmpmaT_c),
        (cursup, cursup_c), (cbmin, cbmin_c),
        (swaarr.rearrange("p a n -> p (a n)"), swaarr_c), (anyfix, anyfix_c),
    ]

    @on("sync")
    def _(e):
        for dst, src in const_pairs:
            e.dma_start(out=dst, in_=src).then_inc(S["c_ld"].h, 16)
        e.dma_start(out=cT_sb[:], in_=cT.rearrange("(c p) b -> p c b", p=128),
                    allow_slow_non_contiguous=True).then_inc(S["c_ld"].h, 16)
    S["c_ld"].n += 16 * (len(const_pairs) + 1)
    C1 = S["c_ld"].n

    @on("gpsimd")
    def _(e):
        e.dma_start(out=wg_sb[:], in_=wg.rearrange("c p n -> p c n")).then_inc(S["wg_ld"].h, 16)
        for jc in range(24):
            for part in range(8):
                i = jc * 8 + part
                if i >= 2:
                    e.wait_ge(S["mod_ps"].h, i - 1)
                e.dma_start(out=wmod_sb[:, i % 2],
                            in_=wmod[jc, part * 2:part * 2 + 2].rearrange("c p n -> p c n")
                            ).then_inc(DS["wmod"].handle(i), 16)
    S["wg_ld"].n += 16
    for _i in range(192):
        DS["wva"].issued(_i)

    @on("scalar")
    def _(e):
        e.wait_ge(S["c_ld"].h, C1)
        e.activation(siluT[:], cT_sb[:], AF.Silu).then_inc(S["silu_rdy"].h, 1)
    S["silu_rdy"].n += 1

    @on("tensor")
    def _(e):
        e.wait_ge(S["silu_rdy"].h, 1)
        for jc in range(24):
            if jc >= 1:
                e.wait_ge(S["mod_evac"].h, jc)
            for part in range(8):
                i = jc * 8 + part
                DS["wmod"].wait(e, i)
                e.matmul(pb[6][0:1, :], siluT[:, part * 2, :],
                         wmod_sb[:, i % 2, 0, :],
                         start=(part == 0), stop=False)
                e.matmul(pb[6][0:1, :], siluT[:, part * 2 + 1, :],
                         wmod_sb[:, i % 2, 1, :],
                         start=False, stop=(part == 7)).then_inc(S["mod_ps"].h, 1)
    S["mod_ps"].n += 192

    @on("vector")
    def _(e):
        for jc in range(24):
            e.wait_ge(S["mod_ps"].h, (jc + 1) * 8)
            DS["modch"].wait(e, jc - 2)
            e.tensor_copy(modrow[0:1, jc % 2, :], pb[6][0:1, :]
                          ).then_inc(S["mod_evac"].h, 1)
    S["mod_evac"].n += 24

    @on("sync")
    def _(e):
        for jc in range(24):
            e.wait_ge(S["mod_evac"].h, jc + 1)
            e.dma_start(out=modrow_dr[0:1, jc * 512:(jc + 1) * 512],
                        in_=modrow[0:1, jc % 2, :]).then_inc(DS["modch"].handle(jc), 16)
        DS["modch"].wait_all(e)
        e.dma_start(out=modv[:], in_=modrow_dr.rearrange("o (m p) -> (o p) m", p=128),
                    allow_slow_non_contiguous=True).then_inc(S["modt2"].h, 16)
    S["modt2"].n += 16
    for _i in range(24):
        DS["modch"].issued(_i)

    @on("vector")
    def _(e):
        e.wait_ge(S["modt2"].h, 16)
        for c0 in (32, 80):
            sl = modv[:, c0:c0 + 16]
            ci(e.tensor_scalar(clsc[:, 0:16], sl, 0.0, 2.0, ALU.is_ge, ALU.mult))
            cw(e)
            ci(e.tensor_scalar(clsc[:, 16:32], clsc[:, 0:16], 1.0, None, ALU.subtract))
            cw(e)
            ci(e.tensor_tensor(clsc[:, 0:16], sl, clsc[:, 16:32], ALU.mult))
            cw(e)
            ci(e.tensor_scalar(clsc[:, 32:48], clsc[:, 0:16], 0.25, None, ALU.max))
            cw(e)
            ci(e.tensor_tensor(sl, clsc[:, 16:32], clsc[:, 32:48], ALU.mult))
            cw(e)
        ci(e.tensor_scalar(onepc[:, 0:16], modv[:, 16:32], 1.0, None, ALU.add))
        ci(e.tensor_scalar(onepc[:, 16:32], modv[:, 64:80], 1.0, None, ALU.add))
        e.memset(epsv[:], EPS)
        e.engine_nop().then_inc(S["mod_rdy"].h, 1)
    S["mod_rdy"].n += 1

    if "mod" in tap_t:
        @on("sync")
        def _(e):
            e.wait_ge(S["mod_rdy"].h, 1)
            e.dma_start(out=tap_t["mod"], in_=modrow_dr).then_inc(S["tap_st"].h, 16)
        S["tap_st"].n += 16

    @on("gpsimd")
    def _(e):
        for w in range(4):
            if w >= 2:
                e.wait_ge(S["h_rdy"].h, (w - 1) * NDC)
            e.dma_start(out=xwin[:, w % 2],
                        in_=xT.rearrange("(c p) t -> p c t", p=128)[:, :, w * 512:(w + 1) * 512]
                        ).then_inc(DS["xw"].handle(w), 16)

    for _i in range(4):
        DS["xw"].issued(_i)

    @on("scalar")
    def _(e):
        for w in range(4):
            DS["xw"].wait(e, w)
            for dc in range(NDC):
                i = w * NDC + dc
                if i >= 2:
                    e.wait_ge(S["sq_free"].h, i - 1)
                e.activation(sqring[:, i % 2, :], xwin[:, w % 2, dc, :], AF.Square
                             ).then_inc(S["sq_rdy"].h, 1)
            e.wait_ge(S["ln_mid"].h, 2 * w + 1)
            ci(e.activation(lnv[:], varrep[:], AF.Ln, bias=epsv[:]), "scalar")
            cw(e, "scalar")
            e.activation(rstdrep[:], lnv[:], AF.Exp, scale=-0.5).then_inc(S["ln_mid"].h, 1)
    S["sq_rdy"].n += 64

    @on("tensor")
    def _(e):
        e.wait_ge(S["c_ld"].h, C1)
        for w in range(4):
            DS["xw"].wait(e, w)
            if w >= 1:
                e.wait_ge(S["ln_mid"].h, 2 * w - 1)
            for dc in range(NDC):
                i = w * NDC + dc
                mm0 = e.matmul(pb[0][:], ones[:], xwin[:, w % 2, dc, :],
                               start=(dc == 0), stop=(dc == NDC - 1))
                if dc == NDC - 1:
                    mm0.then_inc(S["stats_ps"].h, 1)
                e.wait_ge(S["sq_rdy"].h, i + 1)
                e.matmul(pb[1][:], ones[:], sqring[:, i % 2, :],
                         start=(dc == 0), stop=(dc == NDC - 1)).then_inc(S["sq_free"].h, 1)
    S["stats_ps"].n += 4
    S["sq_free"].n += 64

    @on("vector")
    def _(e):
        e.wait_ge(S["mod_rdy"].h, 1)
        for w in range(4):
            e.wait_ge(S["stats_ps"].h, w + 1)
            e.wait_ge(S["sq_free"].h, 16 * (w + 1))
            ci(e.tensor_scalar(mrep[:], pb[0][:], 1.0 / D, None, ALU.mult))
            cw(e)
            ci(e.tensor_tensor(lnv[:], mrep[:], mrep[:], ALU.mult))
            cw(e)
            e.scalar_tensor_tensor(varrep[:], pb[1][:], 1.0 / D, lnv[:],
                                   ALU.mult, ALU.subtract).then_inc(S["ln_mid"].h, 1)
            e.wait_ge(S["ln_mid"].h, 2 * w + 2)
            for dc in range(NDC):
                ci(e.tensor_tensor(ropeA[:], xwin[:, w % 2, dc, :], mrep[:], ALU.subtract))
                cw(e)
                ci(e.tensor_tensor(scrB[:], ropeA[:], rstdrep[:], ALU.mult))
                cw(e)
                e.tensor_scalar(hT[:, dc, w * 512:(w + 1) * 512], scrB[:],
                                onepc[:, dc:dc + 1], modv[:, dc:dc + 1],
                                ALU.mult, ALU.add).then_inc(S["h_rdy"].h, 1)
    S["ln_mid"].n = 8
    S["h_rdy"].n += 64
    H_ALL = S["h_rdy"].n

    if "h" in tap_t:
        @on("sync")
        def _(e):
            e.wait_ge(S["h_rdy"].h, H_ALL)
            e.dma_start(out=tap_t["h"], in_=hT[:]).then_inc(S["tap_st"].h, 16)
        S["tap_st"].n += 16

    @on("sync")
    def _(e):
        for dst, src in const_pairs2:
            e.dma_start(out=dst, in_=src).then_inc(S["c2_ld"].h, 16)
    S["c2_ld"].n += 16 * len(const_pairs2)
    C2 = S["c2_ld"].n

    def emit_qk(proj):
        wsrc, ddst, nwin = (wq, qdram, 2) if proj == 0 else (wk, kdram, 4)
        ps0 = S["qk_ps"].n
        raw0, out0 = S["kraw_rdy"].n, S["rope_out"].n
        wq0, sw0, st0 = DS["wqk"].uses, DS["kswp"].uses, DS["kqst"].uses
        NTOT = NDC * nwin

        @on("gpsimd")
        def _(e):
            for ec in range(NDC):
                if ec >= 2:
                    e.wait_ge(S["qk_ps"].h, ps0 + (ec - 1) * nwin)
                elif ps0 > 0:
                    e.wait_ge(S["qk_ps"].h, ps0)
                else:
                    e.wait_ge(S["h_rdy"].h, H_ALL)
                e.dma_start(out=wqk_sb[:, ec % 2],
                            in_=wsrc[ec].rearrange("c p n -> p c n")
                            ).then_inc(DS["wqk"].handle(wq0 + ec), 16)

        for _i in range(NDC):
            DS["wqk"].issued(wq0 + _i)

        @on("tensor")
        def _(e):
            if proj == 0:
                e.wait_ge(S["h_rdy"].h, H_ALL)
            for ec in range(NDC):
                DS["wqk"].wait(e, wq0 + ec)
                for w in range(nwin):
                    i = ec * nwin + w
                    if raw0 + i >= 4:
                        e.wait_ge(S["kraw_rdy"].h, raw0 + i - 3)
                    for dc in range(NDC):
                        mm = e.matmul(pb[i % 4][:], wqk_sb[:, ec % 2, dc, :],
                                      hT[:, dc, w * 512:(w + 1) * 512],
                                      start=(dc == 0), stop=(dc == NDC - 1))
                    mm.then_inc(S["qk_ps"].h, 1)
        S["qk_ps"].n += NTOT

        @on("scalar")
        def _(e):
            for i in range(NTOT):
                e.wait_ge(S["kraw_rdy"].h, raw0 + i + 1)
                if i >= 2:
                    e.wait_ge(S["rope_out"].h, out0 + i - 1)
                e.dma_start(out=kswp[0:64, i % 2, :], in_=kraw[64:128, i % 2, :]
                            ).then_inc(DS["kswp"].handle(sw0 + i), 16)
                e.dma_start(out=kswp[64:128, i % 2, :], in_=kraw[0:64, i % 2, :]
                            ).then_inc(DS["kswp"].handle(sw0 + i), 16)

        for _i in range(NTOT):
            DS["kacc"].issued(sw0 + _i, 32)

        @on("vector")
        def _(e):
            def rope(j):
                ec, w = j // nwin, j % nwin
                DS["kswp"].wait(e, sw0 + j)
                e.wait_ge(S["rope_out"].h, out0 + j)
                if st0 + ec >= 2 and w == 0:
                    DS["kqst"].wait(e, st0 + ec - 2)
                ci(e.tensor_tensor(ropeA[:], kraw[:, j % 2, :],
                                   cosk_sb[:, w * 512:(w + 1) * 512], ALU.mult))
                ci(e.tensor_tensor(vstg[:, j % 2, :], kswp[:, j % 2, :],
                                   sink_sb[:, w * 512:(w + 1) * 512], ALU.mult))
                cw(e)
                e.tensor_tensor(kqst[:, ec % 2, w * 512:(w + 1) * 512],
                                ropeA[:], vstg[:, j % 2, :], ALU.add
                                ).then_inc(S["rope_out"].h, 1)
            for i in range(NTOT):
                e.wait_ge(S["qk_ps"].h, ps0 + i + 1)
                DS["kswp"].wait(e, sw0 + i - 2)
                e.tensor_copy(kraw[:, i % 2, :], pb[i % 4][:]).then_inc(S["kraw_rdy"].h, 1)
                if i >= 1:
                    rope(i - 1)
            rope(NTOT - 1)
        S["kraw_rdy"].n += NTOT
        S["rope_out"].n += NTOT

        @on("sync")
        def _(e):
            for ec in range(NDC):
                e.wait_ge(S["rope_out"].h, out0 + (ec + 1) * nwin)
                e.dma_start(out=ddst[ec], in_=kqst[:, ec % 2, 0:nwin * 512]
                            ).then_inc(DS["kqst"].handle(st0 + ec), 16)

        for _i in range(NDC):
            DS["kqw"].issued(st0 + _i)

    emit_qk(0)
    emit_qk(1)
    S_KRAW_ALL = S["kraw_rdy"].n

    @on("tensor")
    def _(e):
        e.wait_ge(S["wg_ld"].h, 16)
        e.wait_ge(S["mod_evac"].h, 24)
        for w in range(2):
            if w >= 1:
                e.wait_ge(S["g_rdy"].h, w)
            for dc in range(NDC):
                mm = e.matmul(pb[6][0:48, :], wg_sb[:, dc, :],
                              hT[:, dc, w * 512:(w + 1) * 512],
                              start=(dc == 0), stop=(dc == NDC - 1))
            mm.then_inc(S["g_ps"].h, 1)
    S["g_ps"].n += 2

    @on("scalar")
    def _(e):
        for w in range(2):
            e.wait_ge(S["g_ps"].h, w + 1)
            e.activation(gT[:, w * 512:(w + 1) * 512], pb[6][0:48, :], AF.Sigmoid
                         ).then_inc(S["g_rdy"].h, 1)
    S["g_rdy"].n += 2

    if "g" in tap_t:
        @on("sync")
        def _(e):
            e.wait_ge(S["g_rdy"].h, 2)
            e.dma_start(out=tap_t["g"], in_=gT[:]).then_inc(S["tap_st"].h, 16)
        S["tap_st"].n += 16

    WVB0 = DS["wva"].uses
    for _i in range(128):
        DS["wva"].issued(WVB0 + _i)

    @on("gpsimd")
    def _(e):
        for es in range(4):
            for th in range(2):
                for dc in range(NDC):
                    i = (es * 2 + th) * NDC + dc
                    if i >= 2:
                        e.wait_ge(S["v_ps"].h, i - 1)
                    else:
                        e.wait_ge(S["mod_ps"].h, 192)
                    e.dma_start(out=wvb_sb[:, i % 2, :], in_=wv[es, dc]
                                ).then_inc(DS["wvb"].handle(WVB0 + i), 16)

    @on("tensor")
    def _(e):
        e.wait_ge(S["g_rdy"].h, 2)
        e.wait_ge(S["kraw_rdy"].h, S_KRAW_ALL)
        for es in range(4):
            for th in range(2):
                u = es * 2 + th
                if u >= 1:
                    e.wait_ge(S["v_evac"].h, u * 8)
                for dc in range(NDC):
                    i = u * NDC + dc
                    DS["wvb"].wait(e, WVB0 + i)
                    for bk in range(8):
                        tc = th * 8 + bk
                        mm = e.matmul(pb[bk][:], hT[:, dc, tc * 128:(tc + 1) * 128],
                                      wvb_sb[:, i % 2, :],
                                      start=(dc == 0), stop=(dc == NDC - 1))
                    mm.then_inc(S["v_ps"].h, 1)
    S["v_ps"].n += 128

    @on("vector")
    def _(e):
        e.wait_ge(S["rope_out"].h, S["rope_out"].n)
        for es in range(4):
            for th in range(2):
                u = es * 2 + th
                e.wait_ge(S["v_ps"].h, (u + 1) * NDC)
                for bk in range(8):
                    i = u * 8 + bk
                    DS["vst"].wait(e, i - 2)
                    e.tensor_copy(vstg[:, i % 2, :], pb[bk][:]).then_inc(S["v_evac"].h, 1)
    S["v_evac"].n += 64

    @on("sync")
    def _(e):
        for es in range(4):
            for th in range(2):
                for bk in range(8):
                    i = (es * 2 + th) * 8 + bk
                    tc = th * 8 + bk
                    e.wait_ge(S["v_evac"].h, i + 1)
                    e.dma_start(out=vdram[tc, es], in_=vstg[:, i % 2, :]
                                ).then_inc(DS["vst"].handle(i), 16)

    for _i in range(64):
        DS["vst"].issued(_i)

    @on("sync")
    def _(e):
        e.wait_ge(S["v_evac"].h, 64)
        e.dma_start(out=rowsel, in_=rowsel_c).then_inc(S["c3_ld"].h, 16)
    S["c3_ld"].n += 16

    KQ_T = tuple(sub.n for sub in DS["kqst"].s)
    VST_T = tuple(sub.n for sub in DS["vst"].s)

    @on("gpsimd")
    def _(e, KQ_T=KQ_T, VST_T=VST_T):
        for _k, _sub in enumerate(DS["kqst"].s):
            if KQ_T[_k]:
                e.wait_ge(_sub.h, KQ_T[_k])
        for _k, _sub in enumerate(DS["vst"].s):
            if VST_T[_k]:
                e.wait_ge(_sub.h, VST_T[_k])
        e.wait_ge(S["c2_ld"].h, C2)
        for h in range(H):
            if h >= 2:
                e.wait_ge(S["kc_rdy"].h, h - 1)
                e.wait_ge(S["esc_rdy"].h, h - 1)
                e.wait_ge(S["vc_rdy"].h, h - 1)
            e.dma_start(out=kr2[:, h % 2, :], in_=kdram[h]).then_inc(DS["kra"].handle(h), 16)
            e.dma_start(out=qr2[:, h % 2, :], in_=qdram[h]).then_inc(DS["qra"].handle(h), 16)
            e.dma_start(out=vr2[:, h % 2],
                        in_=vdram[:, h // 4, :, (h % 4) * 128:(h % 4 + 1) * 128
                                  ].rearrange("c p n -> p c n")).then_inc(DS["vra"].handle(h), 16)

    for _h in range(H):
        DS["kr"].issued(_h)
        DS["qr"].issued(_h)
        DS["vr"].issued(_h)

    @on("tensor")
    def _(e):
        e.wait_ge(S["c2_ld"].h, C2)
        for h in range(H):
            DS["qra"].wait(e, h)
            DS["vra"].wait(e, h)
            e.wait_ge(S["kc_rdy"].h, h + 1)
            if h >= 1:
                e.wait_ge(S["cm_rdy"].h, h)
            if h >= 2:
                e.wait_ge(S["vc_rdy"].h, h - 1)
            for qb in range(8):
                mm = e.matmul(pb[0][:, qb * 32:(qb + 1) * 32],
                              qr2[:, h % 2, qb * 128:(qb + 1) * 128],
                              kcmpall[:, h, :], start=True, stop=True)
            mm.then_inc(S["scmp_ps"].h, 1)
            for c in range(NTC):
                mm = e.matmul(pb[2 + h % 2][0:32, 0:128], ameans[:, c * 32:(c + 1) * 32],
                              vr2[:, h % 2, c, :], start=(c == 0), stop=(c == NTC - 1))
            mm.then_inc(S["vc_ps"].h, 1)
    S["scmp_ps"].n += H
    S["vc_ps"].n += H

    @on("vector")
    def _(e):
        e.wait_ge(S["c2_ld"].h, C2)
        for h in range(H):
            DS["kra"].wait(e, h)
            ci(e.reduce_sum(kcf[:], kr2[:, h % 2, :].rearrange("p (b t) -> p b t", b=32),
                            axis=AX.X))
            cw(e)
            e.tensor_scalar(kcmpall[:, h, :], kcf[:], 1.0 / 64.0, None, ALU.mult
                            ).then_inc(S["kc_rdy"].h, 1)
            e.wait_ge(S["vc_ps"].h, h + 1)
            e.tensor_copy(vcmpall[:, h, :], pb[2 + h % 2][0:32, 0:128]).then_inc(S["vc_rdy"].h, 1)
            e.wait_ge(S["scmp_ps"].h, h + 1)
            e.tensor_tensor(cmadd[:], pb[0][:, 0:256], cmpma[:], ALU.add).then_inc(S["cm_rdy"].h, 1)
            e.wait_ge(S["esc_rdy"].h, h + 1)
            e.tensor_tensor(impa[:, h * 256:(h + 1) * 256], esc[:], cursup[:], ALU.add
                            ).then_inc(S["imp_init"].h, 1)
    S["kc_rdy"].n += H
    S["vc_rdy"].n += H
    S["cm_rdy"].n += H
    S["imp_init"].n += H

    @on("scalar")
    def _(e):
        for h in range(H):
            e.wait_ge(S["cm_rdy"].h, h + 1)
            e.activation(esc[:], cmadd[:], AF.Exp, scale=KAPPA).then_inc(S["esc_rdy"].h, 1)
    S["esc_rdy"].n += H

    if "imp" in tap_t:
        @on("sync")
        def _(e):
            e.wait_ge(S["imp_init"].h, H)
            e.dma_start(out=tap_t["imp"], in_=impa[:]).then_inc(S["tap_st"].h, 16)
        S["tap_st"].n += 16
    if "kcmp" in tap_t:
        @on("sync")
        def _(e):
            e.wait_ge(S["kc_rdy"].h, H)
            e.dma_start(out=tap_t["kcmp"], in_=kcmpall[:]).then_inc(S["tap_st"].h, 16)
        S["tap_st"].n += 16
    if "vcmp" in tap_t:
        @on("sync")
        def _(e):
            e.wait_ge(S["vc_rdy"].h, H)
            e.dma_start(out=tap_t["vcmp"], in_=vcmpall[:]).then_inc(S["tap_st"].h, 16)
        S["tap_st"].n += 16

    @on("vector")
    def _(e):
        for hf in range(2):
            sl = slice(hf * 2048, (hf + 1) * 2048)
            e.wait_ge(S["imp_init"].h, 8 * (hf + 1))
            for it in range(7):
                src = (impa if it % 2 == 0 else impb)[:, sl]
                dst = (impb if it % 2 == 0 else impa)[:, sl]
                ci(e.reduce_max(mxb[:, hf * 64:(hf + 1) * 64],
                                src.rearrange("p (g b) -> p g b", b=32), axis=AX.X))
                cw(e)
                ci(e.tensor_tensor(selb[:, sl].rearrange("p (g b) -> p g b", b=32),
                                   src.rearrange("p (g b) -> p g b", b=32),
                                   mxb[:, hf * 64:(hf + 1) * 64].unsqueeze(2)
                                   .broadcast_to([128, 64, 32]), ALU.is_ge))
                cw(e)
                ci(e.scalar_tensor_tensor(dst, selb[:, sl], -SUP, src, ALU.mult, ALU.add))
                cw(e)
            e.engine_nop().then_inc(S["topk_done"].h, 1)
    S["topk_done"].n += 2

    gc = [0]
    pvst = [0]
    cr = [0]
    swp_ = [0]
    rep_ = [0]
    units = [0]

    @on("gpsimd")
    def _(e):
        for h in range(H):
            e.wait_ge(S["kc_rdy"].h, H)
            e.wait_ge(S["vc_rdy"].h, H)
            e.wait_ge(S["esc_rdy"].h, H)
            if h >= 2:
                e.wait_ge(S["den_done"].h, 2 * (h - 1))
            e.dma_start(out=kr2[:, h % 2, :], in_=kdram[h]).then_inc(DS["krb"].handle(16 + h), 16)
            e.dma_start(out=qr2[:, h % 2, :], in_=qdram[h]).then_inc(DS["qrb"].handle(16 + h), 16)
            e.dma_start(out=vr2[:, h % 2],
                        in_=vdram[:, h // 4, :, (h % 4) * 128:(h % 4 + 1) * 128
                                  ].rearrange("c p n -> p c n")).then_inc(DS["vrb"].handle(16 + h), 16)

    for _h in range(H):
        DS["kr"].issued(16 + _h)
        DS["qr"].issued(16 + _h)
        DS["vr"].issued(16 + _h)

    for h in range(H):
        trbase = S["tr_ps"].n
        gc_at_build = gc[0]

        @on("tensor")
        def _(e, h=h, gc0=gc_at_build):
            e.wait_ge(S["topk_done"].h, 1 if h < 8 else 2)
            if gc0 > 0:
                e.wait_ge(S["px"].h, gc0)
            if h > 0:
                e.wait_ge(S["mt_rdy"].h, h)
            for qt in range(8):
                mm = e.transpose(pb0bf[:, qt * 128:(qt + 1) * 128],
                                 impb[:, h * 256 + qt * 32:h * 256 + (qt + 1) * 32], ident[:])
            mm.then_inc(S["tr_ps"].h, 1)
        S["tr_ps"].n += 1

        @on("vector")
        def _(e, h=h):
            e.wait_ge(S["tr_ps"].h, h + 1)
            ci(e.tensor_scalar(mtmp[:], pb0bf[:], SEL_THR, MASKPRE, ALU.is_ge, ALU.mult))
            cw(e)
            e.tensor_tensor(maskT[:, h % 2, :], mtmp[:], cbmin[:], ALU.min
                            ).then_inc(S["mt_rdy"].h, 1)
        S["mt_rdy"].n += 1

        @on("tensor")
        def _(e, h=h):
            e.wait_ge(S["mt_rdy"].h, h + 1)
            DS["qrb"].wait(e, 16 + h)
            if h > 0:
                e.wait_ge(S["scT_f"].h, 2 * h)
            for g in range(2):
                e.matmul(pb[g][0:32, :], kcmpall[:, h, :], qr2[:, h % 2, g * 512:(g + 1) * 512],
                         start=True, stop=True).then_inc(S["scT_ps"].h, 1)
        S["scT_ps"].n += 2

        @on("vector")
        def _(e, h=h):
            for g in range(2):
                e.wait_ge(S["scT_ps"].h, 2 * h + g + 1)
                if 2 * h + g >= 1:
                    e.wait_ge(S["pcT_rdy"].h, 2 * h + g)
                e.tensor_tensor(scmpTf[:], pb[g][0:32, :], cmpmaT[:, g * 512:(g + 1) * 512],
                                ALU.add).then_inc(S["scT_f"].h, 1)
        S["scT_f"].n += 2

        @on("scalar")
        def _(e, h=h):
            for g in range(2):
                e.wait_ge(S["scT_f"].h, 2 * h + g + 1)
                e.activation(pcmpT[:, h % 2, g * 512:(g + 1) * 512], scmpTf[:],
                             AF.Exp, scale=KAPPA).then_inc(S["pcT_rdy"].h, 1)
        S["pcT_rdy"].n += 2

        if "maskT" in tap_t:
            @on("sync")
            def _(e, h=h):
                e.wait_ge(S["mt_rdy"].h, h + 1)
                e.dma_start(out=tap_t["maskT"][:, h, :], in_=maskT[:, h % 2, :]
                            ).then_inc(S["tap_st"].h, 16)
            S["tap_st"].n += 16
        if "pcmpT" in tap_t:
            @on("sync")
            def _(e, h=h):
                e.wait_ge(S["pcT_rdy"].h, 2 * h + 2)
                e.dma_start(out=tap_t["pcmpT"][:, h, :], in_=pcmpT[:, h % 2, :]
                            ).then_inc(S["tap_st"].h, 16)
            S["tap_st"].n += 16

        for g in range(2):
            ECH = 4 * g + 4
            chunks = list(range(ECH)) + list(range(8, 8 + ECH))
            dpar = (2 * h + g) % 2
            unit = 2 * h + g
            gbase = gc[0]
            pvbase = pvst[0]
            denb = pb[5 + dpar]

            @on("tensor")
            def _(e, h=h, g=g, chunks=tuple(chunks), gbase=gbase, unit=unit,
                  dpar=dpar, denb=denb, pvbase=pvbase, crbase=cr[0]):
                DS["krb"].wait(e, 16 + h)
                DS["vrb"].wait(e, 16 + h)
                e.wait_ge(S["mt_rdy"].h, h + 1)
                e.wait_ge(S["scT_f"].h, 2 * (h + 1))
                ci = [0]

                def pvden(j):
                    c2 = chunks[j]
                    if j == 0:
                        if unit >= 1:
                            e.wait_ge(S["cmb"].h, unit)
                        if unit >= 2:
                            e.wait_ge(S["coef_u"].h, unit - 1)
                    e.wait_ge(S["px"].h, gbase + j + 1)
                    e.matmul(pb[2][:], vr2[:, h % 2, c2, :], pring[:, (gbase + j) % 3, :],
                             start=(j == 0), stop=(j == len(chunks) - 1))
                    mm = e.matmul(denb[0:48, :], toep[:, 31 - h:79 - h],
                                  pring[:, (gbase + j) % 3, :],
                                  start=(j == 0), stop=False)
                    if 4 * g <= c2 < 4 * g + 4:
                        jl = c2 - 4 * g
                        e.wait_ge(S["corr_i"].h, crbase + ci[0] + 1)
                        e.matmul(pb[2][:, jl * 128:(jl + 1) * 128], vr2[:, h % 2, c2, :],
                                 corrsc[:, ci[0] % 2, :], start=False, stop=False)
                        mm = e.matmul(denb[0:48, jl * 128:(jl + 1) * 128],
                                      toep[:, 31 - h:79 - h], corrsc[:, ci[0] % 2, :],
                                      start=False, stop=False)
                        ci[0] += 1
                    mm.then_inc(S["pvstep"].h, 1)

                for i, c in enumerate(chunks):
                    if gbase + i >= 2:
                        e.wait_ge(S["px"].h, gbase + i - 1)
                    e.matmul(pb[i % 2][:], kr2[:, h % 2, c * 128:(c + 1) * 128],
                             qr2[:, h % 2, g * 512:(g + 1) * 512], start=True, stop=False)
                    e.matmul(pb[i % 2][:], eexp[:, c * 128:(c + 1) * 128],
                             maskT[:, h % 2, g * 512:(g + 1) * 512], start=False, stop=True
                             ).then_inc(S["sc_ps"].h, 1)
                    if i >= 1:
                        pvden(i - 1)
                pvden(len(chunks) - 1)
            S["sc_ps"].n += len(chunks)
            S["pvstep"].n += len(chunks)

            @on("scalar")
            def _(e, gbase=gbase, nch=len(chunks), pvbase=pvbase):
                for i in range(nch):
                    e.wait_ge(S["sc_ps"].h, gbase + i + 1)
                    if gbase + i >= 3:
                        e.wait_ge(S["pvstep"].h, gbase + i - 2)
                    e.activation(pring[:, (gbase + i) % 3, :], pb[i % 2][:],
                                 AF.Exp, scale=KAPPA).then_inc(S["px"].h, 1)
            S["px"].n += len(chunks)

            corr_chunks = [i for i, c in enumerate(chunks) if 4 * g <= c < 4 * g + 4]

            @on("vector")
            def _(e, g=g, chunks=tuple(chunks), gbase=gbase,
                  corr_chunks=tuple(corr_chunks)):
                for ci, i in enumerate(corr_chunks):
                    c = chunks[i]
                    jl = c - 4 * g
                    e.wait_ge(S["px"].h, gbase + i + 1)
                    if ci >= 2:
                        e.wait_ge(S["pvstep"].h, gbase + corr_chunks[ci - 2] + 1)
                    e.tensor_tensor(corrsc[:, ci % 2, :],
                                    pring[:, (gbase + i) % 3, jl * 128:(jl + 1) * 128],
                                    uptri[:], ALU.mult).then_inc(S["corr_i"].h, 1)
            S["corr_i"].n += 4
            gc[0] += len(chunks)
            pvst[0] += len(chunks)
            cr[0] += 4

            @on("tensor")
            def _(e, h=h, g=g, unit=unit, denb=denb, swb=swp_[0]):
                if unit >= 1:
                    e.wait_ge(S["rep_ev"].h, 3 * unit)
                for jl in range(4):
                    j = 4 * g + jl
                    E = 2 * (j + 1)
                    cands = [j, max(8 + j - 1, 8), 8 + j]
                    u = swb + jl
                    if u >= swb + 1:
                        e.wait_ge(S["pswa_e"].h, u)
                    for p, cp in enumerate(cands):
                        mm = e.matmul(pb[7][:, p * 128:(p + 1) * 128],
                                      kr2[:, h % 2, cp * 128:(cp + 1) * 128],
                                      qr2[:, h % 2, j * 128:(j + 1) * 128],
                                      start=True, stop=True)
                    mm.then_inc(S["swa_ps"].h, 1)
                    e.wait_ge(S["pswa_m"].h, u + 1)
                    for p, cp in enumerate(cands):
                        e.matmul(pb[3][:, jl * 128:(jl + 1) * 128], vr2[:, h % 2, cp, :],
                                 pswa2[:, jl % 2, p * 128:(p + 1) * 128],
                                 start=(jl == 0 and p == 0), stop=(jl == 3 and p == 2))
                        mm = e.matmul(denb[0:48, jl * 128:(jl + 1) * 128],
                                      toep[:, 15 - h:63 - h],
                                      pswa2[:, jl % 2, p * 128:(p + 1) * 128],
                                      start=False, stop=False)
                    mm.then_inc(S["swa_pv"].h, 1)
            S["swa_ps"].n += 4
            S["swa_pv"].n += 4

            @on("scalar")
            def _(e, swb=swp_[0]):
                for jl in range(4):
                    e.wait_ge(S["swa_ps"].h, swb + jl + 1)
                    e.activation(pswa[:, jl % 2, :], pb[7][:, 0:384], AF.Exp, scale=KAPPA
                                 ).then_inc(S["pswa_e"].h, 1)
            S["pswa_e"].n += 4

            @on("vector")
            def _(e, g=g, swb=swp_[0]):
                for jl in range(4):
                    e.wait_ge(S["pswa_e"].h, swb + jl + 1)
                    if jl >= 2:
                        e.wait_ge(S["swa_pv"].h, swb + jl - 1)
                    cls = 0 if (g == 0 and jl == 0) else 1
                    e.tensor_tensor(pswa2[:, jl % 2, :], pswa[:, jl % 2, :],
                                    swaarr[:, cls, :], ALU.mult).then_inc(S["pswa_m"].h, 1)
            S["pswa_m"].n += 4
            swp_[0] += 4

            @on("tensor")
            def _(e, h=h, g=g, denb=denb, unit=unit):
                e.wait_ge(S["pcT_rdy"].h, 2 * h + 2)
                e.matmul(pb[4][:], vcmpall[:, h, :], pcmpT[:, h % 2, g * 512:(g + 1) * 512],
                         start=True, stop=True).then_inc(S["pv_done"].h, 1)
                e.matmul(denb[0:48, :], toep32[:, 47 - h:95 - h],
                         pcmpT[:, h % 2, g * 512:(g + 1) * 512], start=False, stop=True
                         ).then_inc(S["den_done"].h, 1)
            S["pv_done"].n += 1
            S["den_done"].n += 1

            @on("vector")
            def _(e, g=g, denb=denb, unit=unit):
                e.wait_ge(S["den_done"].h, unit + 1)
                ci(e.tensor_tensor(varrep[0:48, :], denb[0:48, :],
                                   anyfix[:, g * 512:(g + 1) * 512], ALU.add))
                cw(e)
                e.tensor_scalar(lnv[0:48, :], varrep[0:48, :], 1e-20, None, ALU.max
                                ).then_inc(S["coef_u"].h, 1)
            S["coef_u"].n += 1

            @on("scalar")
            def _(e, unit=unit):
                e.wait_ge(S["coef_u"].h, unit + 1)
                ci(e.activation(varrep[0:48, :], lnv[0:48, :], AF.Ln), "scalar")
                cw(e, "scalar")
                e.activation(lnv[0:48, :], varrep[0:48, :], AF.Exp, scale=-1.0
                             ).then_inc(S["coef_r"].h, 1)
            S["coef_r"].n += 1

            @on("vector")
            def _(e, h=h, g=g, dpar=dpar, unit=unit):
                e.wait_ge(S["coef_r"].h, unit + 1)
                e.wait_ge(S["g_rdy"].h, 2)
                e.tensor_tensor(coefb[:, dpar, :], lnv[0:48, :],
                                gT[:, g * 512:(g + 1) * 512], ALU.mult)
                e.engine_nop().then_inc(S["coef_rdy"].h, 1)
            S["coef_rdy"].n += 1

            @on("tensor")
            def _(e, h=h, dpar=dpar, unit=unit, rb=rep_[0]):
                e.wait_ge(S["coef_rdy"].h, unit + 1)
                e.wait_ge(S["pswa_e"].h, (unit + 1) * 4)
                if unit == 0:
                    e.wait_ge(S["c3_ld"].h, 16)
                for b in range(3):
                    if rb + b >= 1:
                        e.wait_ge(S["rep_ev"].h, rb + b)
                    e.matmul(pb[7][:], rowsel[:, (16 * b + h) * 128:(16 * b + h + 1) * 128],
                             coefb[0:48, dpar, :], start=True, stop=True
                             ).then_inc(S["rep_ps"].h, 1)
            S["rep_ps"].n += 3

            @on("vector")
            def _(e, rb=rep_[0]):
                for b in range(3):
                    e.wait_ge(S["rep_ps"].h, rb + b + 1)
                    e.tensor_copy(crep[:, b, :], pb[7][:]).then_inc(S["rep_ev"].h, 1)
            S["rep_ev"].n += 3
            rep_[0] += 3

            @on("vector")
            def _(e, h=h, g=g, unit=unit):
                e.wait_ge(S["rep_ev"].h, 3 * (unit + 1))
                e.wait_ge(S["pv_done"].h, unit + 1)
                ci(e.tensor_tensor(csc[:, 0, :], pb[4][:], crep[:, 0, :], ALU.mult))
                ci(e.tensor_tensor(csc[:, 1, :], pb[2][:], crep[:, 1, :], ALU.mult))
                cw(e)
                ci(e.tensor_tensor(csc[:, 2, :], csc[:, 0, :], csc[:, 1, :], ALU.add))
                cw(e)
                ci(e.tensor_tensor(csc[:, 0, :], pb[3][:], crep[:, 2, :], ALU.mult))
                cw(e)
                e.tensor_tensor(oT[:, h, g * 512:(g + 1) * 512], csc[:, 2, :], csc[:, 0, :],
                                ALU.add).then_inc(S["cmb"].h, 1)
            S["cmb"].n += 1
            units[0] += 1

    CMB_ALL = S["cmb"].n

    if "o" in tap_t:
        @on("sync")
        def _(e):
            e.wait_ge(S["cmb"].h, CMB_ALL)
            e.dma_start(out=tap_t["o"], in_=oT[:]).then_inc(S["tap_st"].h, 16)
        S["tap_st"].n += 16

    XOL0 = DS["xol"].uses
    WOL0 = DS["wqk"].uses
    for _i in range(NDC):
        DS["xol"].issued(XOL0 + _i)
        DS["wqk"].issued(WOL0 + _i)
    @on("sync")
    def _(e):
        e.wait_ge(S["cmb"].h, CMB_ALL)
        for dc in range(NDC):
            if dc >= 2:
                e.wait_ge(S["x2_rdy"].h, 2 * (dc - 1))
            e.dma_start(out=xo[:, dc % 2, :],
                        in_=xT.rearrange("(c p) t -> p c t", p=128)[:, dc, 0:OT]
                        ).then_inc(DS["xol"].handle(XOL0 + dc), 16)

    @on("gpsimd")
    def _(e):
        e.wait_ge(S["cmb"].h, CMB_ALL)
        for dc in range(NDC):
            if dc >= 2:
                e.wait_ge(S["wo_ps"].h, 2 * (dc - 1))
            e.dma_start(out=wo_sb[:, dc % 2], in_=wo[dc].rearrange("c p n -> p c n")
                        ).then_inc(DS["wol"].handle(WOL0 + dc), 16)

    @on("tensor")
    def _(e):
        e.wait_ge(S["cmb"].h, CMB_ALL)
        for dc in range(NDC):
            DS["wol"].wait(e, WOL0 + dc)
            if dc >= 2:
                e.wait_ge(S["x2_rdy"].h, 2 * (dc - 1))
            for half in range(2):
                bank = pb[(dc % 2) * 2 + half]
                for ec in range(NDC):
                    mm = e.matmul(bank[:], wo_sb[:, dc % 2, ec, :],
                                  oT[:, ec, half * 512:(half + 1) * 512],
                                  start=(ec == 0), stop=(ec == NDC - 1))
                mm.then_inc(S["wo_ps"].h, 1)
    S["wo_ps"].n += 32

    @on("vector")
    def _(e):
        for dc in range(NDC):
            DS["xol"].wait(e, XOL0 + dc)
            for half in range(2):
                e.wait_ge(S["wo_ps"].h, 2 * dc + half + 1)
                e.scalar_tensor_tensor(x2[:, dc, half * 512:(half + 1) * 512],
                                       pb[(dc % 2) * 2 + half][:],
                                       modv[:, 32 + dc:33 + dc],
                                       xo[:, dc % 2, half * 512:(half + 1) * 512],
                                       ALU.mult, ALU.add).then_inc(S["x2_rdy"].h, 1)
            e.wait_ge(S["x2_rdy"].h, 2 * dc + 2)
            e.tensor_copy(x2b(dc), x2[:, dc, :]).then_inc(S["x2b_rdy"].h, 1)
    S["x2_rdy"].n += 32
    S["x2b_rdy"].n += NDC

    if "x2" in tap_t:
        @on("sync")
        def _(e):
            e.wait_ge(S["x2_rdy"].h, 32)
            e.dma_start(out=tap_t["x2"], in_=x2[:]).then_inc(S["tap_st"].h, 16)
        S["tap_st"].n += 16

    @on("scalar")
    def _(e):
        for w in range(2):
            for dc in range(NDC):
                i = w * NDC + dc
                e.wait_ge(S["x2b_rdy"].h, dc + 1)
                if i >= 2:
                    e.wait_ge(S["sq2_free"].h, i - 1)
                e.activation(sqring[:, i % 2, :], x2b(dc)[:, w * 512:(w + 1) * 512],
                             AF.Square).then_inc(S["sq2_rdy"].h, 1)
    S["sq2_rdy"].n += 32

    @on("tensor")
    def _(e):
        e.wait_ge(S["x2_rdy"].h, 32)
        for w in range(2):
            if w >= 1:
                e.wait_ge(S["ln2_mid"].h, 2 * w - 1)
            for dc in range(NDC):
                i = w * NDC + dc
                e.wait_ge(S["x2b_rdy"].h, dc + 1)
                mm0 = e.matmul(pb[0][:], ones[:], x2b(dc)[:, w * 512:(w + 1) * 512],
                               start=(dc == 0), stop=(dc == NDC - 1))
                if dc == NDC - 1:
                    mm0.then_inc(S["st2_ps"].h, 1)
                e.wait_ge(S["sq2_rdy"].h, i + 1)
                e.matmul(pb[1][:], ones[:], sqring[:, i % 2, :],
                         start=(dc == 0), stop=(dc == NDC - 1)).then_inc(S["sq2_free"].h, 1)
    S["st2_ps"].n += 2
    S["sq2_free"].n += 32

    @on("scalar")
    def _(e):
        for w in range(2):
            e.wait_ge(S["ln2_mid"].h, 2 * w + 1)
            ci(e.activation(lnv[:], varrep[:], AF.Ln, bias=epsv[:]), "scalar")
            cw(e, "scalar")
            e.activation(rstdrep[:], lnv[:], AF.Exp, scale=-0.5).then_inc(S["ln2_mid"].h, 1)

    @on("vector")
    def _(e):
        for w in range(2):
            e.wait_ge(S["st2_ps"].h, w + 1)
            e.wait_ge(S["sq2_free"].h, 16 * (w + 1))
            ci(e.tensor_scalar(mrep[:], pb[0][:], 1.0 / D, None, ALU.mult))
            cw(e)
            ci(e.tensor_tensor(lnv[:], mrep[:], mrep[:], ALU.mult))
            cw(e)
            e.scalar_tensor_tensor(varrep[:], pb[1][:], 1.0 / D, lnv[:],
                                   ALU.mult, ALU.subtract).then_inc(S["ln2_mid"].h, 1)
            e.wait_ge(S["ln2_mid"].h, 2 * w + 2)
            for dc in range(NDC):
                ci(e.tensor_tensor(ropeA[:], x2[:, dc, w * 512:(w + 1) * 512], mrep[:],
                                   ALU.subtract))
                cw(e)
                ci(e.tensor_tensor(scrB[:], ropeA[:], rstdrep[:], ALU.mult))
                cw(e)
                e.tensor_scalar(h2sb[:, dc, w * 512:(w + 1) * 512], scrB[:],
                                onepc[:, 16 + dc:17 + dc], modv[:, 48 + dc:49 + dc],
                                ALU.mult, ALU.add).then_inc(S["h2_rdy"].h, 1)
    S["ln2_mid"].n = 4
    S["h2_rdy"].n += 32
    H2_ALL = S["h2_rdy"].n

    if "h2" in tap_t:
        @on("sync")
        def _(e):
            e.wait_ge(S["h2_rdy"].h, H2_ALL)
            e.dma_start(out=tap_t["h2"], in_=h2sb[:]).then_inc(S["tap_st"].h, 16)
        S["tap_st"].n += 16

    @on("sync")
    def _(e):
        e.wait_ge(S["h2_rdy"].h, H2_ALL)
        e.dma_start(out=outT.rearrange("(c p) t -> p c t", p=128), in_=x2[:]
                    ).then_inc(S["x2_sp"].h, 16)
    S["x2_sp"].n += 16

    WUL0 = DS["wul"].uses
    WLL0 = DS["wqk"].uses
    ACC0 = DS["acc"].uses
    assert (ACC0 % 2) == 0
    for _hc in range(64):
        DS["wul"].issued(WUL0 + _hc, 32)
    for _u in range(64):
        DS["wqk"].issued(WLL0 + _u)
        DS["acc"].issued(ACC0 + _u)
    @on("gpsimd")
    def _(e):
        e.wait_ge(S["x2_sp"].h, 16)

        def acc_dma(ua):
            e.wait_ge(S["stg_rdy"].h, ua + 1)
            DS["acc"].wait(e, ACC0 + ua - 16)
            e.dma_start(out=outT.rearrange("(c p) t -> p c t", p=128)[:, ua % NDC, :],
                        in_=stage[:, ua % 2, :], accum_op=ALU.add
                        ).then_inc(DS["acc"].handle(ACC0 + ua), 16)

        def wll_load(u2):
            if u2 >= 2:
                e.wait_ge(S["dn_ps"].h, u2 - 1)
            e.dma_start(out=wl_sb[:, u2 % 2],
                        in_=wl[u2 % NDC, (u2 // NDC) * 16:(u2 // NDC) * 16 + 16
                               ].rearrange("c p n -> p c n")
                        ).then_inc(DS["wll"].handle(WLL0 + u2), 16)

        for grp in range(4):
            for hl in range(16):
                hc = grp * 16 + hl
                if hc >= 2:
                    e.wait_ge(S["up_ps"].h, 2 * (hc - 1))
                e.dma_start(out=wu_sb[:, hc % 2, 0], in_=wu0[hc].rearrange("c p n -> p c n")
                            ).then_inc(DS["wul"].handle(WUL0 + hc), 16)
                e.dma_start(out=wu_sb[:, hc % 2, 1], in_=wu1[hc].rearrange("c p n -> p c n")
                            ).then_inc(DS["wul"].handle(WUL0 + hc), 16)
                if grp >= 1:
                    u2 = (grp - 1) * 16 + hl
                    wll_load(u2)
                    if u2 >= 2:
                        acc_dma(u2 - 2)

        for hl in range(16):
            wll_load(48 + hl)
            acc_dma(46 + hl)
        acc_dma(62)
        acc_dma(63)

    @on("tensor")
    def _(e):
        e.wait_ge(S["h2_rdy"].h, H2_ALL)

        def dn_dc(gp, dc):
            u = gp * NDC + dc
            abuf = act if gp % 2 == 0 else actB
            DS["wll"].wait(e, WLL0 + u)
            if dc == 0:
                e.wait_ge(S["act_rdy"].h, 32 * (gp + 1))
            if u >= 2:
                e.wait_ge(S["stg_rdy"].h, u - 1)
            for half in range(2):
                bank = pb[4 + (dc % 2) * 2 + half]
                for hl2 in range(16):
                    mm = e.matmul(bank[:], wl_sb[:, u % 2, hl2, :],
                                  abuf[:, hl2, half * 512:(half + 1) * 512],
                                  start=(hl2 == 0), stop=(hl2 == 15))
            mm.then_inc(S["dn_ps"].h, 1)

        for grp in range(4):
            for hl in range(16):
                hc = grp * 16 + hl
                DS["wul"].wait(e, WUL0 + hc)
                for half in range(2):
                    hh = 2 * hc + half
                    if hh >= 2:
                        e.wait_ge(S["act_rdy"].h, hh - 1)
                    for proj in range(2):
                        bank = pb[2 * (hh % 2) + proj]
                        for dc in range(NDC):
                            mm = e.matmul(bank[:], wu_sb[:, hc % 2, proj, dc, :],
                                          h2sb[:, dc, half * 512:(half + 1) * 512],
                                          start=(dc == 0), stop=(dc == NDC - 1))
                    mm.then_inc(S["up_ps"].h, 1)
                if grp >= 1:
                    dn_dc(grp - 1, hl)
        for dc in range(NDC):
            dn_dc(3, dc)
    S["up_ps"].n += 128
    S["dn_ps"].n += 64

    @on("scalar")
    def _(e):
        for hh in range(128):
            e.wait_ge(S["up_ps"].h, hh + 1)
            if hh >= 2:
                e.wait_ge(S["act_rdy"].h, hh - 1)
            e.activation(gel_sb[:, hh % 2, :], pb[2 * (hh % 2)][:],
                         AF.Gelu_apprx_tanh).then_inc(S["gel_rdy"].h, 1)
    S["gel_rdy"].n += 128

    @on("vector")
    def _(e):
        def stage_u(u):
            e.wait_ge(S["dn_ps"].h, u + 1)
            DS["acc"].wait(e, ACC0 + u - 2)
            for half in range(2):
                e.tensor_scalar(stage[:, u % 2, half * 512:(half + 1) * 512],
                                pb[4 + (u % 2) * 2 + half][:],
                                modv[:, 80 + u % NDC:81 + u % NDC], None, ALU.mult)
            e.engine_nop().then_inc(S["stg_rdy"].h, 1)

        for grp in range(4):
            abuf = act if grp % 2 == 0 else actB
            for hl in range(16):
                hc = grp * 16 + hl
                for half in range(2):
                    hh = 2 * hc + half
                    if hl == 0 and half == 0 and grp >= 2:
                        e.wait_ge(S["dn_ps"].h, 16 * (grp - 1))
                    e.wait_ge(S["gel_rdy"].h, hh + 1)
                    e.tensor_tensor(abuf[:, hl, half * 512:(half + 1) * 512],
                                    gel_sb[:, hh % 2, :], pb[2 * (hh % 2) + 1][:],
                                    ALU.mult).then_inc(S["act_rdy"].h, 1)
                if grp >= 1:
                    stage_u((grp - 1) * NDC + hl)
        for u in range(48, 64):
            stage_u(u)
    S["act_rdy"].n += 128
    S["stg_rdy"].n += 64

    if "act" in tap_t:
        @on("sync")
        def _(e):
            e.wait_ge(S["act_rdy"].h, 32)
            e.dma_start(out=tap_t["act"], in_=act[:]).then_inc(S["tap_st"].h, 16)
        S["tap_st"].n += 16

    @on("sync")
    def _(e):
        DS["acc"].wait_all(e)
        if S["tap_st"].n:
            e.wait_ge(S["tap_st"].h, S["tap_st"].n)

    with nc.Block() as blk:
        @blk.sync
        def _(eng):
            for fn in OPS["sync"]:
                fn(eng)

        @blk.gpsimd
        def _(eng):
            for fn in OPS["gpsimd"]:
                fn(eng)

        @blk.tensor
        def _(eng):
            for fn in OPS["tensor"]:
                fn(eng)

        @blk.vector
        def _(eng):
            for fn in OPS["vector"]:
                fn(eng)

        @blk.scalar
        def _(eng):
            for fn in OPS["scalar"]:
                fn(eng)
    return nc


def kernel(**inputs):
    maps = _stage(inputs)
    nc = build_nc(DEBUG_TAPS)
    res = run_bass_kernel_spmd(nc, maps, core_ids=list(range(8)))
    out = np.zeros((4, NT, D), np.float32)
    for core in range(8):
        b, s = core // 2, core % 2
        own = _perm_ids(s)[0:OT]
        oTv = np.asarray(res.results[core]["outT"])
        out[b, own] = oTv.T
    return out


# version 07
# speedup vs baseline: 1.2042x; 1.0851x over previous
import numpy as np
import ml_dtypes
import concourse.bass as bass
import concourse.mybir as mybir
from concourse.bass_utils import run_bass_kernel_spmd

F32 = mybir.dt.float32
BF16 = mybir.dt.bfloat16
ALU = mybir.AluOpType
AF = mybir.ActivationFunctionType
AX = mybir.AxisListType
BF = ml_dtypes.bfloat16

H, HD, D, NT, OT = 16, 128, 2048, 2048, 1024
NDC, NTC = 16, 16
KAPPA = float(HD) ** -0.5
MASKPRE = -2048.0
NEG_BIG = -3.0e6
SUP = 1.0e6
SEL_THR = -1.0e5
EPS = 1e-5

DEBUG_TAPS = ()


def _perm_ids(s):
    ids = []
    for j in range(8):
        ids.extend(range((2 * j + s) * 128, (2 * j + s) * 128 + 128))
    for j in range(8):
        ids.extend(range((2 * j + 1 - s) * 128, (2 * j + 1 - s) * 128 + 128))
    return np.array(ids)


def _consts_for_parity(s):
    perm = _perm_ids(s)
    own = perm[0:OT]
    c = {}
    inv = 1.0 / 10000.0 ** (np.arange(64) / 64.0)
    th = perm[:, None].astype(np.float64) * inv[None, :]
    ck, sk = np.cos(th).T, np.sin(th).T
    c["cosk"] = np.tile(ck, (2, 1)).astype(BF)
    c["sink"] = np.concatenate([-sk, sk], 0).astype(BF)
    c["ident"] = np.eye(128, dtype=BF)
    c["ones_lhs"] = np.ones((128, 128), dtype=BF)
    c["eexp"] = (np.arange(32)[:, None] == (np.arange(NT)[None, :] // 64)).astype(BF)
    am = np.zeros((128, NTC, 32), dtype=np.float32)
    for cc in range(NTC):
        for t in range(128):
            am[t, cc, 2 * cc + t // 64] = 1.0 / 64.0
    c["ameans"] = am.reshape(128, NTC * 32).astype(BF)
    tp = np.zeros((128, 95), dtype=BF)
    tp[:, 47] = 1.0
    c["toep"] = tp
    tp32 = np.zeros((32, 95), dtype=BF)
    tp32[:, 47] = 1.0
    c["toep32"] = tp32
    rs = np.zeros((48, 48, 128), dtype=BF)
    for r in range(48):
        rs[r, r, :] = 1.0
    c["rowsel16"] = rs.reshape(48, 48 * 128)
    blk_seq = perm.reshape(32, 64)[:, 0] // 64
    blk_end = (blk_seq + 1) * 64 - 1
    blk_start = blk_seq * 64
    cm = np.where(own[:, None] >= blk_end[None, :], 0.0, MASKPRE).astype(np.float32)
    c["cmp_maskadd"] = np.ascontiguousarray(
        cm.reshape(8, 128, 32).transpose(1, 0, 2).reshape(128, 256)).astype(BF)
    c["cmp_maskaddT"] = np.ascontiguousarray(cm.T).astype(BF)
    cs_ = np.zeros((OT, 32), np.float32)
    cs_[np.arange(OT), np.arange(OT) // 64] = NEG_BIG
    c["cur_sup"] = np.ascontiguousarray(
        cs_.reshape(8, 128, 32).transpose(1, 0, 2).reshape(128, 256)).astype(BF)
    c["cb_min"] = np.where(blk_start[:, None] <= own[None, :], 0.0, MASKPRE).astype(BF)
    tq = np.arange(128)[None, :]
    tk = np.arange(128)[:, None]
    B1 = ((tq - tk >= 0) & (tq - tk <= 64)).astype(np.float32)
    B2 = (tk - tq >= 64).astype(np.float32)
    Z = np.zeros((128, 128), np.float32)
    gen = [B1, Z, B2] if s == 1 else [B1, B2, Z]
    sl0 = [B1, Z, B2] if s == 1 else [B1, Z, Z]
    c["swa_arr"] = np.stack([np.concatenate(sl0, 1), np.concatenate(gen, 1)],
                            1).reshape(128, 2 * 384).astype(BF)
    c["uptri"] = (-(tk > tq).astype(np.float32)).astype(BF)
    any01 = (own >= 63).astype(np.float32)
    af = np.zeros((48, OT), np.float32)
    af[0:16] = (1.0 - any01)[None, :]
    c["anyfix"] = af.astype(BF)
    return c


def _tile2(w, bk0, bk1, order="km"):
    K_, M_ = w.shape
    t = w.reshape(K_ // bk0, bk0, M_ // bk1, bk1).transpose(0, 2, 1, 3)
    if order == "mk":
        t = t.transpose(1, 0, 2, 3)
    return np.ascontiguousarray(t)


def _stage(ins):
    x = np.asarray(ins["L0_attn_x"], dtype=np.float32)
    shared = {
        "wq": _tile2(np.asarray(ins["L0_attn_Wq"], np.float32), 128, 128, "mk"),
        "wk": _tile2(np.asarray(ins["L0_attn_Wk"], np.float32), 128, 128, "mk"),
        "wv": _tile2(np.asarray(ins["L0_attn_Wv"], np.float32), 128, 512, "mk"),
        "wo": _tile2(np.asarray(ins["L0_attn_Wo"], np.float32), 128, 128, "mk"),
        "wg": np.ascontiguousarray(np.asarray(ins["L0_attn_Wg"], np.float32).reshape(NDC, 128, 48)),
        "wu0": _tile2(np.asarray(ins["L0_ffn_w_gating"], np.float32)[0], 128, 128, "mk"),
        "wu1": _tile2(np.asarray(ins["L0_ffn_w_gating"], np.float32)[1], 128, 128, "mk"),
        "wl": _tile2(np.asarray(ins["L0_ffn_w_linear"], np.float32), 128, 128, "mk"),
        "wmod": _tile2(np.asarray(ins["L0_wmod"], np.float32), 128, 512, "mk"),
    }
    pc = [_consts_for_parity(s) for s in (0, 1)]
    maps = []
    for core in range(8):
        b, s = core // 2, core % 2
        m = dict(shared)
        m.update(pc[s])
        m["xT"] = np.ascontiguousarray(x[b].T[:, _perm_ids(s)])
        m["cT"] = np.ascontiguousarray(np.asarray(ins["L0_c"], np.float32)[b:b + 1].T)
        maps.append(m)
    return maps


class Sem:
    __slots__ = ("h", "n")

    def __init__(self, nc, name):
        self.h = nc.alloc_semaphore(name)
        self.n = 0


class DSem:

    def __init__(self, nc, name, depth=2):
        self.s = [Sem(nc, f"{name}_{k}") for k in range(depth)]
        self.depth = depth
        self.tot = {}
        self.uses = 0

    def handle(self, i):
        return self.s[i % self.depth].h

    def issued(self, i, k=16):
        sub = self.s[i % self.depth]
        sub.n += k
        self.tot[i] = sub.n
        self.uses = max(self.uses, i + 1)

    def wait(self, e, i):
        if i < 0:
            return
        assert i in self.tot, (i, sorted(self.tot)[:4])
        e.wait_ge(self.s[i % self.depth].h, self.tot[i])

    def wait_all(self, e):
        for sub in self.s:
            if sub.n:
                e.wait_ge(sub.h, sub.n)


def build_nc(taps=()):
    nc = bass.Bass("TRN2", target_bir_lowering=False)
    OPS = {e: [] for e in ("sync", "gpsimd", "tensor", "vector", "scalar")}

    def on(e):
        def dec(fn):
            OPS[e].append(fn)
            return fn
        return dec

    def din(name, shape, dt=F32):
        return nc.dram_tensor(name, list(shape), dt, kind="ExternalInput").ap()

    xT = din("xT", (D, NT))
    wq = din("wq", (NDC, NDC, 128, 128))
    wk = din("wk", (NDC, NDC, 128, 128))
    wv = din("wv", (4, NDC, 128, 512))
    wo = din("wo", (NDC, NDC, 128, 128))
    wg = din("wg", (NDC, 128, 48))
    wu0 = din("wu0", (64, NDC, 128, 128))
    wu1 = din("wu1", (64, NDC, 128, 128))
    wl = din("wl", (NDC, 64, 128, 128))
    wmod = din("wmod", (24, 16, 128, 512))
    cT = din("cT", (D, 1))
    cosk_c = din("cosk", (128, NT), BF16)
    sink_c = din("sink", (128, NT), BF16)
    ident_c = din("ident", (128, 128), BF16)
    ones_c = din("ones_lhs", (128, 128), BF16)
    eexp_c = din("eexp", (32, NT), BF16)
    ameans_c = din("ameans", (128, NTC * 32), BF16)
    toep_c = din("toep", (128, 95), BF16)
    toep32_c = din("toep32", (32, 95), BF16)
    rowsel_c = din("rowsel16", (48, 48 * 128), BF16)
    cmpma_c = din("cmp_maskadd", (128, 256), BF16)
    cmpmaT_c = din("cmp_maskaddT", (32, OT), BF16)
    cursup_c = din("cur_sup", (128, 256), BF16)
    cbmin_c = din("cb_min", (32, OT), BF16)
    swaarr_c = din("swa_arr", (128, 2 * 384), BF16)
    uptri_c = din("uptri", (128, 128), BF16)
    anyfix_c = din("anyfix", (48, OT), BF16)

    outT = nc.dram_tensor("outT", [D, OT], F32, kind="ExternalOutput").ap()
    kdram = nc.dram_tensor("kdram", [H, 128, NT], BF16,
                           kind="ExternalOutput" if "k" in taps else "Internal").ap()
    qdram = nc.dram_tensor("qdram", [H, 128, OT], BF16,
                           kind="ExternalOutput" if "q" in taps else "Internal").ap()
    vdram = nc.dram_tensor("vdram", [NTC, 4, 128, 512], BF16,
                           kind="ExternalOutput" if "v" in taps else "Internal").ap()
    tap_t = {}
    tap_shapes = {
        "mod": ([1, 6 * D], F32), "h": ([128, NDC, NT], BF16),
        "g": ([48, OT], F32), "kcmp": ([128, H, 32], BF16),
        "vcmp": ([32, H, 128], BF16), "imp": ([128, H * 256], BF16),
        "maskT": ([32, H, OT], BF16), "pcmpT": ([32, H, OT], BF16),
        "o": ([128, H, OT], BF16), "x2": ([128, NDC, OT], F32),
        "h2": ([128, NDC, OT], BF16), "act": ([128, 16, OT], BF16),
        "coef": ([16, H, 2, 3, 512], BF16),
    }
    for t in taps:
        if t in ("k", "q", "v"):
            continue
        sh, dt = tap_shapes[t]
        tap_t[t] = nc.dram_tensor("tap_" + t, sh, dt, kind="ExternalOutput").ap()

    names = [
        "c_ld", "c2_ld", "wg_ld", "silu_rdy", "mod_ps", "mod_evac",
        "modt2", "mod_rdy",
        "sq_rdy", "sq_free", "stats_ps", "ln_mid", "h_rdy",
        "qk_ps", "kraw_rdy", "rope_out",
        "g_ps", "g_rdy", "v_ps", "v_evac",
        "kc_rdy", "vc_ps", "vc_rdy",
        "scmp_ps", "cm_rdy", "esc_rdy", "imp_init", "topk_done",
        "krfree_a", "qrfree_a", "vrfree_a",
        "krfree_b", "qrfree_b", "vrfree_b",
        "tr_ps", "mt_rdy", "scT_ps", "scT_f", "pcT_rdy",
        "sc_ps", "px", "pvstep", "corr_i", "swa_ps", "swa_pv", "pswa_e", "pswa_m",
        "pv_done", "den_done", "coef_u", "coef_r", "coef_rdy",
        "rep_ps", "rep_ev", "cmb",
        "wo_ps", "x2_rdy",
        "x2b_rdy", "sq2_rdy", "sq2_free", "st2_ps", "ln2_mid", "h2_rdy", "x2_sp",
        "up_ps", "gel_rdy", "act_rdy", "dn_ps", "stg_rdy",
        "tap_st", "c3_ld",
    ]
    S = {n: Sem(nc, n) for n in names}
    DS = {n: DSem(nc, n) for n in (
        "wva", "xw", "wqk", "kacc", "kqw", "vst", "modch", "kr", "qr", "vr")}
    DS["wmd"] = DSem(nc, "wmd", depth=4)
    DSAL = {"wmod": "wmd", "wvb": "wva", "wul": "wva",
            "xol": "vst",
            "wol": "wqk", "wll": "wqk", "acc": "xw",
            "kswp": "kacc", "kqst": "kqw",
            "kra": "kr", "krb": "kr", "qra": "qr", "qrb": "qr",
            "vra": "vr", "vrb": "vr"}
    for _a, _b in DSAL.items():
        DS[_a] = DS[_b]

    CHN = {"vector": [Sem(nc, "chv"), 0], "scalar": [Sem(nc, "chs"), 0]}

    def ci(instr, eng="vector"):
        CHN[eng][1] += 1
        return instr.then_inc(CHN[eng][0].h, 1)

    def cw(e, eng="vector"):
        e.wait_ge(CHN[eng][0].h, CHN[eng][1])

    sbt = nc.alloc_sbuf_tensor
    ident = sbt("ident_sb", [128, 128], BF16)
    ones = sbt("ones_sb", [128, 128], BF16)
    uptri = sbt("uptri_sb", [128, 128], BF16)
    modv = sbt("modv", [128, 96], F32)
    epsv = sbt("epsv", [128, 1], F32)
    clsc = sbt("clsc", [128, 48], F32)
    mrep = sbt("mrep", [128, 512], BF16)
    varrep = sbt("varrep", [128, 512], F32)
    rstdrep = sbt("rstdrep", [128, 512], BF16)
    sqring = sbt("sqring", [128, 2, 512], BF16)
    esc = sbt("esc", [128, 256], BF16)
    cmadd = sbt("cmadd", [128, 256], F32)
    mxb = sbt("mxb", [128, 128], BF16)
    ropeA = sbt("ropeA", [128, 512], BF16)
    scrB = sbt("scrB", [128, 512], BF16)
    cT_sb = sbt("cT_sb", [128, 16, 1], F32)
    siluT = sbt("siluT", [128, 16, 1], BF16)
    lnv = sbt("lnv", [128, 512], F32)
    onepc = sbt("onepc", [128, 32], F32)
    wg_sb = sbt("wg_sb", [128, NDC, 48], BF16)
    gT = sbt("gT", [48, OT], F32)
    kcmpall = sbt("kcmpall", [128, H, 32], BF16)
    kcf = sbt("kcf", [128, 32], F32)
    modrow_dr = nc.dram_tensor("modrow_dr", [1, 6 * D], F32, kind="Internal").ap()

    AMAX = 186 * 1024
    AR = sbt("ARENA", [128, AMAX], mybir.dt.uint8)
    spans = []

    def V_(name, off_kb, shape, dt, ph):
        off = int(off_kb * 1024)
        per = int(np.prod(shape[1:])) * mybir.dt.size(dt)
        assert off % 256 == 0 and off + per <= AMAX, (name, off, per, AMAX)
        for (n2, o2, p2, ph2) in spans:
            if not (off >= o2 + p2 or o2 >= off + per or ph[1] < ph2[0] or ph2[1] < ph[0]):
                raise AssertionError(f"overlap {name} vs {n2}")
        spans.append((name, off, per, ph))
        ap = AR[0:shape[0], off:off + per].bitcast(dt)
        if len(shape) > 2:
            pat = "p (" + " ".join(f"d{i}" for i in range(1, len(shape))) + ") -> p " + \
                  " ".join(f"d{i}" for i in range(1, len(shape)))
            kw = {f"d{i}": shape[i] for i in range(1, len(shape))}
            ap = ap.rearrange(pat, **kw)
        return ap

    xwin = V_("xwin", 0, [128, 2, NDC, 512], BF16, (0, 0))
    modrow = V_("modrow", 44, [1, 2, 512], F32, (0, 0))
    wmod_sb = V_("wmod", 72, [128, 4, 2, 512], BF16, (0, 1))
    cosk_sb = V_("cosk", 36, [128, NT], BF16, (0, 1))
    sink_sb = V_("sink", 40, [128, NT], BF16, (0, 1))

    wqk_sb = V_("wqk", 0, [128, 2, NDC, 128], BF16, (1, 1))
    wvb_sb = V_("wvb", 8, [128, 2, 512], BF16, (1, 1))
    kraw = V_("kraw", 10, [128, 2, 512], BF16, (1, 1))
    kswp = V_("kswp", 12, [128, 2, 512], BF16, (1, 1))
    kqst = V_("kqst", 14, [128, 2, 2048], BF16, (1, 1))
    vstg = V_("vstg", 22, [128, 2, 512], BF16, (1, 1))

    kr2 = V_("kr2", 0, [128, 2, NT], BF16, (2, 3))
    vr2 = V_("vr2", 8, [128, 2, NTC, 128], BF16, (2, 3))
    qr2 = V_("qr2", 16, [128, 2, OT], BF16, (2, 3))
    vcmpall = V_("vcmpall", 20, [32, H, 128], BF16, (2, 3))
    impa = V_("impa", 24, [128, H * 256], BF16, (2, 3))
    impb = V_("impb", 32, [128, H * 256], BF16, (2, 3))
    selb = V_("selb", 40, [128, H * 256], BF16, (2, 2))
    eexp = V_("eexp", 48, [32, NT], BF16, (2, 3))
    ameans = V_("ameans", 52, [128, NTC * 32], BF16, (2, 2))
    toep = V_("toep", 53, [128, 95], BF16, (2, 3))
    toep32 = V_("toep32", 53.25, [32, 95], BF16, (2, 3))
    rowsel = V_("rowsel", 130, [48, 48 * 128], BF16, (2, 3))
    cmpma = V_("cmpma", 57.5, [128, 256], BF16, (2, 2))
    cmpmaT = V_("cmpmaT", 58, [32, OT], BF16, (2, 3))
    cursup = V_("cursup", 60, [128, 256], BF16, (2, 2))
    cbmin = V_("cbmin", 60.5, [32, OT], BF16, (2, 3))
    swaarr = V_("swaarr", 62.5, [128, 2, 384], BF16, (2, 3))
    anyfix = V_("anyfix", 64, [48, OT], BF16, (2, 3))
    pring = V_("pring", 66, [128, 3, 512], BF16, (3, 3))
    pswa = V_("pswa", 69, [128, 2, 384], BF16, (3, 3))
    pswa2 = V_("pswa2", 70.5, [128, 2, 384], BF16, (3, 3))
    pcmpT = V_("pcmpT", 72, [32, 2, OT], BF16, (3, 3))
    maskT = V_("maskT", 76, [32, 2, OT], BF16, (3, 3))
    mtmp = V_("mtmp", 80, [32, OT], BF16, (3, 3))
    scmpTf = V_("scmpTf", 82, [32, 512], F32, (3, 3))
    coefb = V_("coefb", 84, [48, 2, 512], BF16, (3, 3))
    crep = V_("crep", 90, [128, 3, 512], BF16, (3, 3))
    csc = V_("csc", 93, [128, 3, 512], BF16, (3, 3))
    corrsc = V_("corrsc", 96, [128, 2, 128], BF16, (3, 3))

    oT = V_("oT", 97, [128, H, OT], BF16, (3, 4))
    hT = V_("hT", 122, [128, NDC, NT], BF16, (0, 1))
    xo = V_("xo", 0, [128, 2, OT], F32, (4, 4))
    wo_sb = V_("wo_sb", 8, [128, 2, NDC, 128], BF16, (4, 4))
    x2 = V_("x2", 16, [128, NDC, OT], F32, (4, 5))
    x2b_lo = V_("x2b_lo", 80, [128, 8, OT], BF16, (4, 5))
    x2b_hi = V_("x2b_hi", 144, [128, 8, OT], BF16, (4, 5))

    def x2b(dc):
        return (x2b_lo if dc < 8 else x2b_hi)[:, dc % 8, :]
    h2sb = V_("h2sb", 112, [128, NDC, OT], BF16, (5, 6))
    act = V_("act", 144, [128, 16, OT], BF16, (6, 6))
    actB = V_("actB", 80, [128, 16, OT], BF16, (6, 6))
    wu_sb = V_("wu_sb", 16, [128, 2, 2, NDC, 128], BF16, (6, 6))
    wl_sb = V_("wl_sb", 32, [128, 2, 16, 128], BF16, (6, 6))
    stage = V_("stage", 40, [128, 2, OT], F32, (6, 6))
    gel_sb = V_("gel", 176, [128, 2, 512], BF16, (6, 6))

    pb = [nc.alloc_psum_tensor(f"pb{i}", [128, 512], F32) for i in range(8)]
    pb0bf = pb[0][0:32, :].bitcast(BF16)

    const_pairs = [
        (ident[:], ident_c), (ones[:], ones_c), (uptri[:], uptri_c),
        (cosk_sb, cosk_c), (sink_sb, sink_c),
    ]
    const_pairs2 = [
        (eexp, eexp_c), (ameans, ameans_c), (toep, toep_c), (toep32, toep32_c),
        (cmpma, cmpma_c), (cmpmaT, cmpmaT_c),
        (cursup, cursup_c), (cbmin, cbmin_c),
        (swaarr.rearrange("p a n -> p (a n)"), swaarr_c), (anyfix, anyfix_c),
    ]

    @on("sync")
    def _(e):
        for dst, src in const_pairs:
            e.dma_start(out=dst, in_=src).then_inc(S["c_ld"].h, 16)
        e.dma_start(out=cT_sb[:], in_=cT.rearrange("(c p) b -> p c b", p=128),
                    allow_slow_non_contiguous=True).then_inc(S["c_ld"].h, 16)
    S["c_ld"].n += 16 * (len(const_pairs) + 1)
    C1 = S["c_ld"].n

    def wmod_load(e, jc0, jc1):
        for jc in range(jc0, jc1):
            for part in range(8):
                i = jc * 8 + part
                if i >= 4:
                    e.wait_ge(S["mod_ps"].h, i - 3)
                e.dma_start(out=wmod_sb[:, i % 4],
                            in_=wmod[jc, part * 2:part * 2 + 2].rearrange("c p n -> p c n")
                            ).then_inc(DS["wmod"].handle(i), 16)

    @on("gpsimd")
    def _(e):
        e.dma_start(out=wg_sb[:], in_=wg.rearrange("c p n -> p c n")).then_inc(S["wg_ld"].h, 16)
        wmod_load(e, 0, 8)
    S["wg_ld"].n += 16
    for _i in range(192):
        DS["wmd"].issued(_i)

    @on("scalar")
    def _(e):
        e.wait_ge(S["c_ld"].h, C1)
        e.activation(siluT[:], cT_sb[:], AF.Silu).then_inc(S["silu_rdy"].h, 1)
    S["silu_rdy"].n += 1

    def gemv(e, jc0, jc1):
        for jc in range(jc0, jc1):
            if jc >= 1:
                e.wait_ge(S["mod_evac"].h, jc)
            for part in range(8):
                i = jc * 8 + part
                DS["wmod"].wait(e, i)
                e.matmul(pb[6][0:1, :], siluT[:, part * 2, :],
                         wmod_sb[:, i % 4, 0, :],
                         start=(part == 0), stop=False)
                e.matmul(pb[6][0:1, :], siluT[:, part * 2 + 1, :],
                         wmod_sb[:, i % 4, 1, :],
                         start=False, stop=(part == 7)).then_inc(S["mod_ps"].h, 1)

    @on("tensor")
    def _(e):
        e.wait_ge(S["silu_rdy"].h, 1)
        gemv(e, 0, 8)
    S["mod_ps"].n += 192

    def mod_evac(e, jc0, jc1):
        for jc in range(jc0, jc1):
            e.wait_ge(S["mod_ps"].h, (jc + 1) * 8)
            DS["modch"].wait(e, jc - 2)
            e.tensor_copy(modrow[0:1, jc % 2, :], pb[6][0:1, :]
                          ).then_inc(S["mod_evac"].h, 1)

    @on("vector")
    def _(e):
        mod_evac(e, 0, 8)
    S["mod_evac"].n += 24

    def mod_store(e, jc0, jc1):
        for jc in range(jc0, jc1):
            e.wait_ge(S["mod_evac"].h, jc + 1)
            e.dma_start(out=modrow_dr[0:1, jc * 512:(jc + 1) * 512],
                        in_=modrow[0:1, jc % 2, :]).then_inc(DS["modch"].handle(jc), 16)

    @on("sync")
    def _(e):
        mod_store(e, 0, 8)
        for _j in range(8):
            DS["modch"].wait(e, _j)
        e.dma_start(out=modv[:, 0:32],
                    in_=modrow_dr[0:1, 0:4096].rearrange("o (m p) -> (o p) m", p=128),
                    allow_slow_non_contiguous=True).then_inc(S["modt2"].h, 16)
    S["modt2"].n += 32
    for _i in range(24):
        DS["modch"].issued(_i)

    @on("vector")
    def _(e):
        e.wait_ge(S["modt2"].h, 16)
        ci(e.tensor_scalar(onepc[:, 0:16], modv[:, 16:32], 1.0, None, ALU.add))
        e.memset(epsv[:], EPS)
        e.engine_nop().then_inc(S["mod_rdy"].h, 1)
    S["mod_rdy"].n += 2

    @on("gpsimd")
    def _(e):
        for w in range(4):
            if w >= 2:
                e.wait_ge(S["h_rdy"].h, (w - 1) * NDC)
            e.dma_start(out=xwin[:, w % 2],
                        in_=xT.rearrange("(c p) t -> p c t", p=128)[:, :, w * 512:(w + 1) * 512]
                        ).then_inc(DS["xw"].handle(w), 16)

    for _i in range(4):
        DS["xw"].issued(_i)

    @on("scalar")
    def _(e):
        for w in range(4):
            DS["xw"].wait(e, w)
            for dc in range(NDC):
                i = w * NDC + dc
                if i >= 2:
                    e.wait_ge(S["sq_free"].h, i - 1)
                e.activation(sqring[:, i % 2, :], xwin[:, w % 2, dc, :], AF.Square
                             ).then_inc(S["sq_rdy"].h, 1)
            e.wait_ge(S["ln_mid"].h, 2 * w + 1)
            ci(e.activation(lnv[:], varrep[:], AF.Ln, bias=epsv[:]), "scalar")
            cw(e, "scalar")
            e.activation(rstdrep[:], lnv[:], AF.Exp, scale=-0.5).then_inc(S["ln_mid"].h, 1)
    S["sq_rdy"].n += 64

    @on("tensor")
    def _(e):
        e.wait_ge(S["c_ld"].h, C1)
        for w in range(4):
            DS["xw"].wait(e, w)
            if w >= 1:
                e.wait_ge(S["ln_mid"].h, 2 * w - 1)
            for dc in range(NDC):
                i = w * NDC + dc
                mm0 = e.matmul(pb[0][:], ones[:], xwin[:, w % 2, dc, :],
                               start=(dc == 0), stop=(dc == NDC - 1))
                if dc == NDC - 1:
                    mm0.then_inc(S["stats_ps"].h, 1)
                e.wait_ge(S["sq_rdy"].h, i + 1)
                e.matmul(pb[1][:], ones[:], sqring[:, i % 2, :],
                         start=(dc == 0), stop=(dc == NDC - 1)).then_inc(S["sq_free"].h, 1)
    S["stats_ps"].n += 4
    S["sq_free"].n += 64

    @on("vector")
    def _(e):
        e.wait_ge(S["mod_rdy"].h, 1)
        for w in range(4):
            e.wait_ge(S["stats_ps"].h, w + 1)
            e.wait_ge(S["sq_free"].h, 16 * (w + 1))
            ci(e.tensor_scalar(mrep[:], pb[0][:], 1.0 / D, None, ALU.mult))
            cw(e)
            ci(e.tensor_tensor(lnv[:], mrep[:], mrep[:], ALU.mult))
            cw(e)
            e.scalar_tensor_tensor(varrep[:], pb[1][:], 1.0 / D, lnv[:],
                                   ALU.mult, ALU.subtract).then_inc(S["ln_mid"].h, 1)
            e.wait_ge(S["ln_mid"].h, 2 * w + 2)
            for dc in range(NDC):
                ci(e.tensor_tensor(ropeA[:], xwin[:, w % 2, dc, :], mrep[:], ALU.subtract))
                cw(e)
                ci(e.tensor_tensor(scrB[:], ropeA[:], rstdrep[:], ALU.mult))
                cw(e)
                e.tensor_scalar(hT[:, dc, w * 512:(w + 1) * 512], scrB[:],
                                onepc[:, dc:dc + 1], modv[:, dc:dc + 1],
                                ALU.mult, ALU.add).then_inc(S["h_rdy"].h, 1)
    S["ln_mid"].n = 8
    S["h_rdy"].n += 64
    H_ALL = S["h_rdy"].n

    if "h" in tap_t:
        @on("sync")
        def _(e):
            e.wait_ge(S["h_rdy"].h, H_ALL)
            e.dma_start(out=tap_t["h"], in_=hT[:]).then_inc(S["tap_st"].h, 16)
        S["tap_st"].n += 16

    @on("sync")
    def _(e):
        for dst, src in const_pairs2:
            e.dma_start(out=dst, in_=src).then_inc(S["c2_ld"].h, 16)
    S["c2_ld"].n += 16 * len(const_pairs2)
    C2 = S["c2_ld"].n

    @on("gpsimd")
    def _(e):
        wmod_load(e, 8, 16)

    @on("tensor")
    def _(e):
        gemv(e, 8, 16)

    @on("vector")
    def _(e):
        mod_evac(e, 8, 16)

    @on("sync")
    def _(e):
        mod_store(e, 8, 16)
    def emit_qk(proj):
        wsrc, ddst, nwin = (wq, qdram, 2) if proj == 0 else (wk, kdram, 4)
        ps0 = S["qk_ps"].n
        raw0, out0 = S["kraw_rdy"].n, S["rope_out"].n
        wq0, sw0, st0 = DS["wqk"].uses, DS["kswp"].uses, DS["kqst"].uses
        NTOT = NDC * nwin

        @on("gpsimd")
        def _(e):
            for ec in range(NDC):
                if ec >= 2:
                    e.wait_ge(S["qk_ps"].h, ps0 + (ec - 1) * nwin)
                elif ps0 > 0:
                    e.wait_ge(S["qk_ps"].h, ps0)
                else:
                    e.wait_ge(S["h_rdy"].h, H_ALL)
                e.dma_start(out=wqk_sb[:, ec % 2],
                            in_=wsrc[ec].rearrange("c p n -> p c n")
                            ).then_inc(DS["wqk"].handle(wq0 + ec), 16)

        for _i in range(NDC):
            DS["wqk"].issued(wq0 + _i)

        @on("tensor")
        def _(e):
            if proj == 0:
                e.wait_ge(S["h_rdy"].h, H_ALL)
            for ec in range(NDC):
                DS["wqk"].wait(e, wq0 + ec)
                for w in range(nwin):
                    i = ec * nwin + w
                    if raw0 + i >= 4:
                        e.wait_ge(S["kraw_rdy"].h, raw0 + i - 3)
                    for dc in range(NDC):
                        mm = e.matmul(pb[i % 4][:], wqk_sb[:, ec % 2, dc, :],
                                      hT[:, dc, w * 512:(w + 1) * 512],
                                      start=(dc == 0), stop=(dc == NDC - 1))
                    mm.then_inc(S["qk_ps"].h, 1)
        S["qk_ps"].n += NTOT

        @on("scalar")
        def _(e):
            for i in range(NTOT):
                e.wait_ge(S["kraw_rdy"].h, raw0 + i + 1)
                if i >= 2:
                    e.wait_ge(S["rope_out"].h, out0 + i - 1)
                e.dma_start(out=kswp[0:64, i % 2, :], in_=kraw[64:128, i % 2, :]
                            ).then_inc(DS["kswp"].handle(sw0 + i), 16)
                e.dma_start(out=kswp[64:128, i % 2, :], in_=kraw[0:64, i % 2, :]
                            ).then_inc(DS["kswp"].handle(sw0 + i), 16)

        for _i in range(NTOT):
            DS["kacc"].issued(sw0 + _i, 32)

        @on("vector")
        def _(e):
            def rope(j):
                ec, w = j // nwin, j % nwin
                DS["kswp"].wait(e, sw0 + j)
                e.wait_ge(S["rope_out"].h, out0 + j)
                if st0 + ec >= 2 and w == 0:
                    DS["kqst"].wait(e, st0 + ec - 2)
                ci(e.tensor_tensor(ropeA[:], kraw[:, j % 2, :],
                                   cosk_sb[:, w * 512:(w + 1) * 512], ALU.mult))
                ci(e.tensor_tensor(vstg[:, j % 2, :], kswp[:, j % 2, :],
                                   sink_sb[:, w * 512:(w + 1) * 512], ALU.mult))
                cw(e)
                e.tensor_tensor(kqst[:, ec % 2, w * 512:(w + 1) * 512],
                                ropeA[:], vstg[:, j % 2, :], ALU.add
                                ).then_inc(S["rope_out"].h, 1)
            for i in range(NTOT):
                e.wait_ge(S["qk_ps"].h, ps0 + i + 1)
                DS["kswp"].wait(e, sw0 + i - 2)
                e.tensor_copy(kraw[:, i % 2, :], pb[i % 4][:]).then_inc(S["kraw_rdy"].h, 1)
                if i >= 1:
                    rope(i - 1)
            rope(NTOT - 1)
        S["kraw_rdy"].n += NTOT
        S["rope_out"].n += NTOT

        @on("sync")
        def _(e):
            for ec in range(NDC):
                e.wait_ge(S["rope_out"].h, out0 + (ec + 1) * nwin)
                e.dma_start(out=ddst[ec], in_=kqst[:, ec % 2, 0:nwin * 512]
                            ).then_inc(DS["kqst"].handle(st0 + ec), 16)

        for _i in range(NDC):
            DS["kqw"].issued(st0 + _i)

    emit_qk(0)

    @on("gpsimd")
    def _(e):
        wmod_load(e, 16, 24)

    @on("tensor")
    def _(e):
        gemv(e, 16, 24)

    @on("vector")
    def _(e):
        mod_evac(e, 16, 24)
        e.wait_ge(S["modt2"].h, 32)
        for c0 in (32, 80):
            sl = modv[:, c0:c0 + 16]
            ci(e.tensor_scalar(clsc[:, 0:16], sl, 0.0, 2.0, ALU.is_ge, ALU.mult))
            cw(e)
            ci(e.tensor_scalar(clsc[:, 16:32], clsc[:, 0:16], 1.0, None, ALU.subtract))
            cw(e)
            ci(e.tensor_tensor(clsc[:, 0:16], sl, clsc[:, 16:32], ALU.mult))
            cw(e)
            ci(e.tensor_scalar(clsc[:, 32:48], clsc[:, 0:16], 0.25, None, ALU.max))
            cw(e)
            ci(e.tensor_tensor(sl, clsc[:, 16:32], clsc[:, 32:48], ALU.mult))
            cw(e)
        ci(e.tensor_scalar(onepc[:, 16:32], modv[:, 64:80], 1.0, None, ALU.add))
        e.engine_nop().then_inc(S["mod_rdy"].h, 1)

    @on("sync")
    def _(e):
        mod_store(e, 16, 24)
        for _j in range(8, 24):
            DS["modch"].wait(e, _j)
        e.dma_start(out=modv[:, 32:96],
                    in_=modrow_dr[0:1, 4096:12288].rearrange("o (m p) -> (o p) m", p=128),
                    allow_slow_non_contiguous=True).then_inc(S["modt2"].h, 16)

    if "mod" in tap_t:
        @on("sync")
        def _(e):
            e.wait_ge(S["mod_rdy"].h, 2)
            e.dma_start(out=tap_t["mod"], in_=modrow_dr).then_inc(S["tap_st"].h, 16)
        S["tap_st"].n += 16

    emit_qk(1)
    S_KRAW_ALL = S["kraw_rdy"].n

    @on("tensor")
    def _(e):
        e.wait_ge(S["wg_ld"].h, 16)
        e.wait_ge(S["mod_evac"].h, 24)
        for w in range(2):
            if w >= 1:
                e.wait_ge(S["g_rdy"].h, w)
            for dc in range(NDC):
                mm = e.matmul(pb[6][0:48, :], wg_sb[:, dc, :],
                              hT[:, dc, w * 512:(w + 1) * 512],
                              start=(dc == 0), stop=(dc == NDC - 1))
            mm.then_inc(S["g_ps"].h, 1)
    S["g_ps"].n += 2

    @on("scalar")
    def _(e):
        for w in range(2):
            e.wait_ge(S["g_ps"].h, w + 1)
            e.activation(gT[:, w * 512:(w + 1) * 512], pb[6][0:48, :], AF.Sigmoid
                         ).then_inc(S["g_rdy"].h, 1)
    S["g_rdy"].n += 2

    if "g" in tap_t:
        @on("sync")
        def _(e):
            e.wait_ge(S["g_rdy"].h, 2)
            e.dma_start(out=tap_t["g"], in_=gT[:]).then_inc(S["tap_st"].h, 16)
        S["tap_st"].n += 16

    WVB0 = DS["wva"].uses
    for _i in range(128):
        DS["wva"].issued(WVB0 + _i)

    @on("gpsimd")
    def _(e):
        for es in range(4):
            for th in range(2):
                for dc in range(NDC):
                    i = (es * 2 + th) * NDC + dc
                    if i >= 2:
                        e.wait_ge(S["v_ps"].h, i - 1)
                    else:
                        e.wait_ge(S["h_rdy"].h, H_ALL)
                    e.dma_start(out=wvb_sb[:, i % 2, :], in_=wv[es, dc]
                                ).then_inc(DS["wvb"].handle(WVB0 + i), 16)

    @on("tensor")
    def _(e):
        e.wait_ge(S["g_rdy"].h, 2)
        e.wait_ge(S["kraw_rdy"].h, S_KRAW_ALL)
        for es in range(4):
            for th in range(2):
                u = es * 2 + th
                if u >= 1:
                    e.wait_ge(S["v_evac"].h, u * 8)
                for dc in range(NDC):
                    i = u * NDC + dc
                    DS["wvb"].wait(e, WVB0 + i)
                    for bk in range(8):
                        tc = th * 8 + bk
                        mm = e.matmul(pb[bk][:], hT[:, dc, tc * 128:(tc + 1) * 128],
                                      wvb_sb[:, i % 2, :],
                                      start=(dc == 0), stop=(dc == NDC - 1))
                    mm.then_inc(S["v_ps"].h, 1)
    S["v_ps"].n += 128

    @on("vector")
    def _(e):
        e.wait_ge(S["rope_out"].h, S["rope_out"].n)
        for es in range(4):
            for th in range(2):
                u = es * 2 + th
                e.wait_ge(S["v_ps"].h, (u + 1) * NDC)
                for bk in range(8):
                    i = u * 8 + bk
                    DS["vst"].wait(e, i - 2)
                    e.tensor_copy(vstg[:, i % 2, :], pb[bk][:]).then_inc(S["v_evac"].h, 1)
    S["v_evac"].n += 64

    @on("sync")
    def _(e):
        for es in range(4):
            for th in range(2):
                for bk in range(8):
                    i = (es * 2 + th) * 8 + bk
                    tc = th * 8 + bk
                    e.wait_ge(S["v_evac"].h, i + 1)
                    e.dma_start(out=vdram[tc, es], in_=vstg[:, i % 2, :]
                                ).then_inc(DS["vst"].handle(i), 16)

    for _i in range(64):
        DS["vst"].issued(_i)

    @on("sync")
    def _(e):
        e.wait_ge(S["v_evac"].h, 64)
        e.dma_start(out=rowsel, in_=rowsel_c).then_inc(S["c3_ld"].h, 16)
    S["c3_ld"].n += 16

    KQ_T = tuple(sub.n for sub in DS["kqst"].s)
    VST_T = tuple(sub.n for sub in DS["vst"].s)

    @on("gpsimd")
    def _(e, KQ_T=KQ_T, VST_T=VST_T):
        for _k, _sub in enumerate(DS["kqst"].s):
            if KQ_T[_k]:
                e.wait_ge(_sub.h, KQ_T[_k])
        for _k, _sub in enumerate(DS["vst"].s):
            if VST_T[_k]:
                e.wait_ge(_sub.h, VST_T[_k])
        e.wait_ge(S["c2_ld"].h, C2)
        for h in range(H):
            if h >= 2:
                e.wait_ge(S["kc_rdy"].h, h - 1)
                e.wait_ge(S["esc_rdy"].h, h - 1)
                e.wait_ge(S["vc_rdy"].h, h - 1)
            e.dma_start(out=kr2[:, h % 2, :], in_=kdram[h]).then_inc(DS["kra"].handle(h), 16)
            e.dma_start(out=qr2[:, h % 2, :], in_=qdram[h]).then_inc(DS["qra"].handle(h), 16)
            e.dma_start(out=vr2[:, h % 2],
                        in_=vdram[:, h // 4, :, (h % 4) * 128:(h % 4 + 1) * 128
                                  ].rearrange("c p n -> p c n")).then_inc(DS["vra"].handle(h), 16)

    for _h in range(H):
        DS["kr"].issued(_h)
        DS["qr"].issued(_h)
        DS["vr"].issued(_h)

    @on("tensor")
    def _(e):
        e.wait_ge(S["c2_ld"].h, C2)
        for h in range(H):
            DS["qra"].wait(e, h)
            DS["vra"].wait(e, h)
            e.wait_ge(S["kc_rdy"].h, h + 1)
            if h >= 1:
                e.wait_ge(S["cm_rdy"].h, h)
            if h >= 2:
                e.wait_ge(S["vc_rdy"].h, h - 1)
            for qb in range(8):
                mm = e.matmul(pb[0][:, qb * 32:(qb + 1) * 32],
                              qr2[:, h % 2, qb * 128:(qb + 1) * 128],
                              kcmpall[:, h, :], start=True, stop=True)
            mm.then_inc(S["scmp_ps"].h, 1)
            for c in range(NTC):
                mm = e.matmul(pb[2 + h % 2][0:32, 0:128], ameans[:, c * 32:(c + 1) * 32],
                              vr2[:, h % 2, c, :], start=(c == 0), stop=(c == NTC - 1))
            mm.then_inc(S["vc_ps"].h, 1)
    S["scmp_ps"].n += H
    S["vc_ps"].n += H

    @on("vector")
    def _(e):
        e.wait_ge(S["c2_ld"].h, C2)
        for h in range(H):
            DS["kra"].wait(e, h)
            ci(e.reduce_sum(kcf[:], kr2[:, h % 2, :].rearrange("p (b t) -> p b t", b=32),
                            axis=AX.X))
            cw(e)
            e.tensor_scalar(kcmpall[:, h, :], kcf[:], 1.0 / 64.0, None, ALU.mult
                            ).then_inc(S["kc_rdy"].h, 1)
            e.wait_ge(S["vc_ps"].h, h + 1)
            e.tensor_copy(vcmpall[:, h, :], pb[2 + h % 2][0:32, 0:128]).then_inc(S["vc_rdy"].h, 1)
            e.wait_ge(S["scmp_ps"].h, h + 1)
            e.tensor_tensor(cmadd[:], pb[0][:, 0:256], cmpma[:], ALU.add).then_inc(S["cm_rdy"].h, 1)
            e.wait_ge(S["esc_rdy"].h, h + 1)
            e.tensor_tensor(impa[:, h * 256:(h + 1) * 256], esc[:], cursup[:], ALU.add
                            ).then_inc(S["imp_init"].h, 1)
    S["kc_rdy"].n += H
    S["vc_rdy"].n += H
    S["cm_rdy"].n += H
    S["imp_init"].n += H

    @on("scalar")
    def _(e):
        for h in range(H):
            e.wait_ge(S["cm_rdy"].h, h + 1)
            e.activation(esc[:], cmadd[:], AF.Exp, scale=KAPPA).then_inc(S["esc_rdy"].h, 1)
    S["esc_rdy"].n += H

    if "imp" in tap_t:
        @on("sync")
        def _(e):
            e.wait_ge(S["imp_init"].h, H)
            e.dma_start(out=tap_t["imp"], in_=impa[:]).then_inc(S["tap_st"].h, 16)
        S["tap_st"].n += 16
    if "kcmp" in tap_t:
        @on("sync")
        def _(e):
            e.wait_ge(S["kc_rdy"].h, H)
            e.dma_start(out=tap_t["kcmp"], in_=kcmpall[:]).then_inc(S["tap_st"].h, 16)
        S["tap_st"].n += 16
    if "vcmp" in tap_t:
        @on("sync")
        def _(e):
            e.wait_ge(S["vc_rdy"].h, H)
            e.dma_start(out=tap_t["vcmp"], in_=vcmpall[:]).then_inc(S["tap_st"].h, 16)
        S["tap_st"].n += 16

    @on("vector")
    def _(e):
        for hf in range(2):
            sl = slice(hf * 2048, (hf + 1) * 2048)
            e.wait_ge(S["imp_init"].h, 8 * (hf + 1))
            for it in range(7):
                src = (impa if it % 2 == 0 else impb)[:, sl]
                dst = (impb if it % 2 == 0 else impa)[:, sl]
                ci(e.reduce_max(mxb[:, hf * 64:(hf + 1) * 64],
                                src.rearrange("p (g b) -> p g b", b=32), axis=AX.X))
                cw(e)
                ci(e.tensor_tensor(selb[:, sl].rearrange("p (g b) -> p g b", b=32),
                                   src.rearrange("p (g b) -> p g b", b=32),
                                   mxb[:, hf * 64:(hf + 1) * 64].unsqueeze(2)
                                   .broadcast_to([128, 64, 32]), ALU.is_ge))
                cw(e)
                ci(e.scalar_tensor_tensor(dst, selb[:, sl], -SUP, src, ALU.mult, ALU.add))
                cw(e)
            e.engine_nop().then_inc(S["topk_done"].h, 1)
    S["topk_done"].n += 2

    gc = [0]
    pvst = [0]
    cr = [0]
    swp_ = [0]
    rep_ = [0]
    units = [0]

    @on("gpsimd")
    def _(e):
        for h in range(H):
            e.wait_ge(S["kc_rdy"].h, H)
            e.wait_ge(S["vc_rdy"].h, H)
            e.wait_ge(S["esc_rdy"].h, H)
            if h >= 2:
                e.wait_ge(S["den_done"].h, 2 * (h - 1))
            e.dma_start(out=kr2[:, h % 2, :], in_=kdram[h]).then_inc(DS["krb"].handle(16 + h), 16)
            e.dma_start(out=qr2[:, h % 2, :], in_=qdram[h]).then_inc(DS["qrb"].handle(16 + h), 16)
            e.dma_start(out=vr2[:, h % 2],
                        in_=vdram[:, h // 4, :, (h % 4) * 128:(h % 4 + 1) * 128
                                  ].rearrange("c p n -> p c n")).then_inc(DS["vrb"].handle(16 + h), 16)

    for _h in range(H):
        DS["kr"].issued(16 + _h)
        DS["qr"].issued(16 + _h)
        DS["vr"].issued(16 + _h)

    for h in range(H):
        trbase = S["tr_ps"].n
        gc_at_build = gc[0]

        @on("tensor")
        def _(e, h=h, gc0=gc_at_build):
            e.wait_ge(S["topk_done"].h, 1 if h < 8 else 2)
            if gc0 > 0:
                e.wait_ge(S["px"].h, gc0)
            if h > 0:
                e.wait_ge(S["mt_rdy"].h, h)
            for qt in range(8):
                mm = e.transpose(pb0bf[:, qt * 128:(qt + 1) * 128],
                                 impb[:, h * 256 + qt * 32:h * 256 + (qt + 1) * 32], ident[:])
            mm.then_inc(S["tr_ps"].h, 1)
        S["tr_ps"].n += 1

        @on("vector")
        def _(e, h=h):
            e.wait_ge(S["tr_ps"].h, h + 1)
            ci(e.tensor_scalar(mtmp[:], pb0bf[:], SEL_THR, MASKPRE, ALU.is_ge, ALU.mult))
            cw(e)
            e.tensor_tensor(maskT[:, h % 2, :], mtmp[:], cbmin[:], ALU.min
                            ).then_inc(S["mt_rdy"].h, 1)
        S["mt_rdy"].n += 1

        @on("tensor")
        def _(e, h=h):
            e.wait_ge(S["mt_rdy"].h, h + 1)
            DS["qrb"].wait(e, 16 + h)
            if h > 0:
                e.wait_ge(S["scT_f"].h, 2 * h)
            for g in range(2):
                e.matmul(pb[g][0:32, :], kcmpall[:, h, :], qr2[:, h % 2, g * 512:(g + 1) * 512],
                         start=True, stop=True).then_inc(S["scT_ps"].h, 1)
        S["scT_ps"].n += 2

        @on("vector")
        def _(e, h=h):
            for g in range(2):
                e.wait_ge(S["scT_ps"].h, 2 * h + g + 1)
                if 2 * h + g >= 1:
                    e.wait_ge(S["pcT_rdy"].h, 2 * h + g)
                e.tensor_tensor(scmpTf[:], pb[g][0:32, :], cmpmaT[:, g * 512:(g + 1) * 512],
                                ALU.add).then_inc(S["scT_f"].h, 1)
        S["scT_f"].n += 2

        @on("scalar")
        def _(e, h=h):
            for g in range(2):
                e.wait_ge(S["scT_f"].h, 2 * h + g + 1)
                e.activation(pcmpT[:, h % 2, g * 512:(g + 1) * 512], scmpTf[:],
                             AF.Exp, scale=KAPPA).then_inc(S["pcT_rdy"].h, 1)
        S["pcT_rdy"].n += 2

        if "maskT" in tap_t:
            @on("sync")
            def _(e, h=h):
                e.wait_ge(S["mt_rdy"].h, h + 1)
                e.dma_start(out=tap_t["maskT"][:, h, :], in_=maskT[:, h % 2, :]
                            ).then_inc(S["tap_st"].h, 16)
            S["tap_st"].n += 16
        if "pcmpT" in tap_t:
            @on("sync")
            def _(e, h=h):
                e.wait_ge(S["pcT_rdy"].h, 2 * h + 2)
                e.dma_start(out=tap_t["pcmpT"][:, h, :], in_=pcmpT[:, h % 2, :]
                            ).then_inc(S["tap_st"].h, 16)
            S["tap_st"].n += 16

        for g in range(2):
            ECH = 4 * g + 4
            chunks = list(range(ECH)) + list(range(8, 8 + ECH))
            dpar = (2 * h + g) % 2
            unit = 2 * h + g
            gbase = gc[0]
            pvbase = pvst[0]
            denb = pb[5 + dpar]

            @on("tensor")
            def _(e, h=h, g=g, chunks=tuple(chunks), gbase=gbase, unit=unit,
                  dpar=dpar, denb=denb, pvbase=pvbase, crbase=cr[0]):
                DS["krb"].wait(e, 16 + h)
                DS["vrb"].wait(e, 16 + h)
                e.wait_ge(S["mt_rdy"].h, h + 1)
                e.wait_ge(S["scT_f"].h, 2 * (h + 1))
                ci = [0]

                def pvden(j):
                    c2 = chunks[j]
                    if j == 0:
                        if unit >= 1:
                            e.wait_ge(S["cmb"].h, unit)
                        if unit >= 2:
                            e.wait_ge(S["coef_u"].h, unit - 1)
                    e.wait_ge(S["px"].h, gbase + j + 1)
                    e.matmul(pb[2][:], vr2[:, h % 2, c2, :], pring[:, (gbase + j) % 3, :],
                             start=(j == 0), stop=(j == len(chunks) - 1))
                    mm = e.matmul(denb[0:48, :], toep[:, 31 - h:79 - h],
                                  pring[:, (gbase + j) % 3, :],
                                  start=(j == 0), stop=False)
                    if 4 * g <= c2 < 4 * g + 4:
                        jl = c2 - 4 * g
                        e.wait_ge(S["corr_i"].h, crbase + ci[0] + 1)
                        e.matmul(pb[2][:, jl * 128:(jl + 1) * 128], vr2[:, h % 2, c2, :],
                                 corrsc[:, ci[0] % 2, :], start=False, stop=False)
                        mm = e.matmul(denb[0:48, jl * 128:(jl + 1) * 128],
                                      toep[:, 31 - h:79 - h], corrsc[:, ci[0] % 2, :],
                                      start=False, stop=False)
                        ci[0] += 1
                    mm.then_inc(S["pvstep"].h, 1)

                for i, c in enumerate(chunks):
                    if gbase + i >= 2:
                        e.wait_ge(S["px"].h, gbase + i - 1)
                    e.matmul(pb[i % 2][:], kr2[:, h % 2, c * 128:(c + 1) * 128],
                             qr2[:, h % 2, g * 512:(g + 1) * 512], start=True, stop=False)
                    e.matmul(pb[i % 2][:], eexp[:, c * 128:(c + 1) * 128],
                             maskT[:, h % 2, g * 512:(g + 1) * 512], start=False, stop=True
                             ).then_inc(S["sc_ps"].h, 1)
                    if i >= 1:
                        pvden(i - 1)
                pvden(len(chunks) - 1)
            S["sc_ps"].n += len(chunks)
            S["pvstep"].n += len(chunks)

            @on("scalar")
            def _(e, gbase=gbase, nch=len(chunks), pvbase=pvbase):
                for i in range(nch):
                    e.wait_ge(S["sc_ps"].h, gbase + i + 1)
                    if gbase + i >= 3:
                        e.wait_ge(S["pvstep"].h, gbase + i - 2)
                    e.activation(pring[:, (gbase + i) % 3, :], pb[i % 2][:],
                                 AF.Exp, scale=KAPPA).then_inc(S["px"].h, 1)
            S["px"].n += len(chunks)

            corr_chunks = [i for i, c in enumerate(chunks) if 4 * g <= c < 4 * g + 4]

            @on("vector")
            def _(e, g=g, chunks=tuple(chunks), gbase=gbase,
                  corr_chunks=tuple(corr_chunks)):
                for ci, i in enumerate(corr_chunks):
                    c = chunks[i]
                    jl = c - 4 * g
                    e.wait_ge(S["px"].h, gbase + i + 1)
                    if ci >= 2:
                        e.wait_ge(S["pvstep"].h, gbase + corr_chunks[ci - 2] + 1)
                    e.tensor_tensor(corrsc[:, ci % 2, :],
                                    pring[:, (gbase + i) % 3, jl * 128:(jl + 1) * 128],
                                    uptri[:], ALU.mult).then_inc(S["corr_i"].h, 1)
            S["corr_i"].n += 4
            gc[0] += len(chunks)
            pvst[0] += len(chunks)
            cr[0] += 4

            @on("tensor")
            def _(e, h=h, g=g, unit=unit, denb=denb, swb=swp_[0]):
                if unit >= 1:
                    e.wait_ge(S["rep_ev"].h, 3 * unit)

                def swasc(jl):
                    j = 4 * g + jl
                    cands = [j, max(8 + j - 1, 8), 8 + j]
                    if jl >= 2:
                        e.wait_ge(S["pswa_e"].h, swb + jl - 1)
                    bank = pb[7 if jl % 2 == 0 else 4]
                    for p, cp in enumerate(cands):
                        mm = e.matmul(bank[:, p * 128:(p + 1) * 128],
                                      kr2[:, h % 2, cp * 128:(cp + 1) * 128],
                                      qr2[:, h % 2, j * 128:(j + 1) * 128],
                                      start=True, stop=True)
                    mm.then_inc(S["swa_ps"].h, 1)

                def swapv(jl):
                    j = 4 * g + jl
                    cands = [j, max(8 + j - 1, 8), 8 + j]
                    e.wait_ge(S["pswa_m"].h, swb + jl + 1)
                    for p, cp in enumerate(cands):
                        e.matmul(pb[3][:, jl * 128:(jl + 1) * 128], vr2[:, h % 2, cp, :],
                                 pswa2[:, jl % 2, p * 128:(p + 1) * 128],
                                 start=(jl == 0 and p == 0), stop=(jl == 3 and p == 2))
                        mm = e.matmul(denb[0:48, jl * 128:(jl + 1) * 128],
                                      toep[:, 15 - h:63 - h],
                                      pswa2[:, jl % 2, p * 128:(p + 1) * 128],
                                      start=False, stop=False)
                    mm.then_inc(S["swa_pv"].h, 1)

                for jl in range(4):
                    swasc(jl)
                    if jl >= 1:
                        swapv(jl - 1)
                swapv(3)
            S["swa_ps"].n += 4
            S["swa_pv"].n += 4

            @on("scalar")
            def _(e, swb=swp_[0]):
                for jl in range(4):
                    e.wait_ge(S["swa_ps"].h, swb + jl + 1)
                    e.activation(pswa[:, jl % 2, :], pb[7 if jl % 2 == 0 else 4][:, 0:384],
                                 AF.Exp, scale=KAPPA).then_inc(S["pswa_e"].h, 1)
            S["pswa_e"].n += 4

            @on("vector")
            def _(e, g=g, swb=swp_[0]):
                for jl in range(4):
                    e.wait_ge(S["pswa_e"].h, swb + jl + 1)
                    if jl >= 2:
                        e.wait_ge(S["swa_pv"].h, swb + jl - 1)
                    cls = 0 if (g == 0 and jl == 0) else 1
                    e.tensor_tensor(pswa2[:, jl % 2, :], pswa[:, jl % 2, :],
                                    swaarr[:, cls, :], ALU.mult).then_inc(S["pswa_m"].h, 1)
            S["pswa_m"].n += 4
            swp_[0] += 4

            @on("tensor")
            def _(e, h=h, g=g, denb=denb, unit=unit):
                e.wait_ge(S["pcT_rdy"].h, 2 * h + 2)
                e.matmul(pb[4][:], vcmpall[:, h, :], pcmpT[:, h % 2, g * 512:(g + 1) * 512],
                         start=True, stop=True).then_inc(S["pv_done"].h, 1)
                e.matmul(denb[0:48, :], toep32[:, 47 - h:95 - h],
                         pcmpT[:, h % 2, g * 512:(g + 1) * 512], start=False, stop=True
                         ).then_inc(S["den_done"].h, 1)
            S["pv_done"].n += 1
            S["den_done"].n += 1

            @on("vector")
            def _(e, g=g, denb=denb, unit=unit):
                e.wait_ge(S["den_done"].h, unit + 1)
                ci(e.tensor_tensor(varrep[0:48, :], denb[0:48, :],
                                   anyfix[:, g * 512:(g + 1) * 512], ALU.add))
                cw(e)
                e.tensor_scalar(lnv[0:48, :], varrep[0:48, :], 1e-20, None, ALU.max
                                ).then_inc(S["coef_u"].h, 1)
            S["coef_u"].n += 1

            @on("scalar")
            def _(e, unit=unit):
                e.wait_ge(S["coef_u"].h, unit + 1)
                ci(e.activation(varrep[0:48, :], lnv[0:48, :], AF.Ln), "scalar")
                cw(e, "scalar")
                e.activation(lnv[0:48, :], varrep[0:48, :], AF.Exp, scale=-1.0
                             ).then_inc(S["coef_r"].h, 1)
            S["coef_r"].n += 1

            @on("vector")
            def _(e, h=h, g=g, dpar=dpar, unit=unit):
                e.wait_ge(S["coef_r"].h, unit + 1)
                e.wait_ge(S["g_rdy"].h, 2)
                e.tensor_tensor(coefb[:, dpar, :], lnv[0:48, :],
                                gT[:, g * 512:(g + 1) * 512], ALU.mult)
                e.engine_nop().then_inc(S["coef_rdy"].h, 1)
            S["coef_rdy"].n += 1

            @on("tensor")
            def _(e, h=h, dpar=dpar, unit=unit, rb=rep_[0]):
                e.wait_ge(S["coef_rdy"].h, unit + 1)
                e.wait_ge(S["pswa_e"].h, (unit + 1) * 4)
                if unit == 0:
                    e.wait_ge(S["c3_ld"].h, 16)
                for b in range(3):
                    if rb + b >= 1:
                        e.wait_ge(S["rep_ev"].h, rb + b)
                    e.matmul(pb[7][:], rowsel[:, (16 * b + h) * 128:(16 * b + h + 1) * 128],
                             coefb[0:48, dpar, :], start=True, stop=True
                             ).then_inc(S["rep_ps"].h, 1)
            S["rep_ps"].n += 3

            @on("vector")
            def _(e, rb=rep_[0]):
                for b in range(3):
                    e.wait_ge(S["rep_ps"].h, rb + b + 1)
                    e.tensor_copy(crep[:, b, :], pb[7][:]).then_inc(S["rep_ev"].h, 1)
            S["rep_ev"].n += 3
            rep_[0] += 3

            @on("vector")
            def _(e, h=h, g=g, unit=unit):
                e.wait_ge(S["rep_ev"].h, 3 * (unit + 1))
                e.wait_ge(S["pv_done"].h, unit + 1)
                ci(e.tensor_tensor(csc[:, 0, :], pb[4][:], crep[:, 0, :], ALU.mult))
                ci(e.tensor_tensor(csc[:, 1, :], pb[2][:], crep[:, 1, :], ALU.mult))
                cw(e)
                ci(e.tensor_tensor(csc[:, 2, :], csc[:, 0, :], csc[:, 1, :], ALU.add))
                cw(e)
                ci(e.tensor_tensor(csc[:, 0, :], pb[3][:], crep[:, 2, :], ALU.mult))
                cw(e)
                e.tensor_tensor(oT[:, h, g * 512:(g + 1) * 512], csc[:, 2, :], csc[:, 0, :],
                                ALU.add).then_inc(S["cmb"].h, 1)
            S["cmb"].n += 1
            units[0] += 1

    CMB_ALL = S["cmb"].n

    if "o" in tap_t:
        @on("sync")
        def _(e):
            e.wait_ge(S["cmb"].h, CMB_ALL)
            e.dma_start(out=tap_t["o"], in_=oT[:]).then_inc(S["tap_st"].h, 16)
        S["tap_st"].n += 16

    XOL0 = DS["xol"].uses
    WOL0 = DS["wqk"].uses
    for _i in range(NDC):
        DS["xol"].issued(XOL0 + _i)
        DS["wqk"].issued(WOL0 + _i)
    @on("sync")
    def _(e):
        e.wait_ge(S["cmb"].h, CMB_ALL)
        for dc in range(NDC):
            if dc >= 2:
                e.wait_ge(S["x2_rdy"].h, 2 * (dc - 1))
            e.dma_start(out=xo[:, dc % 2, :],
                        in_=xT.rearrange("(c p) t -> p c t", p=128)[:, dc, 0:OT]
                        ).then_inc(DS["xol"].handle(XOL0 + dc), 16)

    @on("gpsimd")
    def _(e):
        e.wait_ge(S["cmb"].h, CMB_ALL)
        for dc in range(NDC):
            if dc >= 2:
                e.wait_ge(S["wo_ps"].h, 2 * (dc - 1))
            e.dma_start(out=wo_sb[:, dc % 2], in_=wo[dc].rearrange("c p n -> p c n")
                        ).then_inc(DS["wol"].handle(WOL0 + dc), 16)

    @on("tensor")
    def _(e):
        e.wait_ge(S["cmb"].h, CMB_ALL)
        for dc in range(NDC):
            DS["wol"].wait(e, WOL0 + dc)
            if dc >= 2:
                e.wait_ge(S["x2_rdy"].h, 2 * (dc - 1))
            for half in range(2):
                bank = pb[(dc % 2) * 2 + half]
                for ec in range(NDC):
                    mm = e.matmul(bank[:], wo_sb[:, dc % 2, ec, :],
                                  oT[:, ec, half * 512:(half + 1) * 512],
                                  start=(ec == 0), stop=(ec == NDC - 1))
                mm.then_inc(S["wo_ps"].h, 1)
    S["wo_ps"].n += 32

    @on("vector")
    def _(e):
        e.wait_ge(S["mod_rdy"].h, 2)
        for dc in range(NDC):
            DS["xol"].wait(e, XOL0 + dc)
            for half in range(2):
                e.wait_ge(S["wo_ps"].h, 2 * dc + half + 1)
                e.scalar_tensor_tensor(x2[:, dc, half * 512:(half + 1) * 512],
                                       pb[(dc % 2) * 2 + half][:],
                                       modv[:, 32 + dc:33 + dc],
                                       xo[:, dc % 2, half * 512:(half + 1) * 512],
                                       ALU.mult, ALU.add).then_inc(S["x2_rdy"].h, 1)
            e.wait_ge(S["x2_rdy"].h, 2 * dc + 2)
            e.tensor_copy(x2b(dc), x2[:, dc, :]).then_inc(S["x2b_rdy"].h, 1)
    S["x2_rdy"].n += 32
    S["x2b_rdy"].n += NDC

    if "x2" in tap_t:
        @on("sync")
        def _(e):
            e.wait_ge(S["x2_rdy"].h, 32)
            e.dma_start(out=tap_t["x2"], in_=x2[:]).then_inc(S["tap_st"].h, 16)
        S["tap_st"].n += 16

    @on("scalar")
    def _(e):
        for w in range(2):
            for dc in range(NDC):
                i = w * NDC + dc
                e.wait_ge(S["x2b_rdy"].h, dc + 1)
                if i >= 2:
                    e.wait_ge(S["sq2_free"].h, i - 1)
                e.activation(sqring[:, i % 2, :], x2b(dc)[:, w * 512:(w + 1) * 512],
                             AF.Square).then_inc(S["sq2_rdy"].h, 1)
    S["sq2_rdy"].n += 32

    @on("tensor")
    def _(e):
        e.wait_ge(S["x2_rdy"].h, 32)
        for w in range(2):
            if w >= 1:
                e.wait_ge(S["ln2_mid"].h, 2 * w - 1)
            for dc in range(NDC):
                i = w * NDC + dc
                e.wait_ge(S["x2b_rdy"].h, dc + 1)
                mm0 = e.matmul(pb[0][:], ones[:], x2b(dc)[:, w * 512:(w + 1) * 512],
                               start=(dc == 0), stop=(dc == NDC - 1))
                if dc == NDC - 1:
                    mm0.then_inc(S["st2_ps"].h, 1)
                e.wait_ge(S["sq2_rdy"].h, i + 1)
                e.matmul(pb[1][:], ones[:], sqring[:, i % 2, :],
                         start=(dc == 0), stop=(dc == NDC - 1)).then_inc(S["sq2_free"].h, 1)
    S["st2_ps"].n += 2
    S["sq2_free"].n += 32

    @on("scalar")
    def _(e):
        for w in range(2):
            e.wait_ge(S["ln2_mid"].h, 2 * w + 1)
            ci(e.activation(lnv[:], varrep[:], AF.Ln, bias=epsv[:]), "scalar")
            cw(e, "scalar")
            e.activation(rstdrep[:], lnv[:], AF.Exp, scale=-0.5).then_inc(S["ln2_mid"].h, 1)

    @on("vector")
    def _(e):
        e.wait_ge(S["mod_rdy"].h, 2)
        for w in range(2):
            e.wait_ge(S["st2_ps"].h, w + 1)
            e.wait_ge(S["sq2_free"].h, 16 * (w + 1))
            ci(e.tensor_scalar(mrep[:], pb[0][:], 1.0 / D, None, ALU.mult))
            cw(e)
            ci(e.tensor_tensor(lnv[:], mrep[:], mrep[:], ALU.mult))
            cw(e)
            e.scalar_tensor_tensor(varrep[:], pb[1][:], 1.0 / D, lnv[:],
                                   ALU.mult, ALU.subtract).then_inc(S["ln2_mid"].h, 1)
            e.wait_ge(S["ln2_mid"].h, 2 * w + 2)
            for dc in range(NDC):
                ci(e.tensor_tensor(ropeA[:], x2[:, dc, w * 512:(w + 1) * 512], mrep[:],
                                   ALU.subtract))
                cw(e)
                ci(e.tensor_tensor(scrB[:], ropeA[:], rstdrep[:], ALU.mult))
                cw(e)
                e.tensor_scalar(h2sb[:, dc, w * 512:(w + 1) * 512], scrB[:],
                                onepc[:, 16 + dc:17 + dc], modv[:, 48 + dc:49 + dc],
                                ALU.mult, ALU.add).then_inc(S["h2_rdy"].h, 1)
    S["ln2_mid"].n = 4
    S["h2_rdy"].n += 32
    H2_ALL = S["h2_rdy"].n

    if "h2" in tap_t:
        @on("sync")
        def _(e):
            e.wait_ge(S["h2_rdy"].h, H2_ALL)
            e.dma_start(out=tap_t["h2"], in_=h2sb[:]).then_inc(S["tap_st"].h, 16)
        S["tap_st"].n += 16

    @on("sync")
    def _(e):
        e.wait_ge(S["h2_rdy"].h, H2_ALL)
        e.dma_start(out=outT.rearrange("(c p) t -> p c t", p=128), in_=x2[:]
                    ).then_inc(S["x2_sp"].h, 16)
    S["x2_sp"].n += 16

    WUL0 = DS["wul"].uses
    WLL0 = DS["wqk"].uses
    ACC0 = DS["acc"].uses
    assert (ACC0 % 2) == 0
    for _hc in range(64):
        DS["wul"].issued(WUL0 + _hc, 32)
    for _u in range(64):
        DS["wqk"].issued(WLL0 + _u)
        DS["acc"].issued(ACC0 + _u)
    @on("gpsimd")
    def _(e):
        e.wait_ge(S["x2_sp"].h, 16)

        def acc_dma(ua):
            e.wait_ge(S["stg_rdy"].h, ua + 1)
            DS["acc"].wait(e, ACC0 + ua - 16)
            e.dma_start(out=outT.rearrange("(c p) t -> p c t", p=128)[:, ua % NDC, :],
                        in_=stage[:, ua % 2, :], accum_op=ALU.add
                        ).then_inc(DS["acc"].handle(ACC0 + ua), 16)

        def wll_load(u2):
            if u2 >= 2:
                e.wait_ge(S["dn_ps"].h, u2 - 1)
            e.dma_start(out=wl_sb[:, u2 % 2],
                        in_=wl[u2 % NDC, (u2 // NDC) * 16:(u2 // NDC) * 16 + 16
                               ].rearrange("c p n -> p c n")
                        ).then_inc(DS["wll"].handle(WLL0 + u2), 16)

        for grp in range(4):
            for hl in range(16):
                hc = grp * 16 + hl
                if hc >= 2:
                    e.wait_ge(S["up_ps"].h, 2 * (hc - 1))
                e.dma_start(out=wu_sb[:, hc % 2, 0], in_=wu0[hc].rearrange("c p n -> p c n")
                            ).then_inc(DS["wul"].handle(WUL0 + hc), 16)
                e.dma_start(out=wu_sb[:, hc % 2, 1], in_=wu1[hc].rearrange("c p n -> p c n")
                            ).then_inc(DS["wul"].handle(WUL0 + hc), 16)
                if grp >= 1:
                    u2 = (grp - 1) * 16 + hl
                    wll_load(u2)
                    if u2 >= 2:
                        acc_dma(u2 - 2)

        for hl in range(16):
            wll_load(48 + hl)
            acc_dma(46 + hl)
        acc_dma(62)
        acc_dma(63)

    @on("tensor")
    def _(e):
        e.wait_ge(S["h2_rdy"].h, H2_ALL)

        def dn_dc(gp, dc):
            u = gp * NDC + dc
            abuf = act if gp % 2 == 0 else actB
            DS["wll"].wait(e, WLL0 + u)
            if dc == 0:
                e.wait_ge(S["act_rdy"].h, 32 * (gp + 1))
            if u >= 2:
                e.wait_ge(S["stg_rdy"].h, u - 1)
            for half in range(2):
                bank = pb[4 + (dc % 2) * 2 + half]
                for hl2 in range(16):
                    mm = e.matmul(bank[:], wl_sb[:, u % 2, hl2, :],
                                  abuf[:, hl2, half * 512:(half + 1) * 512],
                                  start=(hl2 == 0), stop=(hl2 == 15))
            mm.then_inc(S["dn_ps"].h, 1)

        for grp in range(4):
            for hl in range(16):
                hc = grp * 16 + hl
                DS["wul"].wait(e, WUL0 + hc)
                for half in range(2):
                    hh = 2 * hc + half
                    if hh >= 2:
                        e.wait_ge(S["act_rdy"].h, hh - 1)
                    for proj in range(2):
                        bank = pb[2 * (hh % 2) + proj]
                        for dc in range(NDC):
                            mm = e.matmul(bank[:], wu_sb[:, hc % 2, proj, dc, :],
                                          h2sb[:, dc, half * 512:(half + 1) * 512],
                                          start=(dc == 0), stop=(dc == NDC - 1))
                    mm.then_inc(S["up_ps"].h, 1)
                if grp >= 1:
                    dn_dc(grp - 1, hl)
        for dc in range(NDC):
            dn_dc(3, dc)
    S["up_ps"].n += 128
    S["dn_ps"].n += 64

    @on("scalar")
    def _(e):
        for hh in range(128):
            e.wait_ge(S["up_ps"].h, hh + 1)
            if hh >= 2:
                e.wait_ge(S["act_rdy"].h, hh - 1)
            e.activation(gel_sb[:, hh % 2, :], pb[2 * (hh % 2)][:],
                         AF.Gelu_apprx_tanh).then_inc(S["gel_rdy"].h, 1)
    S["gel_rdy"].n += 128

    @on("vector")
    def _(e):
        e.wait_ge(S["mod_rdy"].h, 2)

        def stage_u(u):
            e.wait_ge(S["dn_ps"].h, u + 1)
            DS["acc"].wait(e, ACC0 + u - 2)
            for half in range(2):
                e.tensor_scalar(stage[:, u % 2, half * 512:(half + 1) * 512],
                                pb[4 + (u % 2) * 2 + half][:],
                                modv[:, 80 + u % NDC:81 + u % NDC], None, ALU.mult)
            e.engine_nop().then_inc(S["stg_rdy"].h, 1)

        for grp in range(4):
            abuf = act if grp % 2 == 0 else actB
            for hl in range(16):
                hc = grp * 16 + hl
                for half in range(2):
                    hh = 2 * hc + half
                    if hl == 0 and half == 0 and grp >= 2:
                        e.wait_ge(S["dn_ps"].h, 16 * (grp - 1))
                    e.wait_ge(S["gel_rdy"].h, hh + 1)
                    e.tensor_tensor(abuf[:, hl, half * 512:(half + 1) * 512],
                                    gel_sb[:, hh % 2, :], pb[2 * (hh % 2) + 1][:],
                                    ALU.mult).then_inc(S["act_rdy"].h, 1)
                if grp >= 1:
                    stage_u((grp - 1) * NDC + hl)
        for u in range(48, 64):
            stage_u(u)
    S["act_rdy"].n += 128
    S["stg_rdy"].n += 64

    if "act" in tap_t:
        @on("sync")
        def _(e):
            e.wait_ge(S["act_rdy"].h, 32)
            e.dma_start(out=tap_t["act"], in_=act[:]).then_inc(S["tap_st"].h, 16)
        S["tap_st"].n += 16

    @on("sync")
    def _(e):
        DS["acc"].wait_all(e)
        if S["tap_st"].n:
            e.wait_ge(S["tap_st"].h, S["tap_st"].n)

    with nc.Block() as blk:
        @blk.sync
        def _(eng):
            for fn in OPS["sync"]:
                fn(eng)

        @blk.gpsimd
        def _(eng):
            for fn in OPS["gpsimd"]:
                fn(eng)

        @blk.tensor
        def _(eng):
            for fn in OPS["tensor"]:
                fn(eng)

        @blk.vector
        def _(eng):
            for fn in OPS["vector"]:
                fn(eng)

        @blk.scalar
        def _(eng):
            for fn in OPS["scalar"]:
                fn(eng)
    return nc


def kernel(**inputs):
    maps = _stage(inputs)
    nc = build_nc(DEBUG_TAPS)
    res = run_bass_kernel_spmd(nc, maps, core_ids=list(range(8)))
    out = np.zeros((4, NT, D), np.float32)
    for core in range(8):
        b, s = core // 2, core % 2
        own = _perm_ids(s)[0:OT]
        oTv = np.asarray(res.results[core]["outT"])
        out[b, own] = oTv.T
    return out


# version 08
# speedup vs baseline: 1.2289x; 1.0206x over previous
import numpy as np
import ml_dtypes
import concourse.bass as bass
import concourse.mybir as mybir
from concourse.bass_utils import run_bass_kernel_spmd

F32 = mybir.dt.float32
BF16 = mybir.dt.bfloat16
ALU = mybir.AluOpType
AF = mybir.ActivationFunctionType
AX = mybir.AxisListType
BF = ml_dtypes.bfloat16

H, HD, D, NT, OT = 16, 128, 2048, 2048, 1024
NDC, NTC = 16, 16
KAPPA = float(HD) ** -0.5
MASKPRE = -2048.0
NEG_BIG = -3.0e6
SUP = 1.0e6
SEL_THR = -1.0e5
EPS = 1e-5

DEBUG_TAPS = ()


def _perm_ids(s):
    ids = []
    for j in range(8):
        ids.extend(range((2 * j + s) * 128, (2 * j + s) * 128 + 128))
    for j in range(8):
        ids.extend(range((2 * j + 1 - s) * 128, (2 * j + 1 - s) * 128 + 128))
    return np.array(ids)


def _consts_for_parity(s):
    perm = _perm_ids(s)
    own = perm[0:OT]
    c = {}
    inv = 1.0 / 10000.0 ** (np.arange(64) / 64.0)
    th = perm[:, None].astype(np.float64) * inv[None, :]
    ck, sk = np.cos(th).T, np.sin(th).T
    c["cosk"] = np.tile(ck, (2, 1)).astype(BF)
    c["sink"] = np.concatenate([-sk, sk], 0).astype(BF)
    c["ident"] = np.eye(128, dtype=BF)
    c["ones_lhs"] = np.ones((128, 128), dtype=BF)
    c["eexp"] = (np.arange(32)[:, None] == (np.arange(NT)[None, :] // 64)).astype(BF)
    am = np.zeros((128, NTC, 32), dtype=np.float32)
    for cc in range(NTC):
        for t in range(128):
            am[t, cc, 2 * cc + t // 64] = 1.0 / 64.0
    c["ameans"] = am.reshape(128, NTC * 32).astype(BF)
    tp = np.zeros((128, 95), dtype=BF)
    tp[:, 47] = 1.0
    c["toep"] = tp
    tp32 = np.zeros((32, 95), dtype=BF)
    tp32[:, 47] = 1.0
    c["toep32"] = tp32
    rs = np.zeros((48, 48, 128), dtype=BF)
    for r in range(48):
        rs[r, r, :] = 1.0
    c["rowsel16"] = rs.reshape(48, 48 * 128)
    blk_seq = perm.reshape(32, 64)[:, 0] // 64
    blk_end = (blk_seq + 1) * 64 - 1
    blk_start = blk_seq * 64
    cm = np.where(own[:, None] >= blk_end[None, :], 0.0, MASKPRE).astype(np.float32)
    c["cmp_maskadd"] = np.ascontiguousarray(
        cm.reshape(8, 128, 32).transpose(1, 0, 2).reshape(128, 256)).astype(BF)
    c["cmp_maskaddT"] = np.ascontiguousarray(cm.T).astype(BF)
    cs_ = np.zeros((OT, 32), np.float32)
    cs_[np.arange(OT), np.arange(OT) // 64] = NEG_BIG
    c["cur_sup"] = np.ascontiguousarray(
        cs_.reshape(8, 128, 32).transpose(1, 0, 2).reshape(128, 256)).astype(BF)
    c["cb_min"] = np.where(blk_start[:, None] <= own[None, :], 0.0, MASKPRE).astype(BF)
    tq = np.arange(128)[None, :]
    tk = np.arange(128)[:, None]
    B1 = ((tq - tk >= 0) & (tq - tk <= 64)).astype(np.float32)
    B2 = (tk - tq >= 64).astype(np.float32)
    Z = np.zeros((128, 128), np.float32)
    gen = [B1, Z, B2] if s == 1 else [B1, B2, Z]
    sl0 = [B1, Z, B2] if s == 1 else [B1, Z, Z]
    c["swa_arr"] = np.stack([np.concatenate(sl0, 1), np.concatenate(gen, 1)],
                            1).reshape(128, 2 * 384).astype(BF)
    c["uptri"] = (-(tk > tq).astype(np.float32)).astype(BF)
    any01 = (own >= 63).astype(np.float32)
    af = np.zeros((48, OT), np.float32)
    af[0:16] = (1.0 - any01)[None, :]
    c["anyfix"] = af.astype(BF)
    return c


def _tile2(w, bk0, bk1, order="km"):
    K_, M_ = w.shape
    t = w.reshape(K_ // bk0, bk0, M_ // bk1, bk1).transpose(0, 2, 1, 3)
    if order == "mk":
        t = t.transpose(1, 0, 2, 3)
    return np.ascontiguousarray(t)


def _stage(ins):
    x = np.asarray(ins["L0_attn_x"], dtype=np.float32)
    shared = {
        "wq": _tile2(np.asarray(ins["L0_attn_Wq"], np.float32), 128, 128, "mk"),
        "wk": _tile2(np.asarray(ins["L0_attn_Wk"], np.float32), 128, 128, "mk"),
        "wv": _tile2(np.asarray(ins["L0_attn_Wv"], np.float32), 128, 512, "mk"),
        "wo": _tile2(np.asarray(ins["L0_attn_Wo"], np.float32), 128, 128, "mk"),
        "wg": np.ascontiguousarray(np.asarray(ins["L0_attn_Wg"], np.float32).reshape(NDC, 128, 48)),
        "wu0": _tile2(np.asarray(ins["L0_ffn_w_gating"], np.float32)[0], 128, 128, "mk"),
        "wu1": _tile2(np.asarray(ins["L0_ffn_w_gating"], np.float32)[1], 128, 128, "mk"),
        "wl": _tile2(np.asarray(ins["L0_ffn_w_linear"], np.float32), 128, 128, "mk"),
        "wmod": _tile2(np.asarray(ins["L0_wmod"], np.float32), 128, 512, "mk"),
    }
    pc = [_consts_for_parity(s) for s in (0, 1)]
    maps = []
    for core in range(8):
        b, s = core // 2, core % 2
        m = dict(shared)
        m.update(pc[s])
        m["xT"] = np.ascontiguousarray(x[b].T[:, _perm_ids(s)])
        m["cT"] = np.ascontiguousarray(np.asarray(ins["L0_c"], np.float32)[b:b + 1].T)
        maps.append(m)
    return maps


class Sem:
    __slots__ = ("h", "n")

    def __init__(self, nc, name):
        self.h = nc.alloc_semaphore(name)
        self.n = 0


class DSem:

    def __init__(self, nc, name, depth=2):
        self.s = [Sem(nc, f"{name}_{k}") for k in range(depth)]
        self.depth = depth
        self.tot = {}
        self.uses = 0

    def handle(self, i):
        return self.s[i % self.depth].h

    def issued(self, i, k=16):
        sub = self.s[i % self.depth]
        sub.n += k
        self.tot[i] = sub.n
        self.uses = max(self.uses, i + 1)

    def wait(self, e, i):
        if i < 0:
            return
        assert i in self.tot, (i, sorted(self.tot)[:4])
        e.wait_ge(self.s[i % self.depth].h, self.tot[i])

    def wait_all(self, e):
        for sub in self.s:
            if sub.n:
                e.wait_ge(sub.h, sub.n)


def build_nc(taps=()):
    nc = bass.Bass("TRN2", target_bir_lowering=False)
    OPS = {e: [] for e in ("sync", "gpsimd", "tensor", "vector", "scalar")}

    def on(e):
        def dec(fn):
            OPS[e].append(fn)
            return fn
        return dec

    def din(name, shape, dt=F32):
        return nc.dram_tensor(name, list(shape), dt, kind="ExternalInput").ap()

    xT = din("xT", (D, NT))
    wq = din("wq", (NDC, NDC, 128, 128))
    wk = din("wk", (NDC, NDC, 128, 128))
    wv = din("wv", (4, NDC, 128, 512))
    wo = din("wo", (NDC, NDC, 128, 128))
    wg = din("wg", (NDC, 128, 48))
    wu0 = din("wu0", (64, NDC, 128, 128))
    wu1 = din("wu1", (64, NDC, 128, 128))
    wl = din("wl", (NDC, 64, 128, 128))
    wmod = din("wmod", (24, 16, 128, 512))
    cT = din("cT", (D, 1))
    cosk_c = din("cosk", (128, NT), BF16)
    sink_c = din("sink", (128, NT), BF16)
    ident_c = din("ident", (128, 128), BF16)
    ones_c = din("ones_lhs", (128, 128), BF16)
    eexp_c = din("eexp", (32, NT), BF16)
    ameans_c = din("ameans", (128, NTC * 32), BF16)
    toep_c = din("toep", (128, 95), BF16)
    toep32_c = din("toep32", (32, 95), BF16)
    rowsel_c = din("rowsel16", (48, 48 * 128), BF16)
    cmpma_c = din("cmp_maskadd", (128, 256), BF16)
    cmpmaT_c = din("cmp_maskaddT", (32, OT), BF16)
    cursup_c = din("cur_sup", (128, 256), BF16)
    cbmin_c = din("cb_min", (32, OT), BF16)
    swaarr_c = din("swa_arr", (128, 2 * 384), BF16)
    uptri_c = din("uptri", (128, 128), BF16)
    anyfix_c = din("anyfix", (48, OT), BF16)

    outT = nc.dram_tensor("outT", [D, OT], F32, kind="ExternalOutput").ap()
    kdram = nc.dram_tensor("kdram", [H, 128, NT], BF16,
                           kind="ExternalOutput" if "k" in taps else "Internal").ap()
    qdram = nc.dram_tensor("qdram", [H, 128, OT], BF16,
                           kind="ExternalOutput" if "q" in taps else "Internal").ap()
    vdram = nc.dram_tensor("vdram", [NTC, 4, 128, 512], BF16,
                           kind="ExternalOutput" if "v" in taps else "Internal").ap()
    tap_t = {}
    tap_shapes = {
        "mod": ([1, 6 * D], F32), "h": ([128, NDC, NT], BF16),
        "g": ([48, OT], F32), "kcmp": ([128, H, 32], BF16),
        "vcmp": ([32, H, 128], BF16), "imp": ([128, H * 256], BF16),
        "maskT": ([32, H, OT], BF16), "pcmpT": ([32, H, OT], BF16),
        "o": ([128, H, OT], BF16), "x2": ([128, NDC, OT], F32),
        "h2": ([128, NDC, OT], BF16), "act": ([128, 16, OT], BF16),
        "coef": ([16, H, 2, 3, 512], BF16),
    }
    for t in taps:
        if t in ("k", "q", "v"):
            continue
        sh, dt = tap_shapes[t]
        tap_t[t] = nc.dram_tensor("tap_" + t, sh, dt, kind="ExternalOutput").ap()

    names = [
        "c_ld", "c2_ld", "wg_ld", "silu_rdy", "mod_ps", "mod_evac",
        "modt2", "mod_rdy",
        "sq_rdy", "sq_free", "stats_ps", "ln_mid", "h_rdy",
        "qk_ps", "kraw_rdy", "rope_out",
        "g_ps", "g_rdy", "v_ps", "v_evac",
        "kc_rdy", "vc_ps", "vc_rdy",
        "scmp_ps", "cm_rdy", "esc_rdy", "imp_init", "topk_done",
        "krfree_a", "qrfree_a", "vrfree_a",
        "krfree_b", "qrfree_b", "vrfree_b",
        "tr_ps", "mt_rdy", "scT_ps", "scT_f", "pcT_rdy",
        "sc_ps", "px", "pvstep", "corr_i", "swa_ps", "swa_pv", "pswa_e", "pswa_m",
        "pv_done", "den_done", "coef_u", "coef_r", "coef_rdy",
        "rep_ps", "rep_ev", "cmb",
        "wo_ps", "x2_rdy",
        "x2b_rdy", "sq2_rdy", "sq2_free", "st2_ps", "ln2_mid", "h2_rdy", "x2_sp",
        "up_ps", "gel_rdy", "act_rdy", "dn_ps", "stg_rdy",
        "tap_st", "c3_ld",
    ]
    S = {n: Sem(nc, n) for n in names}
    DS = {n: DSem(nc, n) for n in (
        "wva", "xw", "wqk", "kqw", "vst", "modch", "kr", "qr", "vr")}
    DS["wmd"] = DSem(nc, "wmd", depth=4)
    DS["kacc"] = DSem(nc, "kacc", depth=4)
    DSAL = {"wmod": "wmd", "wvb": "wva", "wul": "wva",
            "xol": "vst",
            "wol": "wqk", "wll": "wqk", "acc": "xw",
            "kswp": "kacc", "kqst": "kqw",
            "kra": "kr", "krb": "kr", "qra": "qr", "qrb": "qr",
            "vra": "vr", "vrb": "vr"}
    for _a, _b in DSAL.items():
        DS[_a] = DS[_b]

    CHN = {"vector": [Sem(nc, "chv"), 0], "scalar": [Sem(nc, "chs"), 0]}

    def ci(instr, eng="vector"):
        CHN[eng][1] += 1
        return instr.then_inc(CHN[eng][0].h, 1)

    def cw(e, eng="vector"):
        e.wait_ge(CHN[eng][0].h, CHN[eng][1])

    sbt = nc.alloc_sbuf_tensor
    ident = sbt("ident_sb", [128, 128], BF16)
    ones = sbt("ones_sb", [128, 128], BF16)
    uptri = sbt("uptri_sb", [128, 128], BF16)
    modv = sbt("modv", [128, 96], F32)
    epsv = sbt("epsv", [128, 1], F32)
    clsc = sbt("clsc", [128, 48], F32)
    mrep = sbt("mrep", [128, 512], BF16)
    varrep = sbt("varrep", [128, 512], F32)
    rstdrep = sbt("rstdrep", [128, 512], BF16)
    sqring = sbt("sqring", [128, 2, 512], BF16)
    esc = sbt("esc", [128, 256], BF16)
    cmadd = sbt("cmadd", [128, 256], F32)
    mxb = sbt("mxb", [128, 128], BF16)
    ropeA = sbt("ropeA", [128, 512], BF16)
    scrB = sbt("scrB", [128, 512], BF16)
    cT_sb = sbt("cT_sb", [128, 16, 1], F32)
    siluT = sbt("siluT", [128, 16, 1], BF16)
    lnv = sbt("lnv", [128, 512], F32)
    onepc = sbt("onepc", [128, 32], F32)
    wg_sb = sbt("wg_sb", [128, NDC, 48], BF16)
    gT = sbt("gT", [48, OT], F32)
    kcmpall = sbt("kcmpall", [128, H, 32], BF16)
    kcf = sbt("kcf", [128, 32], F32)
    modrow_dr = nc.dram_tensor("modrow_dr", [1, 6 * D], F32, kind="Internal").ap()

    AMAX = 186 * 1024
    AR = sbt("ARENA", [128, AMAX], mybir.dt.uint8)
    spans = []

    def V_(name, off_kb, shape, dt, ph):
        off = int(off_kb * 1024)
        per = int(np.prod(shape[1:])) * mybir.dt.size(dt)
        assert off % 256 == 0 and off + per <= AMAX, (name, off, per, AMAX)
        for (n2, o2, p2, ph2) in spans:
            if not (off >= o2 + p2 or o2 >= off + per or ph[1] < ph2[0] or ph2[1] < ph[0]):
                raise AssertionError(f"overlap {name} vs {n2}")
        spans.append((name, off, per, ph))
        ap = AR[0:shape[0], off:off + per].bitcast(dt)
        if len(shape) > 2:
            pat = "p (" + " ".join(f"d{i}" for i in range(1, len(shape))) + ") -> p " + \
                  " ".join(f"d{i}" for i in range(1, len(shape)))
            kw = {f"d{i}": shape[i] for i in range(1, len(shape))}
            ap = ap.rearrange(pat, **kw)
        return ap

    xwin = V_("xwin", 0, [128, 2, NDC, 512], BF16, (0, 0))
    modrow = V_("modrow", 44, [1, 2, 512], F32, (0, 0))
    wmod_sb = V_("wmod", 72, [128, 4, 2, 512], BF16, (0, 1))
    cosk_sb = V_("cosk", 36, [128, NT], BF16, (0, 1))
    sink_sb = V_("sink", 40, [128, NT], BF16, (0, 1))

    wqk_sb = V_("wqk", 0, [128, 2, NDC, 128], BF16, (1, 1))
    wvb_sb = V_("wvb", 8, [128, 2, 512], BF16, (1, 1))
    kraw = V_("kraw", 10, [128, 4, 512], BF16, (1, 1))
    kswp = V_("kswp", 14, [128, 4, 512], BF16, (1, 1))
    kqst = V_("kqst", 18, [128, 2, 2048], BF16, (1, 1))
    vstg = V_("vstg", 26, [128, 2, 512], BF16, (1, 1))

    kr2 = V_("kr2", 0, [128, 2, NT], BF16, (2, 3))
    vr2 = V_("vr2", 8, [128, 2, NTC, 128], BF16, (2, 3))
    qr2 = V_("qr2", 16, [128, 2, OT], BF16, (2, 3))
    vcmpall = V_("vcmpall", 20, [32, H, 128], BF16, (2, 3))
    impa = V_("impa", 24, [128, H * 256], BF16, (2, 3))
    impb = V_("impb", 32, [128, H * 256], BF16, (2, 3))
    selb = V_("selb", 40, [128, H * 256], BF16, (2, 2))
    eexp = V_("eexp", 48, [32, NT], BF16, (2, 3))
    ameans = V_("ameans", 52, [128, NTC * 32], BF16, (2, 2))
    toep = V_("toep", 53, [128, 95], BF16, (2, 3))
    toep32 = V_("toep32", 53.25, [32, 95], BF16, (2, 3))
    rowsel = V_("rowsel", 130, [48, 48 * 128], BF16, (2, 3))
    cmpma = V_("cmpma", 57.5, [128, 256], BF16, (2, 2))
    cmpmaT = V_("cmpmaT", 58, [32, OT], BF16, (2, 3))
    cursup = V_("cursup", 60, [128, 256], BF16, (2, 2))
    cbmin = V_("cbmin", 60.5, [32, OT], BF16, (2, 3))
    swaarr = V_("swaarr", 62.5, [128, 2, 384], BF16, (2, 3))
    anyfix = V_("anyfix", 64, [48, OT], BF16, (2, 3))
    pring = V_("pring", 66, [128, 3, 512], BF16, (3, 3))
    pswa = V_("pswa", 69, [128, 2, 384], BF16, (3, 3))
    pswa2 = V_("pswa2", 70.5, [128, 2, 384], BF16, (3, 3))
    pcmpT = V_("pcmpT", 72, [32, 2, OT], BF16, (3, 3))
    maskT = V_("maskT", 76, [32, 2, OT], BF16, (3, 3))
    mtmp = V_("mtmp", 80, [32, OT], BF16, (3, 3))
    scmpTf = V_("scmpTf", 82, [32, 512], F32, (3, 3))
    coefb = V_("coefb", 84, [48, 2, 512], BF16, (3, 3))
    crep = V_("crep", 90, [128, 3, 512], BF16, (3, 3))
    csc = V_("csc", 93, [128, 3, 512], BF16, (3, 3))
    corrsc = V_("corrsc", 96, [128, 2, 128], BF16, (3, 3))

    oT = V_("oT", 97, [128, H, OT], BF16, (3, 4))
    hT = V_("hT", 122, [128, NDC, NT], BF16, (0, 1))
    xo = V_("xo", 0, [128, 2, OT], F32, (4, 4))
    wo_sb = V_("wo_sb", 8, [128, 2, NDC, 128], BF16, (4, 4))
    x2 = V_("x2", 16, [128, NDC, OT], F32, (4, 5))
    x2b_lo = V_("x2b_lo", 80, [128, 8, OT], BF16, (4, 5))
    x2b_hi = V_("x2b_hi", 144, [128, 8, OT], BF16, (4, 5))

    def x2b(dc):
        return (x2b_lo if dc < 8 else x2b_hi)[:, dc % 8, :]
    h2sb = V_("h2sb", 112, [128, NDC, OT], BF16, (5, 6))
    act = V_("act", 144, [128, 16, OT], BF16, (6, 6))
    actB = V_("actB", 80, [128, 16, OT], BF16, (6, 6))
    wu_sb = V_("wu_sb", 16, [128, 2, 2, NDC, 128], BF16, (6, 6))
    wl_sb = V_("wl_sb", 32, [128, 2, 16, 128], BF16, (6, 6))
    stage = V_("stage", 40, [128, 2, OT], F32, (6, 6))
    gel_sb = V_("gel", 176, [128, 2, 512], BF16, (6, 6))

    pb = [nc.alloc_psum_tensor(f"pb{i}", [128, 512], F32) for i in range(8)]
    pb0bf = pb[0][0:32, :].bitcast(BF16)

    const_pairs = [
        (ident[:], ident_c), (ones[:], ones_c), (uptri[:], uptri_c),
        (cosk_sb, cosk_c), (sink_sb, sink_c),
    ]
    const_pairs2 = [
        (eexp, eexp_c), (ameans, ameans_c), (toep, toep_c), (toep32, toep32_c),
        (cmpma, cmpma_c), (cmpmaT, cmpmaT_c),
        (cursup, cursup_c), (cbmin, cbmin_c),
        (swaarr.rearrange("p a n -> p (a n)"), swaarr_c), (anyfix, anyfix_c),
    ]

    @on("sync")
    def _(e):
        for dst, src in const_pairs:
            e.dma_start(out=dst, in_=src).then_inc(S["c_ld"].h, 16)
        e.dma_start(out=cT_sb[:], in_=cT.rearrange("(c p) b -> p c b", p=128),
                    allow_slow_non_contiguous=True).then_inc(S["c_ld"].h, 16)
    S["c_ld"].n += 16 * (len(const_pairs) + 1)
    C1 = S["c_ld"].n

    def wmod_load(e, jc0, jc1):
        for jc in range(jc0, jc1):
            for part in range(8):
                i = jc * 8 + part
                if i >= 4:
                    e.wait_ge(S["mod_ps"].h, i - 3)
                e.dma_start(out=wmod_sb[:, i % 4],
                            in_=wmod[jc, part * 2:part * 2 + 2].rearrange("c p n -> p c n")
                            ).then_inc(DS["wmod"].handle(i), 16)

    @on("gpsimd")
    def _(e):
        e.dma_start(out=wg_sb[:], in_=wg.rearrange("c p n -> p c n")).then_inc(S["wg_ld"].h, 16)
        wmod_load(e, 0, 8)
    S["wg_ld"].n += 16
    for _i in range(192):
        DS["wmd"].issued(_i)

    @on("scalar")
    def _(e):
        e.wait_ge(S["c_ld"].h, C1)
        e.activation(siluT[:], cT_sb[:], AF.Silu).then_inc(S["silu_rdy"].h, 1)
    S["silu_rdy"].n += 1

    def gemv(e, jc0, jc1):
        for jc in range(jc0, jc1):
            if jc >= 1:
                e.wait_ge(S["mod_evac"].h, jc)
            for part in range(8):
                i = jc * 8 + part
                DS["wmod"].wait(e, i)
                e.matmul(pb[6][0:1, :], siluT[:, part * 2, :],
                         wmod_sb[:, i % 4, 0, :],
                         start=(part == 0), stop=False)
                e.matmul(pb[6][0:1, :], siluT[:, part * 2 + 1, :],
                         wmod_sb[:, i % 4, 1, :],
                         start=False, stop=(part == 7)).then_inc(S["mod_ps"].h, 1)

    @on("tensor")
    def _(e):
        e.wait_ge(S["silu_rdy"].h, 1)
        gemv(e, 0, 8)
    S["mod_ps"].n += 192

    def mod_evac(e, jc0, jc1):
        for jc in range(jc0, jc1):
            e.wait_ge(S["mod_ps"].h, (jc + 1) * 8)
            DS["modch"].wait(e, jc - 2)
            e.tensor_copy(modrow[0:1, jc % 2, :], pb[6][0:1, :]
                          ).then_inc(S["mod_evac"].h, 1)

    @on("vector")
    def _(e):
        mod_evac(e, 0, 8)
    S["mod_evac"].n += 24

    def mod_store(e, jc0, jc1):
        for jc in range(jc0, jc1):
            e.wait_ge(S["mod_evac"].h, jc + 1)
            e.dma_start(out=modrow_dr[0:1, jc * 512:(jc + 1) * 512],
                        in_=modrow[0:1, jc % 2, :]).then_inc(DS["modch"].handle(jc), 16)

    @on("sync")
    def _(e):
        mod_store(e, 0, 8)
        for _j in range(8):
            DS["modch"].wait(e, _j)
        e.dma_start(out=modv[:, 0:32],
                    in_=modrow_dr[0:1, 0:4096].rearrange("o (m p) -> (o p) m", p=128),
                    allow_slow_non_contiguous=True).then_inc(S["modt2"].h, 16)
    S["modt2"].n += 32
    for _i in range(24):
        DS["modch"].issued(_i)

    @on("vector")
    def _(e):
        e.wait_ge(S["modt2"].h, 16)
        ci(e.tensor_scalar(onepc[:, 0:16], modv[:, 16:32], 1.0, None, ALU.add))
        e.memset(epsv[:], EPS)
        e.engine_nop().then_inc(S["mod_rdy"].h, 1)
    S["mod_rdy"].n += 2

    @on("gpsimd")
    def _(e):
        for w in range(4):
            if w >= 2:
                e.wait_ge(S["h_rdy"].h, (w - 1) * NDC)
            e.dma_start(out=xwin[:, w % 2],
                        in_=xT.rearrange("(c p) t -> p c t", p=128)[:, :, w * 512:(w + 1) * 512]
                        ).then_inc(DS["xw"].handle(w), 16)

    for _i in range(4):
        DS["xw"].issued(_i)

    @on("scalar")
    def _(e):
        for w in range(4):
            DS["xw"].wait(e, w)
            for dc in range(NDC):
                i = w * NDC + dc
                if i >= 2:
                    e.wait_ge(S["sq_free"].h, i - 1)
                e.activation(sqring[:, i % 2, :], xwin[:, w % 2, dc, :], AF.Square
                             ).then_inc(S["sq_rdy"].h, 1)
            e.wait_ge(S["ln_mid"].h, 2 * w + 1)
            ci(e.activation(lnv[:], varrep[:], AF.Ln, bias=epsv[:]), "scalar")
            cw(e, "scalar")
            e.activation(rstdrep[:], lnv[:], AF.Exp, scale=-0.5).then_inc(S["ln_mid"].h, 1)
    S["sq_rdy"].n += 64

    @on("tensor")
    def _(e):
        e.wait_ge(S["c_ld"].h, C1)
        for w in range(4):
            DS["xw"].wait(e, w)
            if w >= 1:
                e.wait_ge(S["ln_mid"].h, 2 * w - 1)
            for dc in range(NDC):
                i = w * NDC + dc
                mm0 = e.matmul(pb[0][:], ones[:], xwin[:, w % 2, dc, :],
                               start=(dc == 0), stop=(dc == NDC - 1))
                if dc == NDC - 1:
                    mm0.then_inc(S["stats_ps"].h, 1)
                e.wait_ge(S["sq_rdy"].h, i + 1)
                e.matmul(pb[1][:], ones[:], sqring[:, i % 2, :],
                         start=(dc == 0), stop=(dc == NDC - 1)).then_inc(S["sq_free"].h, 1)
    S["stats_ps"].n += 4
    S["sq_free"].n += 64

    @on("vector")
    def _(e):
        e.wait_ge(S["mod_rdy"].h, 1)
        for w in range(4):
            e.wait_ge(S["stats_ps"].h, w + 1)
            e.wait_ge(S["sq_free"].h, 16 * (w + 1))
            ci(e.tensor_scalar(mrep[:], pb[0][:], 1.0 / D, None, ALU.mult))
            cw(e)
            ci(e.tensor_tensor(lnv[:], mrep[:], mrep[:], ALU.mult))
            cw(e)
            e.scalar_tensor_tensor(varrep[:], pb[1][:], 1.0 / D, lnv[:],
                                   ALU.mult, ALU.subtract).then_inc(S["ln_mid"].h, 1)
            e.wait_ge(S["ln_mid"].h, 2 * w + 2)
            for dc in range(NDC):
                ci(e.tensor_tensor(ropeA[:], xwin[:, w % 2, dc, :], mrep[:], ALU.subtract))
                cw(e)
                ci(e.tensor_tensor(scrB[:], ropeA[:], rstdrep[:], ALU.mult))
                cw(e)
                e.tensor_scalar(hT[:, dc, w * 512:(w + 1) * 512], scrB[:],
                                onepc[:, dc:dc + 1], modv[:, dc:dc + 1],
                                ALU.mult, ALU.add).then_inc(S["h_rdy"].h, 1)
    S["ln_mid"].n = 8
    S["h_rdy"].n += 64
    H_ALL = S["h_rdy"].n

    if "h" in tap_t:
        @on("sync")
        def _(e):
            e.wait_ge(S["h_rdy"].h, H_ALL)
            e.dma_start(out=tap_t["h"], in_=hT[:]).then_inc(S["tap_st"].h, 16)
        S["tap_st"].n += 16

    @on("sync")
    def _(e):
        for dst, src in const_pairs2:
            e.dma_start(out=dst, in_=src).then_inc(S["c2_ld"].h, 16)
    S["c2_ld"].n += 16 * len(const_pairs2)
    C2 = S["c2_ld"].n

    @on("gpsimd")
    def _(e):
        wmod_load(e, 8, 16)

    @on("tensor")
    def _(e):
        gemv(e, 8, 16)

    @on("vector")
    def _(e):
        mod_evac(e, 8, 16)

    @on("sync")
    def _(e):
        mod_store(e, 8, 16)
    def emit_qk(proj):
        wsrc, ddst, nwin = (wq, qdram, 2) if proj == 0 else (wk, kdram, 4)
        ps0 = S["qk_ps"].n
        raw0, out0 = S["kraw_rdy"].n, S["rope_out"].n
        wq0, sw0, st0 = DS["wqk"].uses, DS["kswp"].uses, DS["kqst"].uses
        NTOT = NDC * nwin

        @on("gpsimd")
        def _(e):
            for ec in range(NDC):
                if ec >= 2:
                    e.wait_ge(S["qk_ps"].h, ps0 + (ec - 1) * nwin)
                elif ps0 > 0:
                    e.wait_ge(S["qk_ps"].h, ps0)
                else:
                    e.wait_ge(S["h_rdy"].h, H_ALL)
                e.dma_start(out=wqk_sb[:, ec % 2],
                            in_=wsrc[ec].rearrange("c p n -> p c n")
                            ).then_inc(DS["wqk"].handle(wq0 + ec), 16)

        for _i in range(NDC):
            DS["wqk"].issued(wq0 + _i)

        @on("tensor")
        def _(e):
            if proj == 0:
                e.wait_ge(S["h_rdy"].h, H_ALL)
            for ec in range(NDC):
                DS["wqk"].wait(e, wq0 + ec)
                for w in range(nwin):
                    i = ec * nwin + w
                    if raw0 + i >= 6:
                        e.wait_ge(S["kraw_rdy"].h, raw0 + i - 5)
                    for dc in range(NDC):
                        mm = e.matmul(pb[(raw0 + i) % 6][:], wqk_sb[:, ec % 2, dc, :],
                                      hT[:, dc, w * 512:(w + 1) * 512],
                                      start=(dc == 0), stop=(dc == NDC - 1))
                    mm.then_inc(S["qk_ps"].h, 1)
        S["qk_ps"].n += NTOT

        @on("scalar")
        def _(e):
            for i in range(NTOT):
                e.wait_ge(S["kraw_rdy"].h, raw0 + i + 1)
                if i >= 4:
                    e.wait_ge(S["rope_out"].h, out0 + i - 3)
                e.dma_start(out=kswp[0:64, (raw0 + i) % 4, :], in_=kraw[64:128, (raw0 + i) % 4, :]
                            ).then_inc(DS["kswp"].handle(sw0 + i), 16)
                e.dma_start(out=kswp[64:128, (raw0 + i) % 4, :], in_=kraw[0:64, (raw0 + i) % 4, :]
                            ).then_inc(DS["kswp"].handle(sw0 + i), 16)

        for _i in range(NTOT):
            DS["kacc"].issued(sw0 + _i, 32)

        @on("vector")
        def _(e):
            def rope(j):
                ec, w = j // nwin, j % nwin
                DS["kswp"].wait(e, sw0 + j)
                e.wait_ge(S["rope_out"].h, out0 + j)
                if st0 + ec >= 2 and w == 0:
                    DS["kqst"].wait(e, st0 + ec - 2)
                ci(e.tensor_tensor(ropeA[:], kraw[:, (raw0 + j) % 4, :],
                                   cosk_sb[:, w * 512:(w + 1) * 512], ALU.mult))
                ci(e.tensor_tensor(vstg[:, j % 2, :], kswp[:, (raw0 + j) % 4, :],
                                   sink_sb[:, w * 512:(w + 1) * 512], ALU.mult))
                cw(e)
                e.tensor_tensor(kqst[:, ec % 2, w * 512:(w + 1) * 512],
                                ropeA[:], vstg[:, j % 2, :], ALU.add
                                ).then_inc(S["rope_out"].h, 1)
            for i in range(NTOT):
                e.wait_ge(S["qk_ps"].h, ps0 + i + 1)
                DS["kswp"].wait(e, sw0 + i - 4)
                e.tensor_copy(kraw[:, (raw0 + i) % 4, :], pb[(raw0 + i) % 6][:]
                              ).then_inc(S["kraw_rdy"].h, 1)
                if i >= 3:
                    rope(i - 3)
            for j in range(NTOT - 3, NTOT):
                rope(j)
        S["kraw_rdy"].n += NTOT
        S["rope_out"].n += NTOT

        @on("sync")
        def _(e):
            for ec in range(NDC):
                e.wait_ge(S["rope_out"].h, out0 + (ec + 1) * nwin)
                e.dma_start(out=ddst[ec], in_=kqst[:, ec % 2, 0:nwin * 512]
                            ).then_inc(DS["kqst"].handle(st0 + ec), 16)

        for _i in range(NDC):
            DS["kqw"].issued(st0 + _i)

    emit_qk(0)

    @on("gpsimd")
    def _(e):
        wmod_load(e, 16, 24)

    @on("tensor")
    def _(e):
        gemv(e, 16, 24)

    @on("vector")
    def _(e):
        mod_evac(e, 16, 24)
        e.wait_ge(S["modt2"].h, 32)
        for c0 in (32, 80):
            sl = modv[:, c0:c0 + 16]
            ci(e.tensor_scalar(clsc[:, 0:16], sl, 0.0, 2.0, ALU.is_ge, ALU.mult))
            cw(e)
            ci(e.tensor_scalar(clsc[:, 16:32], clsc[:, 0:16], 1.0, None, ALU.subtract))
            cw(e)
            ci(e.tensor_tensor(clsc[:, 0:16], sl, clsc[:, 16:32], ALU.mult))
            cw(e)
            ci(e.tensor_scalar(clsc[:, 32:48], clsc[:, 0:16], 0.25, None, ALU.max))
            cw(e)
            ci(e.tensor_tensor(sl, clsc[:, 16:32], clsc[:, 32:48], ALU.mult))
            cw(e)
        ci(e.tensor_scalar(onepc[:, 16:32], modv[:, 64:80], 1.0, None, ALU.add))
        e.engine_nop().then_inc(S["mod_rdy"].h, 1)

    @on("sync")
    def _(e):
        mod_store(e, 16, 24)
        for _j in range(8, 24):
            DS["modch"].wait(e, _j)
        e.dma_start(out=modv[:, 32:96],
                    in_=modrow_dr[0:1, 4096:12288].rearrange("o (m p) -> (o p) m", p=128),
                    allow_slow_non_contiguous=True).then_inc(S["modt2"].h, 16)

    if "mod" in tap_t:
        @on("sync")
        def _(e):
            e.wait_ge(S["mod_rdy"].h, 2)
            e.dma_start(out=tap_t["mod"], in_=modrow_dr).then_inc(S["tap_st"].h, 16)
        S["tap_st"].n += 16

    emit_qk(1)
    S_KRAW_ALL = S["kraw_rdy"].n

    @on("tensor")
    def _(e):
        e.wait_ge(S["wg_ld"].h, 16)
        e.wait_ge(S["mod_evac"].h, 24)
        for w in range(2):
            if w >= 1:
                e.wait_ge(S["g_rdy"].h, w)
            for dc in range(NDC):
                mm = e.matmul(pb[6][0:48, :], wg_sb[:, dc, :],
                              hT[:, dc, w * 512:(w + 1) * 512],
                              start=(dc == 0), stop=(dc == NDC - 1))
            mm.then_inc(S["g_ps"].h, 1)
    S["g_ps"].n += 2

    @on("scalar")
    def _(e):
        for w in range(2):
            e.wait_ge(S["g_ps"].h, w + 1)
            e.activation(gT[:, w * 512:(w + 1) * 512], pb[6][0:48, :], AF.Sigmoid
                         ).then_inc(S["g_rdy"].h, 1)
    S["g_rdy"].n += 2

    if "g" in tap_t:
        @on("sync")
        def _(e):
            e.wait_ge(S["g_rdy"].h, 2)
            e.dma_start(out=tap_t["g"], in_=gT[:]).then_inc(S["tap_st"].h, 16)
        S["tap_st"].n += 16

    WVB0 = DS["wva"].uses
    for _i in range(128):
        DS["wva"].issued(WVB0 + _i)

    @on("gpsimd")
    def _(e):
        for es in range(4):
            for th in range(2):
                for dc in range(NDC):
                    i = (es * 2 + th) * NDC + dc
                    if i >= 2:
                        e.wait_ge(S["v_ps"].h, i - 1)
                    else:
                        e.wait_ge(S["h_rdy"].h, H_ALL)
                    e.dma_start(out=wvb_sb[:, i % 2, :], in_=wv[es, dc]
                                ).then_inc(DS["wvb"].handle(WVB0 + i), 16)

    @on("tensor")
    def _(e):
        e.wait_ge(S["g_rdy"].h, 2)
        e.wait_ge(S["kraw_rdy"].h, S_KRAW_ALL)
        for es in range(4):
            for th in range(2):
                u = es * 2 + th
                if u >= 1:
                    e.wait_ge(S["v_evac"].h, u * 8)
                for dc in range(NDC):
                    i = u * NDC + dc
                    DS["wvb"].wait(e, WVB0 + i)
                    for bk in range(8):
                        tc = th * 8 + bk
                        mm = e.matmul(pb[bk][:], hT[:, dc, tc * 128:(tc + 1) * 128],
                                      wvb_sb[:, i % 2, :],
                                      start=(dc == 0), stop=(dc == NDC - 1))
                    mm.then_inc(S["v_ps"].h, 1)
    S["v_ps"].n += 128

    @on("vector")
    def _(e):
        e.wait_ge(S["rope_out"].h, S["rope_out"].n)
        for es in range(4):
            for th in range(2):
                u = es * 2 + th
                e.wait_ge(S["v_ps"].h, (u + 1) * NDC)
                for bk in range(8):
                    i = u * 8 + bk
                    DS["vst"].wait(e, i - 2)
                    e.tensor_copy(vstg[:, i % 2, :], pb[bk][:]).then_inc(S["v_evac"].h, 1)
    S["v_evac"].n += 64

    @on("sync")
    def _(e):
        for es in range(4):
            for th in range(2):
                for bk in range(8):
                    i = (es * 2 + th) * 8 + bk
                    tc = th * 8 + bk
                    e.wait_ge(S["v_evac"].h, i + 1)
                    e.dma_start(out=vdram[tc, es], in_=vstg[:, i % 2, :]
                                ).then_inc(DS["vst"].handle(i), 16)

    for _i in range(64):
        DS["vst"].issued(_i)

    @on("sync")
    def _(e):
        e.wait_ge(S["v_evac"].h, 64)
        e.dma_start(out=rowsel, in_=rowsel_c).then_inc(S["c3_ld"].h, 16)
    S["c3_ld"].n += 16

    KQ_T = tuple(sub.n for sub in DS["kqst"].s)
    VST_T = tuple(sub.n for sub in DS["vst"].s)

    @on("gpsimd")
    def _(e, KQ_T=KQ_T, VST_T=VST_T):
        for _k, _sub in enumerate(DS["kqst"].s):
            if KQ_T[_k]:
                e.wait_ge(_sub.h, KQ_T[_k])
        for _k, _sub in enumerate(DS["vst"].s):
            if VST_T[_k]:
                e.wait_ge(_sub.h, VST_T[_k])
        e.wait_ge(S["c2_ld"].h, C2)
        for h in range(H):
            if h >= 2:
                e.wait_ge(S["kc_rdy"].h, h - 1)
                e.wait_ge(S["esc_rdy"].h, h - 1)
                e.wait_ge(S["vc_rdy"].h, h - 1)
            e.dma_start(out=kr2[:, h % 2, :], in_=kdram[h]).then_inc(DS["kra"].handle(h), 16)
            e.dma_start(out=qr2[:, h % 2, :], in_=qdram[h]).then_inc(DS["qra"].handle(h), 16)
            e.dma_start(out=vr2[:, h % 2],
                        in_=vdram[:, h // 4, :, (h % 4) * 128:(h % 4 + 1) * 128
                                  ].rearrange("c p n -> p c n")).then_inc(DS["vra"].handle(h), 16)

    for _h in range(H):
        DS["kr"].issued(_h)
        DS["qr"].issued(_h)
        DS["vr"].issued(_h)

    @on("tensor")
    def _(e):
        e.wait_ge(S["c2_ld"].h, C2)
        for h in range(H):
            DS["qra"].wait(e, h)
            DS["vra"].wait(e, h)
            e.wait_ge(S["kc_rdy"].h, h + 1)
            if h >= 1:
                e.wait_ge(S["cm_rdy"].h, h)
            if h >= 2:
                e.wait_ge(S["vc_rdy"].h, h - 1)
            for qb in range(8):
                mm = e.matmul(pb[0][:, qb * 32:(qb + 1) * 32],
                              qr2[:, h % 2, qb * 128:(qb + 1) * 128],
                              kcmpall[:, h, :], start=True, stop=True)
            mm.then_inc(S["scmp_ps"].h, 1)
            for c in range(NTC):
                mm = e.matmul(pb[2 + h % 2][0:32, 0:128], ameans[:, c * 32:(c + 1) * 32],
                              vr2[:, h % 2, c, :], start=(c == 0), stop=(c == NTC - 1))
            mm.then_inc(S["vc_ps"].h, 1)
    S["scmp_ps"].n += H
    S["vc_ps"].n += H

    @on("vector")
    def _(e):
        e.wait_ge(S["c2_ld"].h, C2)
        for h in range(H):
            DS["kra"].wait(e, h)
            ci(e.reduce_sum(kcf[:], kr2[:, h % 2, :].rearrange("p (b t) -> p b t", b=32),
                            axis=AX.X))
            cw(e)
            e.tensor_scalar(kcmpall[:, h, :], kcf[:], 1.0 / 64.0, None, ALU.mult
                            ).then_inc(S["kc_rdy"].h, 1)
            e.wait_ge(S["vc_ps"].h, h + 1)
            e.tensor_copy(vcmpall[:, h, :], pb[2 + h % 2][0:32, 0:128]).then_inc(S["vc_rdy"].h, 1)
            e.wait_ge(S["scmp_ps"].h, h + 1)
            e.tensor_tensor(cmadd[:], pb[0][:, 0:256], cmpma[:], ALU.add).then_inc(S["cm_rdy"].h, 1)
            e.wait_ge(S["esc_rdy"].h, h + 1)
            e.tensor_tensor(impa[:, h * 256:(h + 1) * 256], esc[:], cursup[:], ALU.add
                            ).then_inc(S["imp_init"].h, 1)
    S["kc_rdy"].n += H
    S["vc_rdy"].n += H
    S["cm_rdy"].n += H
    S["imp_init"].n += H

    @on("scalar")
    def _(e):
        for h in range(H):
            e.wait_ge(S["cm_rdy"].h, h + 1)
            e.activation(esc[:], cmadd[:], AF.Exp, scale=KAPPA).then_inc(S["esc_rdy"].h, 1)
    S["esc_rdy"].n += H

    if "imp" in tap_t:
        @on("sync")
        def _(e):
            e.wait_ge(S["imp_init"].h, H)
            e.dma_start(out=tap_t["imp"], in_=impa[:]).then_inc(S["tap_st"].h, 16)
        S["tap_st"].n += 16
    if "kcmp" in tap_t:
        @on("sync")
        def _(e):
            e.wait_ge(S["kc_rdy"].h, H)
            e.dma_start(out=tap_t["kcmp"], in_=kcmpall[:]).then_inc(S["tap_st"].h, 16)
        S["tap_st"].n += 16
    if "vcmp" in tap_t:
        @on("sync")
        def _(e):
            e.wait_ge(S["vc_rdy"].h, H)
            e.dma_start(out=tap_t["vcmp"], in_=vcmpall[:]).then_inc(S["tap_st"].h, 16)
        S["tap_st"].n += 16

    def topk_half(e, hf):
        sl = slice(hf * 2048, (hf + 1) * 2048)
        e.wait_ge(S["imp_init"].h, 8 * (hf + 1))
        for it in range(7):
            src = (impa if it % 2 == 0 else impb)[:, sl]
            dst = (impb if it % 2 == 0 else impa)[:, sl]
            ci(e.reduce_max(mxb[:, hf * 64:(hf + 1) * 64],
                            src.rearrange("p (g b) -> p g b", b=32), axis=AX.X))
            cw(e)
            ci(e.tensor_tensor(selb[:, sl].rearrange("p (g b) -> p g b", b=32),
                               src.rearrange("p (g b) -> p g b", b=32),
                               mxb[:, hf * 64:(hf + 1) * 64].unsqueeze(2)
                               .broadcast_to([128, 64, 32]), ALU.is_ge))
            cw(e)
            ci(e.scalar_tensor_tensor(dst, selb[:, sl], -SUP, src, ALU.mult, ALU.add))
            cw(e)
        e.engine_nop().then_inc(S["topk_done"].h, 1)

    @on("vector")
    def _(e):
        topk_half(e, 0)
    S["topk_done"].n += 2

    gc = [0]
    pvst = [0]
    cr = [0]
    swp_ = [0]
    rep_ = [0]
    units = [0]

    @on("gpsimd")
    def _(e):
        for h in range(H):
            e.wait_ge(S["kc_rdy"].h, H)
            e.wait_ge(S["vc_rdy"].h, H)
            e.wait_ge(S["esc_rdy"].h, H)
            if h >= 2:
                e.wait_ge(S["den_done"].h, 2 * (h - 1))
            e.dma_start(out=kr2[:, h % 2, :], in_=kdram[h]).then_inc(DS["krb"].handle(16 + h), 16)
            e.dma_start(out=qr2[:, h % 2, :], in_=qdram[h]).then_inc(DS["qrb"].handle(16 + h), 16)
            e.dma_start(out=vr2[:, h % 2],
                        in_=vdram[:, h // 4, :, (h % 4) * 128:(h % 4 + 1) * 128
                                  ].rearrange("c p n -> p c n")).then_inc(DS["vrb"].handle(16 + h), 16)

    for _h in range(H):
        DS["kr"].issued(16 + _h)
        DS["qr"].issued(16 + _h)
        DS["vr"].issued(16 + _h)

    for h in range(H):
        if h == 2:
            @on("vector")
            def _(e):
                topk_half(e, 1)
        trbase = S["tr_ps"].n
        gc_at_build = gc[0]

        @on("tensor")
        def _(e, h=h, gc0=gc_at_build):
            e.wait_ge(S["topk_done"].h, 1 if h < 8 else 2)
            if gc0 > 0:
                e.wait_ge(S["px"].h, gc0)
            if h > 0:
                e.wait_ge(S["mt_rdy"].h, h)
            for qt in range(8):
                mm = e.transpose(pb0bf[:, qt * 128:(qt + 1) * 128],
                                 impb[:, h * 256 + qt * 32:h * 256 + (qt + 1) * 32], ident[:])
            mm.then_inc(S["tr_ps"].h, 1)
        S["tr_ps"].n += 1

        @on("vector")
        def _(e, h=h):
            e.wait_ge(S["tr_ps"].h, h + 1)
            ci(e.tensor_scalar(mtmp[:], pb0bf[:], SEL_THR, MASKPRE, ALU.is_ge, ALU.mult))
            cw(e)
            e.tensor_tensor(maskT[:, h % 2, :], mtmp[:], cbmin[:], ALU.min
                            ).then_inc(S["mt_rdy"].h, 1)
        S["mt_rdy"].n += 1

        @on("tensor")
        def _(e, h=h):
            e.wait_ge(S["mt_rdy"].h, h + 1)
            DS["qrb"].wait(e, 16 + h)
            if h > 0:
                e.wait_ge(S["scT_f"].h, 2 * h)
            for g in range(2):
                e.matmul(pb[g][0:32, :], kcmpall[:, h, :], qr2[:, h % 2, g * 512:(g + 1) * 512],
                         start=True, stop=True).then_inc(S["scT_ps"].h, 1)
        S["scT_ps"].n += 2

        @on("vector")
        def _(e, h=h):
            for g in range(2):
                e.wait_ge(S["scT_ps"].h, 2 * h + g + 1)
                if 2 * h + g >= 1:
                    e.wait_ge(S["pcT_rdy"].h, 2 * h + g)
                e.tensor_tensor(scmpTf[:], pb[g][0:32, :], cmpmaT[:, g * 512:(g + 1) * 512],
                                ALU.add).then_inc(S["scT_f"].h, 1)
        S["scT_f"].n += 2

        @on("scalar")
        def _(e, h=h):
            for g in range(2):
                e.wait_ge(S["scT_f"].h, 2 * h + g + 1)
                e.activation(pcmpT[:, h % 2, g * 512:(g + 1) * 512], scmpTf[:],
                             AF.Exp, scale=KAPPA).then_inc(S["pcT_rdy"].h, 1)
        S["pcT_rdy"].n += 2

        if "maskT" in tap_t:
            @on("sync")
            def _(e, h=h):
                e.wait_ge(S["mt_rdy"].h, h + 1)
                e.dma_start(out=tap_t["maskT"][:, h, :], in_=maskT[:, h % 2, :]
                            ).then_inc(S["tap_st"].h, 16)
            S["tap_st"].n += 16
        if "pcmpT" in tap_t:
            @on("sync")
            def _(e, h=h):
                e.wait_ge(S["pcT_rdy"].h, 2 * h + 2)
                e.dma_start(out=tap_t["pcmpT"][:, h, :], in_=pcmpT[:, h % 2, :]
                            ).then_inc(S["tap_st"].h, 16)
            S["tap_st"].n += 16

        for g in range(2):
            ECH = 4 * g + 4
            chunks = list(range(ECH)) + list(range(8, 8 + ECH))
            dpar = (2 * h + g) % 2
            unit = 2 * h + g
            gbase = gc[0]
            pvbase = pvst[0]
            denb = pb[5 + dpar]

            @on("tensor")
            def _(e, h=h, g=g, chunks=tuple(chunks), gbase=gbase, unit=unit,
                  dpar=dpar, denb=denb, pvbase=pvbase, crbase=cr[0]):
                DS["krb"].wait(e, 16 + h)
                DS["vrb"].wait(e, 16 + h)
                e.wait_ge(S["mt_rdy"].h, h + 1)
                e.wait_ge(S["scT_f"].h, 2 * (h + 1))
                ci = [0]

                def pvden(j):
                    c2 = chunks[j]
                    if j == 0:
                        if unit >= 1:
                            e.wait_ge(S["cmb"].h, unit)
                        if unit >= 2:
                            e.wait_ge(S["coef_u"].h, unit - 1)
                    e.wait_ge(S["px"].h, gbase + j + 1)
                    e.matmul(pb[2][:], vr2[:, h % 2, c2, :], pring[:, (gbase + j) % 3, :],
                             start=(j == 0), stop=(j == len(chunks) - 1))
                    mm = e.matmul(denb[0:48, :], toep[:, 31 - h:79 - h],
                                  pring[:, (gbase + j) % 3, :],
                                  start=(j == 0), stop=False)
                    if 4 * g <= c2 < 4 * g + 4:
                        jl = c2 - 4 * g
                        e.wait_ge(S["corr_i"].h, crbase + ci[0] + 1)
                        e.matmul(pb[2][:, jl * 128:(jl + 1) * 128], vr2[:, h % 2, c2, :],
                                 corrsc[:, ci[0] % 2, :], start=False, stop=False)
                        mm = e.matmul(denb[0:48, jl * 128:(jl + 1) * 128],
                                      toep[:, 31 - h:79 - h], corrsc[:, ci[0] % 2, :],
                                      start=False, stop=False)
                        ci[0] += 1
                    mm.then_inc(S["pvstep"].h, 1)

                for i, c in enumerate(chunks):
                    if gbase + i >= 2:
                        e.wait_ge(S["px"].h, gbase + i - 1)
                    e.matmul(pb[i % 2][:], kr2[:, h % 2, c * 128:(c + 1) * 128],
                             qr2[:, h % 2, g * 512:(g + 1) * 512], start=True, stop=False)
                    e.matmul(pb[i % 2][:], eexp[:, c * 128:(c + 1) * 128],
                             maskT[:, h % 2, g * 512:(g + 1) * 512], start=False, stop=True
                             ).then_inc(S["sc_ps"].h, 1)
                    if i >= 1:
                        pvden(i - 1)
                pvden(len(chunks) - 1)
            S["sc_ps"].n += len(chunks)
            S["pvstep"].n += len(chunks)

            @on("scalar")
            def _(e, gbase=gbase, nch=len(chunks), pvbase=pvbase):
                for i in range(nch):
                    e.wait_ge(S["sc_ps"].h, gbase + i + 1)
                    if gbase + i >= 3:
                        e.wait_ge(S["pvstep"].h, gbase + i - 2)
                    e.activation(pring[:, (gbase + i) % 3, :], pb[i % 2][:],
                                 AF.Exp, scale=KAPPA).then_inc(S["px"].h, 1)
            S["px"].n += len(chunks)

            corr_chunks = [i for i, c in enumerate(chunks) if 4 * g <= c < 4 * g + 4]

            @on("vector")
            def _(e, g=g, chunks=tuple(chunks), gbase=gbase,
                  corr_chunks=tuple(corr_chunks)):
                for ci, i in enumerate(corr_chunks):
                    c = chunks[i]
                    jl = c - 4 * g
                    e.wait_ge(S["px"].h, gbase + i + 1)
                    if ci >= 2:
                        e.wait_ge(S["pvstep"].h, gbase + corr_chunks[ci - 2] + 1)
                    e.tensor_tensor(corrsc[:, ci % 2, :],
                                    pring[:, (gbase + i) % 3, jl * 128:(jl + 1) * 128],
                                    uptri[:], ALU.mult).then_inc(S["corr_i"].h, 1)
            S["corr_i"].n += 4
            gc[0] += len(chunks)
            pvst[0] += len(chunks)
            cr[0] += 4

            @on("tensor")
            def _(e, h=h, g=g, unit=unit, denb=denb, swb=swp_[0]):
                if unit >= 1:
                    e.wait_ge(S["rep_ev"].h, 3 * unit)

                def swasc(jl):
                    j = 4 * g + jl
                    cands = [j, max(8 + j - 1, 8), 8 + j]
                    if jl >= 2:
                        e.wait_ge(S["pswa_e"].h, swb + jl - 1)
                    bank = pb[7 if jl % 2 == 0 else 4]
                    for p, cp in enumerate(cands):
                        mm = e.matmul(bank[:, p * 128:(p + 1) * 128],
                                      kr2[:, h % 2, cp * 128:(cp + 1) * 128],
                                      qr2[:, h % 2, j * 128:(j + 1) * 128],
                                      start=True, stop=True)
                    mm.then_inc(S["swa_ps"].h, 1)

                def swapv(jl):
                    j = 4 * g + jl
                    cands = [j, max(8 + j - 1, 8), 8 + j]
                    e.wait_ge(S["pswa_m"].h, swb + jl + 1)
                    for p, cp in enumerate(cands):
                        e.matmul(pb[3][:, jl * 128:(jl + 1) * 128], vr2[:, h % 2, cp, :],
                                 pswa2[:, jl % 2, p * 128:(p + 1) * 128],
                                 start=(jl == 0 and p == 0), stop=(jl == 3 and p == 2))
                        mm = e.matmul(denb[0:48, jl * 128:(jl + 1) * 128],
                                      toep[:, 15 - h:63 - h],
                                      pswa2[:, jl % 2, p * 128:(p + 1) * 128],
                                      start=False, stop=False)
                    mm.then_inc(S["swa_pv"].h, 1)

                for jl in range(4):
                    swasc(jl)
                    if jl >= 1:
                        swapv(jl - 1)
                swapv(3)
            S["swa_ps"].n += 4
            S["swa_pv"].n += 4

            @on("scalar")
            def _(e, swb=swp_[0]):
                for jl in range(4):
                    e.wait_ge(S["swa_ps"].h, swb + jl + 1)
                    e.activation(pswa[:, jl % 2, :], pb[7 if jl % 2 == 0 else 4][:, 0:384],
                                 AF.Exp, scale=KAPPA).then_inc(S["pswa_e"].h, 1)
            S["pswa_e"].n += 4

            @on("vector")
            def _(e, g=g, swb=swp_[0]):
                for jl in range(4):
                    e.wait_ge(S["pswa_e"].h, swb + jl + 1)
                    if jl >= 2:
                        e.wait_ge(S["swa_pv"].h, swb + jl - 1)
                    cls = 0 if (g == 0 and jl == 0) else 1
                    e.tensor_tensor(pswa2[:, jl % 2, :], pswa[:, jl % 2, :],
                                    swaarr[:, cls, :], ALU.mult).then_inc(S["pswa_m"].h, 1)
            S["pswa_m"].n += 4
            swp_[0] += 4

            @on("tensor")
            def _(e, h=h, g=g, denb=denb, unit=unit):
                e.wait_ge(S["pcT_rdy"].h, 2 * h + 2)
                e.matmul(pb[4][:], vcmpall[:, h, :], pcmpT[:, h % 2, g * 512:(g + 1) * 512],
                         start=True, stop=True).then_inc(S["pv_done"].h, 1)
                e.matmul(denb[0:48, :], toep32[:, 47 - h:95 - h],
                         pcmpT[:, h % 2, g * 512:(g + 1) * 512], start=False, stop=True
                         ).then_inc(S["den_done"].h, 1)
            S["pv_done"].n += 1
            S["den_done"].n += 1

            @on("vector")
            def _(e, g=g, denb=denb, unit=unit):
                e.wait_ge(S["den_done"].h, unit + 1)
                ci(e.tensor_tensor(varrep[0:48, :], denb[0:48, :],
                                   anyfix[:, g * 512:(g + 1) * 512], ALU.add))
                cw(e)
                e.tensor_scalar(lnv[0:48, :], varrep[0:48, :], 1e-20, None, ALU.max
                                ).then_inc(S["coef_u"].h, 1)
            S["coef_u"].n += 1

            @on("scalar")
            def _(e, unit=unit):
                e.wait_ge(S["coef_u"].h, unit + 1)
                ci(e.activation(varrep[0:48, :], lnv[0:48, :], AF.Ln), "scalar")
                cw(e, "scalar")
                e.activation(lnv[0:48, :], varrep[0:48, :], AF.Exp, scale=-1.0
                             ).then_inc(S["coef_r"].h, 1)
            S["coef_r"].n += 1

            @on("vector")
            def _(e, h=h, g=g, dpar=dpar, unit=unit):
                e.wait_ge(S["coef_r"].h, unit + 1)
                e.wait_ge(S["g_rdy"].h, 2)
                e.tensor_tensor(coefb[:, dpar, :], lnv[0:48, :],
                                gT[:, g * 512:(g + 1) * 512], ALU.mult)
                e.engine_nop().then_inc(S["coef_rdy"].h, 1)
            S["coef_rdy"].n += 1

            @on("tensor")
            def _(e, h=h, dpar=dpar, unit=unit, rb=rep_[0]):
                e.wait_ge(S["coef_rdy"].h, unit + 1)
                e.wait_ge(S["pswa_e"].h, (unit + 1) * 4)
                if unit == 0:
                    e.wait_ge(S["c3_ld"].h, 16)
                for b in range(3):
                    if rb + b >= 1:
                        e.wait_ge(S["rep_ev"].h, rb + b)
                    e.matmul(pb[7][:], rowsel[:, (16 * b + h) * 128:(16 * b + h + 1) * 128],
                             coefb[0:48, dpar, :], start=True, stop=True
                             ).then_inc(S["rep_ps"].h, 1)
            S["rep_ps"].n += 3

            @on("vector")
            def _(e, rb=rep_[0]):
                for b in range(3):
                    e.wait_ge(S["rep_ps"].h, rb + b + 1)
                    e.tensor_copy(crep[:, b, :], pb[7][:]).then_inc(S["rep_ev"].h, 1)
            S["rep_ev"].n += 3
            rep_[0] += 3

            @on("vector")
            def _(e, h=h, g=g, unit=unit):
                e.wait_ge(S["rep_ev"].h, 3 * (unit + 1))
                e.wait_ge(S["pv_done"].h, unit + 1)
                ci(e.tensor_tensor(csc[:, 0, :], pb[4][:], crep[:, 0, :], ALU.mult))
                ci(e.tensor_tensor(csc[:, 1, :], pb[2][:], crep[:, 1, :], ALU.mult))
                cw(e)
                ci(e.tensor_tensor(csc[:, 2, :], csc[:, 0, :], csc[:, 1, :], ALU.add))
                cw(e)
                ci(e.tensor_tensor(csc[:, 0, :], pb[3][:], crep[:, 2, :], ALU.mult))
                cw(e)
                e.tensor_tensor(oT[:, h, g * 512:(g + 1) * 512], csc[:, 2, :], csc[:, 0, :],
                                ALU.add).then_inc(S["cmb"].h, 1)
            S["cmb"].n += 1
            units[0] += 1

    CMB_ALL = S["cmb"].n

    if "o" in tap_t:
        @on("sync")
        def _(e):
            e.wait_ge(S["cmb"].h, CMB_ALL)
            e.dma_start(out=tap_t["o"], in_=oT[:]).then_inc(S["tap_st"].h, 16)
        S["tap_st"].n += 16

    XOL0 = DS["xol"].uses
    WOL0 = DS["wqk"].uses
    for _i in range(NDC):
        DS["xol"].issued(XOL0 + _i)
        DS["wqk"].issued(WOL0 + _i)
    @on("sync")
    def _(e):
        e.wait_ge(S["cmb"].h, CMB_ALL)
        for dc in range(NDC):
            if dc >= 2:
                e.wait_ge(S["x2_rdy"].h, 2 * (dc - 1))
            e.dma_start(out=xo[:, dc % 2, :],
                        in_=xT.rearrange("(c p) t -> p c t", p=128)[:, dc, 0:OT]
                        ).then_inc(DS["xol"].handle(XOL0 + dc), 16)

    @on("gpsimd")
    def _(e):
        e.wait_ge(S["cmb"].h, CMB_ALL)
        for dc in range(NDC):
            if dc >= 2:
                e.wait_ge(S["wo_ps"].h, 2 * (dc - 1))
            e.dma_start(out=wo_sb[:, dc % 2], in_=wo[dc].rearrange("c p n -> p c n")
                        ).then_inc(DS["wol"].handle(WOL0 + dc), 16)

    @on("tensor")
    def _(e):
        e.wait_ge(S["cmb"].h, CMB_ALL)
        for dc in range(NDC):
            DS["wol"].wait(e, WOL0 + dc)
            if dc >= 2:
                e.wait_ge(S["x2_rdy"].h, 2 * (dc - 1))
            for half in range(2):
                bank = pb[(dc % 2) * 2 + half]
                for ec in range(NDC):
                    mm = e.matmul(bank[:], wo_sb[:, dc % 2, ec, :],
                                  oT[:, ec, half * 512:(half + 1) * 512],
                                  start=(ec == 0), stop=(ec == NDC - 1))
                mm.then_inc(S["wo_ps"].h, 1)
    S["wo_ps"].n += 32

    @on("vector")
    def _(e):
        e.wait_ge(S["mod_rdy"].h, 2)
        for dc in range(NDC):
            DS["xol"].wait(e, XOL0 + dc)
            for half in range(2):
                e.wait_ge(S["wo_ps"].h, 2 * dc + half + 1)
                e.scalar_tensor_tensor(x2[:, dc, half * 512:(half + 1) * 512],
                                       pb[(dc % 2) * 2 + half][:],
                                       modv[:, 32 + dc:33 + dc],
                                       xo[:, dc % 2, half * 512:(half + 1) * 512],
                                       ALU.mult, ALU.add).then_inc(S["x2_rdy"].h, 1)
            e.wait_ge(S["x2_rdy"].h, 2 * dc + 2)
            e.tensor_copy(x2b(dc), x2[:, dc, :]).then_inc(S["x2b_rdy"].h, 1)
    S["x2_rdy"].n += 32
    S["x2b_rdy"].n += NDC

    if "x2" in tap_t:
        @on("sync")
        def _(e):
            e.wait_ge(S["x2_rdy"].h, 32)
            e.dma_start(out=tap_t["x2"], in_=x2[:]).then_inc(S["tap_st"].h, 16)
        S["tap_st"].n += 16

    @on("scalar")
    def _(e):
        for w in range(2):
            for dc in range(NDC):
                i = w * NDC + dc
                e.wait_ge(S["x2b_rdy"].h, dc + 1)
                if i >= 2:
                    e.wait_ge(S["sq2_free"].h, i - 1)
                e.activation(sqring[:, i % 2, :], x2b(dc)[:, w * 512:(w + 1) * 512],
                             AF.Square).then_inc(S["sq2_rdy"].h, 1)
    S["sq2_rdy"].n += 32

    @on("tensor")
    def _(e):
        e.wait_ge(S["x2_rdy"].h, 32)
        for w in range(2):
            if w >= 1:
                e.wait_ge(S["ln2_mid"].h, 2 * w - 1)
            for dc in range(NDC):
                i = w * NDC + dc
                e.wait_ge(S["x2b_rdy"].h, dc + 1)
                mm0 = e.matmul(pb[0][:], ones[:], x2b(dc)[:, w * 512:(w + 1) * 512],
                               start=(dc == 0), stop=(dc == NDC - 1))
                if dc == NDC - 1:
                    mm0.then_inc(S["st2_ps"].h, 1)
                e.wait_ge(S["sq2_rdy"].h, i + 1)
                e.matmul(pb[1][:], ones[:], sqring[:, i % 2, :],
                         start=(dc == 0), stop=(dc == NDC - 1)).then_inc(S["sq2_free"].h, 1)
    S["st2_ps"].n += 2
    S["sq2_free"].n += 32

    @on("scalar")
    def _(e):
        for w in range(2):
            e.wait_ge(S["ln2_mid"].h, 2 * w + 1)
            ci(e.activation(lnv[:], varrep[:], AF.Ln, bias=epsv[:]), "scalar")
            cw(e, "scalar")
            e.activation(rstdrep[:], lnv[:], AF.Exp, scale=-0.5).then_inc(S["ln2_mid"].h, 1)

    @on("vector")
    def _(e):
        e.wait_ge(S["mod_rdy"].h, 2)
        for w in range(2):
            e.wait_ge(S["st2_ps"].h, w + 1)
            e.wait_ge(S["sq2_free"].h, 16 * (w + 1))
            ci(e.tensor_scalar(mrep[:], pb[0][:], 1.0 / D, None, ALU.mult))
            cw(e)
            ci(e.tensor_tensor(lnv[:], mrep[:], mrep[:], ALU.mult))
            cw(e)
            e.scalar_tensor_tensor(varrep[:], pb[1][:], 1.0 / D, lnv[:],
                                   ALU.mult, ALU.subtract).then_inc(S["ln2_mid"].h, 1)
            e.wait_ge(S["ln2_mid"].h, 2 * w + 2)
            for dc in range(NDC):
                ci(e.tensor_tensor(ropeA[:], x2[:, dc, w * 512:(w + 1) * 512], mrep[:],
                                   ALU.subtract))
                cw(e)
                ci(e.tensor_tensor(scrB[:], ropeA[:], rstdrep[:], ALU.mult))
                cw(e)
                e.tensor_scalar(h2sb[:, dc, w * 512:(w + 1) * 512], scrB[:],
                                onepc[:, 16 + dc:17 + dc], modv[:, 48 + dc:49 + dc],
                                ALU.mult, ALU.add).then_inc(S["h2_rdy"].h, 1)
    S["ln2_mid"].n = 4
    S["h2_rdy"].n += 32
    H2_ALL = S["h2_rdy"].n

    if "h2" in tap_t:
        @on("sync")
        def _(e):
            e.wait_ge(S["h2_rdy"].h, H2_ALL)
            e.dma_start(out=tap_t["h2"], in_=h2sb[:]).then_inc(S["tap_st"].h, 16)
        S["tap_st"].n += 16

    @on("sync")
    def _(e):
        e.wait_ge(S["h2_rdy"].h, NDC + 4)
        e.dma_start(out=outT.rearrange("(c p) t -> p c t", p=128)[:, 0:4, :],
                    in_=x2[:, 0:4, :]).then_inc(S["x2_sp"].h, 16)
        e.wait_ge(S["h2_rdy"].h, H2_ALL)
        e.wait_ge(S["x2_sp"].h, 16)
        e.dma_start(out=outT.rearrange("(c p) t -> p c t", p=128)[:, 4:16, :],
                    in_=x2[:, 4:16, :]).then_inc(S["x2_sp"].h, 16)
    S["x2_sp"].n += 32

    WUL0 = DS["wul"].uses
    WLL0 = DS["wqk"].uses
    ACC0 = DS["acc"].uses
    assert (ACC0 % 2) == 0
    for _hc in range(64):
        DS["wul"].issued(WUL0 + _hc, 32)
    for _u in range(64):
        DS["wqk"].issued(WLL0 + _u)
        DS["acc"].issued(ACC0 + _u)
    @on("gpsimd")
    def _(e):
        e.wait_ge(S["x2_sp"].h, 16)

        def acc_dma(ua):
            if ua == 0:
                e.wait_ge(S["x2_sp"].h, 32)
            e.wait_ge(S["stg_rdy"].h, ua + 1)
            DS["acc"].wait(e, ACC0 + ua - 16)
            e.dma_start(out=outT.rearrange("(c p) t -> p c t", p=128)[:, ua % NDC, :],
                        in_=stage[:, ua % 2, :], accum_op=ALU.add
                        ).then_inc(DS["acc"].handle(ACC0 + ua), 16)

        def wll_load(u2):
            if u2 == 0:
                e.wait_ge(S["x2_sp"].h, 32)
            if u2 >= 2:
                e.wait_ge(S["dn_ps"].h, u2 - 1)
            e.dma_start(out=wl_sb[:, u2 % 2],
                        in_=wl[u2 % NDC, (u2 // NDC) * 16:(u2 // NDC) * 16 + 16
                               ].rearrange("c p n -> p c n")
                        ).then_inc(DS["wll"].handle(WLL0 + u2), 16)

        for grp in range(4):
            for hl in range(16):
                hc = grp * 16 + hl
                if hc >= 2:
                    e.wait_ge(S["up_ps"].h, 2 * (hc - 1))
                e.dma_start(out=wu_sb[:, hc % 2, 0], in_=wu0[hc].rearrange("c p n -> p c n")
                            ).then_inc(DS["wul"].handle(WUL0 + hc), 16)
                e.dma_start(out=wu_sb[:, hc % 2, 1], in_=wu1[hc].rearrange("c p n -> p c n")
                            ).then_inc(DS["wul"].handle(WUL0 + hc), 16)
                if grp >= 1:
                    u2 = (grp - 1) * 16 + hl
                    wll_load(u2)
                    if u2 >= 2:
                        acc_dma(u2 - 2)

        for hl in range(16):
            wll_load(48 + hl)
            acc_dma(46 + hl)
        acc_dma(62)
        acc_dma(63)

    @on("tensor")
    def _(e):

        def dn_dc(gp, dc):
            u = gp * NDC + dc
            abuf = act if gp % 2 == 0 else actB
            DS["wll"].wait(e, WLL0 + u)
            if dc == 0:
                e.wait_ge(S["act_rdy"].h, 32 * (gp + 1))
            if u >= 2:
                e.wait_ge(S["stg_rdy"].h, u - 1)
            for half in range(2):
                bank = pb[4 + (dc % 2) * 2 + half]
                for hl2 in range(16):
                    mm = e.matmul(bank[:], wl_sb[:, u % 2, hl2, :],
                                  abuf[:, hl2, half * 512:(half + 1) * 512],
                                  start=(hl2 == 0), stop=(hl2 == 15))
            mm.then_inc(S["dn_ps"].h, 1)

        for grp in range(4):
            for hl in range(16):
                hc = grp * 16 + hl
                DS["wul"].wait(e, WUL0 + hc)
                for half in range(2):
                    hh = 2 * hc + half
                    if hc == 0:
                        e.wait_ge(S["h2_rdy"].h, NDC * (half + 1))
                    if hh >= 2:
                        e.wait_ge(S["act_rdy"].h, hh - 1)
                    for proj in range(2):
                        bank = pb[2 * (hh % 2) + proj]
                        for dc in range(NDC):
                            mm = e.matmul(bank[:], wu_sb[:, hc % 2, proj, dc, :],
                                          h2sb[:, dc, half * 512:(half + 1) * 512],
                                          start=(dc == 0), stop=(dc == NDC - 1))
                    mm.then_inc(S["up_ps"].h, 1)
                if grp >= 1:
                    dn_dc(grp - 1, hl)
        for dc in range(NDC):
            dn_dc(3, dc)
    S["up_ps"].n += 128
    S["dn_ps"].n += 64

    @on("scalar")
    def _(e):
        for hh in range(128):
            e.wait_ge(S["up_ps"].h, hh + 1)
            if hh >= 2:
                e.wait_ge(S["act_rdy"].h, hh - 1)
            e.activation(gel_sb[:, hh % 2, :], pb[2 * (hh % 2)][:],
                         AF.Gelu_apprx_tanh).then_inc(S["gel_rdy"].h, 1)
    S["gel_rdy"].n += 128

    @on("vector")
    def _(e):
        e.wait_ge(S["mod_rdy"].h, 2)

        def stage_u(u):
            e.wait_ge(S["dn_ps"].h, u + 1)
            DS["acc"].wait(e, ACC0 + u - 2)
            for half in range(2):
                e.tensor_scalar(stage[:, u % 2, half * 512:(half + 1) * 512],
                                pb[4 + (u % 2) * 2 + half][:],
                                modv[:, 80 + u % NDC:81 + u % NDC], None, ALU.mult)
            e.engine_nop().then_inc(S["stg_rdy"].h, 1)

        for grp in range(4):
            abuf = act if grp % 2 == 0 else actB
            for hl in range(16):
                hc = grp * 16 + hl
                for half in range(2):
                    hh = 2 * hc + half
                    if hl == 0 and half == 0 and grp >= 2:
                        e.wait_ge(S["dn_ps"].h, 16 * (grp - 1))
                    e.wait_ge(S["gel_rdy"].h, hh + 1)
                    e.tensor_tensor(abuf[:, hl, half * 512:(half + 1) * 512],
                                    gel_sb[:, hh % 2, :], pb[2 * (hh % 2) + 1][:],
                                    ALU.mult).then_inc(S["act_rdy"].h, 1)
                if grp >= 1:
                    stage_u((grp - 1) * NDC + hl)
        for u in range(48, 64):
            stage_u(u)
    S["act_rdy"].n += 128
    S["stg_rdy"].n += 64

    if "act" in tap_t:
        @on("sync")
        def _(e):
            e.wait_ge(S["act_rdy"].h, 32)
            e.dma_start(out=tap_t["act"], in_=act[:]).then_inc(S["tap_st"].h, 16)
        S["tap_st"].n += 16

    @on("sync")
    def _(e):
        DS["acc"].wait_all(e)
        if S["tap_st"].n:
            e.wait_ge(S["tap_st"].h, S["tap_st"].n)

    with nc.Block() as blk:
        @blk.sync
        def _(eng):
            for fn in OPS["sync"]:
                fn(eng)

        @blk.gpsimd
        def _(eng):
            for fn in OPS["gpsimd"]:
                fn(eng)

        @blk.tensor
        def _(eng):
            for fn in OPS["tensor"]:
                fn(eng)

        @blk.vector
        def _(eng):
            for fn in OPS["vector"]:
                fn(eng)

        @blk.scalar
        def _(eng):
            for fn in OPS["scalar"]:
                fn(eng)
    return nc


def kernel(**inputs):
    maps = _stage(inputs)
    nc = build_nc(DEBUG_TAPS)
    res = run_bass_kernel_spmd(nc, maps, core_ids=list(range(8)))
    out = np.zeros((4, NT, D), np.float32)
    for core in range(8):
        b, s = core // 2, core % 2
        own = _perm_ids(s)[0:OT]
        oTv = np.asarray(res.results[core]["outT"])
        out[b, own] = oTv.T
    return out
